# version 01
# baseline (speedup 1.0000x reference)
import contextlib
import numpy as np
import concourse.bass as bass
import concourse.mybir as mybir
from concourse.bass_utils import run_bass_kernel_spmd

F32 = mybir.dt.float32
BF16 = mybir.dt.bfloat16
U32 = mybir.dt.uint32
AF = mybir.ActivationFunctionType
ALU = mybir.AluOpType
AX = mybir.AxisListType

B, T, D = 2, 8192, 1024
TC = 2048
NCH = D // 128
NCORE = 8

ENGS = ("sync", "gpsimd", "tensor", "vector", "scalar")


class L:

    def __init__(self, name):
        self.nc = bass.Bass("TRN2", target_bir_lowering=False, name=name)
        self.es = contextlib.ExitStack()
        self.sem = {e: self.es.enter_context(self.nc.semaphore(f"sem_{e}"))
                    for e in ENGS}
        self.dsem = {e: self.es.enter_context(self.nc.semaphore(f"dma_{e}"))
                     for e in ("sync", "gpsimd", "scalar")}
        self._pfpool = [self.es.enter_context(self.nc.semaphore(f"pf{i}"))
                        for i in range(8)]
        self._pfcnt = [0] * 8
        self.pfmap = {}
        self.pftarget = {}
        self.cnt = {e: 0 for e in ENGS}
        self.dcnt = {e: 0 for e in self.dsem}
        self.seen = {e: {**{oe: 0 for oe in ENGS},
                         **{"d" + q: 0 for q in self.dsem}} for e in ENGS}
        self._phases = []

    def pfsem(self, key):
        if key not in self.pfmap:
            self.pfmap[key] = len(self.pfmap) % len(self._pfpool)
        return self.pfmap[key]

    def sbuf(self, name, shape, dt):
        return self.es.enter_context(self.nc.sbuf_tensor(name, shape, dt))

    def psum(self, name, shape, dt=F32):
        return self.es.enter_context(self.nc.psum_tensor(name, shape, dt))

    def inp(self, name, shape, dt=F32):
        return self.nc.dram_tensor(name, shape, dt, kind="ExternalInput").ap()

    def out(self, name, shape, dt=F32):
        return self.nc.dram_tensor(name, shape, dt, kind="ExternalOutput").ap()

    def dram(self, name, shape, dt=F32):
        return self.nc.dram_tensor(name, shape, dt, kind="Internal").ap()

    def dma(self, qname, eng, out, in_, pf=None, **kw):
        i = eng.dma_start(out=out, in_=in_, **kw)
        if pf is not None:
            idx = self.pfsem(pf)
            i.then_inc(self._pfpool[idx], 16)
            self._pfcnt[idx] += 16
            self.pftarget[pf] = self._pfcnt[idx]
            return self.pftarget[pf]
        i.then_inc(self.dsem[qname], 16)
        self.dcnt[qname] += 16
        return self.dcnt[qname]

    def wait_pf(self, eng, key):
        idx = self.pfmap[key]
        eng.wait_ge(self._pfpool[idx], self.pftarget[key])

    def phase(self, work):
        self._phases.append(work)

    def build(self):
        nc = self.nc
        with nc.Block() as blk:
            for work in self._phases:
                want = dict(self.cnt)
                dwant = dict(self.dcnt)
                for e in ENGS:
                    if e not in work:
                        continue
                    fn = work[e]

                    def section(eng, fn=fn, e=e, want=want, dwant=dwant):
                        sn = self.seen[e]
                        for oe in ENGS:
                            if oe != e and want[oe] > sn[oe]:
                                eng.wait_ge(self.sem[oe], want[oe])
                                sn[oe] = want[oe]
                        for oq, vv in dwant.items():
                            if vv > sn["d" + oq]:
                                eng.wait_ge(self.dsem[oq], vv)
                                sn["d" + oq] = vv
                        last = fn(eng, self)
                        if last is not None and hasattr(last, "then_inc"):
                            last.then_inc(self.sem[e], 1)
                        elif e in ("tensor", "sync"):
                            eng.nop().then_inc(self.sem[e], 1)
                        else:
                            eng.drain().then_inc(self.sem[e], 1)
                        self.cnt[e] += 1

                    getattr(blk, e)(section)
        return nc

    def close(self):
        self.es.close()



def colsum(t, ones_bf, src, ps_row, n, start, stop=False):
    i = None
    for j in range(0, n, 512):
        i = t.matmul(ps_row[0:1, j:j + 512], ones_bf[:, 0:1], src[:, j:j + 512],
                     start=start, stop=stop)
    return i


def bcast(t, ones_row_f32, row, ps_bc, n):
    i = None
    for j in range(0, n, 512):
        w = min(512, n - j)
        i = t.matmul(ps_bc[:, j:j + w], ones_row_f32[0:1, :], row[0:1, j:j + w],
                     start=True, stop=True)
    return i



def build_launchA(max_phases=None):
    l = L("launchA")
    x = l.inp("x", [D, TC], F32)
    n1w = l.inp("n1w", [128, NCH], F32)
    wq = l.inp("wq", [D, D], F32)
    wk = l.inp("wk", [D, D], F32)
    wv = l.inp("wv", [D, D], F32)
    bw = l.inp("bw", [128, NCH], F32)
    bb = l.inp("bb", [128, 1], F32)
    q_o = l.out("q", [D, TC], BF16)
    k_o = l.out("k", [D, TC], BF16)
    v_o = l.out("v", [D, TC], BF16)
    beta_o = l.out("beta_tm", [128, TC // 128], F32)

    h_sb = l.sbuf("h", [128, NCH * TC], BF16)
    xc = l.sbuf("xc", [128, TC], F32)
    sq = l.sbuf("sq", [128, TC], BF16)
    w_sb = [l.sbuf(f"w{i}", [128, NCH * D], BF16) for i in range(2)]
    out_sb = l.sbuf("outb", [128, NCH * TC], BF16)
    n1w_sb = l.sbuf("n1w_sb", [128, NCH], F32)
    bw_sb = l.sbuf("bw_sb", [128, NCH], BF16)
    bb_sb = l.sbuf("bb_sb", [128, 1], F32)
    ones_bf = l.sbuf("ones_bf", [128, 1], BF16)
    ones_row = l.sbuf("ones_row", [1, 128], F32)
    row_a = l.sbuf("row_a", [1, TC], F32)
    row_b = l.sbuf("row_b", [1, TC], F32)
    beta_sb = l.sbuf("beta_sb", [128, TC // 128], F32)
    cst = l.sbuf("cst", [128, 4], F32)

    u1 = l.psum("u1", [128, TC], F32)
    bps = l.psum("bps", [128, TC // 128], F32)
    pp = [l.psum(f"pp{i}", [128, 512], F32) for i in range(2)]

    wv_views = {
        "q": wq.rearrange("(c p) o -> p c o", p=128),
        "k": wk.rearrange("(c p) o -> p c o", p=128),
        "v": wv.rearrange("(c p) o -> p c o", p=128),
    }
    w_sb3 = [w_sb[i].ap().rearrange("p (c o) -> p c o", c=NCH) for i in range(2)]

    def p0_sync(e, l):
        l.dma("sync", e, n1w_sb[:], n1w)
        l.dma("sync", e, bb_sb[:], bb)

    def p0_gp(e, l):
        l.dma("gpsimd", e, w_sb3[0], wv_views["q"], pf="q")
        l.dma("gpsimd", e, bw_sb[:], bw)

    def p0_vec(e, l):
        e.memset(ones_bf[:], 1.0)
        e.memset(ones_row[:], 1.0)
        e.memset(cst[:, 0:1], 1e-6)
        e.memset(cst[:, 1:2], 1e-5)
        e.memset(cst[:, 2:3], 1e-8)

    l.phase({"sync": p0_sync, "gpsimd": p0_gp, "vector": p0_vec})

    for ci in range(NCH):
        l.phase({"sync": (lambda e, l, ci=ci:
                          l.dma("sync", e, xc[:], x[ci * 128:(ci + 1) * 128, :]))})
        l.phase({"scalar": (lambda e, l:
                            e.activation(out=sq[:], in_=xc[:], func=AF.Square))})
        l.phase({"tensor": (lambda e, l, ci=ci:
                            colsum(e, ones_bf, sq, u1, TC, start=(ci == 0),
                                   stop=(ci == NCH - 1)))})

    l.phase({"scalar": (lambda e, l:
                        e.activation(out=row_a[0:1, :], in_=u1[0:1, :],
                                     func=AF.Sqrt, scale=1.0 / D, bias=cst[0:1, 0:1]))})
    l.phase({"vector": (lambda e, l: e.reciprocal(row_b[0:1, :], row_a[0:1, :]))})
    l.phase({"tensor": (lambda e, l: bcast(e, ones_row, row_b, u1, TC))})

    for ci in range(NCH):
        l.phase({"sync": (lambda e, l, ci=ci:
                          l.dma("sync", e, xc[:], x[ci * 128:(ci + 1) * 128, :]))})
        l.phase({"vector": (lambda e, l, ci=ci:
                            e.scalar_tensor_tensor(
                                out=h_sb[:, ci * TC:(ci + 1) * TC], in0=xc[:],
                                scalar=n1w_sb[:, ci:ci + 1], in1=u1[:],
                                op0=ALU.mult, op1=ALU.mult))})

    def p_beta(e, l):
        i = None
        for tt in range(TC // 128):
            for ci in range(NCH):
                i = e.matmul(bps[:, tt:tt + 1],
                             h_sb[:, ci * TC + tt * 128: ci * TC + (tt + 1) * 128],
                             bw_sb[:, ci:ci + 1],
                             start=(tt == 0 and ci == 0),
                             stop=(tt == TC // 128 - 1 and ci == NCH - 1))
        return i
    l.phase({"tensor": p_beta})
    l.phase({"scalar": (lambda e, l:
                        e.activation(out=beta_sb[:], in_=bps[:], func=AF.Sigmoid,
                                     bias=bb_sb[:, 0:1]))})
    l.phase({"gpsimd": (lambda e, l: l.dma("gpsimd", e, beta_o, beta_sb[:]))})

    def proj(wbuf, pf_key, next_key):
        for do in range(NCH):
            for nt in range(TC // 512):
                ps = pp[(do * (TC // 512) + nt) % 2]

                def p_mm(e, l, do=do, nt=nt, ps=ps, first=(do == 0 and nt == 0)):
                    if first:
                        l.wait_pf(e, pf_key)
                    i = None
                    for ci in range(NCH):
                        i = e.matmul(ps[:],
                                     w_sb[wbuf][:, ci * D + do * 128:ci * D + (do + 1) * 128],
                                     h_sb[:, ci * TC + nt * 512:ci * TC + (nt + 1) * 512],
                                     start=(ci == 0), stop=(ci == NCH - 1))
                    return i
                work = {"tensor": p_mm}
                if do == 0 and nt == 0 and next_key is not None:
                    def p_pf(e, l, nk=next_key, wb=1 - wbuf):
                        l.dma("gpsimd", e, w_sb3[wb], wv_views[nk], pf=nk)
                    work["gpsimd"] = p_pf
                l.phase(work)
                l.phase({"scalar": (lambda e, l, do=do, nt=nt, ps=ps:
                                    e.copy(out=out_sb[:, do * TC + nt * 512:
                                                      do * TC + (nt + 1) * 512],
                                           in_=ps[:]))})

    def store(out_ap):
        def p_st(e, l):
            for ci in range(NCH):
                l.dma("gpsimd", e, out_ap[ci * 128:(ci + 1) * 128, :],
                      out_sb[:, ci * TC:(ci + 1) * TC])
        l.phase({"gpsimd": p_st})

    proj(0, "q", "k")
    store(q_o)

    proj(1, "k", "v")
    for ci in range(NCH):
        l.phase({"scalar": (lambda e, l, ci=ci:
                            e.activation(out=sq[:], in_=out_sb[:, ci * TC:(ci + 1) * TC],
                                         func=AF.Square))})
        l.phase({"tensor": (lambda e, l, ci=ci:
                            colsum(e, ones_bf, sq, u1, TC, start=(ci == 0),
                                   stop=(ci == NCH - 1)))})
    l.phase({"scalar": (lambda e, l:
                        e.activation(out=row_a[0:1, :], in_=u1[0:1, :], func=AF.Sqrt))})

    def p_knv(e, l):
        e.tensor_scalar_max(row_a[0:1, :], row_a[0:1, :], 1e-12)
        e.drain()
        e.reciprocal(row_b[0:1, :], row_a[0:1, :])
    l.phase({"vector": p_knv})
    l.phase({"tensor": (lambda e, l: bcast(e, ones_row, row_b, u1, TC))})

    def p_kmul(e, l):
        for ci in range(NCH):
            e.tensor_tensor(out=out_sb[:, ci * TC:(ci + 1) * TC],
                            in0=out_sb[:, ci * TC:(ci + 1) * TC],
                            in1=u1[:], op=ALU.mult)
    l.phase({"vector": p_kmul})
    store(k_o)

    proj(0, "v", None)
    store(v_o)

    def p_fin(e, l):
        e.wait_ge(l.dsem["gpsimd"], l.dcnt["gpsimd"])
    l.phase({"gpsimd": p_fin})

    if max_phases is not None:
        l._phases = l._phases[:max_phases]
    nc = l.build()
    l.close()
    return nc



def build_launchC(max_phases=None):
    l = L("launchC")
    NC64 = T // 128
    qt = l.inp("qt", [NC64, NCH, 128, 128], BF16)
    kt = l.inp("kt", [NC64, NCH, 128, 128], BF16)
    ktm = l.inp("ktm", [NC64, 128, D], BF16)
    bcol = l.inp("bcol", [128, NC64], F32)
    va = l.inp("va", [128, 2], F32)
    vsl = l.inp("vsl", [256, T], BF16)
    ident = l.inp("ident", [128, 128], BF16)
    nlmask = l.inp("nlmask", [128, 128], BF16)
    umaski = l.inp("umaski", [128, 128], BF16)
    ao = l.out("ao", [256, T], BF16)

    vts = l.sbuf("vts", [128, 2 * T], BF16)
    vtm = l.sbuf("vtm", [128, 2 * T], BF16)
    wT = l.sbuf("wT", [128, NC64 * 128], BF16)
    hS = l.sbuf("hS", [128, NC64 * 128], BF16)
    st_sb = l.sbuf("st_sb", [128, NCH * 256], BF16)
    kc = l.sbuf("kc", [128, D], BF16)
    qc = l.sbuf("qc", [128, D], BF16)
    kmc = l.sbuf("kmc", [128, D], BF16)
    rhs_sb = l.sbuf("rhs_sb", [128, 256], BF16)
    u_sb = l.sbuf("u_sb", [128, 256], BF16)
    d0 = l.sbuf("d0", [128, T], BF16)
    d1 = l.sbuf("d1", [128, T], BF16)
    sS = l.sbuf("sS", [128, 128], BF16)
    sT = l.sbuf("sT", [128, 128], BF16)
    sX = l.sbuf("sX", [128, 128], BF16)
    sY = l.sbuf("sY", [128, 128], BF16)
    bc_sb = l.sbuf("bc_sb", [128, NC64], F32)
    va_sb = l.sbuf("va_sb", [128, 2], F32)
    alt = l.sbuf("alt", [128, 2], F32)
    oma = l.sbuf("oma", [128, 2], F32)
    id_sb = l.sbuf("id_sb", [128, 128], BF16)
    nl_sb = l.sbuf("nl_sb", [128, 128], BF16)
    ui_sb = l.sbuf("ui_sb", [128, 128], BF16)

    st_ps = [l.psum(f"st{i}", [128, 512], F32) for i in range(4)]
    ps_b = l.psum("ps_b", [128, 512], F32)
    ps_c = l.psum("ps_c", [128, 512], F32)
    ps_t = l.psum("ps_t", [128, 512], BF16)

    ktv = kt.rearrange("c dc p t -> c p dc t")
    qtv = qt.rearrange("c dc p t -> c p dc t")
    kc3 = kc.ap().rearrange("p (dc t) -> p dc t", dc=NCH)
    qc3 = qc.ap().rearrange("p (dc t) -> p dc t", dc=NCH)
    vslv = vsl.rearrange("(j p) t -> j p t", p=128)
    aov = ao.rearrange("(j p) t -> j p t", p=128)

    def p0(e, l):
        l.dma("sync", e, vts[:, 0:T], vslv[0])
        l.dma("sync", e, vts[:, T:2 * T], vslv[1])
        l.dma("sync", e, bc_sb[:], bcol)
        l.dma("sync", e, va_sb[:], va)
        l.dma("sync", e, id_sb[:], ident)
        l.dma("sync", e, nl_sb[:], nlmask)
        l.dma("sync", e, ui_sb[:], umaski)
    l.phase({"sync": p0})

    l.phase({"scalar": (lambda e, l: e.activation(out=alt[:], in_=va_sb[:],
                                                  func=AF.Sigmoid))})
    def p_ema_v(e, l):
        e.tensor_scalar(out=oma[:], in0=alt[:], scalar1=-1.0, scalar2=1.0,
                        op0=ALU.mult, op1=ALU.add)
        e.drain()
        for j in range(2):
            e.tensor_scalar(out=d1[:], in0=vts[:, j * T:(j + 1) * T],
                            scalar1=alt[:, j:j + 1], scalar2=None, op0=ALU.mult)
            e.memset(d0[:], 1.0)
            e.drain()
            e.tensor_scalar(out=d0[:], in0=d0[:], scalar1=oma[:, j:j + 1],
                            scalar2=None, op0=ALU.mult)
            e.drain()
            e.tensor_tensor_scan(out=vts[:, j * T:(j + 1) * T], data0=d0[:],
                                 data1=d1[:], initial=0.0,
                                 op0=ALU.mult, op1=ALU.add)
            e.drain()
    l.phase({"vector": p_ema_v})

    for g in range(NC64 * 2 // 4):
        def p_tr(e, l, g=g):
            i = None
            for r in range(4):
                idx = g * 4 + r
                j, c = idx % 2, idx // 2
                i = e.transpose(ps_t[:, r * 128:(r + 1) * 128],
                                vts[:, j * T + c * 128: j * T + (c + 1) * 128],
                                id_sb[:])
            return i
        l.phase({"tensor": p_tr})

        def p_tre(e, l, g=g):
            for r in range(4):
                idx = g * 4 + r
                j, c = idx % 2, idx // 2
                e.copy(out=vtm[:, c * 256 + j * 128: c * 256 + (j + 1) * 128],
                       in_=ps_t[:, r * 128:(r + 1) * 128])
        l.phase({"scalar": p_tre})

    for c in range(NC64):
        def p_ld(e, l, c=c):
            l.dma("sync", e, kc3, ktv[c])
            l.dma("sync", e, qc3, qtv[c])
        l.phase({"sync": p_ld})

        def p_kk(e, l, c=c):
            i = None
            for dc in range(NCH):
                e.matmul(ps_b[:, 0:128], kc[:, dc * 128:(dc + 1) * 128],
                         kc[:, dc * 128:(dc + 1) * 128],
                         start=(dc == 0), stop=(dc == NCH - 1))
                i = e.matmul(ps_c[:, 0:128], kc[:, dc * 128:(dc + 1) * 128],
                             qc[:, dc * 128:(dc + 1) * 128],
                             start=(dc == 0), stop=(dc == NCH - 1))
            return i
        l.phase({"tensor": p_kk})

        def p_xh(e, l, c=c):
            e.scalar_tensor_tensor(out=sX[:], in0=ps_b[:, 0:128],
                                   scalar=bc_sb[:, c:c + 1], in1=nl_sb[:],
                                   op0=ALU.mult, op1=ALU.mult)
            e.scalar_tensor_tensor(out=hS[:, c * 128:(c + 1) * 128],
                                   in0=ps_c[:, 0:128], scalar=1.0,
                                   in1=ui_sb[:], op0=ALU.mult, op1=ALU.mult)
        l.phase({"vector": p_xh})

        l.phase({"tensor": (lambda e, l: e.transpose(ps_t[:, 0:128], sX[:], id_sb[:]))})
        l.phase({"scalar": (lambda e, l: e.copy(out=sY[:], in_=ps_t[:, 0:128]))})

        def p_i1(e, l):
            e.matmul(ps_c[:, 0:128], sY[:], sX[:], start=True, stop=True)
            return e.matmul(ps_c[:, 128:256], sX[:], sY[:], start=True, stop=True)
        l.phase({"tensor": p_i1})

        def p_i2(e, l):
            e.tensor_tensor(out=sS[:], in0=sX[:], in1=id_sb[:], op=ALU.add)
            e.tensor_tensor(out=sT[:], in0=sY[:], in1=id_sb[:], op=ALU.add)
        l.phase({"vector": p_i2})

        def p_i3(e, l):
            e.copy(out=sX[:], in_=ps_c[:, 0:128])
            e.drain()
            e.copy(out=sY[:], in_=ps_c[:, 128:256])
        l.phase({"scalar": p_i3})

        for it in range(5):
            def p_mm4(e, l):
                e.matmul(ps_c[:, 0:128], sY[:], sS[:], start=True, stop=True)
                e.matmul(ps_c[:, 128:256], sS[:], sY[:], start=True, stop=True)
                e.matmul(ps_c[:, 256:384], sY[:], sX[:], start=True, stop=True)
                return e.matmul(ps_c[:, 384:512], sX[:], sY[:], start=True, stop=True)
            l.phase({"tensor": p_mm4})

            def p_ev4(e, l):
                e.tensor_tensor(out=sS[:], in0=sS[:], in1=ps_c[:, 0:128], op=ALU.add)
                e.drain()
                e.tensor_tensor(out=sT[:], in0=sT[:], in1=ps_c[:, 128:256], op=ALU.add)
            l.phase({"vector": p_ev4})

            def p_ev4b(e, l):
                e.copy(out=sX[:], in_=ps_c[:, 256:384])
                e.drain()
                e.copy(out=sY[:], in_=ps_c[:, 384:512])
            l.phase({"scalar": p_ev4b})

        l.phase({"tensor": (lambda e, l: e.matmul(ps_c[:, 0:128], sS[:], sY[:],
                                                  start=True, stop=True))})
        l.phase({"vector": (lambda e, l, c=c:
                            e.tensor_tensor(out=wT[:, c * 128:(c + 1) * 128],
                                            in0=sT[:], in1=ps_c[:, 0:128],
                                            op=ALU.add))})

    def p_z(e, l):
        e.memset(st_sb[:], 0.0)
    l.phase({"vector": p_z})

    for c in range(NC64):
        def p_ld(e, l, c=c):
            l.dma("sync", e, kc3, ktv[c])
            l.dma("sync", e, qc3, qtv[c])
            l.dma("sync", e, kmc[:], ktm[c])
        l.phase({"sync": p_ld})

        def p_ks(e, l, c=c):
            i = None
            for dc in range(NCH):
                i = e.matmul(ps_b[:, 0:256], kc[:, dc * 128:(dc + 1) * 128],
                             st_sb[:, dc * 256:(dc + 1) * 256],
                             start=(dc == 0), stop=(dc == NCH - 1))
            return i
        l.phase({"tensor": p_ks})

        def p_rhs(e, l, c=c):
            e.scalar_tensor_tensor(out=rhs_sb[:], in0=ps_b[:, 0:256],
                                   scalar=-1.0, in1=vtm[:, c * 256:(c + 1) * 256],
                                   op0=ALU.mult, op1=ALU.add)
            e.drain()
            e.tensor_scalar(out=rhs_sb[:], in0=rhs_sb[:],
                            scalar1=bc_sb[:, c:c + 1], scalar2=None, op0=ALU.mult)
        l.phase({"vector": p_rhs})

        l.phase({"tensor": (lambda e, l, c=c:
                            e.matmul(ps_b[:, 256:512], wT[:, c * 128:(c + 1) * 128],
                                     rhs_sb[:], start=True, stop=True))})
        l.phase({"scalar": (lambda e, l: e.copy(out=u_sb[:], in_=ps_b[:, 256:512]))})

        def p_ost(e, l, c=c):
            for ic in range(2):
                for dc in range(NCH):
                    e.matmul(ps_c[:, ic * 128:(ic + 1) * 128],
                             st_sb[:, dc * 256 + ic * 128: dc * 256 + (ic + 1) * 128],
                             qc[:, dc * 128:(dc + 1) * 128],
                             start=(dc == 0), stop=False)
                e.matmul(ps_c[:, ic * 128:(ic + 1) * 128],
                         u_sb[:, ic * 128:(ic + 1) * 128],
                         hS[:, c * 128:(c + 1) * 128], start=False, stop=True)
            i = None
            for j in range(NCH):
                i = e.matmul(st_ps[j // 2][:, (j % 2) * 256:(j % 2) * 256 + 256],
                             kmc[:, j * 128:(j + 1) * 128], u_sb[:],
                             start=(c == 0 and j % 2 == 0), stop=(j % 2 == 1),
                             skip_group_check=True)
            return i
        l.phase({"tensor": p_ost})

        def p_ev(e, l, c=c):
            for ic in range(2):
                e.copy(out=vts[:, ic * T + c * 128: ic * T + (c + 1) * 128],
                       in_=ps_c[:, ic * 128:(ic + 1) * 128])
            e.drain()
            for j in range(NCH):
                e.copy(out=st_sb[:, j * 256:(j + 1) * 256],
                       in_=st_ps[j // 2][:, (j % 2) * 256:(j % 2) * 256 + 256])
        l.phase({"scalar": p_ev})

    def p_st(e, l):
        for j in range(2):
            l.dma("gpsimd", e, aov[j], vts[:, j * T:(j + 1) * T])
        e.wait_ge(l.dsem["gpsimd"], l.dcnt["gpsimd"])
    l.phase({"gpsimd": p_st})

    if max_phases is not None:
        l._phases = l._phases[:max_phases]
    nc = l.build()
    l.close()
    return nc



def emit_ln_start(l, z_in, xb_sb, zc, sq, u1, ones_bf, ones_row, rows, cst_eps,
                  after_chunk=None):
    r0, r1 = rows
    def p_zl(e, l):
        for ci in range(NCH):
            l.dma("gpsimd", e, xb_sb[:, ci * TC:(ci + 1) * TC],
                  z_in[ci * 128:(ci + 1) * 128, :])
    l.phase({"gpsimd": p_zl})
    for ci in range(NCH):
        l.phase({"tensor": (lambda e, l, ci=ci:
                            colsum(e, ones_bf, xb_sb[:, ci * TC:(ci + 1) * TC],
                                   u1, TC, start=(ci == 0), stop=(ci == NCH - 1)))})
    l.phase({"scalar": (lambda e, l:
                        e.activation(out=r0[0:1, :], in_=u1[0:1, :],
                                     func=AF.Identity, scale=1.0 / D))})
    for ci in range(NCH):
        for hf in range(2):
            l.phase({"scalar": (lambda e, l, ci=ci, hf=hf:
                                e.activation(out=sq[:, 0:1024],
                                             in_=xb_sb[:, ci * TC + hf * 1024:
                                                       ci * TC + (hf + 1) * 1024],
                                             func=AF.Square))})

            def p_cs(e, l, ci=ci, hf=hf):
                i = None
                for j in range(0, 1024, 512):
                    i = e.matmul(u1[0:1, hf * 1024 + j:hf * 1024 + j + 512],
                                 ones_bf[:, 0:1], sq[:, j:j + 512],
                                 start=(ci == 0), stop=(ci == NCH - 1))
                return i
            l.phase({"tensor": p_cs})
    def p_rows_s(e, l):
        e.activation(out=sq[0:1, 0:TC], in_=r0[0:1, :], func=AF.Square)
    def p_rows_v(e, l):
        e.tensor_tensor(out=r1[0:1, :], in0=r0[0:1, :], in1=r0[0:1, :], op=ALU.mult)
    l.phase({"vector": p_rows_v})

    def p_rows_s2(e, l):
        pass
    def p_var_v(e, l):
        e.scalar_tensor_tensor(out=r1[0:1, :], in0=u1[0:1, :], scalar=1.0 / D,
                               in1=r1[0:1, :], op0=ALU.mult, op1=ALU.subtract)
    l.phase({"vector": p_var_v})
    l.phase({"scalar": (lambda e, l:
                        e.activation(out=r1[0:1, :], in_=r1[0:1, :], func=AF.Sqrt,
                                     bias=cst_eps))})
    l.phase({"vector": (lambda e, l: e.reciprocal(r1[0:1, :], r1[0:1, :]))})
    l.phase({"tensor": (lambda e, l: bcast(e, ones_row, r0, u1, TC))})
    for ci in range(NCH):
        l.phase({"vector": (lambda e, l, ci=ci:
                            e.scalar_tensor_tensor(
                                out=xb_sb[:, ci * TC:(ci + 1) * TC],
                                in0=xb_sb[:, ci * TC:(ci + 1) * TC], scalar=1.0,
                                in1=u1[:], op0=ALU.mult, op1=ALU.subtract))})
    l.phase({"tensor": (lambda e, l: bcast(e, ones_row, r1, u1, TC))})
    for ci in range(NCH):
        l.phase({"vector": (lambda e, l, ci=ci:
                            e.tensor_tensor(out=xb_sb[:, ci * TC:(ci + 1) * TC],
                                            in0=xb_sb[:, ci * TC:(ci + 1) * TC],
                                            in1=u1[:], op=ALU.mult))})
        if after_chunk is not None:
            w = after_chunk(ci)
            if w:
                l.phase(w)


def emit_proj(l, pf_key, next_pf, w_sb_cur, w_sb3_next, wview_next, h_sb, out_sb, pp):
    for do in range(NCH):
        for nt in range(TC // 512):
            ps = pp[(do * (TC // 512) + nt) % 2]

            def p_mm(e, l, do=do, nt=nt, ps=ps, first=(do == 0 and nt == 0)):
                if first and pf_key is not None:
                    l.wait_pf(e, pf_key)
                i = None
                for ci in range(NCH):
                    i = e.matmul(ps[:],
                                 w_sb_cur[:, ci * D + do * 128:ci * D + (do + 1) * 128],
                                 h_sb[:, ci * TC + nt * 512:ci * TC + (nt + 1) * 512],
                                 start=(ci == 0), stop=(ci == NCH - 1))
                return i
            work = {"tensor": p_mm}
            if do == 0 and nt == 0 and next_pf is not None:
                def p_pf(e, l, nk=next_pf):
                    l.dma("gpsimd", e, w_sb3_next, wview_next, pf=nk)
                work["gpsimd"] = p_pf
            l.phase(work)
            l.phase({"scalar": (lambda e, l, do=do, nt=nt, ps=ps:
                                e.copy(out=out_sb[:, do * TC + nt * 512:
                                                  do * TC + (nt + 1) * 512],
                                       in_=ps[:]))})



def build_launchD(max_phases=None):
    l = L("launchD")
    x = l.inp("x", [D, TC], F32)
    at = l.inp("at", [D, TC], BF16)
    wo = l.inp("wo", [D, D], F32)
    n2w = l.inp("n2w", [128, NCH], F32)
    wg = l.inp("wg", [D, 2048], F32)
    wu = l.inp("wu", [D, 2048], F32)
    wd = l.inp("wd", [2048, D], F32)
    z_o = l.out("z", [D, TC], F32)

    h2_sb = l.sbuf("h2", [128, NCH * TC], BF16)
    x1_sb = l.sbuf("x1", [128, NCH * TC], BF16)
    gu_sb = l.sbuf("gu", [128, 8192], BF16)
    wo_sb = l.sbuf("wo_sb", [128, NCH * D], BF16)
    wg_sb = l.sbuf("wg_sb", [128, NCH * 2048], BF16)
    wu_sb = l.sbuf("wu_sb", [128, NCH * 2048], BF16)
    wd_sb = l.sbuf("wd_sb", [128, 16 * D], BF16)
    n2w_sb = l.sbuf("n2w_sb", [128, NCH], F32)
    xc = l.sbuf("xc", [128, 512], F32)
    zt = l.sbuf("zt", [128, 256], F32)
    sq = l.sbuf("sq", [128, TC], BF16)
    ones_bf = l.sbuf("ones_bf", [128, 1], BF16)
    ones_row = l.sbuf("ones_row", [1, 128], F32)
    r0 = l.sbuf("r0", [1, TC], F32)
    cst = l.sbuf("cst", [128, 1], F32)

    u1 = l.psum("u1", [128, TC], F32)
    pp = [l.psum(f"pp{i}", [128, 512], F32) for i in range(2)]

    at_v = at.rearrange("(c p) t -> p c t", p=128)
    h2_3 = h2_sb.ap().rearrange("p (c t) -> p c t", c=NCH)
    wo_v = wo.rearrange("(c p) o -> p c o", p=128)
    wo_3 = wo_sb.ap().rearrange("p (c o) -> p c o", c=NCH)
    wg_v = wg.rearrange("(c p) o -> p c o", p=128)
    wg_3 = wg_sb.ap().rearrange("p (c o) -> p c o", c=NCH)
    wu_v = wu.rearrange("(c p) o -> p c o", p=128)
    wu_3 = wu_sb.ap().rearrange("p (c o) -> p c o", c=NCH)
    wd_v = wd.rearrange("(c p) o -> p c o", p=128)
    wd_3 = wd_sb.ap().rearrange("p (c o) -> p c o", c=16)

    def p0(e, l):
        l.dma("sync", e, h2_3, at_v)
        l.dma("sync", e, n2w_sb[:], n2w)
    def p0g(e, l):
        l.dma("gpsimd", e, wo_3, wo_v, pf="wo")
        l.dma("gpsimd", e, wg_3, wg_v, pf="wg")
        l.dma("gpsimd", e, wu_3, wu_v, pf="wu")
        l.dma("gpsimd", e, wd_3, wd_v, pf="wd")
    def p0v(e, l):
        e.memset(ones_bf[:], 1.0)
        e.memset(ones_row[:], 1.0)
        e.memset(cst[:], 1e-6)
    l.phase({"sync": p0, "gpsimd": p0g, "vector": p0v})

    for do in range(NCH):
        for nt in range(TC // 512):
            ps = pp[(do * 4 + nt) % 2]

            def p_mm(e, l, do=do, nt=nt, ps=ps, first=(do == 0 and nt == 0)):
                if first:
                    l.wait_pf(e, "wo")
                i = None
                for ci in range(NCH):
                    i = e.matmul(ps[:],
                                 wo_sb[:, ci * D + do * 128:ci * D + (do + 1) * 128],
                                 h2_sb[:, ci * TC + nt * 512:ci * TC + (nt + 1) * 512],
                                 start=(ci == 0), stop=(ci == NCH - 1))
                return i

            def p_xc(e, l, do=do, nt=nt):
                l.dma("sync", e, xc[:], x[do * 128:(do + 1) * 128, nt * 512:(nt + 1) * 512])
            l.phase({"tensor": p_mm, "sync": p_xc})
            l.phase({"vector": (lambda e, l, do=do, nt=nt, ps=ps:
                                e.scalar_tensor_tensor(
                                    out=x1_sb[:, do * TC + nt * 512:do * TC + (nt + 1) * 512],
                                    in0=ps[:], scalar=1.0, in1=xc[:],
                                    op0=ALU.mult, op1=ALU.add))})

    for ci in range(NCH):
        l.phase({"scalar": (lambda e, l, ci=ci:
                            e.activation(out=sq[:], in_=x1_sb[:, ci * TC:(ci + 1) * TC],
                                         func=AF.Square))})
        l.phase({"tensor": (lambda e, l, ci=ci:
                            colsum(e, ones_bf, sq, u1, TC, start=(ci == 0),
                                   stop=(ci == NCH - 1)))})
    l.phase({"scalar": (lambda e, l:
                        e.activation(out=r0[0:1, :], in_=u1[0:1, :], func=AF.Sqrt,
                                     scale=1.0 / D, bias=cst[0:1, 0:1]))})
    l.phase({"vector": (lambda e, l: e.reciprocal(r0[0:1, :], r0[0:1, :]))})
    l.phase({"tensor": (lambda e, l: bcast(e, ones_row, r0, u1, TC))})
    for ci in range(NCH):
        l.phase({"vector": (lambda e, l, ci=ci:
                            e.scalar_tensor_tensor(
                                out=h2_sb[:, ci * TC:(ci + 1) * TC],
                                in0=x1_sb[:, ci * TC:(ci + 1) * TC],
                                scalar=n2w_sb[:, ci:ci + 1], in1=u1[:],
                                op0=ALU.mult, op1=ALU.mult))})

    NS = 256
    for s in range(TC // NS):
        for fc in range(16):
            def p_mm(e, l, s=s, fc=fc, first=(s == 0 and fc == 0)):
                if first:
                    l.wait_pf(e, "wg")
                    l.wait_pf(e, "wu")
                for ci in range(NCH):
                    e.matmul(pp[0][:, 0:NS],
                             wg_sb[:, ci * 2048 + fc * 128:ci * 2048 + (fc + 1) * 128],
                             h2_sb[:, ci * TC + s * NS:ci * TC + (s + 1) * NS],
                             start=(ci == 0), stop=(ci == NCH - 1))
                i = None
                for ci in range(NCH):
                    i = e.matmul(pp[1][:, 0:NS],
                                 wu_sb[:, ci * 2048 + fc * 128:ci * 2048 + (fc + 1) * 128],
                                 h2_sb[:, ci * TC + s * NS:ci * TC + (s + 1) * NS],
                                 start=(ci == 0), stop=(ci == NCH - 1))
                return i
            l.phase({"tensor": p_mm})

            def p_ev(e, l, fc=fc):
                e.activation(out=gu_sb[:, fc * NS:(fc + 1) * NS], in_=pp[0][:, 0:NS],
                             func=AF.Silu)
            def p_ev2(e, l, fc=fc):
                e.tensor_copy(gu_sb[:, 4096 + fc * NS:4096 + (fc + 1) * NS],
                              pp[1][:, 0:NS])
            l.phase({"scalar": p_ev, "vector": p_ev2})

        l.phase({"vector": (lambda e, l:
                            e.tensor_tensor(out=gu_sb[:, 0:4096], in0=gu_sb[:, 0:4096],
                                            in1=gu_sb[:, 4096:8192], op=ALU.mult))})
        for do in range(NCH):
            def p_mm(e, l, s=s, do=do, first=(s == 0 and do == 0)):
                if first:
                    l.wait_pf(e, "wd")
                i = None
                for fc in range(16):
                    i = e.matmul(pp[0][:, 0:NS],
                                 wd_sb[:, fc * D + do * 128:fc * D + (do + 1) * 128],
                                 gu_sb[:, fc * NS:(fc + 1) * NS],
                                 start=(fc == 0), stop=(fc == 15))
                return i

            def p_xc(e, l, s=s, do=do):
                l.dma("sync", e, xc[:, 0:NS],
                      x[do * 128:(do + 1) * 128, s * NS:(s + 1) * NS])
            l.phase({"tensor": p_mm, "sync": p_xc})

            def p_z(e, l, s=s, do=do):
                e.scalar_tensor_tensor(
                    out=zt[:], in0=pp[0][:, 0:NS], scalar=1.0,
                    in1=x1_sb[:, do * TC + s * NS:do * TC + (s + 1) * NS],
                    op0=ALU.mult, op1=ALU.add)
                e.drain()
                e.tensor_tensor(out=zt[:], in0=zt[:], in1=xc[:, 0:NS], op=ALU.add)
            l.phase({"vector": p_z})
            l.phase({"gpsimd": (lambda e, l, s=s, do=do:
                                l.dma("gpsimd", e,
                                      z_o[do * 128:(do + 1) * 128, s * NS:(s + 1) * NS],
                                      zt[:]))})

    def p_fin(e, l):
        e.wait_ge(l.dsem["gpsimd"], l.dcnt["gpsimd"])
    l.phase({"gpsimd": p_fin})

    if max_phases is not None:
        l._phases = l._phases[:max_phases]
    nc = l.build()
    l.close()
    return nc



def build_launchE(max_phases=None):
    l = L("launchE")
    z_i = l.inp("z", [D, TC], F32)
    wrt = l.inp("wrt", [D, 8], F32)
    brr = l.inp("brr", [128, 8], F32)
    w1t = l.inp("w1t", [8, D, 4096], F32)
    w2t = l.inp("w2t", [8, 4096, D], F32)
    b1c = l.inp("b1c", [128, 8 * 32], F32)
    b2c = l.inp("b2c", [128, 8 * 8], F32)
    ident = l.inp("ident", [128, 128], BF16)
    z2_o = l.out("z2", [D, TC], F32)

    xb = l.sbuf("xb", [128, NCH * TC], BF16)
    w1_sb = l.sbuf("w1_sb", [128, NCH * 4096], BF16)
    w2_sb = l.sbuf("w2_sb", [128, 32 * D], BF16)
    h_sb = l.sbuf("h_sb", [128, 32 * 256], BF16)
    x2s = l.sbuf("x2s", [128, NCH * 256], BF16)
    msk = l.sbuf("msk", [128, 512], BF16)
    mrow = l.sbuf("mrow", [1, 512], BF16)
    lg = l.sbuf("lg", [128, 128], F32)
    mx = l.sbuf("mx", [128, 8], F32)
    oh = l.sbuf("oh", [128, 128], BF16)
    wrt_sb = l.sbuf("wrt_sb", [128, NCH * 8], BF16)
    brr_sb = l.sbuf("brr_sb", [128, 8], F32)
    b1_sb = l.sbuf("b1_sb", [128, 8 * 32], F32)
    b2_sb = l.sbuf("b2_sb", [128, 8 * 8], F32)
    id_sb = l.sbuf("id_sb", [128, 128], BF16)
    zt2 = l.sbuf("zt2", [128, 256], F32)
    sq = l.sbuf("sq", [128, 1024], BF16)
    ones_bf = l.sbuf("ones_bf", [128, 1], BF16)
    ones_row = l.sbuf("ones_row", [1, 128], F32)
    ones_rb = l.sbuf("ones_rb", [1, 128], BF16)
    r0 = l.sbuf("r0", [1, TC], F32)
    r1 = l.sbuf("r1", [1, TC], F32)
    tmp_bf = l.sbuf("tmp_bf", [128, 512], BF16)
    cst = l.sbuf("cst", [128, 1], F32)

    u1 = l.psum("u1", [128, TC], F32)
    pp = [l.psum(f"pp{i}", [128, 512], F32) for i in range(2)]
    bps = l.psum("bps", [128, 128], F32)
    msps = l.psum("msps", [128, 512], BF16)

    x2d = l.dram("x2d", [D, TC], BF16)
    x2dv = x2d.rearrange("(c p) t -> p c t", p=128)
    masks_d = l.dram("masks_d", [8, TC], BF16)

    wrt_v = wrt.rearrange("(c p) o -> p c o", p=128)
    wrt_3 = wrt_sb.ap().rearrange("p (c o) -> p c o", c=NCH)
    w1v = w1t.rearrange("e (c p) f -> e p c f", p=128)
    w1_3 = w1_sb.ap().rearrange("p (c f) -> p c f", c=NCH)
    w2v = w2t.rearrange("e (c p) o -> e p c o", p=128)
    w2_3 = w2_sb.ap().rearrange("p (c o) -> p c o", c=32)

    def p0(e, l):
        l.dma("sync", e, brr_sb[:], brr)
        l.dma("sync", e, b1_sb[:], b1c)
        l.dma("sync", e, b2_sb[:], b2c)
        l.dma("sync", e, id_sb[:], ident)
    def p0g(e, l):
        l.dma("gpsimd", e, wrt_3, wrt_v)
    def p0v(e, l):
        e.memset(ones_bf[:], 1.0)
        e.memset(ones_row[:], 1.0)
        e.memset(ones_rb[:], 1.0)
        e.memset(cst[:], 1e-5)
    l.phase({"sync": p0, "gpsimd": p0g, "vector": p0v})

    def after_chunk(ci):
        def p_lg(e, l, ci=ci):
            i = None
            for tt in range(TC // 128):
                i = e.matmul(bps[:, tt * 8:(tt + 1) * 8],
                             xb[:, ci * TC + tt * 128:ci * TC + (tt + 1) * 128],
                             wrt_sb[:, ci * 8:(ci + 1) * 8],
                             start=(ci == 0 and tt == 0),
                             stop=(ci == NCH - 1 and tt == TC // 128 - 1),
                             skip_group_check=True)
            return i
        return {"tensor": p_lg,
                "gpsimd": (lambda e, l, ci=ci:
                           l.dma("gpsimd", e, x2d[ci * 128:(ci + 1) * 128, :],
                                 xb[:, ci * TC:(ci + 1) * TC]))}
    emit_ln_start(l, z_i, xb, None, sq, u1, ones_bf, ones_row, (r0, r1),
                  cst[0:1, 0:1], after_chunk=after_chunk)

    def p_lgf(e, l):
        e.tensor_copy(lg[:], bps[:])
        e.drain()
        for tt in range(TC // 128):
            e.tensor_tensor(out=lg[:, tt * 8:(tt + 1) * 8],
                            in0=lg[:, tt * 8:(tt + 1) * 8], in1=brr_sb[:],
                            op=ALU.add)
    l.phase({"vector": p_lgf})
    for tt in range(TC // 128):
        def p_oh(e, l, tt=tt):
            e.max(mx[:], lg[:, tt * 8:(tt + 1) * 8])
            e.drain()
            e.tensor_scalar(out=oh[:, tt * 8:(tt + 1) * 8],
                            in0=lg[:, tt * 8:(tt + 1) * 8],
                            scalar1=mx[:, 0:1], scalar2=None, op0=ALU.is_equal)
        l.phase({"vector": p_oh})
    for g in range(TC // 128 * 8 // 4):
        def p_tr(e, l, g=g):
            i = None
            for r in range(4):
                idx = g * 4 + r
                tt, ee = idx // 8, idx % 8
                i = e.transpose(msps[0:1, r * 128:(r + 1) * 128],
                                oh[:, tt * 8 + ee:tt * 8 + ee + 1], id_sb[:])
            return i
        l.phase({"tensor": p_tr})

        def p_cp(e, l, g=g):
            e.copy(out=mrow[0:1, :], in_=msps[0:1, 0:512])
        l.phase({"scalar": p_cp})

        def p_md(e, l, g=g):
            idx = g * 4
            tt, ee = idx // 8, idx % 8
            l.dma("gpsimd", e,
                  masks_d.rearrange("(o e) t -> o e t", o=1)[0:1, ee:ee + 4,
                                                            tt * 128:(tt + 1) * 128],
                  mrow.ap().rearrange("p (a t) -> p a t", a=4))
        l.phase({"gpsimd": p_md})

    l.phase({"vector": (lambda e, l: e.memset(xb[:], 0.0))})

    SS = 256
    for ee in range(8):
        def p_pf(e, l, ee=ee):
            l.dma("gpsimd", e, w1_3, w1v[ee], pf=f"w1_{ee}")
            l.dma("gpsimd", e, w2_3, w2v[ee], pf=f"w2_{ee}")
        l.phase({"gpsimd": p_pf})
        for s in range(TC // SS):
            def p_xs(e, l, s=s):
                l.dma("sync", e, x2s.ap().rearrange("p (c t) -> p c t", c=NCH),
                      x2dv[:, :, s * SS:(s + 1) * SS])
            def p_mr(e, l, ee=ee, s=s):
                l.dma("sync", e, mrow[0:1, 0:SS],
                      masks_d[ee:ee + 1, s * SS:(s + 1) * SS])
            l.phase({"sync": (lambda e, l, a=p_xs, b=p_mr: (a(e, l), b(e, l)) and None)})

            def p_mbc(e, l):
                return e.matmul(u1[:, 0:SS], ones_rb[0:1, :], mrow[0:1, 0:SS],
                                start=True, stop=True)
            l.phase({"tensor": p_mbc})
            l.phase({"scalar": (lambda e, l: e.copy(out=msk[:, 0:SS], in_=u1[:, 0:SS]))})

            for fp in range(16):
                def p_mm(e, l, ee=ee, s=s, fp=fp, first=(s == 0 and fp == 0)):
                    if first:
                        l.wait_pf(e, f"w1_{ee}")
                    i = None
                    for half in range(2):
                        fc = fp * 2 + half
                        for ci in range(NCH):
                            i = e.matmul(pp[half][:, 0:SS],
                                         w1_sb[:, ci * 4096 + fc * 128:ci * 4096 + (fc + 1) * 128],
                                         x2s[:, ci * SS:(ci + 1) * SS],
                                         start=(ci == 0), stop=(ci == NCH - 1))
                    return i
                l.phase({"tensor": p_mm})

                def p_e0(e, l, ee=ee, fp=fp):
                    fc = fp * 2
                    e.activation(out=h_sb[:, fc * SS:(fc + 1) * SS], in_=pp[0][:, 0:SS],
                                 func=AF.Relu, bias=b1_sb[:, ee * 32 + fc:ee * 32 + fc + 1])
                def p_e1(e, l, ee=ee, fp=fp):
                    fc = fp * 2 + 1
                    e.tensor_scalar(out=h_sb[:, fc * SS:(fc + 1) * SS], in0=pp[1][:, 0:SS],
                                    scalar1=b1_sb[:, ee * 32 + fc:ee * 32 + fc + 1],
                                    scalar2=0.0, op0=ALU.add, op1=ALU.max)
                l.phase({"scalar": p_e0, "vector": p_e1})

            for do in range(NCH):
                def p_mm2(e, l, ee=ee, do=do, s=s, first=(s == 0 and do == 0)):
                    if first:
                        l.wait_pf(e, f"w2_{ee}")
                    i = None
                    for fc in range(32):
                        i = e.matmul(pp[0][:, 0:SS],
                                     w2_sb[:, fc * D + do * 128:fc * D + (do + 1) * 128],
                                     h_sb[:, fc * SS:(fc + 1) * SS],
                                     start=(fc == 0), stop=(fc == 31))
                    return i
                l.phase({"tensor": p_mm2})

                def p_y(e, l, ee=ee, s=s, do=do):
                    e.scalar_tensor_tensor(out=tmp_bf[:, 0:SS], in0=pp[0][:, 0:SS],
                                           scalar=b2_sb[:, ee * 8 + do:ee * 8 + do + 1],
                                           in1=msk[:, 0:SS], op0=ALU.add, op1=ALU.mult)
                    e.drain()
                    e.tensor_tensor(out=xb[:, do * TC + s * SS:do * TC + (s + 1) * SS],
                                    in0=xb[:, do * TC + s * SS:do * TC + (s + 1) * SS],
                                    in1=tmp_bf[:, 0:SS], op=ALU.add)
                l.phase({"vector": p_y})

    for ci in range(NCH):
        for hf in range(2):
            def p_x2(e, l, ci=ci, hf=hf):
                l.dma("sync", e, sq[:, 0:1024],
                      x2d[ci * 128:(ci + 1) * 128, hf * 1024:(hf + 1) * 1024])
            l.phase({"sync": p_x2})
            for s4 in range(4):
                def p_add(e, l, ci=ci, hf=hf, s4=s4):
                    e.tensor_tensor(out=zt2[:], in0=sq[:, s4 * 256:(s4 + 1) * 256],
                                    in1=xb[:, ci * TC + hf * 1024 + s4 * 256:
                                            ci * TC + hf * 1024 + (s4 + 1) * 256],
                                    op=ALU.add)
                l.phase({"vector": p_add})
                l.phase({"gpsimd": (lambda e, l, ci=ci, hf=hf, s4=s4:
                                    l.dma("gpsimd", e,
                                          z2_o[ci * 128:(ci + 1) * 128,
                                               hf * 1024 + s4 * 256:hf * 1024 + (s4 + 1) * 256],
                                          zt2[:]))})

    def p_fin(e, l):
        e.wait_ge(l.dsem["gpsimd"], l.dcnt["gpsimd"])
    l.phase({"gpsimd": p_fin})

    if max_phases is not None:
        l._phases = l._phases[:max_phases]
    nc = l.build()
    l.close()
    return nc



def build_launchF(max_phases=None):
    l = L("launchF")
    z_i = l.inp("z2", [D, TC], F32)
    wqt = l.inp("wqt", [D, D], F32)
    wkt = l.inp("wkt", [D, D], F32)
    wvt = l.inp("wvt", [D, D], F32)
    cosb = l.inp("cosb", [128, TC], F32)
    sinb = l.inp("sinb", [128, TC], F32)
    x3_o = l.out("x3", [D, TC], F32)
    q_o = l.out("q", [D, TC], BF16)
    k_o = l.out("k", [D, TC], BF16)
    v_o = l.out("v", [D, TC], BF16)

    xb = l.sbuf("xb", [128, NCH * TC], BF16)
    w_sb = [l.sbuf(f"w{i}", [128, NCH * D], BF16) for i in range(2)]
    out_sb = l.sbuf("outb", [128, NCH * TC], BF16)
    cos_sb = l.sbuf("cos_sb", [128, TC], F32)
    sin_sb = l.sbuf("sin_sb", [128, TC], F32)
    t1 = l.sbuf("t1", [128, TC], BF16)
    t2 = l.sbuf("t2", [128, TC], BF16)
    sq = l.sbuf("sq", [128, TC], BF16)
    ones_bf = l.sbuf("ones_bf", [128, 1], BF16)
    ones_row = l.sbuf("ones_row", [1, 128], F32)
    r0 = l.sbuf("r0", [1, TC], F32)
    r1 = l.sbuf("r1", [1, TC], F32)
    cst = l.sbuf("cst", [128, 1], F32)

    u1 = l.psum("u1", [128, TC], F32)
    pp = [l.psum(f"pp{i}", [128, 512], F32) for i in range(2)]

    wv_views = {
        "q": wqt.rearrange("(c p) o -> p c o", p=128),
        "k": wkt.rearrange("(c p) o -> p c o", p=128),
        "v": wvt.rearrange("(c p) o -> p c o", p=128),
    }
    w_sb3 = [w_sb[i].ap().rearrange("p (c o) -> p c o", c=NCH) for i in range(2)]

    def p0(e, l):
        l.dma("sync", e, cos_sb[:], cosb)
        l.dma("sync", e, sin_sb[:], sinb)
    def p0g(e, l):
        l.dma("gpsimd", e, w_sb3[0], wv_views["q"], pf="q")
    def p0v(e, l):
        e.memset(ones_bf[:], 1.0)
        e.memset(ones_row[:], 1.0)
        e.memset(cst[:], 1e-5)
    l.phase({"sync": p0, "gpsimd": p0g, "vector": p0v})

    def after_chunk(ci):
        return {"gpsimd": (lambda e, l, ci=ci:
                           l.dma("gpsimd", e, x3_o[ci * 128:(ci + 1) * 128, :],
                                 xb[:, ci * TC:(ci + 1) * TC]))}
    emit_ln_start(l, z_i, xb, None, sq, u1, ones_bf, ones_row, (r0, r1),
                  cst[0:1, 0:1], after_chunk=after_chunk)

    def rope(src):
        shf = [(i + 16) % 32 for i in range(32)]
        for ci in range(NCH):
            sl = slice(ci * TC, (ci + 1) * TC)

            def p_r1(e, l, sl=sl):
                e.stream_shuffle(t2[:], src[:, sl], shf)
                e.tensor_tensor(out=t1[:], in0=src[:, sl], in1=cos_sb[:], op=ALU.mult)
                e.drain()
                e.tensor_tensor(out=t2[:], in0=t2[:], in1=sin_sb[:], op=ALU.mult)
                e.drain()
                e.tensor_tensor(out=src[:, sl], in0=t1[:], in1=t2[:], op=ALU.add)
            l.phase({"vector": p_r1})

    def store(out_ap):
        def p_st(e, l):
            for ci in range(NCH):
                l.dma("gpsimd", e, out_ap[ci * 128:(ci + 1) * 128, :],
                      out_sb[:, ci * TC:(ci + 1) * TC])
        l.phase({"gpsimd": p_st})

    emit_proj(l, "q", "k", w_sb[0], w_sb3[1], wv_views["k"], xb, out_sb, pp)
    rope(out_sb)
    store(q_o)
    emit_proj(l, "k", "v", w_sb[1], w_sb3[0], wv_views["v"], xb, out_sb, pp)
    rope(out_sb)
    store(k_o)
    emit_proj(l, "v", None, w_sb[0], None, None, xb, out_sb, pp)
    store(v_o)

    def p_fin(e, l):
        e.wait_ge(l.dsem["gpsimd"], l.dcnt["gpsimd"])
    l.phase({"gpsimd": p_fin})

    if max_phases is not None:
        l._phases = l._phases[:max_phases]
    nc = l.build()
    l.close()
    return nc



def build_launchG(max_phases=None):
    l = L("launchG")
    NKT = T // 128
    qr = l.inp("qr", [D, TC], BF16)
    kh = l.inp("kh", [16, 64, T], BF16)
    vh = l.inp("vh", [16, T, 64], BF16)
    x3 = l.inp("x3", [D, TC], F32)
    wot = l.inp("wot", [D, D], F32)
    z3_o = l.out("z3", [D, TC], F32)

    q_sb = l.sbuf("q_sb", [128, NCH * TC], BF16)
    kd_sb = l.sbuf("kd_sb", [128, T], BF16)
    va_sb = l.sbuf("va_sb", [128, NKT * 64], BF16)
    vb_sb = l.sbuf("vb_sb", [128, NKT * 64], BF16)
    ctx_sb = l.sbuf("ctx_sb", [128, NCH * TC], BF16)
    sqk = l.sbuf("sqk", [128, T], BF16)
    et = [l.sbuf(f"et{i}", [128, 2048], BF16) for i in range(2)]
    wot_sb = l.sbuf("wot_sb", [128, NCH * D], BF16)
    bcs = l.sbuf("bcs", [128, 512], F32)
    rws = l.sbuf("rws", [1, 1024], F32)
    mq2 = l.sbuf("mq2", [1, 16], F32)
    mk2 = l.sbuf("mk2", [1, 16], F32)
    sc1 = l.sbuf("sc1", [1, 8], F32)
    nbias = l.sbuf("nbias", [128, 1], F32)
    xc = l.sbuf("xc", [128, 512], F32)
    zt = l.sbuf("zt", [128, 512], F32)
    ones_bf = l.sbuf("ones_bf", [128, 1], BF16)
    ones_row = l.sbuf("ones_row", [1, 128], F32)

    sc = l.psum("sc", [128, 2048], F32)
    ctxp = l.psum("ctxp", [128, 1024], F32)
    denp = l.psum("denp", [128, 1024], F32)

    q_v = qr.rearrange("(c p) t -> p c t", p=128)
    q_sb3 = q_sb.ap().rearrange("p (c t) -> p c t", c=NCH)
    wot_v = wot.rearrange("(c p) o -> p c o", p=128)
    wot_3 = wot_sb.ap().rearrange("p (c o) -> p c o", c=NCH)

    def p0(e, l):
        l.dma("sync", e, q_sb3, q_v)
    def p0g(e, l):
        l.dma("gpsimd", e, wot_3, wot_v, pf="wo")
    def p0v(e, l):
        e.memset(ones_bf[:], 1.0)
        e.memset(ones_row[:], 1.0)
    l.phase({"sync": p0, "gpsimd": p0g, "vector": p0v})

    for ci in range(NCH):
        l.phase({"scalar": (lambda e, l, ci=ci:
                            e.activation(out=sqk[:, 0:TC],
                                         in_=q_sb[:, ci * TC:(ci + 1) * TC],
                                         func=AF.Square))})
        for hh in range(2):
            def p_cs(e, l, hh=hh):
                i = None
                for j in range(0, TC, 512):
                    i = e.matmul(sc[0:1, j:j + 512],
                                 ones_bf[hh * 64:(hh + 1) * 64, 0:1],
                                 sqk[hh * 64:(hh + 1) * 64, j:j + 512],
                                 start=True, stop=True)
                return i
            l.phase({"tensor": p_cs})

            def p_mx(e, l, ci=ci, hh=hh):
                e.reduce_max(mq2[0:1, ci * 2 + hh:ci * 2 + hh + 1],
                             sc[0:1, 0:TC], axis=AX.X)
            l.phase({"vector": p_mx})

    for dj in range(NCH):
        hA, hB = 2 * dj, 2 * dj + 1

        def p_ld(e, l, dj=dj):
            l.dma("sync", e, kd_sb[:],
                  kh.rearrange("h p t -> (h p) t")[dj * 128:(dj + 1) * 128, :])
            l.dma("sync", e, va_sb.ap().rearrange("p (n d) -> p n d", d=64),
                  vh.rearrange("h (n p) d -> h p n d", p=128)[2 * dj])
            l.dma("sync", e, vb_sb.ap().rearrange("p (n d) -> p n d", d=64),
                  vh.rearrange("h (n p) d -> h p n d", p=128)[2 * dj + 1])
        l.phase({"sync": p_ld})

        l.phase({"scalar": (lambda e, l: e.activation(out=sqk[:], in_=kd_sb[:],
                                                      func=AF.Square))})
        for hh in range(2):
            for pc in range(T // 1024):
                def p_cs(e, l, hh=hh, pc=pc):
                    i = None
                    for j in range(0, 1024, 512):
                        i = e.matmul(sc[0:1, j:j + 512],
                                     ones_bf[hh * 64:(hh + 1) * 64, 0:1],
                                     sqk[hh * 64:(hh + 1) * 64, pc * 1024 + j:pc * 1024 + j + 512],
                                     start=True, stop=True)
                    return i
                l.phase({"tensor": p_cs})

                def p_mx(e, l, dj=dj, hh=hh, pc=pc):
                    e.reduce_max(sc1[0:1, 0:1], sc[0:1, 0:1024], axis=AX.X)
                    e.drain()
                    if pc == 0:
                        e.tensor_copy(mk2[0:1, dj * 2 + hh:dj * 2 + hh + 1], sc1[0:1, 0:1])
                    else:
                        e.tensor_tensor(out=mk2[0:1, dj * 2 + hh:dj * 2 + hh + 1],
                                        in0=mk2[0:1, dj * 2 + hh:dj * 2 + hh + 1],
                                        in1=sc1[0:1, 0:1], op=ALU.max)
                l.phase({"vector": p_mx})

        def p_nb1(e, l, dj=dj):
            e.tensor_tensor(out=sc1[0:1, 0:2], in0=mq2[0:1, dj * 2:dj * 2 + 2],
                            in1=mk2[0:1, dj * 2:dj * 2 + 2], op=ALU.mult)
            e.drain()
            e.reduce_max(sc1[0:1, 2:3], sc1[0:1, 0:2], axis=AX.X)
        l.phase({"vector": p_nb1})
        l.phase({"scalar": (lambda e, l:
                            e.activation(out=sc1[0:1, 3:4], in_=sc1[0:1, 2:3],
                                         func=AF.Sqrt))})
        l.phase({"vector": (lambda e, l:
                            e.tensor_scalar_mul(sc1[0:1, 4:5], sc1[0:1, 3:4], -0.125))})
        l.phase({"tensor": (lambda e, l:
                            e.matmul(ctxp[:, 0:1], ones_row[0:1, :], sc1[0:1, 4:5],
                                     start=True, stop=True))})
        l.phase({"scalar": (lambda e, l: e.copy(out=nbias[:], in_=ctxp[:, 0:1]))})

        for qs in range(TC // 512):
            qsl = slice(dj * TC + qs * 512, dj * TC + (qs + 1) * 512)

            def mk_scores(e, l, kp, dj=dj, qsl=qsl):
                i = None
                for half in range(2):
                    kt = 2 * kp + half
                    for hh in range(2):
                        i = e.matmul(
                            sc[:, half * 1024 + hh * 512: half * 1024 + (hh + 1) * 512],
                            kd_sb[hh * 64:(hh + 1) * 64, kt * 128:(kt + 1) * 128],
                            q_sb.ap()[hh * 64:(hh + 1) * 64, qsl],
                            start=True, stop=True,
                            tile_position=(hh * 64, 0))
                return i

            def mk_pv(e, l, kp, dj=dj, buf=None):
                i = None
                for half in range(2):
                    kt = 2 * kp + half
                    first = (kt == 0)
                    e.matmul(ctxp[0:64, 0:512], va_sb[:, kt * 64:(kt + 1) * 64],
                             et[buf][:, half * 1024:half * 1024 + 512],
                             start=first, stop=(kt == NKT - 1),
                             tile_position=(0, 0), skip_group_check=True)
                    e.matmul(ctxp[64:128, 512:1024], vb_sb[:, kt * 64:(kt + 1) * 64],
                             et[buf][:, half * 1024 + 512:half * 1024 + 1024],
                             start=first, stop=(kt == NKT - 1),
                             tile_position=(0, 64), skip_group_check=True)
                    e.matmul(denp[0:1, 0:512], ones_bf[:, 0:1],
                             et[buf][:, half * 1024:half * 1024 + 512],
                             start=first, stop=(kt == NKT - 1),
                             tile_position=(0, 0), skip_group_check=True)
                    i = e.matmul(denp[0:1, 512:1024], ones_bf[:, 0:1],
                                 et[buf][:, half * 1024 + 512:half * 1024 + 1024],
                                 start=first, stop=(kt == NKT - 1),
                                 tile_position=(0, 0), skip_group_check=True)
                return i

            NP = NKT // 2
            for kp in range(NP):
                def p_t(e, l, kp=kp, mk_pv=mk_pv, mk_scores=mk_scores):
                    if kp > 0:
                        mk_pv(e, l, kp - 1, buf=(kp - 1) % 2)
                    return mk_scores(e, l, kp)
                l.phase({"tensor": p_t})

                def p_s(e, l, kp=kp):
                    e.activation(out=et[kp % 2][:], in_=sc[:], func=AF.Exp,
                                 scale=0.125, bias=nbias[:, 0:1])
                l.phase({"scalar": p_s})
            l.phase({"tensor": (lambda e, l, mk_pv=mk_pv:
                                mk_pv(e, l, NP - 1, buf=(NP - 1) % 2))})

            def p_rc(e, l):
                e.reciprocal(rws[0:1, :], denp[0:1, :])
            l.phase({"vector": p_rc})

            def p_bc(e, l):
                e.matmul(sc[0:64, 0:512], ones_row[0:1, 0:64], rws[0:1, 0:512],
                         start=True, stop=True, tile_position=(0, 0))
                return e.matmul(sc[64:128, 0:512], ones_row[0:1, 0:64],
                                rws[0:1, 512:1024],
                                start=True, stop=True, tile_position=(0, 64))
            l.phase({"tensor": p_bc})
            l.phase({"scalar": (lambda e, l: e.copy(out=bcs[:], in_=sc[:, 0:512]))})

            def p_nm(e, l, qsl=qsl):
                e.tensor_tensor(out=ctx_sb.ap()[0:64, qsl], in0=ctxp[0:64, 0:512],
                                in1=bcs[0:64, :], op=ALU.mult)
                e.drain()
                e.tensor_tensor(out=ctx_sb.ap()[64:128, qsl],
                                in0=ctxp[64:128, 512:1024],
                                in1=bcs[64:128, :], op=ALU.mult)
            l.phase({"vector": p_nm})

    for do in range(NCH):
        for nt in range(TC // 512):
            def p_mm(e, l, do=do, nt=nt, first=(do == 0 and nt == 0)):
                if first:
                    l.wait_pf(e, "wo")
                i = None
                for ci in range(NCH):
                    i = e.matmul(sc[:, 0:512],
                                 wot_sb[:, ci * D + do * 128:ci * D + (do + 1) * 128],
                                 ctx_sb[:, ci * TC + nt * 512:ci * TC + (nt + 1) * 512],
                                 start=(ci == 0), stop=(ci == NCH - 1))
                return i

            def p_xc(e, l, do=do, nt=nt):
                l.dma("sync", e, xc[:], x3[do * 128:(do + 1) * 128, nt * 512:(nt + 1) * 512])
            l.phase({"tensor": p_mm, "sync": p_xc})

            def p_z(e, l):
                e.scalar_tensor_tensor(out=zt[:], in0=sc[:, 0:512], scalar=1.0,
                                       in1=xc[:], op0=ALU.mult, op1=ALU.add)
            l.phase({"vector": p_z})
            l.phase({"gpsimd": (lambda e, l, do=do, nt=nt:
                                l.dma("gpsimd", e,
                                      z3_o[do * 128:(do + 1) * 128, nt * 512:(nt + 1) * 512],
                                      zt[:]))})

    def p_fin(e, l):
        e.wait_ge(l.dsem["gpsimd"], l.dcnt["gpsimd"])
    l.phase({"gpsimd": p_fin})

    if max_phases is not None:
        l._phases = l._phases[:max_phases]
    nc = l.build()
    l.close()
    return nc



def build_launchH(max_phases=None):
    l = L("launchH")
    z_i = l.inp("z3", [D, TC], F32)
    wut = l.inp("wut", [D, 4096], F32)
    wdt = l.inp("wdt", [4096, D], F32)
    out_o = l.out("out", [D, TC], F32)

    xb = l.sbuf("xb", [128, NCH * TC], BF16)
    wu_sb = l.sbuf("wu_sb", [128, NCH * 4096], BF16)
    wd_sb = l.sbuf("wd_sb", [128, 32 * D], BF16)
    hs = l.sbuf("hs", [128, 32 * 256], BF16)
    sq = l.sbuf("sq", [128, 1024], BF16)
    tmp = l.sbuf("tmp", [128, 256], BF16)
    zt = l.sbuf("zt", [128, 256], F32)
    rbc1 = l.sbuf("rbc1", [128, 256], F32)
    rbc2 = l.sbuf("rbc2", [128, 256], F32)
    acc = l.sbuf("acc", [128, 32], F32)
    tau = l.sbuf("tau", [128, 2], F32)
    onesf = l.sbuf("onesf", [128, 1], F32)
    ones_bf = l.sbuf("ones_bf", [128, 1], BF16)
    ones_row = l.sbuf("ones_row", [1, 128], F32)
    r0 = l.sbuf("r0", [1, TC], F32)
    r1 = l.sbuf("r1", [1, TC], F32)
    s1 = l.sbuf("s1", [1, 8], F32)
    cst = l.sbuf("cst", [128, 2], F32)

    u1 = l.psum("u1", [128, TC], F32)
    pp = [l.psum(f"pp{i}", [128, 512], F32) for i in range(2)]

    z4d = l.dram("z4d", [D, TC], F32)

    wsl = l.sbuf("wsl", [128, 1024], F32)
    sgn = l.sbuf("sgn", [128, 1024], BF16)
    g1 = l.sbuf("g1", [128, 1024], BF16)

    def p0v(e, l):
        e.memset(ones_bf[:], 1.0)
        e.memset(ones_row[:], 1.0)
        e.memset(onesf[:], 1.0)
        e.memset(cst[:, 0:1], 1e-5)
        e.memset(cst[:, 1:2], 1e-8)
    l.phase({"vector": p0v})

    emit_ln_start(l, z_i, xb, None, sq, u1, ones_bf, ones_row, (r0, r1),
                  cst[0:1, 0:1])

    def ternarize(w_in, R, C, w_out_sb):
        wv = w_in.rearrange("(c p) f -> p c f", p=128)
        nsl = (R // 128) * (C // 1024)

        def sl_view(k):
            c, s = k // (C // 1024), k % (C // 1024)
            return wv[:, c, s * 1024:(s + 1) * 1024], c * C + s * 1024

        for k in range(nsl):
            src_v, _ = sl_view(k)
            l.phase({"sync": (lambda e, l, src_v=src_v:
                              l.dma("sync", e, wsl[:], src_v))})
            l.phase({"scalar": (lambda e, l, k=k:
                                e.activation(out=g1[:], in_=wsl[:], func=AF.Abs,
                                             accum_out=acc[:, k:k + 1]))})
        def p_t1(e, l, nsl=nsl):
            return e.matmul(u1[0:1, 0:nsl], onesf[:, 0:1], acc[:, 0:nsl],
                            start=True, stop=True)
        l.phase({"tensor": p_t1})

        def p_t2(e, l, nsl=nsl):
            e.reduce_sum(s1[0:1, 0:1], u1[0:1, 0:nsl], axis=AX.X)
            e.drain()
            e.tensor_scalar_mul(s1[0:1, 1:2], s1[0:1, 0:1], 0.5 / (R * C))
            e.drain()
            e.tensor_scalar_mul(s1[0:1, 2:3], s1[0:1, 1:2], -1.0)
        l.phase({"vector": p_t2})

        def p_t3(e, l):
            e.matmul(u1[:, 0:1], ones_row[0:1, :], s1[0:1, 1:2], start=True, stop=True)
            return e.matmul(u1[:, 1:2], ones_row[0:1, :], s1[0:1, 2:3],
                            start=True, stop=True)
        l.phase({"tensor": p_t3})
        l.phase({"scalar": (lambda e, l: e.copy(out=tau[:], in_=u1[:, 0:2]))})

        for k in range(nsl):
            src_v, ocol = sl_view(k)
            l.phase({"sync": (lambda e, l, src_v=src_v:
                              l.dma("sync", e, wsl[:], src_v))})
            l.phase({"scalar": (lambda e, l: e.sign(out=sgn[:], in_=wsl[:]))})

            def p_g(e, l, ocol=ocol):
                e.tensor_scalar(out=g1[:], in0=wsl[:], scalar1=tau[:, 0:1],
                                scalar2=None, op0=ALU.is_gt)
                e.drain()
                e.scalar_tensor_tensor(out=g1[:], in0=wsl[:], scalar=tau[:, 1:2],
                                       in1=g1[:], op0=ALU.is_lt, op1=ALU.add)
                e.drain()
                e.tensor_tensor(out=w_out_sb[:, ocol:ocol + 1024],
                                in0=g1[:], in1=sgn[:], op=ALU.mult)
            l.phase({"vector": p_g})

    ternarize(wut, D, 4096, wu_sb)
    ternarize(wdt, 4096, D, wd_sb)

    for ci in range(NCH):
        for hf in range(2):
            l.phase({"scalar": (lambda e, l, ci=ci, hf=hf:
                                e.activation(out=sq[:, 0:1024],
                                             in_=xb[:, ci * TC + hf * 1024:
                                                     ci * TC + (hf + 1) * 1024],
                                             func=AF.Square))})

            def p_cs(e, l, ci=ci, hf=hf):
                i = None
                for j in range(0, 1024, 512):
                    i = e.matmul(u1[0:1, hf * 1024 + j:hf * 1024 + j + 512],
                                 ones_bf[:, 0:1], sq[:, j:j + 512],
                                 start=(ci == 0), stop=(ci == NCH - 1))
                return i
            l.phase({"tensor": p_cs})
    l.phase({"scalar": (lambda e, l:
                        e.activation(out=r0[0:1, :], in_=u1[0:1, :], func=AF.Sqrt,
                                     scale=1.0 / D))})

    def p_r1(e, l):
        e.tensor_scalar_add(r0[0:1, :], r0[0:1, :], 1e-8)
        e.drain()
        e.reciprocal(r0[0:1, :], r0[0:1, :])
    l.phase({"vector": p_r1})

    SS = 256
    for s in range(TC // SS):
        l.phase({"tensor": (lambda e, l, s=s:
                            bcast(e, ones_row, r0[0:1, s * SS:(s + 1) * SS],
                                  u1, SS))})
        l.phase({"scalar": (lambda e, l: e.copy(out=rbc1[:], in_=u1[:, 0:SS]))})

        for fc in range(32):
            def p_mm(e, l, s=s, fc=fc):
                i = None
                for ci in range(NCH):
                    i = e.matmul(pp[fc % 2][:, 0:SS],
                                 wu_sb[:, ci * 4096 + fc * 128:ci * 4096 + (fc + 1) * 128],
                                 xb[:, ci * TC + s * SS:ci * TC + (s + 1) * SS],
                                 start=(ci == 0), stop=(ci == NCH - 1))
                return i
            l.phase({"tensor": p_mm})

            def p_sc(e, l, fc=fc):
                e.scalar_tensor_tensor(out=tmp[:], in0=pp[fc % 2][:, 0:SS],
                                       scalar=1.0, in1=rbc1[:],
                                       op0=ALU.mult, op1=ALU.mult)
            l.phase({"vector": p_sc})
            l.phase({"scalar": (lambda e, l, fc=fc:
                                e.activation(out=hs[:, fc * SS:(fc + 1) * SS],
                                             in_=tmp[:], func=AF.Gelu))})

        for fc in range(32):
            l.phase({"scalar": (lambda e, l, fc=fc:
                                e.activation(out=sq[:, 0:SS],
                                             in_=hs[:, fc * SS:(fc + 1) * SS],
                                             func=AF.Square))})
            l.phase({"tensor": (lambda e, l, fc=fc:
                                e.matmul(u1[0:1, 0:SS], ones_bf[:, 0:1], sq[:, 0:SS],
                                         start=(fc == 0), stop=(fc == 31)))})
        l.phase({"scalar": (lambda e, l:
                            e.activation(out=r1[0:1, 0:SS], in_=u1[0:1, 0:SS],
                                         func=AF.Sqrt, scale=1.0 / 4096))})

        def p_r2(e, l):
            e.tensor_scalar_add(r1[0:1, 0:SS], r1[0:1, 0:SS], 1e-8)
            e.drain()
            e.reciprocal(r1[0:1, 0:SS], r1[0:1, 0:SS])
        l.phase({"vector": p_r2})
        l.phase({"tensor": (lambda e, l:
                            bcast(e, ones_row, r1[0:1, 0:SS], u1, SS))})
        l.phase({"scalar": (lambda e, l: e.copy(out=rbc2[:], in_=u1[:, 0:SS]))})

        for do in range(NCH):
            def p_mm2(e, l, do=do):
                i = None
                for fc in range(32):
                    i = e.matmul(pp[do % 2][:, 0:SS],
                                 wd_sb[:, fc * D + do * 128:fc * D + (do + 1) * 128],
                                 hs[:, fc * SS:(fc + 1) * SS],
                                 start=(fc == 0), stop=(fc == 31))
                return i
            l.phase({"tensor": p_mm2})

            def p_z(e, l, s=s, do=do):
                e.scalar_tensor_tensor(out=zt[:], in0=pp[do % 2][:, 0:SS],
                                       scalar=1.0, in1=rbc2[:],
                                       op0=ALU.mult, op1=ALU.mult)
                e.drain()
                e.tensor_tensor(out=zt[:], in0=zt[:],
                                in1=xb[:, do * TC + s * SS:do * TC + (s + 1) * SS],
                                op=ALU.add)
            l.phase({"vector": p_z})
            l.phase({"gpsimd": (lambda e, l, s=s, do=do:
                                l.dma("gpsimd", e,
                                      z4d[do * 128:(do + 1) * 128, s * SS:(s + 1) * SS],
                                      zt[:]))})

    def after_chunk(ci):
        return {"gpsimd": (lambda e, l, ci=ci:
                           l.dma("gpsimd", e, out_o[ci * 128:(ci + 1) * 128, :],
                                 xb[:, ci * TC:(ci + 1) * TC]))}
    emit_ln_start(l, z4d, xb, None, sq, u1, ones_bf, ones_row, (r0, r1),
                  cst[0:1, 0:1], after_chunk=after_chunk)

    def p_fin(e, l):
        e.wait_ge(l.dsem["gpsimd"], l.dcnt["gpsimd"])
    l.phase({"gpsimd": p_fin})

    if max_phases is not None:
        l._phases = l._phases[:max_phases]
    nc = l.build()
    l.close()
    return nc



def _fm(a):
    return np.ascontiguousarray(np.asarray(a, np.float32).T)


def _chunked(v):
    return np.ascontiguousarray(np.asarray(v, np.float32).reshape(NCH, 128).T)


def _rope_perm():
    perm = np.zeros(D, np.int64)
    for j in range(NCH):
        for r in range(128):
            qd = r // 32
            h = 2 * j + (qd // 2)
            fr = 16 * (qd % 2) + (r % 16)
            x2 = (r % 32) >= 16
            perm[j * 128 + r] = h * 64 + 2 * fr + (1 if x2 else 0)
    return perm


def _cos_sin(t0):
    hd = 64
    inv = 1.0 / 10000.0 ** (np.arange(0, hd, 2, dtype=np.float32) / hd)
    t_ = np.arange(t0, t0 + TC, dtype=np.float32)
    f = np.outer(inv, t_)
    cos = np.zeros((128, TC), np.float32)
    ssin = np.zeros((128, TC), np.float32)
    for r in range(128):
        qd = r // 32
        fr = 16 * (qd % 2) + (r % 16)
        x2 = (r % 32) >= 16
        cos[r] = np.cos(f[fr])
        ssin[r] = np.sin(f[fr]) * (1.0 if x2 else -1.0)
    return cos, ssin


_NC_CACHE = {}


def _nc(name, builder):
    if name not in _NC_CACHE:
        _NC_CACHE[name] = builder()
    return _NC_CACHE[name]


def kernel(**inp):
    import ml_dtypes
    BF = ml_dtypes.bfloat16
    f32 = lambda k: np.asarray(inp[k], np.float32)
    x = f32("L0_attn_x")
    cores = list(range(NCORE))

    n1w = _chunked(inp["L0_attn_norm1_w"])
    bw = _chunked(f32("L0_attn_beta_w")[:, 0])
    bb = np.broadcast_to(f32("L0_attn_beta_b").reshape(1, 1), (128, 1)).copy()
    imA = []
    for c in cores:
        b, qd = c // 4, c % 4
        imA.append({"x": _fm(x[b, qd * TC:(qd + 1) * TC]), "n1w": n1w,
                    "bw": bw, "bb": bb,
                    "wq": f32("L0_attn_Wq"), "wk": f32("L0_attn_Wk"),
                    "wv": f32("L0_attn_Wv")})
    rA = run_bass_kernel_spmd(_nc("A", build_launchA), imA, core_ids=cores).results

    qf = [np.concatenate([rA[b * 4 + qd]["q"] for qd in range(4)], 1) for b in range(B)]
    kf = [np.concatenate([rA[b * 4 + qd]["k"] for qd in range(4)], 1) for b in range(B)]
    vf = [np.concatenate([rA[b * 4 + qd]["v"] for qd in range(4)], 1) for b in range(B)]
    beta = [np.concatenate(
        [np.asarray(rA[b * 4 + qd]["beta_tm"], np.float32).T.reshape(-1)
         for qd in range(4)]) for b in range(B)]

    ident = np.eye(128, dtype=BF)
    nlmask = (-np.tril(np.ones((128, 128), np.float32), -1)).astype(BF)
    umaski = np.triu(np.ones((128, 128), np.float32), 0).astype(BF)
    al = f32("L0_attn_alpha_logit")
    imC = []
    for c in cores:
        b, s = c // 4, c % 4
        imC.append({
            "qt": np.ascontiguousarray(qf[b].reshape(NCH, 128, T // 128, 128)
                                       .transpose(2, 0, 1, 3)),
            "kt": np.ascontiguousarray(kf[b].reshape(NCH, 128, T // 128, 128)
                                       .transpose(2, 0, 1, 3)),
            "ktm": np.ascontiguousarray(kf[b].T.reshape(T // 128, 128, D)),
            "bcol": np.ascontiguousarray(beta[b].reshape(T // 128, 128).T),
            "va": np.ascontiguousarray(al[256 * s:256 * (s + 1)].reshape(2, 128).T),
            "vsl": np.ascontiguousarray(vf[b][256 * s:256 * (s + 1), :]),
            "ident": ident, "nlmask": nlmask, "umaski": umaski})
    rC = run_bass_kernel_spmd(_nc("C", build_launchC), imC, core_ids=cores).results
    attn = [np.concatenate([rC[b * 4 + s]["ao"] for s in range(4)], 0) for b in range(B)]

    n2w = _chunked(inp["L0_attn_norm2_w"])
    imD = []
    for c in cores:
        b, qd = c // 4, c % 4
        imD.append({"x": _fm(x[b, qd * TC:(qd + 1) * TC]),
                    "at": np.ascontiguousarray(attn[b][:, qd * TC:(qd + 1) * TC]),
                    "wo": f32("L0_attn_Wo"), "n2w": n2w,
                    "wg": f32("L0_attn_w_gate"), "wu": f32("L0_attn_w_up"),
                    "wd": f32("L0_attn_w_down")})
    rD = run_bass_kernel_spmd(_nc("D", build_launchD), imD, core_ids=cores).results

    b1 = f32("L0_ffn_b1")
    b2 = f32("L0_ffn_b2")
    b1c = np.ascontiguousarray(b1.reshape(8, 32, 128).transpose(2, 0, 1).reshape(128, 256))
    b2c = np.ascontiguousarray(b2.reshape(8, 8, 128).transpose(2, 0, 1).reshape(128, 64))
    imE = []
    for c in cores:
        imE.append({"z": np.asarray(rD[c]["z"], np.float32),
                    "wrt": np.ascontiguousarray(f32("L0_ffn_Wr").T),
                    "brr": np.ascontiguousarray(np.tile(f32("L0_ffn_br"), (128, 1))),
                    "w1t": np.ascontiguousarray(f32("L0_ffn_W1").transpose(0, 2, 1)),
                    "w2t": np.ascontiguousarray(f32("L0_ffn_W2").transpose(0, 2, 1)),
                    "b1c": b1c, "b2c": b2c, "ident": ident})
    rE = run_bass_kernel_spmd(_nc("E", build_launchE), imE, core_ids=cores).results

    perm = _rope_perm()
    wqt = np.ascontiguousarray(f32("L1_attn_W_Q").T[:, perm])
    wkt = np.ascontiguousarray(f32("L1_attn_W_K").T[:, perm])
    wvt = np.ascontiguousarray(f32("L1_attn_W_V").T)
    tabs = [_cos_sin(qd * TC) for qd in range(4)]
    imF = []
    for c in cores:
        b, qd = c // 4, c % 4
        imF.append({"z2": np.asarray(rE[c]["z2"], np.float32),
                    "wqt": wqt, "wkt": wkt, "wvt": wvt,
                    "cosb": tabs[qd][0], "sinb": tabs[qd][1]})
    rF = run_bass_kernel_spmd(_nc("F", build_launchF), imF, core_ids=cores).results

    khb, vhb = [], []
    for b in range(B):
        k_full = np.concatenate([rF[b * 4 + qd]["k"] for qd in range(4)], 1)
        v_full = np.concatenate([rF[b * 4 + qd]["v"] for qd in range(4)], 1)
        khb.append(np.ascontiguousarray(k_full.reshape(16, 64, T)))
        vhb.append(np.ascontiguousarray(v_full.reshape(16, 64, T).transpose(0, 2, 1)))

    wot = np.ascontiguousarray(f32("L1_attn_W_O").T)
    imG = []
    for c in cores:
        b, qd = c // 4, c % 4
        imG.append({"qr": np.asarray(rF[c]["q"]),
                    "kh": khb[b], "vh": vhb[b],
                    "x3": np.asarray(rF[c]["x3"], np.float32),
                    "wot": wot})
    rG = run_bass_kernel_spmd(_nc("G", build_launchG), imG, core_ids=cores).results

    wut = np.ascontiguousarray(f32("L1_ffn_w_up").T)
    wdt = np.ascontiguousarray(f32("L1_ffn_w_down").T)
    imH = [{"z3": np.asarray(rG[c]["z3"], np.float32), "wut": wut, "wdt": wdt}
           for c in cores]
    rH = run_bass_kernel_spmd(_nc("H", build_launchH), imH, core_ids=cores).results

    out = np.empty((B, T, D), np.float32)
    for c in cores:
        b, qd = c // 4, c % 4
        out[b, qd * TC:(qd + 1) * TC, :] = np.asarray(rH[c]["out"], np.float32).T
    return out


if __name__ == "__main__":
    pass


# version 02
# speedup vs baseline: 1.4147x; 1.4147x over previous
import contextlib
import numpy as np
import concourse.bass as bass
import concourse.mybir as mybir
from concourse.bass_utils import run_bass_kernel_spmd

F32 = mybir.dt.float32
BF16 = mybir.dt.bfloat16
U32 = mybir.dt.uint32
AF = mybir.ActivationFunctionType
ALU = mybir.AluOpType
AX = mybir.AxisListType

B, T, D = 2, 8192, 1024
TC = 2048
NCH = D // 128
NCORE = 8

ENGS = ("sync", "gpsimd", "tensor", "vector", "scalar")


class L:

    def __init__(self, name):
        self.nc = bass.Bass("TRN2", target_bir_lowering=False, name=name)
        self.es = contextlib.ExitStack()
        self.sem = {e: self.es.enter_context(self.nc.semaphore(f"sem_{e}"))
                    for e in ENGS}
        self.dsem = {e: self.es.enter_context(self.nc.semaphore(f"dma_{e}"))
                     for e in ("sync", "gpsimd", "scalar")}
        self._pfpool = [self.es.enter_context(self.nc.semaphore(f"pf{i}"))
                        for i in range(8)]
        self._pfcnt = [0] * 8
        self.pfmap = {}
        self.pftarget = {}
        self.cnt = {e: 0 for e in ENGS}
        self.dcnt = {e: 0 for e in self.dsem}
        self.seen = {e: {**{oe: 0 for oe in ENGS},
                         **{"d" + q: 0 for q in self.dsem}} for e in ENGS}
        self._phases = []

    def pfsem(self, key):
        if key not in self.pfmap:
            self.pfmap[key] = len(self.pfmap) % len(self._pfpool)
        return self.pfmap[key]

    def sbuf(self, name, shape, dt):
        return self.es.enter_context(self.nc.sbuf_tensor(name, shape, dt))

    def psum(self, name, shape, dt=F32):
        return self.es.enter_context(self.nc.psum_tensor(name, shape, dt))

    def inp(self, name, shape, dt=F32):
        return self.nc.dram_tensor(name, shape, dt, kind="ExternalInput").ap()

    def out(self, name, shape, dt=F32):
        return self.nc.dram_tensor(name, shape, dt, kind="ExternalOutput").ap()

    def dram(self, name, shape, dt=F32):
        return self.nc.dram_tensor(name, shape, dt, kind="Internal").ap()

    def dma(self, qname, eng, out, in_, pf=None, **kw):
        i = eng.dma_start(out=out, in_=in_, **kw)
        if pf is not None:
            idx = self.pfsem(pf)
            i.then_inc(self._pfpool[idx], 16)
            self._pfcnt[idx] += 16
            self.pftarget[pf] = self._pfcnt[idx]
            return self.pftarget[pf]
        i.then_inc(self.dsem[qname], 16)
        self.dcnt[qname] += 16
        return self.dcnt[qname]

    def wait_pf(self, eng, key):
        idx = self.pfmap[key]
        eng.wait_ge(self._pfpool[idx], self.pftarget[key])

    def phase(self, work):
        self._phases.append(work)

    def build(self):
        nc = self.nc
        with nc.Block() as blk:
            for work in self._phases:
                want = dict(self.cnt)
                dwant = dict(self.dcnt)
                for e in ENGS:
                    if e not in work:
                        continue
                    fn = work[e]

                    def section(eng, fn=fn, e=e, want=want, dwant=dwant):
                        sn = self.seen[e]
                        for oe in ENGS:
                            if oe != e and want[oe] > sn[oe]:
                                eng.wait_ge(self.sem[oe], want[oe])
                                sn[oe] = want[oe]
                        for oq, vv in dwant.items():
                            if vv > sn["d" + oq]:
                                eng.wait_ge(self.dsem[oq], vv)
                                sn["d" + oq] = vv
                        last = fn(eng, self)
                        if last is not None and hasattr(last, "then_inc"):
                            last.then_inc(self.sem[e], 1)
                        elif e in ("tensor", "sync"):
                            eng.nop().then_inc(self.sem[e], 1)
                        else:
                            eng.drain().then_inc(self.sem[e], 1)
                        self.cnt[e] += 1

                    getattr(blk, e)(section)
        return nc

    def close(self):
        self.es.close()



def colsum(t, ones_bf, src, ps_row, n, start, stop=False):
    i = None
    for j in range(0, n, 512):
        i = t.matmul(ps_row[0:1, j:j + 512], ones_bf[:, 0:1], src[:, j:j + 512],
                     start=start, stop=stop)
    return i


def bcast(t, ones_row_f32, row, ps_bc, n):
    i = None
    for j in range(0, n, 512):
        w = min(512, n - j)
        i = t.matmul(ps_bc[:, j:j + w], ones_row_f32[0:1, :], row[0:1, j:j + w],
                     start=True, stop=True)
    return i



def build_launchA(max_phases=None):
    l = L("launchA")
    x = l.inp("x", [D, TC], F32)
    n1w = l.inp("n1w", [128, NCH], F32)
    wq = l.inp("wq", [D, D], F32)
    wk = l.inp("wk", [D, D], F32)
    wv = l.inp("wv", [D, D], F32)
    bw = l.inp("bw", [128, NCH], F32)
    bb = l.inp("bb", [128, 1], F32)
    q_o = l.out("q", [D, TC], BF16)
    k_o = l.out("k", [D, TC], BF16)
    v_o = l.out("v", [D, TC], BF16)
    beta_o = l.out("beta_tm", [128, TC // 128], F32)

    h_sb = l.sbuf("h", [128, NCH * TC], BF16)
    xc = l.sbuf("xc", [128, TC], F32)
    sq = l.sbuf("sq", [128, TC], BF16)
    w_sb = [l.sbuf(f"w{i}", [128, NCH * D], BF16) for i in range(2)]
    out_sb = l.sbuf("outb", [128, NCH * TC], BF16)
    n1w_sb = l.sbuf("n1w_sb", [128, NCH], F32)
    bw_sb = l.sbuf("bw_sb", [128, NCH], BF16)
    bb_sb = l.sbuf("bb_sb", [128, 1], F32)
    ones_bf = l.sbuf("ones_bf", [128, 1], BF16)
    ones_row = l.sbuf("ones_row", [1, 128], F32)
    row_a = l.sbuf("row_a", [1, TC], F32)
    row_b = l.sbuf("row_b", [1, TC], F32)
    beta_sb = l.sbuf("beta_sb", [128, TC // 128], F32)
    cst = l.sbuf("cst", [128, 4], F32)

    u1 = l.psum("u1", [128, TC], F32)
    bps = l.psum("bps", [128, TC // 128], F32)
    pp = [l.psum(f"pp{i}", [128, 512], F32) for i in range(2)]

    wv_views = {
        "q": wq.rearrange("(c p) o -> p c o", p=128),
        "k": wk.rearrange("(c p) o -> p c o", p=128),
        "v": wv.rearrange("(c p) o -> p c o", p=128),
    }
    w_sb3 = [w_sb[i].ap().rearrange("p (c o) -> p c o", c=NCH) for i in range(2)]

    def p0_sync(e, l):
        l.dma("sync", e, n1w_sb[:], n1w)
        l.dma("sync", e, bb_sb[:], bb)

    def p0_gp(e, l):
        l.dma("gpsimd", e, w_sb3[0], wv_views["q"], pf="q")
        l.dma("gpsimd", e, bw_sb[:], bw)

    def p0_vec(e, l):
        e.memset(ones_bf[:], 1.0)
        e.memset(ones_row[:], 1.0)
        e.memset(cst[:, 0:1], 1e-6)
        e.memset(cst[:, 1:2], 1e-5)
        e.memset(cst[:, 2:3], 1e-8)

    l.phase({"sync": p0_sync, "gpsimd": p0_gp, "vector": p0_vec})

    for ci in range(NCH):
        l.phase({"sync": (lambda e, l, ci=ci:
                          l.dma("sync", e, xc[:], x[ci * 128:(ci + 1) * 128, :]))})
        l.phase({"scalar": (lambda e, l:
                            e.activation(out=sq[:], in_=xc[:], func=AF.Square))})
        l.phase({"tensor": (lambda e, l, ci=ci:
                            colsum(e, ones_bf, sq, u1, TC, start=(ci == 0),
                                   stop=(ci == NCH - 1)))})

    l.phase({"scalar": (lambda e, l:
                        e.activation(out=row_a[0:1, :], in_=u1[0:1, :],
                                     func=AF.Sqrt, scale=1.0 / D, bias=cst[0:1, 0:1]))})
    l.phase({"vector": (lambda e, l: e.reciprocal(row_b[0:1, :], row_a[0:1, :]))})
    l.phase({"tensor": (lambda e, l: bcast(e, ones_row, row_b, u1, TC))})

    for ci in range(NCH):
        l.phase({"sync": (lambda e, l, ci=ci:
                          l.dma("sync", e, xc[:], x[ci * 128:(ci + 1) * 128, :]))})
        l.phase({"vector": (lambda e, l, ci=ci:
                            e.scalar_tensor_tensor(
                                out=h_sb[:, ci * TC:(ci + 1) * TC], in0=xc[:],
                                scalar=n1w_sb[:, ci:ci + 1], in1=u1[:],
                                op0=ALU.mult, op1=ALU.mult))})

    def p_beta(e, l):
        i = None
        for tt in range(TC // 128):
            for ci in range(NCH):
                i = e.matmul(bps[:, tt:tt + 1],
                             h_sb[:, ci * TC + tt * 128: ci * TC + (tt + 1) * 128],
                             bw_sb[:, ci:ci + 1],
                             start=(tt == 0 and ci == 0),
                             stop=(tt == TC // 128 - 1 and ci == NCH - 1))
        return i
    l.phase({"tensor": p_beta})
    l.phase({"scalar": (lambda e, l:
                        e.activation(out=beta_sb[:], in_=bps[:], func=AF.Sigmoid,
                                     bias=bb_sb[:, 0:1]))})
    l.phase({"gpsimd": (lambda e, l: l.dma("gpsimd", e, beta_o, beta_sb[:]))})

    def proj(wbuf, pf_key, next_key):
        for do in range(NCH):
            for nt in range(TC // 512):
                ps = pp[(do * (TC // 512) + nt) % 2]

                def p_mm(e, l, do=do, nt=nt, ps=ps, first=(do == 0 and nt == 0)):
                    if first:
                        l.wait_pf(e, pf_key)
                    i = None
                    for ci in range(NCH):
                        i = e.matmul(ps[:],
                                     w_sb[wbuf][:, ci * D + do * 128:ci * D + (do + 1) * 128],
                                     h_sb[:, ci * TC + nt * 512:ci * TC + (nt + 1) * 512],
                                     start=(ci == 0), stop=(ci == NCH - 1))
                    return i
                work = {"tensor": p_mm}
                if do == 0 and nt == 0 and next_key is not None:
                    def p_pf(e, l, nk=next_key, wb=1 - wbuf):
                        l.dma("gpsimd", e, w_sb3[wb], wv_views[nk], pf=nk)
                    work["gpsimd"] = p_pf
                l.phase(work)
                l.phase({"scalar": (lambda e, l, do=do, nt=nt, ps=ps:
                                    e.copy(out=out_sb[:, do * TC + nt * 512:
                                                      do * TC + (nt + 1) * 512],
                                           in_=ps[:]))})

    def store(out_ap):
        def p_st(e, l):
            for ci in range(NCH):
                l.dma("gpsimd", e, out_ap[ci * 128:(ci + 1) * 128, :],
                      out_sb[:, ci * TC:(ci + 1) * TC])
        l.phase({"gpsimd": p_st})

    proj(0, "q", "k")
    store(q_o)

    proj(1, "k", "v")
    for ci in range(NCH):
        l.phase({"scalar": (lambda e, l, ci=ci:
                            e.activation(out=sq[:], in_=out_sb[:, ci * TC:(ci + 1) * TC],
                                         func=AF.Square))})
        l.phase({"tensor": (lambda e, l, ci=ci:
                            colsum(e, ones_bf, sq, u1, TC, start=(ci == 0),
                                   stop=(ci == NCH - 1)))})
    l.phase({"scalar": (lambda e, l:
                        e.activation(out=row_a[0:1, :], in_=u1[0:1, :], func=AF.Sqrt))})

    def p_knv(e, l):
        e.tensor_scalar_max(row_a[0:1, :], row_a[0:1, :], 1e-12)
        e.drain()
        e.reciprocal(row_b[0:1, :], row_a[0:1, :])
    l.phase({"vector": p_knv})
    l.phase({"tensor": (lambda e, l: bcast(e, ones_row, row_b, u1, TC))})

    def p_kmul(e, l):
        for ci in range(NCH):
            e.tensor_tensor(out=out_sb[:, ci * TC:(ci + 1) * TC],
                            in0=out_sb[:, ci * TC:(ci + 1) * TC],
                            in1=u1[:], op=ALU.mult)
    l.phase({"vector": p_kmul})
    store(k_o)

    proj(0, "v", None)
    store(v_o)

    def p_fin(e, l):
        e.wait_ge(l.dsem["gpsimd"], l.dcnt["gpsimd"])
    l.phase({"gpsimd": p_fin})

    if max_phases is not None:
        l._phases = l._phases[:max_phases]
    nc = l.build()
    l.close()
    return nc



def build_launchC(max_phases=None):
    l = L("launchC")
    NC64 = T // 128
    qt = l.inp("qt", [NC64, NCH, 128, 128], BF16)
    kt = l.inp("kt", [NC64, NCH, 128, 128], BF16)
    ktm = l.inp("ktm", [NC64, 128, D], BF16)
    bcol = l.inp("bcol", [128, NC64], F32)
    va = l.inp("va", [128, 2], F32)
    vsl = l.inp("vsl", [256, T], BF16)
    ident = l.inp("ident", [128, 128], BF16)
    nlmask = l.inp("nlmask", [128, 128], BF16)
    umaski = l.inp("umaski", [128, 128], BF16)
    ao = l.out("ao", [256, T], BF16)

    vts = l.sbuf("vts", [128, 2 * T], BF16)
    vtm = l.sbuf("vtm", [128, 2 * T], BF16)
    wT = l.sbuf("wT", [128, NC64 * 128], BF16)
    hS = l.sbuf("hS", [128, NC64 * 128], BF16)
    st_sb = l.sbuf("st_sb", [128, NCH * 256], BF16)
    kc = l.sbuf("kc", [128, D], BF16)
    qc = l.sbuf("qc", [128, D], BF16)
    kmc = l.sbuf("kmc", [128, D], BF16)
    rhs_sb = l.sbuf("rhs_sb", [128, 256], BF16)
    u_sb = l.sbuf("u_sb", [128, 256], BF16)
    d0 = l.sbuf("d0", [128, T], BF16)
    d1 = l.sbuf("d1", [128, T], BF16)
    sS = l.sbuf("sS", [128, 128], BF16)
    sT = l.sbuf("sT", [128, 128], BF16)
    sX = l.sbuf("sX", [128, 128], BF16)
    sY = l.sbuf("sY", [128, 128], BF16)
    bc_sb = l.sbuf("bc_sb", [128, NC64], F32)
    va_sb = l.sbuf("va_sb", [128, 2], F32)
    alt = l.sbuf("alt", [128, 2], F32)
    oma = l.sbuf("oma", [128, 2], F32)
    id_sb = l.sbuf("id_sb", [128, 128], BF16)
    nl_sb = l.sbuf("nl_sb", [128, 128], BF16)
    ui_sb = l.sbuf("ui_sb", [128, 128], BF16)

    st_ps = [l.psum(f"st{i}", [128, 512], F32) for i in range(4)]
    ps_b = l.psum("ps_b", [128, 512], F32)
    ps_c = l.psum("ps_c", [128, 512], F32)
    ps_t = l.psum("ps_t", [128, 512], BF16)

    ktv = kt.rearrange("c dc p t -> c p dc t")
    qtv = qt.rearrange("c dc p t -> c p dc t")
    kc3 = kc.ap().rearrange("p (dc t) -> p dc t", dc=NCH)
    qc3 = qc.ap().rearrange("p (dc t) -> p dc t", dc=NCH)
    vslv = vsl.rearrange("(j p) t -> j p t", p=128)
    aov = ao.rearrange("(j p) t -> j p t", p=128)

    def p0(e, l):
        l.dma("sync", e, vts[:, 0:T], vslv[0])
        l.dma("sync", e, vts[:, T:2 * T], vslv[1])
        l.dma("sync", e, bc_sb[:], bcol)
        l.dma("sync", e, va_sb[:], va)
        l.dma("sync", e, id_sb[:], ident)
        l.dma("sync", e, nl_sb[:], nlmask)
        l.dma("sync", e, ui_sb[:], umaski)
    l.phase({"sync": p0})

    l.phase({"scalar": (lambda e, l: e.activation(out=alt[:], in_=va_sb[:],
                                                  func=AF.Sigmoid))})
    def p_ema_v(e, l):
        e.tensor_scalar(out=oma[:], in0=alt[:], scalar1=-1.0, scalar2=1.0,
                        op0=ALU.mult, op1=ALU.add)
        e.drain()
        for j in range(2):
            e.tensor_scalar(out=d1[:], in0=vts[:, j * T:(j + 1) * T],
                            scalar1=alt[:, j:j + 1], scalar2=None, op0=ALU.mult)
            e.memset(d0[:], 1.0)
            e.drain()
            e.tensor_scalar(out=d0[:], in0=d0[:], scalar1=oma[:, j:j + 1],
                            scalar2=None, op0=ALU.mult)
            e.drain()
            e.tensor_tensor_scan(out=vts[:, j * T:(j + 1) * T], data0=d0[:],
                                 data1=d1[:], initial=0.0,
                                 op0=ALU.mult, op1=ALU.add)
            e.drain()
    l.phase({"vector": p_ema_v})

    for g in range(NC64 * 2 // 4):
        def p_tr(e, l, g=g):
            i = None
            for r in range(4):
                idx = g * 4 + r
                j, c = idx % 2, idx // 2
                i = e.transpose(ps_t[:, r * 128:(r + 1) * 128],
                                vts[:, j * T + c * 128: j * T + (c + 1) * 128],
                                id_sb[:])
            return i
        l.phase({"tensor": p_tr})

        def p_tre(e, l, g=g):
            for r in range(4):
                idx = g * 4 + r
                j, c = idx % 2, idx // 2
                e.copy(out=vtm[:, c * 256 + j * 128: c * 256 + (j + 1) * 128],
                       in_=ps_t[:, r * 128:(r + 1) * 128])
        l.phase({"scalar": p_tre})

    for c in range(NC64):
        def p_ld(e, l, c=c):
            l.dma("sync", e, kc3, ktv[c])
            l.dma("sync", e, qc3, qtv[c])
        l.phase({"sync": p_ld})

        def p_kk(e, l, c=c):
            i = None
            for dc in range(NCH):
                e.matmul(ps_b[:, 0:128], kc[:, dc * 128:(dc + 1) * 128],
                         kc[:, dc * 128:(dc + 1) * 128],
                         start=(dc == 0), stop=(dc == NCH - 1))
                i = e.matmul(ps_c[:, 0:128], kc[:, dc * 128:(dc + 1) * 128],
                             qc[:, dc * 128:(dc + 1) * 128],
                             start=(dc == 0), stop=(dc == NCH - 1))
            return i
        l.phase({"tensor": p_kk})

        def p_xh(e, l, c=c):
            e.scalar_tensor_tensor(out=sX[:], in0=ps_b[:, 0:128],
                                   scalar=bc_sb[:, c:c + 1], in1=nl_sb[:],
                                   op0=ALU.mult, op1=ALU.mult)
            e.scalar_tensor_tensor(out=hS[:, c * 128:(c + 1) * 128],
                                   in0=ps_c[:, 0:128], scalar=1.0,
                                   in1=ui_sb[:], op0=ALU.mult, op1=ALU.mult)
        l.phase({"vector": p_xh})

        l.phase({"tensor": (lambda e, l: e.transpose(ps_t[:, 0:128], sX[:], id_sb[:]))})
        l.phase({"scalar": (lambda e, l: e.copy(out=sY[:], in_=ps_t[:, 0:128]))})

        def p_i1(e, l):
            e.matmul(ps_c[:, 0:128], sY[:], sX[:], start=True, stop=True)
            return e.matmul(ps_c[:, 128:256], sX[:], sY[:], start=True, stop=True)
        l.phase({"tensor": p_i1})

        def p_i2(e, l):
            e.tensor_tensor(out=sS[:], in0=sX[:], in1=id_sb[:], op=ALU.add)
            e.tensor_tensor(out=sT[:], in0=sY[:], in1=id_sb[:], op=ALU.add)
        l.phase({"vector": p_i2})

        def p_i3(e, l):
            e.copy(out=sX[:], in_=ps_c[:, 0:128])
            e.drain()
            e.copy(out=sY[:], in_=ps_c[:, 128:256])
        l.phase({"scalar": p_i3})

        for it in range(5):
            def p_mm4(e, l):
                e.matmul(ps_c[:, 0:128], sY[:], sS[:], start=True, stop=True)
                e.matmul(ps_c[:, 128:256], sS[:], sY[:], start=True, stop=True)
                e.matmul(ps_c[:, 256:384], sY[:], sX[:], start=True, stop=True)
                return e.matmul(ps_c[:, 384:512], sX[:], sY[:], start=True, stop=True)
            l.phase({"tensor": p_mm4})

            def p_ev4(e, l):
                e.tensor_tensor(out=sS[:], in0=sS[:], in1=ps_c[:, 0:128], op=ALU.add)
                e.drain()
                e.tensor_tensor(out=sT[:], in0=sT[:], in1=ps_c[:, 128:256], op=ALU.add)
            l.phase({"vector": p_ev4})

            def p_ev4b(e, l):
                e.copy(out=sX[:], in_=ps_c[:, 256:384])
                e.drain()
                e.copy(out=sY[:], in_=ps_c[:, 384:512])
            l.phase({"scalar": p_ev4b})

        l.phase({"tensor": (lambda e, l: e.matmul(ps_c[:, 0:128], sS[:], sY[:],
                                                  start=True, stop=True))})
        l.phase({"vector": (lambda e, l, c=c:
                            e.tensor_tensor(out=wT[:, c * 128:(c + 1) * 128],
                                            in0=sT[:], in1=ps_c[:, 0:128],
                                            op=ALU.add))})

    def p_z(e, l):
        e.memset(st_sb[:], 0.0)
    l.phase({"vector": p_z})

    for c in range(NC64):
        def p_ld(e, l, c=c):
            l.dma("sync", e, kc3, ktv[c])
            l.dma("sync", e, qc3, qtv[c])
            l.dma("sync", e, kmc[:], ktm[c])
        l.phase({"sync": p_ld})

        def p_ks(e, l, c=c):
            i = None
            for dc in range(NCH):
                i = e.matmul(ps_b[:, 0:256], kc[:, dc * 128:(dc + 1) * 128],
                             st_sb[:, dc * 256:(dc + 1) * 256],
                             start=(dc == 0), stop=(dc == NCH - 1))
            return i
        l.phase({"tensor": p_ks})

        def p_rhs(e, l, c=c):
            e.scalar_tensor_tensor(out=rhs_sb[:], in0=ps_b[:, 0:256],
                                   scalar=-1.0, in1=vtm[:, c * 256:(c + 1) * 256],
                                   op0=ALU.mult, op1=ALU.add)
            e.drain()
            e.tensor_scalar(out=rhs_sb[:], in0=rhs_sb[:],
                            scalar1=bc_sb[:, c:c + 1], scalar2=None, op0=ALU.mult)
        l.phase({"vector": p_rhs})

        l.phase({"tensor": (lambda e, l, c=c:
                            e.matmul(ps_b[:, 256:512], wT[:, c * 128:(c + 1) * 128],
                                     rhs_sb[:], start=True, stop=True))})
        l.phase({"scalar": (lambda e, l: e.copy(out=u_sb[:], in_=ps_b[:, 256:512]))})

        def p_ost(e, l, c=c):
            for ic in range(2):
                for dc in range(NCH):
                    e.matmul(ps_c[:, ic * 128:(ic + 1) * 128],
                             st_sb[:, dc * 256 + ic * 128: dc * 256 + (ic + 1) * 128],
                             qc[:, dc * 128:(dc + 1) * 128],
                             start=(dc == 0), stop=False)
                e.matmul(ps_c[:, ic * 128:(ic + 1) * 128],
                         u_sb[:, ic * 128:(ic + 1) * 128],
                         hS[:, c * 128:(c + 1) * 128], start=False, stop=True)
            i = None
            for j in range(NCH):
                i = e.matmul(st_ps[j // 2][:, (j % 2) * 256:(j % 2) * 256 + 256],
                             kmc[:, j * 128:(j + 1) * 128], u_sb[:],
                             start=(c == 0 and j % 2 == 0), stop=(j % 2 == 1),
                             skip_group_check=True)
            return i
        l.phase({"tensor": p_ost})

        def p_ev(e, l, c=c):
            for ic in range(2):
                e.copy(out=vts[:, ic * T + c * 128: ic * T + (c + 1) * 128],
                       in_=ps_c[:, ic * 128:(ic + 1) * 128])
            e.drain()
            for j in range(NCH):
                e.copy(out=st_sb[:, j * 256:(j + 1) * 256],
                       in_=st_ps[j // 2][:, (j % 2) * 256:(j % 2) * 256 + 256])
        l.phase({"scalar": p_ev})

    def p_st(e, l):
        for j in range(2):
            l.dma("gpsimd", e, aov[j], vts[:, j * T:(j + 1) * T])
        e.wait_ge(l.dsem["gpsimd"], l.dcnt["gpsimd"])
    l.phase({"gpsimd": p_st})

    if max_phases is not None:
        l._phases = l._phases[:max_phases]
    nc = l.build()
    l.close()
    return nc



def emit_ln_start(l, z_in, xb_sb, zc, sq, u1, ones_bf, ones_row, rows, cst_eps,
                  after_chunk=None):
    r0, r1 = rows
    def p_zl(e, l):
        for ci in range(NCH):
            l.dma("gpsimd", e, xb_sb[:, ci * TC:(ci + 1) * TC],
                  z_in[ci * 128:(ci + 1) * 128, :])
    l.phase({"gpsimd": p_zl})
    for ci in range(NCH):
        l.phase({"tensor": (lambda e, l, ci=ci:
                            colsum(e, ones_bf, xb_sb[:, ci * TC:(ci + 1) * TC],
                                   u1, TC, start=(ci == 0), stop=(ci == NCH - 1)))})
    l.phase({"scalar": (lambda e, l:
                        e.activation(out=r0[0:1, :], in_=u1[0:1, :],
                                     func=AF.Identity, scale=1.0 / D))})
    for ci in range(NCH):
        for hf in range(2):
            l.phase({"scalar": (lambda e, l, ci=ci, hf=hf:
                                e.activation(out=sq[:, 0:1024],
                                             in_=xb_sb[:, ci * TC + hf * 1024:
                                                       ci * TC + (hf + 1) * 1024],
                                             func=AF.Square))})

            def p_cs(e, l, ci=ci, hf=hf):
                i = None
                for j in range(0, 1024, 512):
                    i = e.matmul(u1[0:1, hf * 1024 + j:hf * 1024 + j + 512],
                                 ones_bf[:, 0:1], sq[:, j:j + 512],
                                 start=(ci == 0), stop=(ci == NCH - 1))
                return i
            l.phase({"tensor": p_cs})
    def p_rows_s(e, l):
        e.activation(out=sq[0:1, 0:TC], in_=r0[0:1, :], func=AF.Square)
    def p_rows_v(e, l):
        e.tensor_tensor(out=r1[0:1, :], in0=r0[0:1, :], in1=r0[0:1, :], op=ALU.mult)
    l.phase({"vector": p_rows_v})

    def p_rows_s2(e, l):
        pass
    def p_var_v(e, l):
        e.scalar_tensor_tensor(out=r1[0:1, :], in0=u1[0:1, :], scalar=1.0 / D,
                               in1=r1[0:1, :], op0=ALU.mult, op1=ALU.subtract)
    l.phase({"vector": p_var_v})
    l.phase({"scalar": (lambda e, l:
                        e.activation(out=r1[0:1, :], in_=r1[0:1, :], func=AF.Sqrt,
                                     bias=cst_eps))})
    l.phase({"vector": (lambda e, l: e.reciprocal(r1[0:1, :], r1[0:1, :]))})
    l.phase({"tensor": (lambda e, l: bcast(e, ones_row, r0, u1, TC))})
    for ci in range(NCH):
        l.phase({"vector": (lambda e, l, ci=ci:
                            e.scalar_tensor_tensor(
                                out=xb_sb[:, ci * TC:(ci + 1) * TC],
                                in0=xb_sb[:, ci * TC:(ci + 1) * TC], scalar=1.0,
                                in1=u1[:], op0=ALU.mult, op1=ALU.subtract))})
    l.phase({"tensor": (lambda e, l: bcast(e, ones_row, r1, u1, TC))})
    for ci in range(NCH):
        l.phase({"vector": (lambda e, l, ci=ci:
                            e.tensor_tensor(out=xb_sb[:, ci * TC:(ci + 1) * TC],
                                            in0=xb_sb[:, ci * TC:(ci + 1) * TC],
                                            in1=u1[:], op=ALU.mult))})
        if after_chunk is not None:
            w = after_chunk(ci)
            if w:
                l.phase(w)


def emit_proj(l, pf_key, next_pf, w_sb_cur, w_sb3_next, wview_next, h_sb, out_sb, pp):
    for do in range(NCH):
        for nt in range(TC // 512):
            ps = pp[(do * (TC // 512) + nt) % 2]

            def p_mm(e, l, do=do, nt=nt, ps=ps, first=(do == 0 and nt == 0)):
                if first and pf_key is not None:
                    l.wait_pf(e, pf_key)
                i = None
                for ci in range(NCH):
                    i = e.matmul(ps[:],
                                 w_sb_cur[:, ci * D + do * 128:ci * D + (do + 1) * 128],
                                 h_sb[:, ci * TC + nt * 512:ci * TC + (nt + 1) * 512],
                                 start=(ci == 0), stop=(ci == NCH - 1))
                return i
            work = {"tensor": p_mm}
            if do == 0 and nt == 0 and next_pf is not None:
                def p_pf(e, l, nk=next_pf):
                    l.dma("gpsimd", e, w_sb3_next, wview_next, pf=nk)
                work["gpsimd"] = p_pf
            l.phase(work)
            l.phase({"scalar": (lambda e, l, do=do, nt=nt, ps=ps:
                                e.copy(out=out_sb[:, do * TC + nt * 512:
                                                  do * TC + (nt + 1) * 512],
                                       in_=ps[:]))})



def build_launchD(max_phases=None):
    l = L("launchD")
    x = l.inp("x", [D, TC], F32)
    at = l.inp("at", [D, TC], BF16)
    wo = l.inp("wo", [D, D], F32)
    n2w = l.inp("n2w", [128, NCH], F32)
    wg = l.inp("wg", [D, 2048], F32)
    wu = l.inp("wu", [D, 2048], F32)
    wd = l.inp("wd", [2048, D], F32)
    z_o = l.out("z", [D, TC], F32)

    h2_sb = l.sbuf("h2", [128, NCH * TC], BF16)
    x1_sb = l.sbuf("x1", [128, NCH * TC], BF16)
    gu_sb = l.sbuf("gu", [128, 8192], BF16)
    wo_sb = l.sbuf("wo_sb", [128, NCH * D], BF16)
    wg_sb = l.sbuf("wg_sb", [128, NCH * 2048], BF16)
    wu_sb = l.sbuf("wu_sb", [128, NCH * 2048], BF16)
    wd_sb = l.sbuf("wd_sb", [128, 16 * D], BF16)
    n2w_sb = l.sbuf("n2w_sb", [128, NCH], F32)
    xc = l.sbuf("xc", [128, 512], F32)
    zt = l.sbuf("zt", [128, 256], F32)
    sq = l.sbuf("sq", [128, TC], BF16)
    ones_bf = l.sbuf("ones_bf", [128, 1], BF16)
    ones_row = l.sbuf("ones_row", [1, 128], F32)
    r0 = l.sbuf("r0", [1, TC], F32)
    cst = l.sbuf("cst", [128, 1], F32)

    u1 = l.psum("u1", [128, TC], F32)
    pp = [l.psum(f"pp{i}", [128, 512], F32) for i in range(2)]

    at_v = at.rearrange("(c p) t -> p c t", p=128)
    h2_3 = h2_sb.ap().rearrange("p (c t) -> p c t", c=NCH)
    wo_v = wo.rearrange("(c p) o -> p c o", p=128)
    wo_3 = wo_sb.ap().rearrange("p (c o) -> p c o", c=NCH)
    wg_v = wg.rearrange("(c p) o -> p c o", p=128)
    wg_3 = wg_sb.ap().rearrange("p (c o) -> p c o", c=NCH)
    wu_v = wu.rearrange("(c p) o -> p c o", p=128)
    wu_3 = wu_sb.ap().rearrange("p (c o) -> p c o", c=NCH)
    wd_v = wd.rearrange("(c p) o -> p c o", p=128)
    wd_3 = wd_sb.ap().rearrange("p (c o) -> p c o", c=16)

    def p0(e, l):
        l.dma("sync", e, h2_3, at_v)
        l.dma("sync", e, n2w_sb[:], n2w)
    def p0g(e, l):
        l.dma("gpsimd", e, wo_3, wo_v, pf="wo")
        l.dma("gpsimd", e, wg_3, wg_v, pf="wg")
        l.dma("gpsimd", e, wu_3, wu_v, pf="wu")
        l.dma("gpsimd", e, wd_3, wd_v, pf="wd")
    def p0v(e, l):
        e.memset(ones_bf[:], 1.0)
        e.memset(ones_row[:], 1.0)
        e.memset(cst[:], 1e-6)
    l.phase({"sync": p0, "gpsimd": p0g, "vector": p0v})

    for do in range(NCH):
        for nt in range(TC // 512):
            ps = pp[(do * 4 + nt) % 2]

            def p_mm(e, l, do=do, nt=nt, ps=ps, first=(do == 0 and nt == 0)):
                if first:
                    l.wait_pf(e, "wo")
                i = None
                for ci in range(NCH):
                    i = e.matmul(ps[:],
                                 wo_sb[:, ci * D + do * 128:ci * D + (do + 1) * 128],
                                 h2_sb[:, ci * TC + nt * 512:ci * TC + (nt + 1) * 512],
                                 start=(ci == 0), stop=(ci == NCH - 1))
                return i

            def p_xc(e, l, do=do, nt=nt):
                l.dma("sync", e, xc[:], x[do * 128:(do + 1) * 128, nt * 512:(nt + 1) * 512])
            l.phase({"tensor": p_mm, "sync": p_xc})
            l.phase({"vector": (lambda e, l, do=do, nt=nt, ps=ps:
                                e.scalar_tensor_tensor(
                                    out=x1_sb[:, do * TC + nt * 512:do * TC + (nt + 1) * 512],
                                    in0=ps[:], scalar=1.0, in1=xc[:],
                                    op0=ALU.mult, op1=ALU.add))})

    for ci in range(NCH):
        l.phase({"scalar": (lambda e, l, ci=ci:
                            e.activation(out=sq[:], in_=x1_sb[:, ci * TC:(ci + 1) * TC],
                                         func=AF.Square))})
        l.phase({"tensor": (lambda e, l, ci=ci:
                            colsum(e, ones_bf, sq, u1, TC, start=(ci == 0),
                                   stop=(ci == NCH - 1)))})
    l.phase({"scalar": (lambda e, l:
                        e.activation(out=r0[0:1, :], in_=u1[0:1, :], func=AF.Sqrt,
                                     scale=1.0 / D, bias=cst[0:1, 0:1]))})
    l.phase({"vector": (lambda e, l: e.reciprocal(r0[0:1, :], r0[0:1, :]))})
    l.phase({"tensor": (lambda e, l: bcast(e, ones_row, r0, u1, TC))})
    for ci in range(NCH):
        l.phase({"vector": (lambda e, l, ci=ci:
                            e.scalar_tensor_tensor(
                                out=h2_sb[:, ci * TC:(ci + 1) * TC],
                                in0=x1_sb[:, ci * TC:(ci + 1) * TC],
                                scalar=n2w_sb[:, ci:ci + 1], in1=u1[:],
                                op0=ALU.mult, op1=ALU.mult))})

    NS = 256
    for s in range(TC // NS):
        for fc in range(16):
            def p_mm(e, l, s=s, fc=fc, first=(s == 0 and fc == 0)):
                if first:
                    l.wait_pf(e, "wg")
                    l.wait_pf(e, "wu")
                for ci in range(NCH):
                    e.matmul(pp[0][:, 0:NS],
                             wg_sb[:, ci * 2048 + fc * 128:ci * 2048 + (fc + 1) * 128],
                             h2_sb[:, ci * TC + s * NS:ci * TC + (s + 1) * NS],
                             start=(ci == 0), stop=(ci == NCH - 1))
                i = None
                for ci in range(NCH):
                    i = e.matmul(pp[1][:, 0:NS],
                                 wu_sb[:, ci * 2048 + fc * 128:ci * 2048 + (fc + 1) * 128],
                                 h2_sb[:, ci * TC + s * NS:ci * TC + (s + 1) * NS],
                                 start=(ci == 0), stop=(ci == NCH - 1))
                return i
            l.phase({"tensor": p_mm})

            def p_ev(e, l, fc=fc):
                e.activation(out=gu_sb[:, fc * NS:(fc + 1) * NS], in_=pp[0][:, 0:NS],
                             func=AF.Silu)
            def p_ev2(e, l, fc=fc):
                e.tensor_copy(gu_sb[:, 4096 + fc * NS:4096 + (fc + 1) * NS],
                              pp[1][:, 0:NS])
            l.phase({"scalar": p_ev, "vector": p_ev2})

        l.phase({"vector": (lambda e, l:
                            e.tensor_tensor(out=gu_sb[:, 0:4096], in0=gu_sb[:, 0:4096],
                                            in1=gu_sb[:, 4096:8192], op=ALU.mult))})
        for do in range(NCH):
            def p_mm(e, l, s=s, do=do, first=(s == 0 and do == 0)):
                if first:
                    l.wait_pf(e, "wd")
                i = None
                for fc in range(16):
                    i = e.matmul(pp[0][:, 0:NS],
                                 wd_sb[:, fc * D + do * 128:fc * D + (do + 1) * 128],
                                 gu_sb[:, fc * NS:(fc + 1) * NS],
                                 start=(fc == 0), stop=(fc == 15))
                return i

            def p_xc(e, l, s=s, do=do):
                l.dma("sync", e, xc[:, 0:NS],
                      x[do * 128:(do + 1) * 128, s * NS:(s + 1) * NS])
            l.phase({"tensor": p_mm, "sync": p_xc})

            def p_z(e, l, s=s, do=do):
                e.scalar_tensor_tensor(
                    out=zt[:], in0=pp[0][:, 0:NS], scalar=1.0,
                    in1=x1_sb[:, do * TC + s * NS:do * TC + (s + 1) * NS],
                    op0=ALU.mult, op1=ALU.add)
                e.drain()
                e.tensor_tensor(out=zt[:], in0=zt[:], in1=xc[:, 0:NS], op=ALU.add)
            l.phase({"vector": p_z})
            l.phase({"gpsimd": (lambda e, l, s=s, do=do:
                                l.dma("gpsimd", e,
                                      z_o[do * 128:(do + 1) * 128, s * NS:(s + 1) * NS],
                                      zt[:]))})

    def p_fin(e, l):
        e.wait_ge(l.dsem["gpsimd"], l.dcnt["gpsimd"])
    l.phase({"gpsimd": p_fin})

    if max_phases is not None:
        l._phases = l._phases[:max_phases]
    nc = l.build()
    l.close()
    return nc



def build_launchE(max_phases=None):
    l = L("launchE")
    z_i = l.inp("z", [D, TC], F32)
    wrt = l.inp("wrt", [D, 8], F32)
    brr = l.inp("brr", [128, 8], F32)
    w1t = l.inp("w1t", [8, D, 4096], F32)
    w2t = l.inp("w2t", [8, 4096, D], F32)
    b1c = l.inp("b1c", [128, 8 * 32], F32)
    b2c = l.inp("b2c", [128, 8 * 8], F32)
    ident = l.inp("ident", [128, 128], BF16)
    z2_o = l.out("z2", [D, TC], F32)

    xb = l.sbuf("xb", [128, NCH * TC], BF16)
    w1_sb = l.sbuf("w1_sb", [128, NCH * 4096], BF16)
    w2_sb = l.sbuf("w2_sb", [128, 32 * D], BF16)
    h_sb = l.sbuf("h_sb", [128, 32 * 256], BF16)
    x2s = l.sbuf("x2s", [128, NCH * 256], BF16)
    msk = l.sbuf("msk", [128, 512], BF16)
    mrow = l.sbuf("mrow", [1, 512], BF16)
    lg = l.sbuf("lg", [128, 128], F32)
    mx = l.sbuf("mx", [128, 8], F32)
    oh = l.sbuf("oh", [128, 128], BF16)
    wrt_sb = l.sbuf("wrt_sb", [128, NCH * 8], BF16)
    brr_sb = l.sbuf("brr_sb", [128, 8], F32)
    b1_sb = l.sbuf("b1_sb", [128, 8 * 32], F32)
    b2_sb = l.sbuf("b2_sb", [128, 8 * 8], F32)
    id_sb = l.sbuf("id_sb", [128, 128], BF16)
    zt2 = l.sbuf("zt2", [128, 256], F32)
    sq = l.sbuf("sq", [128, 1024], BF16)
    ones_bf = l.sbuf("ones_bf", [128, 1], BF16)
    ones_row = l.sbuf("ones_row", [1, 128], F32)
    ones_rb = l.sbuf("ones_rb", [1, 128], BF16)
    r0 = l.sbuf("r0", [1, TC], F32)
    r1 = l.sbuf("r1", [1, TC], F32)
    tmp_bf = l.sbuf("tmp_bf", [128, 512], BF16)
    cst = l.sbuf("cst", [128, 1], F32)

    u1 = l.psum("u1", [128, TC], F32)
    pp = [l.psum(f"pp{i}", [128, 512], F32) for i in range(2)]
    bps = l.psum("bps", [128, 128], F32)
    msps = l.psum("msps", [128, 512], BF16)

    x2d = l.dram("x2d", [D, TC], BF16)
    x2dv = x2d.rearrange("(c p) t -> p c t", p=128)
    masks_d = l.dram("masks_d", [8, TC], BF16)

    wrt_v = wrt.rearrange("(c p) o -> p c o", p=128)
    wrt_3 = wrt_sb.ap().rearrange("p (c o) -> p c o", c=NCH)
    w1v = w1t.rearrange("e (c p) f -> e p c f", p=128)
    w1_3 = w1_sb.ap().rearrange("p (c f) -> p c f", c=NCH)
    w2v = w2t.rearrange("e (c p) o -> e p c o", p=128)
    w2_3 = w2_sb.ap().rearrange("p (c o) -> p c o", c=32)

    def p0(e, l):
        l.dma("sync", e, brr_sb[:], brr)
        l.dma("sync", e, b1_sb[:], b1c)
        l.dma("sync", e, b2_sb[:], b2c)
        l.dma("sync", e, id_sb[:], ident)
    def p0g(e, l):
        l.dma("gpsimd", e, wrt_3, wrt_v)
    def p0v(e, l):
        e.memset(ones_bf[:], 1.0)
        e.memset(ones_row[:], 1.0)
        e.memset(ones_rb[:], 1.0)
        e.memset(cst[:], 1e-5)
    l.phase({"sync": p0, "gpsimd": p0g, "vector": p0v})

    def after_chunk(ci):
        def p_lg(e, l, ci=ci):
            i = None
            for tt in range(TC // 128):
                i = e.matmul(bps[:, tt * 8:(tt + 1) * 8],
                             xb[:, ci * TC + tt * 128:ci * TC + (tt + 1) * 128],
                             wrt_sb[:, ci * 8:(ci + 1) * 8],
                             start=(ci == 0 and tt == 0),
                             stop=(ci == NCH - 1 and tt == TC // 128 - 1),
                             skip_group_check=True)
            return i
        return {"tensor": p_lg,
                "gpsimd": (lambda e, l, ci=ci:
                           l.dma("gpsimd", e, x2d[ci * 128:(ci + 1) * 128, :],
                                 xb[:, ci * TC:(ci + 1) * TC]))}
    emit_ln_start(l, z_i, xb, None, sq, u1, ones_bf, ones_row, (r0, r1),
                  cst[0:1, 0:1], after_chunk=after_chunk)

    def p_lgf(e, l):
        e.tensor_copy(lg[:], bps[:])
        e.drain()
        for tt in range(TC // 128):
            e.tensor_tensor(out=lg[:, tt * 8:(tt + 1) * 8],
                            in0=lg[:, tt * 8:(tt + 1) * 8], in1=brr_sb[:],
                            op=ALU.add)
    l.phase({"vector": p_lgf})
    for tt in range(TC // 128):
        def p_oh(e, l, tt=tt):
            e.max(mx[:], lg[:, tt * 8:(tt + 1) * 8])
            e.drain()
            e.tensor_scalar(out=oh[:, tt * 8:(tt + 1) * 8],
                            in0=lg[:, tt * 8:(tt + 1) * 8],
                            scalar1=mx[:, 0:1], scalar2=None, op0=ALU.is_equal)
        l.phase({"vector": p_oh})
    for g in range(TC // 128 * 8 // 4):
        def p_tr(e, l, g=g):
            i = None
            for r in range(4):
                idx = g * 4 + r
                tt, ee = idx // 8, idx % 8
                i = e.transpose(msps[0:1, r * 128:(r + 1) * 128],
                                oh[:, tt * 8 + ee:tt * 8 + ee + 1], id_sb[:])
            return i
        l.phase({"tensor": p_tr})

        def p_cp(e, l, g=g):
            e.copy(out=mrow[0:1, :], in_=msps[0:1, 0:512])
        l.phase({"scalar": p_cp})

        def p_md(e, l, g=g):
            idx = g * 4
            tt, ee = idx // 8, idx % 8
            l.dma("gpsimd", e,
                  masks_d.rearrange("(o e) t -> o e t", o=1)[0:1, ee:ee + 4,
                                                            tt * 128:(tt + 1) * 128],
                  mrow.ap().rearrange("p (a t) -> p a t", a=4))
        l.phase({"gpsimd": p_md})

    l.phase({"vector": (lambda e, l: e.memset(xb[:], 0.0))})

    SS = 256
    et_sem = l.es.enter_context(l.nc.semaphore("e_t"))
    es_sem = l.es.enter_context(l.nc.semaphore("e_s"))
    ev_sem = l.es.enter_context(l.nc.semaphore("e_v"))
    l._e_t = 0
    l._e_s = 0
    l._e_v = 0
    ACC = [pp[0][:, 0:SS], pp[1][:, 0:SS], u1[:, 0:SS], u1[:, 512:512 + SS]]

    for ee in range(8):
        def p_pf(e, l, ee=ee):
            l.dma("gpsimd", e, w1_3, w1v[ee], pf=f"w1_{ee}")
            l.dma("gpsimd", e, w2_3, w2v[ee], pf=f"w2_{ee}")
        l.phase({"gpsimd": p_pf})
        for s in range(TC // SS):
            def p_xs(e, l, s=s):
                l.dma("sync", e, x2s.ap().rearrange("p (c t) -> p c t", c=NCH),
                      x2dv[:, :, s * SS:(s + 1) * SS])
            def p_mr(e, l, ee=ee, s=s):
                l.dma("sync", e, mrow[0:1, 0:SS],
                      masks_d[ee:ee + 1, s * SS:(s + 1) * SS])
            l.phase({"sync": (lambda e, l, a=p_xs, b=p_mr: (a(e, l), b(e, l)) and None)})

            def p_mbc(e, l):
                return e.matmul(u1[:, 1024:1024 + SS], ones_rb[0:1, :],
                                mrow[0:1, 0:SS], start=True, stop=True)
            l.phase({"tensor": p_mbc})
            l.phase({"scalar": (lambda e, l:
                                e.copy(out=msk[:, 0:SS], in_=u1[:, 1024:1024 + SS]))})

            def loopT1(e, l, ee=ee, s=s):
                s0, v0 = l._e_s, l._e_v
                if s == 0:
                    l.wait_pf(e, f"w1_{ee}")
                last = None
                for fc in range(32):
                    if fc >= 4:
                        pv = fc - 4
                        if pv % 2 == 0:
                            e.wait_ge(es_sem, s0 + pv // 2 + 1)
                        else:
                            e.wait_ge(ev_sem, v0 + pv // 2 + 1)
                    for ci in range(NCH):
                        last = e.matmul(ACC[fc % 4],
                                        w1_sb[:, ci * 4096 + fc * 128:
                                              ci * 4096 + (fc + 1) * 128],
                                        x2s[:, ci * SS:(ci + 1) * SS],
                                        start=(ci == 0), stop=(ci == NCH - 1))
                    last.then_inc(et_sem, 1)
                    l._e_t += 1
                return None

            def loopS1(e, l, ee=ee):
                t0 = l._e_t - 32
                for fc in range(0, 32, 2):
                    e.wait_ge(et_sem, t0 + fc + 1)
                    e.activation(out=h_sb[:, fc * SS:(fc + 1) * SS], in_=ACC[fc % 4],
                                 func=AF.Relu,
                                 bias=b1_sb[:, ee * 32 + fc:ee * 32 + fc + 1]
                                 ).then_inc(es_sem, 1)
                    l._e_s += 1
                return None

            def loopV1(e, l, ee=ee):
                t0 = l._e_t - 32
                for fc in range(1, 32, 2):
                    e.wait_ge(et_sem, t0 + fc + 1)
                    i = e.tensor_scalar(out=h_sb[:, fc * SS:(fc + 1) * SS],
                                        in0=ACC[fc % 4],
                                        scalar1=b1_sb[:, ee * 32 + fc:ee * 32 + fc + 1],
                                        scalar2=0.0, op0=ALU.add, op1=ALU.max)
                    e.drain()
                    i.then_inc(ev_sem, 1)
                    l._e_v += 1
                return None

            l.phase({"tensor": loopT1, "scalar": loopS1, "vector": loopV1})

            def loopT2(e, l, ee=ee, s=s):
                v0 = l._e_v
                if s == 0:
                    l.wait_pf(e, f"w2_{ee}")
                last = None
                for do in range(NCH):
                    if do >= 4:
                        e.wait_ge(ev_sem, v0 + do - 3)
                    for fc in range(32):
                        last = e.matmul(ACC[do % 4],
                                        w2_sb[:, fc * D + do * 128:
                                              fc * D + (do + 1) * 128],
                                        h_sb[:, fc * SS:(fc + 1) * SS],
                                        start=(fc == 0), stop=(fc == 31))
                    last.then_inc(et_sem, 1)
                    l._e_t += 1
                return None

            def loopV2(e, l, ee=ee, s=s):
                t0 = l._e_t - NCH
                for do in range(NCH):
                    e.wait_ge(et_sem, t0 + do + 1)
                    e.scalar_tensor_tensor(out=tmp_bf[:, 0:SS], in0=ACC[do % 4],
                                           scalar=b2_sb[:, ee * 8 + do:ee * 8 + do + 1],
                                           in1=msk[:, 0:SS],
                                           op0=ALU.add, op1=ALU.mult)
                    e.drain()
                    i = e.tensor_tensor(
                        out=xb[:, do * TC + s * SS:do * TC + (s + 1) * SS],
                        in0=xb[:, do * TC + s * SS:do * TC + (s + 1) * SS],
                        in1=tmp_bf[:, 0:SS], op=ALU.add)
                    e.drain()
                    i.then_inc(ev_sem, 1)
                    l._e_v += 1
                return None

            l.phase({"tensor": loopT2, "vector": loopV2})

    for ci in range(NCH):
        for hf in range(2):
            def p_x2(e, l, ci=ci, hf=hf):
                l.dma("sync", e, sq[:, 0:1024],
                      x2d[ci * 128:(ci + 1) * 128, hf * 1024:(hf + 1) * 1024])
            l.phase({"sync": p_x2})
            for s4 in range(4):
                def p_add(e, l, ci=ci, hf=hf, s4=s4):
                    e.tensor_tensor(out=zt2[:], in0=sq[:, s4 * 256:(s4 + 1) * 256],
                                    in1=xb[:, ci * TC + hf * 1024 + s4 * 256:
                                            ci * TC + hf * 1024 + (s4 + 1) * 256],
                                    op=ALU.add)
                l.phase({"vector": p_add})
                l.phase({"gpsimd": (lambda e, l, ci=ci, hf=hf, s4=s4:
                                    l.dma("gpsimd", e,
                                          z2_o[ci * 128:(ci + 1) * 128,
                                               hf * 1024 + s4 * 256:hf * 1024 + (s4 + 1) * 256],
                                          zt2[:]))})

    def p_fin(e, l):
        e.wait_ge(l.dsem["gpsimd"], l.dcnt["gpsimd"])
    l.phase({"gpsimd": p_fin})

    if max_phases is not None:
        l._phases = l._phases[:max_phases]
    nc = l.build()
    l.close()
    return nc



def build_launchF(max_phases=None):
    l = L("launchF")
    z_i = l.inp("z2", [D, TC], F32)
    wqt = l.inp("wqt", [D, D], F32)
    wkt = l.inp("wkt", [D, D], F32)
    wvt = l.inp("wvt", [D, D], F32)
    cosb = l.inp("cosb", [128, TC], F32)
    sinb = l.inp("sinb", [128, TC], F32)
    x3_o = l.out("x3", [D, TC], F32)
    q_o = l.out("q", [D, TC], BF16)
    k_o = l.out("k", [D, TC], BF16)
    v_o = l.out("v", [D, TC], BF16)

    xb = l.sbuf("xb", [128, NCH * TC], BF16)
    w_sb = [l.sbuf(f"w{i}", [128, NCH * D], BF16) for i in range(2)]
    out_sb = l.sbuf("outb", [128, NCH * TC], BF16)
    cos_sb = l.sbuf("cos_sb", [128, TC], F32)
    sin_sb = l.sbuf("sin_sb", [128, TC], F32)
    t1 = l.sbuf("t1", [128, TC], BF16)
    t2 = l.sbuf("t2", [128, TC], BF16)
    sq = l.sbuf("sq", [128, TC], BF16)
    ones_bf = l.sbuf("ones_bf", [128, 1], BF16)
    ones_row = l.sbuf("ones_row", [1, 128], F32)
    r0 = l.sbuf("r0", [1, TC], F32)
    r1 = l.sbuf("r1", [1, TC], F32)
    cst = l.sbuf("cst", [128, 1], F32)

    u1 = l.psum("u1", [128, TC], F32)
    pp = [l.psum(f"pp{i}", [128, 512], F32) for i in range(2)]

    wv_views = {
        "q": wqt.rearrange("(c p) o -> p c o", p=128),
        "k": wkt.rearrange("(c p) o -> p c o", p=128),
        "v": wvt.rearrange("(c p) o -> p c o", p=128),
    }
    w_sb3 = [w_sb[i].ap().rearrange("p (c o) -> p c o", c=NCH) for i in range(2)]

    def p0(e, l):
        l.dma("sync", e, cos_sb[:], cosb)
        l.dma("sync", e, sin_sb[:], sinb)
    def p0g(e, l):
        l.dma("gpsimd", e, w_sb3[0], wv_views["q"], pf="q")
    def p0v(e, l):
        e.memset(ones_bf[:], 1.0)
        e.memset(ones_row[:], 1.0)
        e.memset(cst[:], 1e-5)
    l.phase({"sync": p0, "gpsimd": p0g, "vector": p0v})

    def after_chunk(ci):
        return {"gpsimd": (lambda e, l, ci=ci:
                           l.dma("gpsimd", e, x3_o[ci * 128:(ci + 1) * 128, :],
                                 xb[:, ci * TC:(ci + 1) * TC]))}
    emit_ln_start(l, z_i, xb, None, sq, u1, ones_bf, ones_row, (r0, r1),
                  cst[0:1, 0:1], after_chunk=after_chunk)

    def rope(src):
        shf = [(i + 16) % 32 for i in range(32)]
        for ci in range(NCH):
            sl = slice(ci * TC, (ci + 1) * TC)

            def p_r1(e, l, sl=sl):
                e.stream_shuffle(t2[:], src[:, sl], shf)
                e.tensor_tensor(out=t1[:], in0=src[:, sl], in1=cos_sb[:], op=ALU.mult)
                e.drain()
                e.tensor_tensor(out=t2[:], in0=t2[:], in1=sin_sb[:], op=ALU.mult)
                e.drain()
                e.tensor_tensor(out=src[:, sl], in0=t1[:], in1=t2[:], op=ALU.add)
            l.phase({"vector": p_r1})

    def store(out_ap):
        def p_st(e, l):
            for ci in range(NCH):
                l.dma("gpsimd", e, out_ap[ci * 128:(ci + 1) * 128, :],
                      out_sb[:, ci * TC:(ci + 1) * TC])
        l.phase({"gpsimd": p_st})

    emit_proj(l, "q", "k", w_sb[0], w_sb3[1], wv_views["k"], xb, out_sb, pp)
    rope(out_sb)
    store(q_o)
    emit_proj(l, "k", "v", w_sb[1], w_sb3[0], wv_views["v"], xb, out_sb, pp)
    rope(out_sb)
    store(k_o)
    emit_proj(l, "v", None, w_sb[0], None, None, xb, out_sb, pp)
    store(v_o)

    def p_fin(e, l):
        e.wait_ge(l.dsem["gpsimd"], l.dcnt["gpsimd"])
    l.phase({"gpsimd": p_fin})

    if max_phases is not None:
        l._phases = l._phases[:max_phases]
    nc = l.build()
    l.close()
    return nc



def build_launchG(max_phases=None):
    l = L("launchG")
    NKT = T // 128
    qr = l.inp("qr", [D, TC], BF16)
    kh = l.inp("kh", [16, 64, T], BF16)
    vh = l.inp("vh", [16, T, 64], BF16)
    x3 = l.inp("x3", [D, TC], F32)
    wot = l.inp("wot", [D, D], F32)
    z3_o = l.out("z3", [D, TC], F32)

    q_sb = l.sbuf("q_sb", [128, NCH * TC], BF16)
    kd_sb = l.sbuf("kd_sb", [128, T], BF16)
    va_sb = l.sbuf("va_sb", [128, NKT * 64], BF16)
    vb_sb = l.sbuf("vb_sb", [128, NKT * 64], BF16)
    ctx_sb = l.sbuf("ctx_sb", [128, NCH * TC], BF16)
    sqk = l.sbuf("sqk", [128, T], BF16)
    et = [l.sbuf(f"et{i}", [128, 1024], BF16) for i in range(2)]
    wot_sb = l.sbuf("wot_sb", [128, NCH * D], BF16)
    bcs = l.sbuf("bcs", [128, 512], F32)
    rws = l.sbuf("rws", [1, 1024], F32)
    mq2 = l.sbuf("mq2", [1, 16], F32)
    mk2 = l.sbuf("mk2", [1, 16], F32)
    sc1 = l.sbuf("sc1", [1, 8], F32)
    nbias = l.sbuf("nbias", [128, 1], F32)
    xc = l.sbuf("xc", [128, 512], F32)
    zt = l.sbuf("zt", [128, 512], F32)
    ones_bf = l.sbuf("ones_bf", [128, 1], BF16)
    ones_row = l.sbuf("ones_row", [1, 128], F32)

    sc = [l.psum(f"scps{i}", [128, 1024], F32) for i in range(2)]
    ctxp = l.psum("ctxp", [128, 1024], F32)
    denp = l.psum("denp", [128, 1024], F32)

    tsem = l.es.enter_context(l.nc.semaphore("g_t"))
    ssem = l.es.enter_context(l.nc.semaphore("g_s"))
    l._g_t = 0
    l._g_s = 0

    q_v = qr.rearrange("(c p) t -> p c t", p=128)
    q_sb3 = q_sb.ap().rearrange("p (c t) -> p c t", c=NCH)
    wot_v = wot.rearrange("(c p) o -> p c o", p=128)
    wot_3 = wot_sb.ap().rearrange("p (c o) -> p c o", c=NCH)

    def p0(e, l):
        l.dma("sync", e, q_sb3, q_v)
    def p0g(e, l):
        l.dma("gpsimd", e, wot_3, wot_v, pf="wo")
    def p0v(e, l):
        e.memset(ones_bf[:], 1.0)
        e.memset(ones_row[:], 1.0)
    l.phase({"sync": p0, "gpsimd": p0g, "vector": p0v})

    for ci in range(NCH):
        l.phase({"scalar": (lambda e, l, ci=ci:
                            e.activation(out=sqk[:, 0:TC],
                                         in_=q_sb[:, ci * TC:(ci + 1) * TC],
                                         func=AF.Square))})
        for hh in range(2):
            def p_cs(e, l, hh=hh):
                i = None
                for j in range(0, TC, 512):
                    i = e.matmul(ctxp[0:1, 0:512],
                                 ones_bf[hh * 64:(hh + 1) * 64, 0:1],
                                 sqk[hh * 64:(hh + 1) * 64, j:j + 512],
                                 start=(j == 0), stop=(j == TC - 512))
                return i
            l.phase({"tensor": p_cs})

            def p_mx(e, l, ci=ci, hh=hh):
                pass
            def p_mx2(e, l, ci=ci, hh=hh):
                e.reduce_max(mq2[0:1, ci * 2 + hh:ci * 2 + hh + 1],
                             ctxp[0:1, 0:512], axis=AX.X)
            l.phase({"vector": p_mx2})

    for dj in range(NCH):
        def p_ld(e, l, dj=dj):
            l.dma("sync", e, kd_sb[:],
                  kh.rearrange("h p t -> (h p) t")[dj * 128:(dj + 1) * 128, :])
            l.dma("sync", e, va_sb.ap().rearrange("p (n d) -> p n d", d=64),
                  vh.rearrange("h (n p) d -> h p n d", p=128)[2 * dj])
            l.dma("sync", e, vb_sb.ap().rearrange("p (n d) -> p n d", d=64),
                  vh.rearrange("h (n p) d -> h p n d", p=128)[2 * dj + 1])
        l.phase({"sync": p_ld})

        l.phase({"scalar": (lambda e, l: e.activation(out=sqk[:], in_=kd_sb[:],
                                                      func=AF.Square))})
        for hh in range(2):
            for pc in range(T // 512):
                def p_cs(e, l, hh=hh, pc=pc):
                    return e.matmul(ctxp[0:1, 0:512],
                                    ones_bf[hh * 64:(hh + 1) * 64, 0:1],
                                    sqk[hh * 64:(hh + 1) * 64,
                                        pc * 512:(pc + 1) * 512],
                                    start=True, stop=True)
                l.phase({"tensor": p_cs})

                def p_mx(e, l, dj=dj, hh=hh, pc=pc):
                    e.reduce_max(sc1[0:1, 0:1], ctxp[0:1, 0:512], axis=AX.X)
                    e.drain()
                    if pc == 0:
                        e.tensor_copy(mk2[0:1, dj * 2 + hh:dj * 2 + hh + 1],
                                      sc1[0:1, 0:1])
                    else:
                        e.tensor_tensor(out=mk2[0:1, dj * 2 + hh:dj * 2 + hh + 1],
                                        in0=mk2[0:1, dj * 2 + hh:dj * 2 + hh + 1],
                                        in1=sc1[0:1, 0:1], op=ALU.max)
                l.phase({"vector": p_mx})

        def p_nb1(e, l, dj=dj):
            e.tensor_tensor(out=sc1[0:1, 0:2], in0=mq2[0:1, dj * 2:dj * 2 + 2],
                            in1=mk2[0:1, dj * 2:dj * 2 + 2], op=ALU.mult)
            e.drain()
            e.reduce_max(sc1[0:1, 2:3], sc1[0:1, 0:2], axis=AX.X)
        l.phase({"vector": p_nb1})
        l.phase({"scalar": (lambda e, l:
                            e.activation(out=sc1[0:1, 3:4], in_=sc1[0:1, 2:3],
                                         func=AF.Sqrt))})
        l.phase({"vector": (lambda e, l:
                            e.tensor_scalar_mul(sc1[0:1, 4:5], sc1[0:1, 3:4], -0.125))})
        l.phase({"tensor": (lambda e, l:
                            e.matmul(ctxp[:, 0:1], ones_row[0:1, :], sc1[0:1, 4:5],
                                     start=True, stop=True))})
        l.phase({"scalar": (lambda e, l: e.copy(out=nbias[:], in_=ctxp[:, 0:1]))})

        for qs in range(TC // 512):
            qsl = slice(dj * TC + qs * 512, dj * TC + (qs + 1) * 512)

            def mk_scores(e, kt, qsl=qsl):
                i = None
                for hh in range(2):
                    i = e.matmul(
                        sc[kt % 2][:, hh * 512:(hh + 1) * 512],
                        kd_sb[hh * 64:(hh + 1) * 64, kt * 128:(kt + 1) * 128],
                        q_sb.ap()[hh * 64:(hh + 1) * 64, qsl],
                        start=True, stop=True,
                        tile_position=(hh * 64, 0))
                return i

            def mk_pv(e, kt):
                first = (kt == 0)
                e.matmul(ctxp[0:64, 0:512], va_sb[:, kt * 64:(kt + 1) * 64],
                         et[kt % 2][:, 0:512],
                         start=first, stop=(kt == NKT - 1),
                         tile_position=(0, 0), skip_group_check=True)
                e.matmul(ctxp[64:128, 512:1024], vb_sb[:, kt * 64:(kt + 1) * 64],
                         et[kt % 2][:, 512:1024],
                         start=first, stop=(kt == NKT - 1),
                         tile_position=(0, 64), skip_group_check=True)
                e.matmul(denp[0:1, 0:512], ones_bf[:, 0:1], et[kt % 2][:, 0:512],
                         start=first, stop=(kt == NKT - 1),
                         tile_position=(0, 0), skip_group_check=True)
                return e.matmul(denp[0:1, 512:1024], ones_bf[:, 0:1],
                                et[kt % 2][:, 512:1024],
                                start=first, stop=(kt == NKT - 1),
                                tile_position=(0, 0), skip_group_check=True)

            def loopT(e, l, mk_scores=mk_scores, mk_pv=mk_pv):
                t0, s0 = l._g_t, l._g_s
                for kt in range(NKT):
                    if kt >= 2:
                        e.wait_ge(ssem, s0 + kt - 1)
                    mk_scores(e, kt).then_inc(tsem, 1)
                    l._g_t += 1
                    if kt >= 1:
                        e.wait_ge(ssem, s0 + kt)
                        mk_pv(e, kt - 1)
                e.wait_ge(ssem, s0 + NKT)
                return mk_pv(e, NKT - 1)

            def loopS(e, l):
                t0 = l._g_t - NKT
                for kt in range(NKT):
                    e.wait_ge(tsem, t0 + kt + 1)
                    e.activation(out=et[kt % 2][:], in_=sc[kt % 2][:],
                                 func=AF.Exp, scale=0.125,
                                 bias=nbias[:, 0:1]).then_inc(ssem, 1)
                    l._g_s += 1
                return None

            l.phase({"tensor": loopT, "scalar": loopS})

            def p_rc(e, l):
                e.reciprocal(rws[0:1, :], denp[0:1, :])
            l.phase({"vector": p_rc})

            def p_bc(e, l):
                e.matmul(sc[0][0:64, 0:512], ones_row[0:1, 0:64], rws[0:1, 0:512],
                         start=True, stop=True, tile_position=(0, 0))
                return e.matmul(sc[0][64:128, 0:512], ones_row[0:1, 0:64],
                                rws[0:1, 512:1024],
                                start=True, stop=True, tile_position=(0, 64))
            l.phase({"tensor": p_bc})
            l.phase({"scalar": (lambda e, l: e.copy(out=bcs[:], in_=sc[0][:, 0:512]))})

            def p_nm(e, l, qsl=qsl):
                e.tensor_tensor(out=ctx_sb.ap()[0:64, qsl], in0=ctxp[0:64, 0:512],
                                in1=bcs[0:64, :], op=ALU.mult)
                e.drain()
                e.tensor_tensor(out=ctx_sb.ap()[64:128, qsl],
                                in0=ctxp[64:128, 512:1024],
                                in1=bcs[64:128, :], op=ALU.mult)
            l.phase({"vector": p_nm})

    for do in range(NCH):
        for nt in range(TC // 512):
            def p_mm(e, l, do=do, nt=nt, first=(do == 0 and nt == 0)):
                if first:
                    l.wait_pf(e, "wo")
                i = None
                for ci in range(NCH):
                    i = e.matmul(ctxp[:, 0:512],
                                 wot_sb[:, ci * D + do * 128:ci * D + (do + 1) * 128],
                                 ctx_sb[:, ci * TC + nt * 512:ci * TC + (nt + 1) * 512],
                                 start=(ci == 0), stop=(ci == NCH - 1))
                return i

            def p_xc(e, l, do=do, nt=nt):
                l.dma("sync", e, xc[:], x3[do * 128:(do + 1) * 128, nt * 512:(nt + 1) * 512])
            l.phase({"tensor": p_mm, "sync": p_xc})

            def p_z(e, l):
                e.scalar_tensor_tensor(out=zt[:], in0=ctxp[:, 0:512], scalar=1.0,
                                       in1=xc[:], op0=ALU.mult, op1=ALU.add)
            l.phase({"vector": p_z})
            l.phase({"gpsimd": (lambda e, l, do=do, nt=nt:
                                l.dma("gpsimd", e,
                                      z3_o[do * 128:(do + 1) * 128, nt * 512:(nt + 1) * 512],
                                      zt[:]))})

    def p_fin(e, l):
        e.wait_ge(l.dsem["gpsimd"], l.dcnt["gpsimd"])
    l.phase({"gpsimd": p_fin})

    if max_phases is not None:
        l._phases = l._phases[:max_phases]
    nc = l.build()
    l.close()
    return nc



def build_launchH(max_phases=None):
    l = L("launchH")
    z_i = l.inp("z3", [D, TC], F32)
    wut = l.inp("wut", [D, 4096], F32)
    wdt = l.inp("wdt", [4096, D], F32)
    out_o = l.out("out", [D, TC], F32)

    xb = l.sbuf("xb", [128, NCH * TC], BF16)
    wu_sb = l.sbuf("wu_sb", [128, NCH * 4096], BF16)
    wd_sb = l.sbuf("wd_sb", [128, 32 * D], BF16)
    hs = l.sbuf("hs", [128, 32 * 256], BF16)
    sq = l.sbuf("sq", [128, 1024], BF16)
    tmp = l.sbuf("tmp", [128, 256], BF16)
    zt = l.sbuf("zt", [128, 256], F32)
    rbc1 = l.sbuf("rbc1", [128, 256], F32)
    rbc2 = l.sbuf("rbc2", [128, 256], F32)
    acc = l.sbuf("acc", [128, 32], F32)
    tau = l.sbuf("tau", [128, 2], F32)
    onesf = l.sbuf("onesf", [128, 1], F32)
    ones_bf = l.sbuf("ones_bf", [128, 1], BF16)
    ones_row = l.sbuf("ones_row", [1, 128], F32)
    r0 = l.sbuf("r0", [1, TC], F32)
    r1 = l.sbuf("r1", [1, TC], F32)
    s1 = l.sbuf("s1", [1, 8], F32)
    cst = l.sbuf("cst", [128, 2], F32)

    u1 = l.psum("u1", [128, TC], F32)
    pp = [l.psum(f"pp{i}", [128, 512], F32) for i in range(2)]

    z4d = l.dram("z4d", [D, TC], F32)

    wsl = l.sbuf("wsl", [128, 1024], F32)
    sgn = l.sbuf("sgn", [128, 1024], BF16)
    g1 = l.sbuf("g1", [128, 1024], BF16)

    def p0v(e, l):
        e.memset(ones_bf[:], 1.0)
        e.memset(ones_row[:], 1.0)
        e.memset(onesf[:], 1.0)
        e.memset(cst[:, 0:1], 1e-5)
        e.memset(cst[:, 1:2], 1e-8)
    l.phase({"vector": p0v})

    emit_ln_start(l, z_i, xb, None, sq, u1, ones_bf, ones_row, (r0, r1),
                  cst[0:1, 0:1])

    def ternarize(w_in, R, C, w_out_sb):
        wv = w_in.rearrange("(c p) f -> p c f", p=128)
        nsl = (R // 128) * (C // 1024)

        def sl_view(k):
            c, s = k // (C // 1024), k % (C // 1024)
            return wv[:, c, s * 1024:(s + 1) * 1024], c * C + s * 1024

        for k in range(nsl):
            src_v, _ = sl_view(k)
            l.phase({"sync": (lambda e, l, src_v=src_v:
                              l.dma("sync", e, wsl[:], src_v))})
            l.phase({"scalar": (lambda e, l, k=k:
                                e.activation(out=g1[:], in_=wsl[:], func=AF.Abs,
                                             accum_out=acc[:, k:k + 1]))})
        def p_t1(e, l, nsl=nsl):
            return e.matmul(u1[0:1, 0:nsl], onesf[:, 0:1], acc[:, 0:nsl],
                            start=True, stop=True)
        l.phase({"tensor": p_t1})

        def p_t2(e, l, nsl=nsl):
            e.reduce_sum(s1[0:1, 0:1], u1[0:1, 0:nsl], axis=AX.X)
            e.drain()
            e.tensor_scalar_mul(s1[0:1, 1:2], s1[0:1, 0:1], 0.5 / (R * C))
            e.drain()
            e.tensor_scalar_mul(s1[0:1, 2:3], s1[0:1, 1:2], -1.0)
        l.phase({"vector": p_t2})

        def p_t3(e, l):
            e.matmul(u1[:, 0:1], ones_row[0:1, :], s1[0:1, 1:2], start=True, stop=True)
            return e.matmul(u1[:, 1:2], ones_row[0:1, :], s1[0:1, 2:3],
                            start=True, stop=True)
        l.phase({"tensor": p_t3})
        l.phase({"scalar": (lambda e, l: e.copy(out=tau[:], in_=u1[:, 0:2]))})

        for k in range(nsl):
            src_v, ocol = sl_view(k)
            l.phase({"sync": (lambda e, l, src_v=src_v:
                              l.dma("sync", e, wsl[:], src_v))})
            l.phase({"scalar": (lambda e, l: e.sign(out=sgn[:], in_=wsl[:]))})

            def p_g(e, l, ocol=ocol):
                e.tensor_scalar(out=g1[:], in0=wsl[:], scalar1=tau[:, 0:1],
                                scalar2=None, op0=ALU.is_gt)
                e.drain()
                e.scalar_tensor_tensor(out=g1[:], in0=wsl[:], scalar=tau[:, 1:2],
                                       in1=g1[:], op0=ALU.is_lt, op1=ALU.add)
                e.drain()
                e.tensor_tensor(out=w_out_sb[:, ocol:ocol + 1024],
                                in0=g1[:], in1=sgn[:], op=ALU.mult)
            l.phase({"vector": p_g})

    ternarize(wut, D, 4096, wu_sb)
    ternarize(wdt, 4096, D, wd_sb)

    for ci in range(NCH):
        for hf in range(2):
            l.phase({"scalar": (lambda e, l, ci=ci, hf=hf:
                                e.activation(out=sq[:, 0:1024],
                                             in_=xb[:, ci * TC + hf * 1024:
                                                     ci * TC + (hf + 1) * 1024],
                                             func=AF.Square))})

            def p_cs(e, l, ci=ci, hf=hf):
                i = None
                for j in range(0, 1024, 512):
                    i = e.matmul(u1[0:1, hf * 1024 + j:hf * 1024 + j + 512],
                                 ones_bf[:, 0:1], sq[:, j:j + 512],
                                 start=(ci == 0), stop=(ci == NCH - 1))
                return i
            l.phase({"tensor": p_cs})
    l.phase({"scalar": (lambda e, l:
                        e.activation(out=r0[0:1, :], in_=u1[0:1, :], func=AF.Sqrt,
                                     scale=1.0 / D))})

    def p_r1(e, l):
        e.tensor_scalar_add(r0[0:1, :], r0[0:1, :], 1e-8)
        e.drain()
        e.reciprocal(r0[0:1, :], r0[0:1, :])
    l.phase({"vector": p_r1})

    SS = 256
    for s in range(TC // SS):
        l.phase({"tensor": (lambda e, l, s=s:
                            bcast(e, ones_row, r0[0:1, s * SS:(s + 1) * SS],
                                  u1, SS))})
        l.phase({"scalar": (lambda e, l: e.copy(out=rbc1[:], in_=u1[:, 0:SS]))})

        for fc in range(32):
            def p_mm(e, l, s=s, fc=fc):
                i = None
                for ci in range(NCH):
                    i = e.matmul(pp[fc % 2][:, 0:SS],
                                 wu_sb[:, ci * 4096 + fc * 128:ci * 4096 + (fc + 1) * 128],
                                 xb[:, ci * TC + s * SS:ci * TC + (s + 1) * SS],
                                 start=(ci == 0), stop=(ci == NCH - 1))
                return i
            l.phase({"tensor": p_mm})

            def p_sc(e, l, fc=fc):
                e.scalar_tensor_tensor(out=tmp[:], in0=pp[fc % 2][:, 0:SS],
                                       scalar=1.0, in1=rbc1[:],
                                       op0=ALU.mult, op1=ALU.mult)
            l.phase({"vector": p_sc})
            l.phase({"scalar": (lambda e, l, fc=fc:
                                e.activation(out=hs[:, fc * SS:(fc + 1) * SS],
                                             in_=tmp[:], func=AF.Gelu))})

        for fc in range(32):
            l.phase({"scalar": (lambda e, l, fc=fc:
                                e.activation(out=sq[:, 0:SS],
                                             in_=hs[:, fc * SS:(fc + 1) * SS],
                                             func=AF.Square))})
            l.phase({"tensor": (lambda e, l, fc=fc:
                                e.matmul(u1[0:1, 0:SS], ones_bf[:, 0:1], sq[:, 0:SS],
                                         start=(fc == 0), stop=(fc == 31)))})
        l.phase({"scalar": (lambda e, l:
                            e.activation(out=r1[0:1, 0:SS], in_=u1[0:1, 0:SS],
                                         func=AF.Sqrt, scale=1.0 / 4096))})

        def p_r2(e, l):
            e.tensor_scalar_add(r1[0:1, 0:SS], r1[0:1, 0:SS], 1e-8)
            e.drain()
            e.reciprocal(r1[0:1, 0:SS], r1[0:1, 0:SS])
        l.phase({"vector": p_r2})
        l.phase({"tensor": (lambda e, l:
                            bcast(e, ones_row, r1[0:1, 0:SS], u1, SS))})
        l.phase({"scalar": (lambda e, l: e.copy(out=rbc2[:], in_=u1[:, 0:SS]))})

        for do in range(NCH):
            def p_mm2(e, l, do=do):
                i = None
                for fc in range(32):
                    i = e.matmul(pp[do % 2][:, 0:SS],
                                 wd_sb[:, fc * D + do * 128:fc * D + (do + 1) * 128],
                                 hs[:, fc * SS:(fc + 1) * SS],
                                 start=(fc == 0), stop=(fc == 31))
                return i
            l.phase({"tensor": p_mm2})

            def p_z(e, l, s=s, do=do):
                e.scalar_tensor_tensor(out=zt[:], in0=pp[do % 2][:, 0:SS],
                                       scalar=1.0, in1=rbc2[:],
                                       op0=ALU.mult, op1=ALU.mult)
                e.drain()
                e.tensor_tensor(out=zt[:], in0=zt[:],
                                in1=xb[:, do * TC + s * SS:do * TC + (s + 1) * SS],
                                op=ALU.add)
            l.phase({"vector": p_z})
            l.phase({"gpsimd": (lambda e, l, s=s, do=do:
                                l.dma("gpsimd", e,
                                      z4d[do * 128:(do + 1) * 128, s * SS:(s + 1) * SS],
                                      zt[:]))})

    def after_chunk(ci):
        return {"gpsimd": (lambda e, l, ci=ci:
                           l.dma("gpsimd", e, out_o[ci * 128:(ci + 1) * 128, :],
                                 xb[:, ci * TC:(ci + 1) * TC]))}
    emit_ln_start(l, z4d, xb, None, sq, u1, ones_bf, ones_row, (r0, r1),
                  cst[0:1, 0:1], after_chunk=after_chunk)

    def p_fin(e, l):
        e.wait_ge(l.dsem["gpsimd"], l.dcnt["gpsimd"])
    l.phase({"gpsimd": p_fin})

    if max_phases is not None:
        l._phases = l._phases[:max_phases]
    nc = l.build()
    l.close()
    return nc



def _fm(a):
    return np.ascontiguousarray(np.asarray(a, np.float32).T)


def _chunked(v):
    return np.ascontiguousarray(np.asarray(v, np.float32).reshape(NCH, 128).T)


def _rope_perm():
    perm = np.zeros(D, np.int64)
    for j in range(NCH):
        for r in range(128):
            qd = r // 32
            h = 2 * j + (qd // 2)
            fr = 16 * (qd % 2) + (r % 16)
            x2 = (r % 32) >= 16
            perm[j * 128 + r] = h * 64 + 2 * fr + (1 if x2 else 0)
    return perm


def _cos_sin(t0):
    hd = 64
    inv = 1.0 / 10000.0 ** (np.arange(0, hd, 2, dtype=np.float32) / hd)
    t_ = np.arange(t0, t0 + TC, dtype=np.float32)
    f = np.outer(inv, t_)
    cos = np.zeros((128, TC), np.float32)
    ssin = np.zeros((128, TC), np.float32)
    for r in range(128):
        qd = r // 32
        fr = 16 * (qd % 2) + (r % 16)
        x2 = (r % 32) >= 16
        cos[r] = np.cos(f[fr])
        ssin[r] = np.sin(f[fr]) * (1.0 if x2 else -1.0)
    return cos, ssin


_NC_CACHE = {}


def _nc(name, builder):
    if name not in _NC_CACHE:
        _NC_CACHE[name] = builder()
    return _NC_CACHE[name]


def kernel(**inp):
    import ml_dtypes
    BF = ml_dtypes.bfloat16
    f32 = lambda k: np.asarray(inp[k], np.float32)
    x = f32("L0_attn_x")
    cores = list(range(NCORE))

    n1w = _chunked(inp["L0_attn_norm1_w"])
    bw = _chunked(f32("L0_attn_beta_w")[:, 0])
    bb = np.broadcast_to(f32("L0_attn_beta_b").reshape(1, 1), (128, 1)).copy()
    imA = []
    for c in cores:
        b, qd = c // 4, c % 4
        imA.append({"x": _fm(x[b, qd * TC:(qd + 1) * TC]), "n1w": n1w,
                    "bw": bw, "bb": bb,
                    "wq": f32("L0_attn_Wq"), "wk": f32("L0_attn_Wk"),
                    "wv": f32("L0_attn_Wv")})
    rA = run_bass_kernel_spmd(_nc("A", build_launchA), imA, core_ids=cores).results

    qf = [np.concatenate([rA[b * 4 + qd]["q"] for qd in range(4)], 1) for b in range(B)]
    kf = [np.concatenate([rA[b * 4 + qd]["k"] for qd in range(4)], 1) for b in range(B)]
    vf = [np.concatenate([rA[b * 4 + qd]["v"] for qd in range(4)], 1) for b in range(B)]
    beta = [np.concatenate(
        [np.asarray(rA[b * 4 + qd]["beta_tm"], np.float32).T.reshape(-1)
         for qd in range(4)]) for b in range(B)]

    ident = np.eye(128, dtype=BF)
    nlmask = (-np.tril(np.ones((128, 128), np.float32), -1)).astype(BF)
    umaski = np.triu(np.ones((128, 128), np.float32), 0).astype(BF)
    al = f32("L0_attn_alpha_logit")
    imC = []
    for c in cores:
        b, s = c // 4, c % 4
        imC.append({
            "qt": np.ascontiguousarray(qf[b].reshape(NCH, 128, T // 128, 128)
                                       .transpose(2, 0, 1, 3)),
            "kt": np.ascontiguousarray(kf[b].reshape(NCH, 128, T // 128, 128)
                                       .transpose(2, 0, 1, 3)),
            "ktm": np.ascontiguousarray(kf[b].T.reshape(T // 128, 128, D)),
            "bcol": np.ascontiguousarray(beta[b].reshape(T // 128, 128).T),
            "va": np.ascontiguousarray(al[256 * s:256 * (s + 1)].reshape(2, 128).T),
            "vsl": np.ascontiguousarray(vf[b][256 * s:256 * (s + 1), :]),
            "ident": ident, "nlmask": nlmask, "umaski": umaski})
    rC = run_bass_kernel_spmd(_nc("C", build_launchC), imC, core_ids=cores).results
    attn = [np.concatenate([rC[b * 4 + s]["ao"] for s in range(4)], 0) for b in range(B)]

    n2w = _chunked(inp["L0_attn_norm2_w"])
    imD = []
    for c in cores:
        b, qd = c // 4, c % 4
        imD.append({"x": _fm(x[b, qd * TC:(qd + 1) * TC]),
                    "at": np.ascontiguousarray(attn[b][:, qd * TC:(qd + 1) * TC]),
                    "wo": f32("L0_attn_Wo"), "n2w": n2w,
                    "wg": f32("L0_attn_w_gate"), "wu": f32("L0_attn_w_up"),
                    "wd": f32("L0_attn_w_down")})
    rD = run_bass_kernel_spmd(_nc("D", build_launchD), imD, core_ids=cores).results

    b1 = f32("L0_ffn_b1")
    b2 = f32("L0_ffn_b2")
    b1c = np.ascontiguousarray(b1.reshape(8, 32, 128).transpose(2, 0, 1).reshape(128, 256))
    b2c = np.ascontiguousarray(b2.reshape(8, 8, 128).transpose(2, 0, 1).reshape(128, 64))
    imE = []
    for c in cores:
        imE.append({"z": np.asarray(rD[c]["z"], np.float32),
                    "wrt": np.ascontiguousarray(f32("L0_ffn_Wr").T),
                    "brr": np.ascontiguousarray(np.tile(f32("L0_ffn_br"), (128, 1))),
                    "w1t": np.ascontiguousarray(f32("L0_ffn_W1").transpose(0, 2, 1)),
                    "w2t": np.ascontiguousarray(f32("L0_ffn_W2").transpose(0, 2, 1)),
                    "b1c": b1c, "b2c": b2c, "ident": ident})
    rE = run_bass_kernel_spmd(_nc("E", build_launchE), imE, core_ids=cores).results

    perm = _rope_perm()
    wqt = np.ascontiguousarray(f32("L1_attn_W_Q").T[:, perm])
    wkt = np.ascontiguousarray(f32("L1_attn_W_K").T[:, perm])
    wvt = np.ascontiguousarray(f32("L1_attn_W_V").T)
    tabs = [_cos_sin(qd * TC) for qd in range(4)]
    imF = []
    for c in cores:
        b, qd = c // 4, c % 4
        imF.append({"z2": np.asarray(rE[c]["z2"], np.float32),
                    "wqt": wqt, "wkt": wkt, "wvt": wvt,
                    "cosb": tabs[qd][0], "sinb": tabs[qd][1]})
    rF = run_bass_kernel_spmd(_nc("F", build_launchF), imF, core_ids=cores).results

    khb, vhb = [], []
    for b in range(B):
        k_full = np.concatenate([rF[b * 4 + qd]["k"] for qd in range(4)], 1)
        v_full = np.concatenate([rF[b * 4 + qd]["v"] for qd in range(4)], 1)
        khb.append(np.ascontiguousarray(k_full.reshape(16, 64, T)))
        vhb.append(np.ascontiguousarray(v_full.reshape(16, 64, T).transpose(0, 2, 1)))

    wot = np.ascontiguousarray(f32("L1_attn_W_O").T)
    imG = []
    for c in cores:
        b, qd = c // 4, c % 4
        imG.append({"qr": np.asarray(rF[c]["q"]),
                    "kh": khb[b], "vh": vhb[b],
                    "x3": np.asarray(rF[c]["x3"], np.float32),
                    "wot": wot})
    rG = run_bass_kernel_spmd(_nc("G", build_launchG), imG, core_ids=cores).results

    wut = np.ascontiguousarray(f32("L1_ffn_w_up").T)
    wdt = np.ascontiguousarray(f32("L1_ffn_w_down").T)
    imH = [{"z3": np.asarray(rG[c]["z3"], np.float32), "wut": wut, "wdt": wdt}
           for c in cores]
    rH = run_bass_kernel_spmd(_nc("H", build_launchH), imH, core_ids=cores).results

    out = np.empty((B, T, D), np.float32)
    for c in cores:
        b, qd = c // 4, c % 4
        out[b, qd * TC:(qd + 1) * TC, :] = np.asarray(rH[c]["out"], np.float32).T
    return out


if __name__ == "__main__":
    pass


# version 03
# speedup vs baseline: 1.6051x; 1.1346x over previous
import contextlib
import numpy as np
import concourse.bass as bass
import concourse.mybir as mybir
from concourse.bass_utils import run_bass_kernel_spmd

F32 = mybir.dt.float32
BF16 = mybir.dt.bfloat16
U32 = mybir.dt.uint32
AF = mybir.ActivationFunctionType
ALU = mybir.AluOpType
AX = mybir.AxisListType

B, T, D = 2, 8192, 1024
TC = 2048
NCH = D // 128
NCORE = 8

ENGS = ("sync", "gpsimd", "tensor", "vector", "scalar")


class L:

    def __init__(self, name):
        self.nc = bass.Bass("TRN2", target_bir_lowering=False, name=name)
        self.es = contextlib.ExitStack()
        self.sem = {e: self.es.enter_context(self.nc.semaphore(f"sem_{e}"))
                    for e in ENGS}
        self.dsem = {e: self.es.enter_context(self.nc.semaphore(f"dma_{e}"))
                     for e in ("sync", "gpsimd", "scalar")}
        self._pfpool = [self.es.enter_context(self.nc.semaphore(f"pf{i}"))
                        for i in range(8)]
        self._pfcnt = [0] * 8
        self.pfmap = {}
        self.pftarget = {}
        self.cnt = {e: 0 for e in ENGS}
        self.dcnt = {e: 0 for e in self.dsem}
        self.seen = {e: {**{oe: 0 for oe in ENGS},
                         **{"d" + q: 0 for q in self.dsem}} for e in ENGS}
        self._phases = []

    def pfsem(self, key):
        if key not in self.pfmap:
            self.pfmap[key] = len(self.pfmap) % len(self._pfpool)
        return self.pfmap[key]

    def sbuf(self, name, shape, dt):
        return self.es.enter_context(self.nc.sbuf_tensor(name, shape, dt))

    def psum(self, name, shape, dt=F32):
        return self.es.enter_context(self.nc.psum_tensor(name, shape, dt))

    def inp(self, name, shape, dt=F32):
        return self.nc.dram_tensor(name, shape, dt, kind="ExternalInput").ap()

    def out(self, name, shape, dt=F32):
        return self.nc.dram_tensor(name, shape, dt, kind="ExternalOutput").ap()

    def dram(self, name, shape, dt=F32):
        return self.nc.dram_tensor(name, shape, dt, kind="Internal").ap()

    def dma(self, qname, eng, out, in_, pf=None, **kw):
        i = eng.dma_start(out=out, in_=in_, **kw)
        if pf is not None:
            idx = self.pfsem(pf)
            i.then_inc(self._pfpool[idx], 16)
            self._pfcnt[idx] += 16
            self.pftarget[pf] = self._pfcnt[idx]
            return self.pftarget[pf]
        i.then_inc(self.dsem[qname], 16)
        self.dcnt[qname] += 16
        return self.dcnt[qname]

    def wait_pf(self, eng, key):
        idx = self.pfmap[key]
        eng.wait_ge(self._pfpool[idx], self.pftarget[key])

    def phase(self, work):
        self._phases.append(work)

    def build(self):
        nc = self.nc
        with nc.Block() as blk:
            for work in self._phases:
                want = dict(self.cnt)
                dwant = dict(self.dcnt)
                for e in ENGS:
                    if e not in work:
                        continue
                    fn = work[e]

                    def section(eng, fn=fn, e=e, want=want, dwant=dwant):
                        sn = self.seen[e]
                        for oe in ENGS:
                            if oe != e and want[oe] > sn[oe]:
                                eng.wait_ge(self.sem[oe], want[oe])
                                sn[oe] = want[oe]
                        for oq, vv in dwant.items():
                            if vv > sn["d" + oq]:
                                eng.wait_ge(self.dsem[oq], vv)
                                sn["d" + oq] = vv
                        last = fn(eng, self)
                        if last is not None and hasattr(last, "then_inc"):
                            last.then_inc(self.sem[e], 1)
                        elif e in ("tensor", "sync"):
                            eng.nop().then_inc(self.sem[e], 1)
                        else:
                            eng.drain().then_inc(self.sem[e], 1)
                        self.cnt[e] += 1

                    getattr(blk, e)(section)
        return nc

    def close(self):
        self.es.close()



def colsum(t, ones_bf, src, ps_row, n, start, stop=False):
    i = None
    for j in range(0, n, 512):
        i = t.matmul(ps_row[0:1, j:j + 512], ones_bf[:, 0:1], src[:, j:j + 512],
                     start=start, stop=stop)
    return i


def bcast(t, ones_row_f32, row, ps_bc, n):
    i = None
    for j in range(0, n, 512):
        w = min(512, n - j)
        i = t.matmul(ps_bc[:, j:j + w], ones_row_f32[0:1, :], row[0:1, j:j + w],
                     start=True, stop=True)
    return i



def build_launchA(max_phases=None):
    l = L("launchA")
    x = l.inp("x", [D, TC], F32)
    n1w = l.inp("n1w", [128, NCH], F32)
    wq = l.inp("wq", [D, D], F32)
    wk = l.inp("wk", [D, D], F32)
    wv = l.inp("wv", [D, D], F32)
    bw = l.inp("bw", [128, NCH], F32)
    bb = l.inp("bb", [128, 1], F32)
    q_o = l.out("q", [D, TC], BF16)
    k_o = l.out("k", [D, TC], BF16)
    v_o = l.out("v", [D, TC], BF16)
    beta_o = l.out("beta_tm", [128, TC // 128], F32)

    h_sb = l.sbuf("h", [128, NCH * TC], BF16)
    xc = l.sbuf("xc", [128, TC], F32)
    sq = l.sbuf("sq", [128, TC], BF16)
    w_sb = [l.sbuf(f"w{i}", [128, NCH * D], BF16) for i in range(2)]
    out_sb = l.sbuf("outb", [128, NCH * TC], BF16)
    n1w_sb = l.sbuf("n1w_sb", [128, NCH], F32)
    bw_sb = l.sbuf("bw_sb", [128, NCH], BF16)
    bb_sb = l.sbuf("bb_sb", [128, 1], F32)
    ones_bf = l.sbuf("ones_bf", [128, 1], BF16)
    ones_row = l.sbuf("ones_row", [1, 128], F32)
    row_a = l.sbuf("row_a", [1, TC], F32)
    row_b = l.sbuf("row_b", [1, TC], F32)
    beta_sb = l.sbuf("beta_sb", [128, TC // 128], F32)
    cst = l.sbuf("cst", [128, 4], F32)

    u1 = l.psum("u1", [128, TC], F32)
    bps = l.psum("bps", [128, TC // 128], F32)
    pp = [l.psum(f"pp{i}", [128, 512], F32) for i in range(2)]

    wv_views = {
        "q": wq.rearrange("(c p) o -> p c o", p=128),
        "k": wk.rearrange("(c p) o -> p c o", p=128),
        "v": wv.rearrange("(c p) o -> p c o", p=128),
    }
    w_sb3 = [w_sb[i].ap().rearrange("p (c o) -> p c o", c=NCH) for i in range(2)]

    def p0_sync(e, l):
        l.dma("sync", e, n1w_sb[:], n1w)
        l.dma("sync", e, bb_sb[:], bb)

    def p0_gp(e, l):
        l.dma("gpsimd", e, w_sb3[0], wv_views["q"], pf="q")
        l.dma("gpsimd", e, bw_sb[:], bw)

    def p0_vec(e, l):
        e.memset(ones_bf[:], 1.0)
        e.memset(ones_row[:], 1.0)
        e.memset(cst[:, 0:1], 1e-6)
        e.memset(cst[:, 1:2], 1e-5)
        e.memset(cst[:, 2:3], 1e-8)

    l.phase({"sync": p0_sync, "gpsimd": p0_gp, "vector": p0_vec})

    for ci in range(NCH):
        l.phase({"sync": (lambda e, l, ci=ci:
                          l.dma("sync", e, xc[:], x[ci * 128:(ci + 1) * 128, :]))})
        l.phase({"scalar": (lambda e, l:
                            e.activation(out=sq[:], in_=xc[:], func=AF.Square))})
        l.phase({"tensor": (lambda e, l, ci=ci:
                            colsum(e, ones_bf, sq, u1, TC, start=(ci == 0),
                                   stop=(ci == NCH - 1)))})

    l.phase({"scalar": (lambda e, l:
                        e.activation(out=row_a[0:1, :], in_=u1[0:1, :],
                                     func=AF.Sqrt, scale=1.0 / D, bias=cst[0:1, 0:1]))})
    l.phase({"vector": (lambda e, l: e.reciprocal(row_b[0:1, :], row_a[0:1, :]))})
    l.phase({"tensor": (lambda e, l: bcast(e, ones_row, row_b, u1, TC))})

    for ci in range(NCH):
        l.phase({"sync": (lambda e, l, ci=ci:
                          l.dma("sync", e, xc[:], x[ci * 128:(ci + 1) * 128, :]))})
        l.phase({"vector": (lambda e, l, ci=ci:
                            e.scalar_tensor_tensor(
                                out=h_sb[:, ci * TC:(ci + 1) * TC], in0=xc[:],
                                scalar=n1w_sb[:, ci:ci + 1], in1=u1[:],
                                op0=ALU.mult, op1=ALU.mult))})

    def p_beta(e, l):
        i = None
        for tt in range(TC // 128):
            for ci in range(NCH):
                i = e.matmul(bps[:, tt:tt + 1],
                             h_sb[:, ci * TC + tt * 128: ci * TC + (tt + 1) * 128],
                             bw_sb[:, ci:ci + 1],
                             start=(tt == 0 and ci == 0),
                             stop=(tt == TC // 128 - 1 and ci == NCH - 1))
        return i
    l.phase({"tensor": p_beta})
    l.phase({"scalar": (lambda e, l:
                        e.activation(out=beta_sb[:], in_=bps[:], func=AF.Sigmoid,
                                     bias=bb_sb[:, 0:1]))})
    l.phase({"gpsimd": (lambda e, l: l.dma("gpsimd", e, beta_o, beta_sb[:]))})

    def proj(wbuf, pf_key, next_key):
        for do in range(NCH):
            for nt in range(TC // 512):
                ps = pp[(do * (TC // 512) + nt) % 2]

                def p_mm(e, l, do=do, nt=nt, ps=ps, first=(do == 0 and nt == 0)):
                    if first:
                        l.wait_pf(e, pf_key)
                    i = None
                    for ci in range(NCH):
                        i = e.matmul(ps[:],
                                     w_sb[wbuf][:, ci * D + do * 128:ci * D + (do + 1) * 128],
                                     h_sb[:, ci * TC + nt * 512:ci * TC + (nt + 1) * 512],
                                     start=(ci == 0), stop=(ci == NCH - 1))
                    return i
                work = {"tensor": p_mm}
                if do == 0 and nt == 0 and next_key is not None:
                    def p_pf(e, l, nk=next_key, wb=1 - wbuf):
                        l.dma("gpsimd", e, w_sb3[wb], wv_views[nk], pf=nk)
                    work["gpsimd"] = p_pf
                l.phase(work)
                l.phase({"scalar": (lambda e, l, do=do, nt=nt, ps=ps:
                                    e.copy(out=out_sb[:, do * TC + nt * 512:
                                                      do * TC + (nt + 1) * 512],
                                           in_=ps[:]))})

    def store(out_ap):
        def p_st(e, l):
            for ci in range(NCH):
                l.dma("gpsimd", e, out_ap[ci * 128:(ci + 1) * 128, :],
                      out_sb[:, ci * TC:(ci + 1) * TC])
        l.phase({"gpsimd": p_st})

    proj(0, "q", "k")
    store(q_o)

    proj(1, "k", "v")
    for ci in range(NCH):
        l.phase({"scalar": (lambda e, l, ci=ci:
                            e.activation(out=sq[:], in_=out_sb[:, ci * TC:(ci + 1) * TC],
                                         func=AF.Square))})
        l.phase({"tensor": (lambda e, l, ci=ci:
                            colsum(e, ones_bf, sq, u1, TC, start=(ci == 0),
                                   stop=(ci == NCH - 1)))})
    l.phase({"scalar": (lambda e, l:
                        e.activation(out=row_a[0:1, :], in_=u1[0:1, :], func=AF.Sqrt))})

    def p_knv(e, l):
        e.tensor_scalar_max(row_a[0:1, :], row_a[0:1, :], 1e-12)
        e.drain()
        e.reciprocal(row_b[0:1, :], row_a[0:1, :])
    l.phase({"vector": p_knv})
    l.phase({"tensor": (lambda e, l: bcast(e, ones_row, row_b, u1, TC))})

    def p_kmul(e, l):
        for ci in range(NCH):
            e.tensor_tensor(out=out_sb[:, ci * TC:(ci + 1) * TC],
                            in0=out_sb[:, ci * TC:(ci + 1) * TC],
                            in1=u1[:], op=ALU.mult)
    l.phase({"vector": p_kmul})
    store(k_o)

    proj(0, "v", None)
    store(v_o)

    def p_fin(e, l):
        e.wait_ge(l.dsem["gpsimd"], l.dcnt["gpsimd"])
    l.phase({"gpsimd": p_fin})

    if max_phases is not None:
        l._phases = l._phases[:max_phases]
    nc = l.build()
    l.close()
    return nc



def build_launchC(max_phases=None):
    l = L("launchC")
    NC64 = T // 128
    qt = l.inp("qt", [NC64, NCH, 128, 128], BF16)
    kt = l.inp("kt", [NC64, NCH, 128, 128], BF16)
    ktm = l.inp("ktm", [NC64, 128, D], BF16)
    bcol = l.inp("bcol", [128, NC64], F32)
    va = l.inp("va", [128, 2], F32)
    vsl = l.inp("vsl", [256, T], BF16)
    ident = l.inp("ident", [128, 128], BF16)
    nlmask = l.inp("nlmask", [128, 128], BF16)
    umaski = l.inp("umaski", [128, 128], BF16)
    ao = l.out("ao", [256, T], BF16)

    vts = l.sbuf("vts", [128, 2 * T], BF16)
    vtm = l.sbuf("vtm", [128, 2 * T], BF16)
    wT = l.sbuf("wT", [128, NC64 * 128], BF16)
    hS = l.sbuf("hS", [128, NC64 * 128], BF16)
    st_sb = l.sbuf("st_sb", [128, NCH * 256], BF16)
    kc = l.sbuf("kc", [128, D], BF16)
    qc = l.sbuf("qc", [128, D], BF16)
    kmc = l.sbuf("kmc", [128, D], BF16)
    rhs_sb = l.sbuf("rhs_sb", [128, 256], BF16)
    u_sb = l.sbuf("u_sb", [128, 256], BF16)
    d0 = l.sbuf("d0", [128, T], BF16)
    d1 = l.sbuf("d1", [128, T], BF16)
    sS = l.sbuf("sS", [128, 128], BF16)
    sT = l.sbuf("sT", [128, 128], BF16)
    sX = l.sbuf("sX", [128, 128], BF16)
    sY = l.sbuf("sY", [128, 128], BF16)
    bc_sb = l.sbuf("bc_sb", [128, NC64], F32)
    va_sb = l.sbuf("va_sb", [128, 2], F32)
    alt = l.sbuf("alt", [128, 2], F32)
    oma = l.sbuf("oma", [128, 2], F32)
    id_sb = l.sbuf("id_sb", [128, 128], BF16)
    nl_sb = l.sbuf("nl_sb", [128, 128], BF16)
    ui_sb = l.sbuf("ui_sb", [128, 128], BF16)

    st_ps = [l.psum(f"st{i}", [128, 512], F32) for i in range(4)]
    ps_b = l.psum("ps_b", [128, 512], F32)
    ps_c = l.psum("ps_c", [128, 512], F32)
    ps_t = l.psum("ps_t", [128, 512], BF16)

    ktv = kt.rearrange("c dc p t -> c p dc t")
    qtv = qt.rearrange("c dc p t -> c p dc t")
    kc3 = kc.ap().rearrange("p (dc t) -> p dc t", dc=NCH)
    qc3 = qc.ap().rearrange("p (dc t) -> p dc t", dc=NCH)
    vslv = vsl.rearrange("(j p) t -> j p t", p=128)
    aov = ao.rearrange("(j p) t -> j p t", p=128)

    def p0(e, l):
        l.dma("sync", e, vts[:, 0:T], vslv[0])
        l.dma("sync", e, vts[:, T:2 * T], vslv[1])
        l.dma("sync", e, bc_sb[:], bcol)
        l.dma("sync", e, va_sb[:], va)
        l.dma("sync", e, id_sb[:], ident)
        l.dma("sync", e, nl_sb[:], nlmask)
        l.dma("sync", e, ui_sb[:], umaski)
    l.phase({"sync": p0})

    l.phase({"scalar": (lambda e, l: e.activation(out=alt[:], in_=va_sb[:],
                                                  func=AF.Sigmoid))})
    def p_ema_v(e, l):
        e.tensor_scalar(out=oma[:], in0=alt[:], scalar1=-1.0, scalar2=1.0,
                        op0=ALU.mult, op1=ALU.add)
        e.drain()
        for j in range(2):
            e.tensor_scalar(out=d1[:], in0=vts[:, j * T:(j + 1) * T],
                            scalar1=alt[:, j:j + 1], scalar2=None, op0=ALU.mult)
            e.memset(d0[:], 1.0)
            e.drain()
            e.tensor_scalar(out=d0[:], in0=d0[:], scalar1=oma[:, j:j + 1],
                            scalar2=None, op0=ALU.mult)
            e.drain()
            e.tensor_tensor_scan(out=vts[:, j * T:(j + 1) * T], data0=d0[:],
                                 data1=d1[:], initial=0.0,
                                 op0=ALU.mult, op1=ALU.add)
            e.drain()
    l.phase({"vector": p_ema_v})

    for g in range(NC64 * 2 // 4):
        def p_tr(e, l, g=g):
            i = None
            for r in range(4):
                idx = g * 4 + r
                j, c = idx % 2, idx // 2
                i = e.transpose(ps_t[:, r * 128:(r + 1) * 128],
                                vts[:, j * T + c * 128: j * T + (c + 1) * 128],
                                id_sb[:])
            return i
        l.phase({"tensor": p_tr})

        def p_tre(e, l, g=g):
            for r in range(4):
                idx = g * 4 + r
                j, c = idx % 2, idx // 2
                e.copy(out=vtm[:, c * 256 + j * 128: c * 256 + (j + 1) * 128],
                       in_=ps_t[:, r * 128:(r + 1) * 128])
        l.phase({"scalar": p_tre})

    kc4 = d0.ap()
    qc4 = d1.ap()
    kcv = lambda cc: d0.ap()[:, cc * 1024:(cc + 1) * 1024]
    qcv = lambda cc: d0.ap()[:, 4096 + cc * 1024:4096 + (cc + 1) * 1024]
    sS4 = d1.ap()[:, 0:512]
    sT4 = d1.ap()[:, 512:1024]
    sX4 = d1.ap()[:, 1024:1536]
    sY4 = d1.ap()[:, 1536:2048]

    kc3g = [d0.ap()[:, cc * 1024:(cc + 1) * 1024]
            .rearrange("p (dc t) -> p dc t", dc=NCH) for cc in range(4)]
    qc3g = [d0.ap()[:, 4096 + cc * 1024:4096 + (cc + 1) * 1024]
            .rearrange("p (dc t) -> p dc t", dc=NCH) for cc in range(4)]

    for grp in range(NC64 // 4):
        def p_ld(e, l, grp=grp):
            for cc in range(4):
                l.dma("sync", e, kc3g[cc], ktv[grp * 4 + cc])
                l.dma("sync", e, qc3g[cc], qtv[grp * 4 + cc])
        l.phase({"sync": p_ld})

        def p_kk(e, l, grp=grp):
            i = None
            for cc in range(4):
                for dc in range(NCH):
                    e.matmul(ps_b[:, cc * 128:(cc + 1) * 128],
                             kcv(cc)[:, dc * 128:(dc + 1) * 128],
                             kcv(cc)[:, dc * 128:(dc + 1) * 128],
                             start=(dc == 0), stop=(dc == NCH - 1))
            for cc in range(4):
                for dc in range(NCH):
                    i = e.matmul(ps_c[:, cc * 128:(cc + 1) * 128],
                                 kcv(cc)[:, dc * 128:(dc + 1) * 128],
                                 qcv(cc)[:, dc * 128:(dc + 1) * 128],
                                 start=(dc == 0), stop=(dc == NCH - 1))
            return i
        l.phase({"tensor": p_kk})

        def p_xh(e, l, grp=grp):
            for cc in range(4):
                c = grp * 4 + cc
                e.scalar_tensor_tensor(out=sX4[:, cc * 128:(cc + 1) * 128],
                                       in0=ps_b[:, cc * 128:(cc + 1) * 128],
                                       scalar=bc_sb[:, c:c + 1], in1=nl_sb[:],
                                       op0=ALU.mult, op1=ALU.mult)
                e.drain()
                e.scalar_tensor_tensor(out=hS[:, c * 128:(c + 1) * 128],
                                       in0=ps_c[:, cc * 128:(cc + 1) * 128],
                                       scalar=1.0, in1=ui_sb[:],
                                       op0=ALU.mult, op1=ALU.mult)
                e.drain()
        l.phase({"vector": p_xh})

        def p_yt(e, l):
            i = None
            for cc in range(4):
                i = e.transpose(ps_t[:, cc * 128:(cc + 1) * 128],
                                sX4[:, cc * 128:(cc + 1) * 128], id_sb[:])
            return i
        l.phase({"tensor": p_yt})

        def p_ye(e, l):
            for cc in range(4):
                e.copy(out=sY4[:, cc * 128:(cc + 1) * 128],
                       in_=ps_t[:, cc * 128:(cc + 1) * 128])
        l.phase({"scalar": p_ye})

        def p_i1(e, l):
            i = None
            for cc in range(4):
                e.matmul(ps_c[:, cc * 128:(cc + 1) * 128],
                         sY4[:, cc * 128:(cc + 1) * 128],
                         sX4[:, cc * 128:(cc + 1) * 128], start=True, stop=True)
                i = e.matmul(ps_b[:, cc * 128:(cc + 1) * 128],
                             sX4[:, cc * 128:(cc + 1) * 128],
                             sY4[:, cc * 128:(cc + 1) * 128], start=True, stop=True)
            return i
        l.phase({"tensor": p_i1})

        def p_i2(e, l):
            for cc in range(4):
                e.tensor_tensor(out=sS4[:, cc * 128:(cc + 1) * 128],
                                in0=sX4[:, cc * 128:(cc + 1) * 128],
                                in1=id_sb[:], op=ALU.add)
                e.drain()
                e.tensor_tensor(out=sT4[:, cc * 128:(cc + 1) * 128],
                                in0=sY4[:, cc * 128:(cc + 1) * 128],
                                in1=id_sb[:], op=ALU.add)
                e.drain()
        l.phase({"vector": p_i2})

        def p_i3(e, l):
            for cc in range(4):
                e.copy(out=sX4[:, cc * 128:(cc + 1) * 128],
                       in_=ps_c[:, cc * 128:(cc + 1) * 128])
                e.drain()
                e.copy(out=sY4[:, cc * 128:(cc + 1) * 128],
                       in_=ps_b[:, cc * 128:(cc + 1) * 128])
                e.drain()
        l.phase({"scalar": p_i3})

        for it in range(5):
            def p_mm4(e, l):
                i = None
                for cc in range(4):
                    sl = slice(cc * 128, (cc + 1) * 128)
                    e.matmul(ps_b[:, sl], sY4[:, sl], sS4[:, sl], start=True, stop=True)
                    e.matmul(ps_c[:, sl], sS4[:, sl], sY4[:, sl], start=True, stop=True)
                    e.matmul(st_ps[0][:, sl], sY4[:, sl], sX4[:, sl], start=True, stop=True)
                    i = e.matmul(st_ps[1][:, sl], sX4[:, sl], sY4[:, sl],
                                 start=True, stop=True)
                return i
            l.phase({"tensor": p_mm4})

            def p_ev4(e, l):
                for cc in range(4):
                    sl = slice(cc * 128, (cc + 1) * 128)
                    e.tensor_tensor(out=sS4[:, sl], in0=sS4[:, sl], in1=ps_b[:, sl],
                                    op=ALU.add)
                    e.drain()
                    e.tensor_tensor(out=sT4[:, sl], in0=sT4[:, sl], in1=ps_c[:, sl],
                                    op=ALU.add)
                    e.drain()
            l.phase({"vector": p_ev4})

            def p_ev4b(e, l):
                for cc in range(4):
                    sl = slice(cc * 128, (cc + 1) * 128)
                    e.copy(out=sX4[:, sl], in_=st_ps[0][:, sl])
                    e.drain()
                    e.copy(out=sY4[:, sl], in_=st_ps[1][:, sl])
                    e.drain()
            l.phase({"scalar": p_ev4b})

        def p_f1(e, l):
            i = None
            for cc in range(4):
                sl = slice(cc * 128, (cc + 1) * 128)
                i = e.matmul(ps_b[:, sl], sS4[:, sl], sY4[:, sl], start=True, stop=True)
            return i
        l.phase({"tensor": p_f1})

        def p_f2(e, l, grp=grp):
            for cc in range(4):
                c = grp * 4 + cc
                sl = slice(cc * 128, (cc + 1) * 128)
                e.tensor_tensor(out=wT[:, c * 128:(c + 1) * 128],
                                in0=sT4[:, sl], in1=ps_b[:, sl], op=ALU.add)
                e.drain()
        l.phase({"vector": p_f2})

    def p_z(e, l):
        e.memset(st_sb[:], 0.0)
    l.phase({"vector": p_z})

    for c in range(NC64):
        def p_ld(e, l, c=c):
            l.dma("sync", e, kc3, ktv[c])
            l.dma("sync", e, qc3, qtv[c])
            l.dma("sync", e, kmc[:], ktm[c])
        l.phase({"sync": p_ld})

        def p_ks(e, l, c=c):
            i = None
            for dc in range(NCH):
                i = e.matmul(ps_b[:, 0:256], kc[:, dc * 128:(dc + 1) * 128],
                             st_sb[:, dc * 256:(dc + 1) * 256],
                             start=(dc == 0), stop=(dc == NCH - 1))
            return i
        l.phase({"tensor": p_ks})

        def p_rhs(e, l, c=c):
            e.scalar_tensor_tensor(out=rhs_sb[:], in0=ps_b[:, 0:256],
                                   scalar=-1.0, in1=vtm[:, c * 256:(c + 1) * 256],
                                   op0=ALU.mult, op1=ALU.add)
            e.drain()
            e.tensor_scalar(out=rhs_sb[:], in0=rhs_sb[:],
                            scalar1=bc_sb[:, c:c + 1], scalar2=None, op0=ALU.mult)
        l.phase({"vector": p_rhs})

        l.phase({"tensor": (lambda e, l, c=c:
                            e.matmul(ps_b[:, 256:512], wT[:, c * 128:(c + 1) * 128],
                                     rhs_sb[:], start=True, stop=True))})
        l.phase({"scalar": (lambda e, l: e.copy(out=u_sb[:], in_=ps_b[:, 256:512]))})

        def p_ost(e, l, c=c):
            for ic in range(2):
                for dc in range(NCH):
                    e.matmul(ps_c[:, ic * 128:(ic + 1) * 128],
                             st_sb[:, dc * 256 + ic * 128: dc * 256 + (ic + 1) * 128],
                             qc[:, dc * 128:(dc + 1) * 128],
                             start=(dc == 0), stop=False)
                e.matmul(ps_c[:, ic * 128:(ic + 1) * 128],
                         u_sb[:, ic * 128:(ic + 1) * 128],
                         hS[:, c * 128:(c + 1) * 128], start=False, stop=True)
            i = None
            for j in range(NCH):
                i = e.matmul(st_ps[j // 2][:, (j % 2) * 256:(j % 2) * 256 + 256],
                             kmc[:, j * 128:(j + 1) * 128], u_sb[:],
                             start=(c == 0 and j % 2 == 0), stop=(j % 2 == 1),
                             skip_group_check=True)
            return i
        l.phase({"tensor": p_ost})

        def p_ev(e, l, c=c):
            for ic in range(2):
                e.copy(out=vts[:, ic * T + c * 128: ic * T + (c + 1) * 128],
                       in_=ps_c[:, ic * 128:(ic + 1) * 128])
            e.drain()
            for j in range(NCH):
                e.copy(out=st_sb[:, j * 256:(j + 1) * 256],
                       in_=st_ps[j // 2][:, (j % 2) * 256:(j % 2) * 256 + 256])
        l.phase({"scalar": p_ev})

    def p_st(e, l):
        for j in range(2):
            l.dma("gpsimd", e, aov[j], vts[:, j * T:(j + 1) * T])
        e.wait_ge(l.dsem["gpsimd"], l.dcnt["gpsimd"])
    l.phase({"gpsimd": p_st})

    if max_phases is not None:
        l._phases = l._phases[:max_phases]
    nc = l.build()
    l.close()
    return nc



def emit_ln_start(l, z_in, xb_sb, zc, sq, u1, ones_bf, ones_row, rows, cst_eps,
                  after_chunk=None):
    r0, r1 = rows
    def p_zl(e, l):
        for ci in range(NCH):
            l.dma("gpsimd", e, xb_sb[:, ci * TC:(ci + 1) * TC],
                  z_in[ci * 128:(ci + 1) * 128, :])
    l.phase({"gpsimd": p_zl})
    for ci in range(NCH):
        l.phase({"tensor": (lambda e, l, ci=ci:
                            colsum(e, ones_bf, xb_sb[:, ci * TC:(ci + 1) * TC],
                                   u1, TC, start=(ci == 0), stop=(ci == NCH - 1)))})
    l.phase({"scalar": (lambda e, l:
                        e.activation(out=r0[0:1, :], in_=u1[0:1, :],
                                     func=AF.Identity, scale=1.0 / D))})
    for ci in range(NCH):
        for hf in range(2):
            l.phase({"scalar": (lambda e, l, ci=ci, hf=hf:
                                e.activation(out=sq[:, 0:1024],
                                             in_=xb_sb[:, ci * TC + hf * 1024:
                                                       ci * TC + (hf + 1) * 1024],
                                             func=AF.Square))})

            def p_cs(e, l, ci=ci, hf=hf):
                i = None
                for j in range(0, 1024, 512):
                    i = e.matmul(u1[0:1, hf * 1024 + j:hf * 1024 + j + 512],
                                 ones_bf[:, 0:1], sq[:, j:j + 512],
                                 start=(ci == 0), stop=(ci == NCH - 1))
                return i
            l.phase({"tensor": p_cs})
    def p_rows_s(e, l):
        e.activation(out=sq[0:1, 0:TC], in_=r0[0:1, :], func=AF.Square)
    def p_rows_v(e, l):
        e.tensor_tensor(out=r1[0:1, :], in0=r0[0:1, :], in1=r0[0:1, :], op=ALU.mult)
    l.phase({"vector": p_rows_v})

    def p_rows_s2(e, l):
        pass
    def p_var_v(e, l):
        e.scalar_tensor_tensor(out=r1[0:1, :], in0=u1[0:1, :], scalar=1.0 / D,
                               in1=r1[0:1, :], op0=ALU.mult, op1=ALU.subtract)
    l.phase({"vector": p_var_v})
    l.phase({"scalar": (lambda e, l:
                        e.activation(out=r1[0:1, :], in_=r1[0:1, :], func=AF.Sqrt,
                                     bias=cst_eps))})
    l.phase({"vector": (lambda e, l: e.reciprocal(r1[0:1, :], r1[0:1, :]))})
    l.phase({"tensor": (lambda e, l: bcast(e, ones_row, r0, u1, TC))})
    for ci in range(NCH):
        l.phase({"vector": (lambda e, l, ci=ci:
                            e.scalar_tensor_tensor(
                                out=xb_sb[:, ci * TC:(ci + 1) * TC],
                                in0=xb_sb[:, ci * TC:(ci + 1) * TC], scalar=1.0,
                                in1=u1[:], op0=ALU.mult, op1=ALU.subtract))})
    l.phase({"tensor": (lambda e, l: bcast(e, ones_row, r1, u1, TC))})
    for ci in range(NCH):
        l.phase({"vector": (lambda e, l, ci=ci:
                            e.tensor_tensor(out=xb_sb[:, ci * TC:(ci + 1) * TC],
                                            in0=xb_sb[:, ci * TC:(ci + 1) * TC],
                                            in1=u1[:], op=ALU.mult))})
        if after_chunk is not None:
            w = after_chunk(ci)
            if w:
                l.phase(w)


def emit_proj(l, pf_key, next_pf, w_sb_cur, w_sb3_next, wview_next, h_sb, out_sb, pp):
    for do in range(NCH):
        for nt in range(TC // 512):
            ps = pp[(do * (TC // 512) + nt) % 2]

            def p_mm(e, l, do=do, nt=nt, ps=ps, first=(do == 0 and nt == 0)):
                if first and pf_key is not None:
                    l.wait_pf(e, pf_key)
                i = None
                for ci in range(NCH):
                    i = e.matmul(ps[:],
                                 w_sb_cur[:, ci * D + do * 128:ci * D + (do + 1) * 128],
                                 h_sb[:, ci * TC + nt * 512:ci * TC + (nt + 1) * 512],
                                 start=(ci == 0), stop=(ci == NCH - 1))
                return i
            work = {"tensor": p_mm}
            if do == 0 and nt == 0 and next_pf is not None:
                def p_pf(e, l, nk=next_pf):
                    l.dma("gpsimd", e, w_sb3_next, wview_next, pf=nk)
                work["gpsimd"] = p_pf
            l.phase(work)
            l.phase({"scalar": (lambda e, l, do=do, nt=nt, ps=ps:
                                e.copy(out=out_sb[:, do * TC + nt * 512:
                                                  do * TC + (nt + 1) * 512],
                                       in_=ps[:]))})



def build_launchD(max_phases=None):
    l = L("launchD")
    x = l.inp("x", [D, TC], F32)
    at = l.inp("at", [D, TC], BF16)
    wo = l.inp("wo", [D, D], F32)
    n2w = l.inp("n2w", [128, NCH], F32)
    wg = l.inp("wg", [D, 2048], F32)
    wu = l.inp("wu", [D, 2048], F32)
    wd = l.inp("wd", [2048, D], F32)
    z_o = l.out("z", [D, TC], F32)

    h2_sb = l.sbuf("h2", [128, NCH * TC], BF16)
    x1_sb = l.sbuf("x1", [128, NCH * TC], BF16)
    gu_sb = l.sbuf("gu", [128, 8192], BF16)
    wo_sb = l.sbuf("wo_sb", [128, NCH * D], BF16)
    wg_sb = l.sbuf("wg_sb", [128, NCH * 2048], BF16)
    wu_sb = l.sbuf("wu_sb", [128, NCH * 2048], BF16)
    wd_sb = l.sbuf("wd_sb", [128, 16 * D], BF16)
    n2w_sb = l.sbuf("n2w_sb", [128, NCH], F32)
    xc = l.sbuf("xc", [128, 512], F32)
    zt = l.sbuf("zt", [128, 256], F32)
    sq = l.sbuf("sq", [128, TC], BF16)
    ones_bf = l.sbuf("ones_bf", [128, 1], BF16)
    ones_row = l.sbuf("ones_row", [1, 128], F32)
    r0 = l.sbuf("r0", [1, TC], F32)
    cst = l.sbuf("cst", [128, 1], F32)

    u1 = l.psum("u1", [128, TC], F32)
    pp = [l.psum(f"pp{i}", [128, 512], F32) for i in range(2)]

    at_v = at.rearrange("(c p) t -> p c t", p=128)
    h2_3 = h2_sb.ap().rearrange("p (c t) -> p c t", c=NCH)
    wo_v = wo.rearrange("(c p) o -> p c o", p=128)
    wo_3 = wo_sb.ap().rearrange("p (c o) -> p c o", c=NCH)
    wg_v = wg.rearrange("(c p) o -> p c o", p=128)
    wg_3 = wg_sb.ap().rearrange("p (c o) -> p c o", c=NCH)
    wu_v = wu.rearrange("(c p) o -> p c o", p=128)
    wu_3 = wu_sb.ap().rearrange("p (c o) -> p c o", c=NCH)
    wd_v = wd.rearrange("(c p) o -> p c o", p=128)
    wd_3 = wd_sb.ap().rearrange("p (c o) -> p c o", c=16)

    def p0(e, l):
        l.dma("sync", e, h2_3, at_v)
        l.dma("sync", e, n2w_sb[:], n2w)
    def p0g(e, l):
        l.dma("gpsimd", e, wo_3, wo_v, pf="wo")
        l.dma("gpsimd", e, wg_3, wg_v, pf="wg")
        l.dma("gpsimd", e, wu_3, wu_v, pf="wu")
        l.dma("gpsimd", e, wd_3, wd_v, pf="wd")
    def p0v(e, l):
        e.memset(ones_bf[:], 1.0)
        e.memset(ones_row[:], 1.0)
        e.memset(cst[:], 1e-6)
    l.phase({"sync": p0, "gpsimd": p0g, "vector": p0v})

    for do in range(NCH):
        for nt in range(TC // 512):
            ps = pp[(do * 4 + nt) % 2]

            def p_mm(e, l, do=do, nt=nt, ps=ps, first=(do == 0 and nt == 0)):
                if first:
                    l.wait_pf(e, "wo")
                i = None
                for ci in range(NCH):
                    i = e.matmul(ps[:],
                                 wo_sb[:, ci * D + do * 128:ci * D + (do + 1) * 128],
                                 h2_sb[:, ci * TC + nt * 512:ci * TC + (nt + 1) * 512],
                                 start=(ci == 0), stop=(ci == NCH - 1))
                return i

            def p_xc(e, l, do=do, nt=nt):
                l.dma("sync", e, xc[:], x[do * 128:(do + 1) * 128, nt * 512:(nt + 1) * 512])
            l.phase({"tensor": p_mm, "sync": p_xc})
            l.phase({"vector": (lambda e, l, do=do, nt=nt, ps=ps:
                                e.scalar_tensor_tensor(
                                    out=x1_sb[:, do * TC + nt * 512:do * TC + (nt + 1) * 512],
                                    in0=ps[:], scalar=1.0, in1=xc[:],
                                    op0=ALU.mult, op1=ALU.add))})

    for ci in range(NCH):
        l.phase({"scalar": (lambda e, l, ci=ci:
                            e.activation(out=sq[:], in_=x1_sb[:, ci * TC:(ci + 1) * TC],
                                         func=AF.Square))})
        l.phase({"tensor": (lambda e, l, ci=ci:
                            colsum(e, ones_bf, sq, u1, TC, start=(ci == 0),
                                   stop=(ci == NCH - 1)))})
    l.phase({"scalar": (lambda e, l:
                        e.activation(out=r0[0:1, :], in_=u1[0:1, :], func=AF.Sqrt,
                                     scale=1.0 / D, bias=cst[0:1, 0:1]))})
    l.phase({"vector": (lambda e, l: e.reciprocal(r0[0:1, :], r0[0:1, :]))})
    l.phase({"tensor": (lambda e, l: bcast(e, ones_row, r0, u1, TC))})
    for ci in range(NCH):
        l.phase({"vector": (lambda e, l, ci=ci:
                            e.scalar_tensor_tensor(
                                out=h2_sb[:, ci * TC:(ci + 1) * TC],
                                in0=x1_sb[:, ci * TC:(ci + 1) * TC],
                                scalar=n2w_sb[:, ci:ci + 1], in1=u1[:],
                                op0=ALU.mult, op1=ALU.mult))})

    NS = 256
    for s in range(TC // NS):
        for fc in range(16):
            def p_mm(e, l, s=s, fc=fc, first=(s == 0 and fc == 0)):
                if first:
                    l.wait_pf(e, "wg")
                    l.wait_pf(e, "wu")
                for ci in range(NCH):
                    e.matmul(pp[0][:, 0:NS],
                             wg_sb[:, ci * 2048 + fc * 128:ci * 2048 + (fc + 1) * 128],
                             h2_sb[:, ci * TC + s * NS:ci * TC + (s + 1) * NS],
                             start=(ci == 0), stop=(ci == NCH - 1))
                i = None
                for ci in range(NCH):
                    i = e.matmul(pp[1][:, 0:NS],
                                 wu_sb[:, ci * 2048 + fc * 128:ci * 2048 + (fc + 1) * 128],
                                 h2_sb[:, ci * TC + s * NS:ci * TC + (s + 1) * NS],
                                 start=(ci == 0), stop=(ci == NCH - 1))
                return i
            l.phase({"tensor": p_mm})

            def p_ev(e, l, fc=fc):
                e.activation(out=gu_sb[:, fc * NS:(fc + 1) * NS], in_=pp[0][:, 0:NS],
                             func=AF.Silu)
            def p_ev2(e, l, fc=fc):
                e.tensor_copy(gu_sb[:, 4096 + fc * NS:4096 + (fc + 1) * NS],
                              pp[1][:, 0:NS])
            l.phase({"scalar": p_ev, "vector": p_ev2})

        l.phase({"vector": (lambda e, l:
                            e.tensor_tensor(out=gu_sb[:, 0:4096], in0=gu_sb[:, 0:4096],
                                            in1=gu_sb[:, 4096:8192], op=ALU.mult))})
        for do in range(NCH):
            def p_mm(e, l, s=s, do=do, first=(s == 0 and do == 0)):
                if first:
                    l.wait_pf(e, "wd")
                i = None
                for fc in range(16):
                    i = e.matmul(pp[0][:, 0:NS],
                                 wd_sb[:, fc * D + do * 128:fc * D + (do + 1) * 128],
                                 gu_sb[:, fc * NS:(fc + 1) * NS],
                                 start=(fc == 0), stop=(fc == 15))
                return i

            def p_xc(e, l, s=s, do=do):
                l.dma("sync", e, xc[:, 0:NS],
                      x[do * 128:(do + 1) * 128, s * NS:(s + 1) * NS])
            l.phase({"tensor": p_mm, "sync": p_xc})

            def p_z(e, l, s=s, do=do):
                e.scalar_tensor_tensor(
                    out=zt[:], in0=pp[0][:, 0:NS], scalar=1.0,
                    in1=x1_sb[:, do * TC + s * NS:do * TC + (s + 1) * NS],
                    op0=ALU.mult, op1=ALU.add)
                e.drain()
                e.tensor_tensor(out=zt[:], in0=zt[:], in1=xc[:, 0:NS], op=ALU.add)
            l.phase({"vector": p_z})
            l.phase({"gpsimd": (lambda e, l, s=s, do=do:
                                l.dma("gpsimd", e,
                                      z_o[do * 128:(do + 1) * 128, s * NS:(s + 1) * NS],
                                      zt[:]))})

    def p_fin(e, l):
        e.wait_ge(l.dsem["gpsimd"], l.dcnt["gpsimd"])
    l.phase({"gpsimd": p_fin})

    if max_phases is not None:
        l._phases = l._phases[:max_phases]
    nc = l.build()
    l.close()
    return nc



def build_launchE(max_phases=None):
    l = L("launchE")
    z_i = l.inp("z", [D, TC], F32)
    wrt = l.inp("wrt", [D, 8], F32)
    brr = l.inp("brr", [128, 8], F32)
    w1t = l.inp("w1t", [8, D, 4096], F32)
    w2t = l.inp("w2t", [8, 4096, D], F32)
    b1c = l.inp("b1c", [128, 8 * 32], F32)
    b2c = l.inp("b2c", [128, 8 * 8], F32)
    ident = l.inp("ident", [128, 128], BF16)
    z2_o = l.out("z2", [D, TC], F32)

    xb = l.sbuf("xb", [128, NCH * TC], BF16)
    w1_sb = l.sbuf("w1_sb", [128, NCH * 4096], BF16)
    w2_sb = l.sbuf("w2_sb", [128, 32 * D], BF16)
    h_sb = l.sbuf("h_sb", [128, 32 * 256], BF16)
    x2s = l.sbuf("x2s", [128, NCH * 256], BF16)
    msk = l.sbuf("msk", [128, 512], BF16)
    mrow = l.sbuf("mrow", [1, 512], BF16)
    lg = l.sbuf("lg", [128, 128], F32)
    mx = l.sbuf("mx", [128, 8], F32)
    oh = l.sbuf("oh", [128, 128], BF16)
    wrt_sb = l.sbuf("wrt_sb", [128, NCH * 8], BF16)
    brr_sb = l.sbuf("brr_sb", [128, 8], F32)
    b1_sb = l.sbuf("b1_sb", [128, 8 * 32], F32)
    b2_sb = l.sbuf("b2_sb", [128, 8 * 8], F32)
    id_sb = l.sbuf("id_sb", [128, 128], BF16)
    zt2 = l.sbuf("zt2", [128, 256], F32)
    sq = l.sbuf("sq", [128, 1024], BF16)
    ones_bf = l.sbuf("ones_bf", [128, 1], BF16)
    ones_row = l.sbuf("ones_row", [1, 128], F32)
    ones_rb = l.sbuf("ones_rb", [1, 128], BF16)
    r0 = l.sbuf("r0", [1, TC], F32)
    r1 = l.sbuf("r1", [1, TC], F32)
    tmp_bf = l.sbuf("tmp_bf", [128, 512], BF16)
    cst = l.sbuf("cst", [128, 1], F32)

    u1 = l.psum("u1", [128, TC], F32)
    pp = [l.psum(f"pp{i}", [128, 512], F32) for i in range(2)]
    bps = l.psum("bps", [128, 128], F32)
    msps = l.psum("msps", [128, 512], BF16)

    x2d = l.dram("x2d", [D, TC], BF16)
    x2dv = x2d.rearrange("(c p) t -> p c t", p=128)
    masks_d = l.dram("masks_d", [8, TC], BF16)

    wrt_v = wrt.rearrange("(c p) o -> p c o", p=128)
    wrt_3 = wrt_sb.ap().rearrange("p (c o) -> p c o", c=NCH)
    w1v = w1t.rearrange("e (c p) f -> e p c f", p=128)
    w1_3 = w1_sb.ap().rearrange("p (c f) -> p c f", c=NCH)
    w2v = w2t.rearrange("e (c p) o -> e p c o", p=128)
    w2_3 = w2_sb.ap().rearrange("p (c o) -> p c o", c=32)

    def p0(e, l):
        l.dma("sync", e, brr_sb[:], brr)
        l.dma("sync", e, b1_sb[:], b1c)
        l.dma("sync", e, b2_sb[:], b2c)
        l.dma("sync", e, id_sb[:], ident)
    def p0g(e, l):
        l.dma("gpsimd", e, wrt_3, wrt_v)
    def p0v(e, l):
        e.memset(ones_bf[:], 1.0)
        e.memset(ones_row[:], 1.0)
        e.memset(ones_rb[:], 1.0)
        e.memset(cst[:], 1e-5)
    l.phase({"sync": p0, "gpsimd": p0g, "vector": p0v})

    def after_chunk(ci):
        def p_lg(e, l, ci=ci):
            i = None
            for tt in range(TC // 128):
                i = e.matmul(bps[:, tt * 8:(tt + 1) * 8],
                             xb[:, ci * TC + tt * 128:ci * TC + (tt + 1) * 128],
                             wrt_sb[:, ci * 8:(ci + 1) * 8],
                             start=(ci == 0 and tt == 0),
                             stop=(ci == NCH - 1 and tt == TC // 128 - 1),
                             skip_group_check=True)
            return i
        return {"tensor": p_lg,
                "gpsimd": (lambda e, l, ci=ci:
                           l.dma("gpsimd", e, x2d[ci * 128:(ci + 1) * 128, :],
                                 xb[:, ci * TC:(ci + 1) * TC]))}
    emit_ln_start(l, z_i, xb, None, sq, u1, ones_bf, ones_row, (r0, r1),
                  cst[0:1, 0:1], after_chunk=after_chunk)

    def p_lgf(e, l):
        e.tensor_copy(lg[:], bps[:])
        e.drain()
        for tt in range(TC // 128):
            e.tensor_tensor(out=lg[:, tt * 8:(tt + 1) * 8],
                            in0=lg[:, tt * 8:(tt + 1) * 8], in1=brr_sb[:],
                            op=ALU.add)
    l.phase({"vector": p_lgf})
    for tt in range(TC // 128):
        def p_oh(e, l, tt=tt):
            e.max(mx[:], lg[:, tt * 8:(tt + 1) * 8])
            e.drain()
            e.tensor_scalar(out=oh[:, tt * 8:(tt + 1) * 8],
                            in0=lg[:, tt * 8:(tt + 1) * 8],
                            scalar1=mx[:, 0:1], scalar2=None, op0=ALU.is_equal)
        l.phase({"vector": p_oh})
    for g in range(TC // 128 * 8 // 4):
        def p_tr(e, l, g=g):
            i = None
            for r in range(4):
                idx = g * 4 + r
                tt, ee = idx // 8, idx % 8
                i = e.transpose(msps[0:1, r * 128:(r + 1) * 128],
                                oh[:, tt * 8 + ee:tt * 8 + ee + 1], id_sb[:])
            return i
        l.phase({"tensor": p_tr})

        def p_cp(e, l, g=g):
            e.copy(out=mrow[0:1, :], in_=msps[0:1, 0:512])
        l.phase({"scalar": p_cp})

        def p_md(e, l, g=g):
            idx = g * 4
            tt, ee = idx // 8, idx % 8
            l.dma("gpsimd", e,
                  masks_d.rearrange("(o e) t -> o e t", o=1)[0:1, ee:ee + 4,
                                                            tt * 128:(tt + 1) * 128],
                  mrow.ap().rearrange("p (a t) -> p a t", a=4))
        l.phase({"gpsimd": p_md})

    l.phase({"vector": (lambda e, l: e.memset(xb[:], 0.0))})

    SS = 256
    et_sem = l.es.enter_context(l.nc.semaphore("e_t"))
    es_sem = l.es.enter_context(l.nc.semaphore("e_s"))
    ev_sem = l.es.enter_context(l.nc.semaphore("e_v"))
    l._e_t = 0
    l._e_s = 0
    l._e_v = 0
    ACC = [pp[0][:, 0:SS], pp[1][:, 0:SS], u1[:, 0:SS], u1[:, 512:512 + SS]]

    for ee in range(8):
        def p_pf(e, l, ee=ee):
            l.dma("gpsimd", e, w1_3, w1v[ee], pf=f"w1_{ee}")
            l.dma("gpsimd", e, w2_3, w2v[ee], pf=f"w2_{ee}")
        l.phase({"gpsimd": p_pf})
        for s in range(TC // SS):
            def p_xs(e, l, s=s):
                l.dma("sync", e, x2s.ap().rearrange("p (c t) -> p c t", c=NCH),
                      x2dv[:, :, s * SS:(s + 1) * SS])
            def p_mr(e, l, ee=ee, s=s):
                l.dma("sync", e, mrow[0:1, 0:SS],
                      masks_d[ee:ee + 1, s * SS:(s + 1) * SS])
            l.phase({"sync": (lambda e, l, a=p_xs, b=p_mr: (a(e, l), b(e, l)) and None)})

            def p_mbc(e, l):
                return e.matmul(u1[:, 1024:1024 + SS], ones_rb[0:1, :],
                                mrow[0:1, 0:SS], start=True, stop=True)
            l.phase({"tensor": p_mbc})
            l.phase({"scalar": (lambda e, l:
                                e.copy(out=msk[:, 0:SS], in_=u1[:, 1024:1024 + SS]))})

            def loopT1(e, l, ee=ee, s=s):
                s0, v0 = l._e_s, l._e_v
                if s == 0:
                    l.wait_pf(e, f"w1_{ee}")
                last = None
                for fc in range(32):
                    if fc >= 4:
                        pv = fc - 4
                        if pv % 2 == 0:
                            e.wait_ge(es_sem, s0 + pv // 2 + 1)
                        else:
                            e.wait_ge(ev_sem, v0 + pv // 2 + 1)
                    for ci in range(NCH):
                        last = e.matmul(ACC[fc % 4],
                                        w1_sb[:, ci * 4096 + fc * 128:
                                              ci * 4096 + (fc + 1) * 128],
                                        x2s[:, ci * SS:(ci + 1) * SS],
                                        start=(ci == 0), stop=(ci == NCH - 1))
                    last.then_inc(et_sem, 1)
                    l._e_t += 1
                return None

            def loopS1(e, l, ee=ee):
                t0 = l._e_t - 32
                for fc in range(0, 32, 2):
                    e.wait_ge(et_sem, t0 + fc + 1)
                    e.activation(out=h_sb[:, fc * SS:(fc + 1) * SS], in_=ACC[fc % 4],
                                 func=AF.Relu,
                                 bias=b1_sb[:, ee * 32 + fc:ee * 32 + fc + 1]
                                 ).then_inc(es_sem, 1)
                    l._e_s += 1
                return None

            def loopV1(e, l, ee=ee):
                t0 = l._e_t - 32
                for fc in range(1, 32, 2):
                    e.wait_ge(et_sem, t0 + fc + 1)
                    i = e.tensor_scalar(out=h_sb[:, fc * SS:(fc + 1) * SS],
                                        in0=ACC[fc % 4],
                                        scalar1=b1_sb[:, ee * 32 + fc:ee * 32 + fc + 1],
                                        scalar2=0.0, op0=ALU.add, op1=ALU.max)
                    e.drain()
                    i.then_inc(ev_sem, 1)
                    l._e_v += 1
                return None

            l.phase({"tensor": loopT1, "scalar": loopS1, "vector": loopV1})

            def loopT2(e, l, ee=ee, s=s):
                v0 = l._e_v
                if s == 0:
                    l.wait_pf(e, f"w2_{ee}")
                last = None
                for do in range(NCH):
                    if do >= 4:
                        e.wait_ge(ev_sem, v0 + do - 3)
                    for fc in range(32):
                        last = e.matmul(ACC[do % 4],
                                        w2_sb[:, fc * D + do * 128:
                                              fc * D + (do + 1) * 128],
                                        h_sb[:, fc * SS:(fc + 1) * SS],
                                        start=(fc == 0), stop=(fc == 31))
                    last.then_inc(et_sem, 1)
                    l._e_t += 1
                return None

            def loopV2(e, l, ee=ee, s=s):
                t0 = l._e_t - NCH
                for do in range(NCH):
                    e.wait_ge(et_sem, t0 + do + 1)
                    e.scalar_tensor_tensor(out=tmp_bf[:, 0:SS], in0=ACC[do % 4],
                                           scalar=b2_sb[:, ee * 8 + do:ee * 8 + do + 1],
                                           in1=msk[:, 0:SS],
                                           op0=ALU.add, op1=ALU.mult)
                    e.drain()
                    i = e.tensor_tensor(
                        out=xb[:, do * TC + s * SS:do * TC + (s + 1) * SS],
                        in0=xb[:, do * TC + s * SS:do * TC + (s + 1) * SS],
                        in1=tmp_bf[:, 0:SS], op=ALU.add)
                    e.drain()
                    i.then_inc(ev_sem, 1)
                    l._e_v += 1
                return None

            l.phase({"tensor": loopT2, "vector": loopV2})

    for ci in range(NCH):
        for hf in range(2):
            def p_x2(e, l, ci=ci, hf=hf):
                l.dma("sync", e, sq[:, 0:1024],
                      x2d[ci * 128:(ci + 1) * 128, hf * 1024:(hf + 1) * 1024])
            l.phase({"sync": p_x2})
            for s4 in range(4):
                def p_add(e, l, ci=ci, hf=hf, s4=s4):
                    e.tensor_tensor(out=zt2[:], in0=sq[:, s4 * 256:(s4 + 1) * 256],
                                    in1=xb[:, ci * TC + hf * 1024 + s4 * 256:
                                            ci * TC + hf * 1024 + (s4 + 1) * 256],
                                    op=ALU.add)
                l.phase({"vector": p_add})
                l.phase({"gpsimd": (lambda e, l, ci=ci, hf=hf, s4=s4:
                                    l.dma("gpsimd", e,
                                          z2_o[ci * 128:(ci + 1) * 128,
                                               hf * 1024 + s4 * 256:hf * 1024 + (s4 + 1) * 256],
                                          zt2[:]))})

    def p_fin(e, l):
        e.wait_ge(l.dsem["gpsimd"], l.dcnt["gpsimd"])
    l.phase({"gpsimd": p_fin})

    if max_phases is not None:
        l._phases = l._phases[:max_phases]
    nc = l.build()
    l.close()
    return nc



def build_launchF(max_phases=None):
    l = L("launchF")
    z_i = l.inp("z2", [D, TC], F32)
    wqt = l.inp("wqt", [D, D], F32)
    wkt = l.inp("wkt", [D, D], F32)
    wvt = l.inp("wvt", [D, D], F32)
    cosb = l.inp("cosb", [128, TC], F32)
    sinb = l.inp("sinb", [128, TC], F32)
    x3_o = l.out("x3", [D, TC], F32)
    q_o = l.out("q", [D, TC], BF16)
    k_o = l.out("k", [D, TC], BF16)
    v_o = l.out("v", [D, TC], BF16)

    xb = l.sbuf("xb", [128, NCH * TC], BF16)
    w_sb = [l.sbuf(f"w{i}", [128, NCH * D], BF16) for i in range(2)]
    out_sb = l.sbuf("outb", [128, NCH * TC], BF16)
    cos_sb = l.sbuf("cos_sb", [128, TC], F32)
    sin_sb = l.sbuf("sin_sb", [128, TC], F32)
    t1 = l.sbuf("t1", [128, TC], BF16)
    t2 = l.sbuf("t2", [128, TC], BF16)
    sq = l.sbuf("sq", [128, TC], BF16)
    ones_bf = l.sbuf("ones_bf", [128, 1], BF16)
    ones_row = l.sbuf("ones_row", [1, 128], F32)
    r0 = l.sbuf("r0", [1, TC], F32)
    r1 = l.sbuf("r1", [1, TC], F32)
    cst = l.sbuf("cst", [128, 1], F32)

    u1 = l.psum("u1", [128, TC], F32)
    pp = [l.psum(f"pp{i}", [128, 512], F32) for i in range(2)]

    wv_views = {
        "q": wqt.rearrange("(c p) o -> p c o", p=128),
        "k": wkt.rearrange("(c p) o -> p c o", p=128),
        "v": wvt.rearrange("(c p) o -> p c o", p=128),
    }
    w_sb3 = [w_sb[i].ap().rearrange("p (c o) -> p c o", c=NCH) for i in range(2)]

    def p0(e, l):
        l.dma("sync", e, cos_sb[:], cosb)
        l.dma("sync", e, sin_sb[:], sinb)
    def p0g(e, l):
        l.dma("gpsimd", e, w_sb3[0], wv_views["q"], pf="q")
    def p0v(e, l):
        e.memset(ones_bf[:], 1.0)
        e.memset(ones_row[:], 1.0)
        e.memset(cst[:], 1e-5)
    l.phase({"sync": p0, "gpsimd": p0g, "vector": p0v})

    def after_chunk(ci):
        return {"gpsimd": (lambda e, l, ci=ci:
                           l.dma("gpsimd", e, x3_o[ci * 128:(ci + 1) * 128, :],
                                 xb[:, ci * TC:(ci + 1) * TC]))}
    emit_ln_start(l, z_i, xb, None, sq, u1, ones_bf, ones_row, (r0, r1),
                  cst[0:1, 0:1], after_chunk=after_chunk)

    def rope(src):
        shf = [(i + 16) % 32 for i in range(32)]
        for ci in range(NCH):
            sl = slice(ci * TC, (ci + 1) * TC)

            def p_r1(e, l, sl=sl):
                e.stream_shuffle(t2[:], src[:, sl], shf)
                e.tensor_tensor(out=t1[:], in0=src[:, sl], in1=cos_sb[:], op=ALU.mult)
                e.drain()
                e.tensor_tensor(out=t2[:], in0=t2[:], in1=sin_sb[:], op=ALU.mult)
                e.drain()
                e.tensor_tensor(out=src[:, sl], in0=t1[:], in1=t2[:], op=ALU.add)
            l.phase({"vector": p_r1})

    def store(out_ap):
        def p_st(e, l):
            for ci in range(NCH):
                l.dma("gpsimd", e, out_ap[ci * 128:(ci + 1) * 128, :],
                      out_sb[:, ci * TC:(ci + 1) * TC])
        l.phase({"gpsimd": p_st})

    emit_proj(l, "q", "k", w_sb[0], w_sb3[1], wv_views["k"], xb, out_sb, pp)
    rope(out_sb)
    store(q_o)
    emit_proj(l, "k", "v", w_sb[1], w_sb3[0], wv_views["v"], xb, out_sb, pp)
    rope(out_sb)
    store(k_o)
    emit_proj(l, "v", None, w_sb[0], None, None, xb, out_sb, pp)
    store(v_o)

    def p_fin(e, l):
        e.wait_ge(l.dsem["gpsimd"], l.dcnt["gpsimd"])
    l.phase({"gpsimd": p_fin})

    if max_phases is not None:
        l._phases = l._phases[:max_phases]
    nc = l.build()
    l.close()
    return nc



def build_launchG(max_phases=None):
    l = L("launchG")
    NKT = T // 128
    qr = l.inp("qr", [D, TC], BF16)
    kh = l.inp("kh", [16, 64, T], BF16)
    vh = l.inp("vh", [16, T, 64], BF16)
    x3 = l.inp("x3", [D, TC], F32)
    wot = l.inp("wot", [D, D], F32)
    z3_o = l.out("z3", [D, TC], F32)

    q_sb = l.sbuf("q_sb", [128, NCH * TC], BF16)
    kd_sb = l.sbuf("kd_sb", [128, T], BF16)
    va_sb = l.sbuf("va_sb", [128, NKT * 64], BF16)
    vb_sb = l.sbuf("vb_sb", [128, NKT * 64], BF16)
    ctx_sb = l.sbuf("ctx_sb", [128, NCH * TC], BF16)
    sqk = l.sbuf("sqk", [128, T], BF16)
    et = [l.sbuf(f"et{i}", [128, 1024], BF16) for i in range(2)]
    wot_sb = l.sbuf("wot_sb", [128, NCH * D], BF16)
    bcs = l.sbuf("bcs", [128, 512], F32)
    rws = l.sbuf("rws", [1, 1024], F32)
    mq2 = l.sbuf("mq2", [1, 16], F32)
    mk2 = l.sbuf("mk2", [1, 16], F32)
    sc1 = l.sbuf("sc1", [1, 8], F32)
    nbias = l.sbuf("nbias", [128, 1], F32)
    xc = l.sbuf("xc", [128, 512], F32)
    zt = l.sbuf("zt", [128, 512], F32)
    ones_bf = l.sbuf("ones_bf", [128, 1], BF16)
    ones_row = l.sbuf("ones_row", [1, 128], F32)

    sc = [l.psum(f"scps{i}", [128, 1024], F32) for i in range(2)]
    ctxp = l.psum("ctxp", [128, 1024], F32)
    denp = l.psum("denp", [128, 1024], F32)

    tsem = l.es.enter_context(l.nc.semaphore("g_t"))
    ssem = l.es.enter_context(l.nc.semaphore("g_s"))
    l._g_t = 0
    l._g_s = 0

    q_v = qr.rearrange("(c p) t -> p c t", p=128)
    q_sb3 = q_sb.ap().rearrange("p (c t) -> p c t", c=NCH)
    wot_v = wot.rearrange("(c p) o -> p c o", p=128)
    wot_3 = wot_sb.ap().rearrange("p (c o) -> p c o", c=NCH)

    def p0(e, l):
        l.dma("sync", e, q_sb3, q_v)
    def p0g(e, l):
        l.dma("gpsimd", e, wot_3, wot_v, pf="wo")
    def p0v(e, l):
        e.memset(ones_bf[:], 1.0)
        e.memset(ones_row[:], 1.0)
    l.phase({"sync": p0, "gpsimd": p0g, "vector": p0v})

    for ci in range(NCH):
        l.phase({"scalar": (lambda e, l, ci=ci:
                            e.activation(out=sqk[:, 0:TC],
                                         in_=q_sb[:, ci * TC:(ci + 1) * TC],
                                         func=AF.Square))})
        for hh in range(2):
            def p_cs(e, l, hh=hh):
                i = None
                for j in range(0, TC, 512):
                    i = e.matmul(ctxp[0:1, 0:512],
                                 ones_bf[hh * 64:(hh + 1) * 64, 0:1],
                                 sqk[hh * 64:(hh + 1) * 64, j:j + 512],
                                 start=(j == 0), stop=(j == TC - 512))
                return i
            l.phase({"tensor": p_cs})

            def p_mx(e, l, ci=ci, hh=hh):
                pass
            def p_mx2(e, l, ci=ci, hh=hh):
                e.reduce_max(mq2[0:1, ci * 2 + hh:ci * 2 + hh + 1],
                             ctxp[0:1, 0:512], axis=AX.X)
            l.phase({"vector": p_mx2})

    for dj in range(NCH):
        def p_ld(e, l, dj=dj):
            l.dma("sync", e, kd_sb[:],
                  kh.rearrange("h p t -> (h p) t")[dj * 128:(dj + 1) * 128, :])
            l.dma("sync", e, va_sb.ap().rearrange("p (n d) -> p n d", d=64),
                  vh.rearrange("h (n p) d -> h p n d", p=128)[2 * dj])
            l.dma("sync", e, vb_sb.ap().rearrange("p (n d) -> p n d", d=64),
                  vh.rearrange("h (n p) d -> h p n d", p=128)[2 * dj + 1])
        l.phase({"sync": p_ld})

        l.phase({"scalar": (lambda e, l: e.activation(out=sqk[:], in_=kd_sb[:],
                                                      func=AF.Square))})
        for hh in range(2):
            for pc in range(T // 512):
                def p_cs(e, l, hh=hh, pc=pc):
                    return e.matmul(ctxp[0:1, 0:512],
                                    ones_bf[hh * 64:(hh + 1) * 64, 0:1],
                                    sqk[hh * 64:(hh + 1) * 64,
                                        pc * 512:(pc + 1) * 512],
                                    start=True, stop=True)
                l.phase({"tensor": p_cs})

                def p_mx(e, l, dj=dj, hh=hh, pc=pc):
                    e.reduce_max(sc1[0:1, 0:1], ctxp[0:1, 0:512], axis=AX.X)
                    e.drain()
                    if pc == 0:
                        e.tensor_copy(mk2[0:1, dj * 2 + hh:dj * 2 + hh + 1],
                                      sc1[0:1, 0:1])
                    else:
                        e.tensor_tensor(out=mk2[0:1, dj * 2 + hh:dj * 2 + hh + 1],
                                        in0=mk2[0:1, dj * 2 + hh:dj * 2 + hh + 1],
                                        in1=sc1[0:1, 0:1], op=ALU.max)
                l.phase({"vector": p_mx})

        def p_nb1(e, l, dj=dj):
            e.tensor_tensor(out=sc1[0:1, 0:2], in0=mq2[0:1, dj * 2:dj * 2 + 2],
                            in1=mk2[0:1, dj * 2:dj * 2 + 2], op=ALU.mult)
            e.drain()
            e.reduce_max(sc1[0:1, 2:3], sc1[0:1, 0:2], axis=AX.X)
        l.phase({"vector": p_nb1})
        l.phase({"scalar": (lambda e, l:
                            e.activation(out=sc1[0:1, 3:4], in_=sc1[0:1, 2:3],
                                         func=AF.Sqrt))})
        l.phase({"vector": (lambda e, l:
                            e.tensor_scalar_mul(sc1[0:1, 4:5], sc1[0:1, 3:4], -0.125))})
        l.phase({"tensor": (lambda e, l:
                            e.matmul(ctxp[:, 0:1], ones_row[0:1, :], sc1[0:1, 4:5],
                                     start=True, stop=True))})
        l.phase({"scalar": (lambda e, l: e.copy(out=nbias[:], in_=ctxp[:, 0:1]))})

        for qs in range(TC // 512):
            qsl = slice(dj * TC + qs * 512, dj * TC + (qs + 1) * 512)

            def mk_scores(e, kt, qsl=qsl):
                i = None
                for hh in range(2):
                    i = e.matmul(
                        sc[kt % 2][:, hh * 512:(hh + 1) * 512],
                        kd_sb[hh * 64:(hh + 1) * 64, kt * 128:(kt + 1) * 128],
                        q_sb.ap()[hh * 64:(hh + 1) * 64, qsl],
                        start=True, stop=True,
                        tile_position=(hh * 64, 0))
                return i

            def mk_pv(e, kt):
                first = (kt == 0)
                e.matmul(ctxp[0:64, 0:512], va_sb[:, kt * 64:(kt + 1) * 64],
                         et[kt % 2][:, 0:512],
                         start=first, stop=(kt == NKT - 1),
                         tile_position=(0, 0), skip_group_check=True)
                e.matmul(ctxp[64:128, 512:1024], vb_sb[:, kt * 64:(kt + 1) * 64],
                         et[kt % 2][:, 512:1024],
                         start=first, stop=(kt == NKT - 1),
                         tile_position=(0, 64), skip_group_check=True)
                e.matmul(denp[0:1, 0:512], ones_bf[:, 0:1], et[kt % 2][:, 0:512],
                         start=first, stop=(kt == NKT - 1),
                         tile_position=(0, 0), skip_group_check=True)
                return e.matmul(denp[0:1, 512:1024], ones_bf[:, 0:1],
                                et[kt % 2][:, 512:1024],
                                start=first, stop=(kt == NKT - 1),
                                tile_position=(0, 0), skip_group_check=True)

            def loopT(e, l, mk_scores=mk_scores, mk_pv=mk_pv):
                t0, s0 = l._g_t, l._g_s
                for kt in range(NKT):
                    if kt >= 2:
                        e.wait_ge(ssem, s0 + kt - 1)
                    mk_scores(e, kt).then_inc(tsem, 1)
                    l._g_t += 1
                    if kt >= 1:
                        e.wait_ge(ssem, s0 + kt)
                        mk_pv(e, kt - 1)
                e.wait_ge(ssem, s0 + NKT)
                return mk_pv(e, NKT - 1)

            def loopS(e, l):
                t0 = l._g_t - NKT
                for kt in range(NKT):
                    e.wait_ge(tsem, t0 + kt + 1)
                    e.activation(out=et[kt % 2][:], in_=sc[kt % 2][:],
                                 func=AF.Exp, scale=0.125,
                                 bias=nbias[:, 0:1]).then_inc(ssem, 1)
                    l._g_s += 1
                return None

            l.phase({"tensor": loopT, "scalar": loopS})

            def p_rc(e, l):
                e.reciprocal(rws[0:1, :], denp[0:1, :])
            l.phase({"vector": p_rc})

            def p_bc(e, l):
                e.matmul(sc[0][0:64, 0:512], ones_row[0:1, 0:64], rws[0:1, 0:512],
                         start=True, stop=True, tile_position=(0, 0))
                return e.matmul(sc[0][64:128, 0:512], ones_row[0:1, 0:64],
                                rws[0:1, 512:1024],
                                start=True, stop=True, tile_position=(0, 64))
            l.phase({"tensor": p_bc})
            l.phase({"scalar": (lambda e, l: e.copy(out=bcs[:], in_=sc[0][:, 0:512]))})

            def p_nm(e, l, qsl=qsl):
                e.tensor_tensor(out=ctx_sb.ap()[0:64, qsl], in0=ctxp[0:64, 0:512],
                                in1=bcs[0:64, :], op=ALU.mult)
                e.drain()
                e.tensor_tensor(out=ctx_sb.ap()[64:128, qsl],
                                in0=ctxp[64:128, 512:1024],
                                in1=bcs[64:128, :], op=ALU.mult)
            l.phase({"vector": p_nm})

    for do in range(NCH):
        for nt in range(TC // 512):
            def p_mm(e, l, do=do, nt=nt, first=(do == 0 and nt == 0)):
                if first:
                    l.wait_pf(e, "wo")
                i = None
                for ci in range(NCH):
                    i = e.matmul(ctxp[:, 0:512],
                                 wot_sb[:, ci * D + do * 128:ci * D + (do + 1) * 128],
                                 ctx_sb[:, ci * TC + nt * 512:ci * TC + (nt + 1) * 512],
                                 start=(ci == 0), stop=(ci == NCH - 1))
                return i

            def p_xc(e, l, do=do, nt=nt):
                l.dma("sync", e, xc[:], x3[do * 128:(do + 1) * 128, nt * 512:(nt + 1) * 512])
            l.phase({"tensor": p_mm, "sync": p_xc})

            def p_z(e, l):
                e.scalar_tensor_tensor(out=zt[:], in0=ctxp[:, 0:512], scalar=1.0,
                                       in1=xc[:], op0=ALU.mult, op1=ALU.add)
            l.phase({"vector": p_z})
            l.phase({"gpsimd": (lambda e, l, do=do, nt=nt:
                                l.dma("gpsimd", e,
                                      z3_o[do * 128:(do + 1) * 128, nt * 512:(nt + 1) * 512],
                                      zt[:]))})

    def p_fin(e, l):
        e.wait_ge(l.dsem["gpsimd"], l.dcnt["gpsimd"])
    l.phase({"gpsimd": p_fin})

    if max_phases is not None:
        l._phases = l._phases[:max_phases]
    nc = l.build()
    l.close()
    return nc



def build_launchH(max_phases=None):
    l = L("launchH")
    z_i = l.inp("z3", [D, TC], F32)
    wut = l.inp("wut", [D, 4096], F32)
    wdt = l.inp("wdt", [4096, D], F32)
    out_o = l.out("out", [D, TC], F32)

    xb = l.sbuf("xb", [128, NCH * TC], BF16)
    wu_sb = l.sbuf("wu_sb", [128, NCH * 4096], BF16)
    wd_sb = l.sbuf("wd_sb", [128, 32 * D], BF16)
    hs = l.sbuf("hs", [128, 32 * 256], BF16)
    sq = l.sbuf("sq", [128, 1024], BF16)
    tmp = l.sbuf("tmp", [128, 256], BF16)
    zt = l.sbuf("zt", [128, 256], F32)
    rbc1 = l.sbuf("rbc1", [128, 256], F32)
    rbc2 = l.sbuf("rbc2", [128, 256], F32)
    acc = l.sbuf("acc", [128, 32], F32)
    tau = l.sbuf("tau", [128, 2], F32)
    onesf = l.sbuf("onesf", [128, 1], F32)
    ones_bf = l.sbuf("ones_bf", [128, 1], BF16)
    ones_row = l.sbuf("ones_row", [1, 128], F32)
    r0 = l.sbuf("r0", [1, TC], F32)
    r1 = l.sbuf("r1", [1, TC], F32)
    s1 = l.sbuf("s1", [1, 8], F32)
    cst = l.sbuf("cst", [128, 2], F32)

    u1 = l.psum("u1", [128, TC], F32)
    pp = [l.psum(f"pp{i}", [128, 512], F32) for i in range(2)]

    z4d = l.dram("z4d", [D, TC], F32)

    wsl = l.sbuf("wsl", [128, 1024], F32)
    sgn = l.sbuf("sgn", [128, 1024], BF16)
    g1 = l.sbuf("g1", [128, 1024], BF16)

    def p0v(e, l):
        e.memset(ones_bf[:], 1.0)
        e.memset(ones_row[:], 1.0)
        e.memset(onesf[:], 1.0)
        e.memset(cst[:, 0:1], 1e-5)
        e.memset(cst[:, 1:2], 1e-8)
    l.phase({"vector": p0v})

    emit_ln_start(l, z_i, xb, None, sq, u1, ones_bf, ones_row, (r0, r1),
                  cst[0:1, 0:1])

    def ternarize(w_in, R, C, w_out_sb):
        wv = w_in.rearrange("(c p) f -> p c f", p=128)
        nsl = (R // 128) * (C // 1024)

        def sl_view(k):
            c, s = k // (C // 1024), k % (C // 1024)
            return wv[:, c, s * 1024:(s + 1) * 1024], c * C + s * 1024

        for k in range(nsl):
            src_v, _ = sl_view(k)
            l.phase({"sync": (lambda e, l, src_v=src_v:
                              l.dma("sync", e, wsl[:], src_v))})
            l.phase({"scalar": (lambda e, l, k=k:
                                e.activation(out=g1[:], in_=wsl[:], func=AF.Abs,
                                             accum_out=acc[:, k:k + 1]))})
        def p_t1(e, l, nsl=nsl):
            return e.matmul(u1[0:1, 0:nsl], onesf[:, 0:1], acc[:, 0:nsl],
                            start=True, stop=True)
        l.phase({"tensor": p_t1})

        def p_t2(e, l, nsl=nsl):
            e.reduce_sum(s1[0:1, 0:1], u1[0:1, 0:nsl], axis=AX.X)
            e.drain()
            e.tensor_scalar_mul(s1[0:1, 1:2], s1[0:1, 0:1], 0.5 / (R * C))
            e.drain()
            e.tensor_scalar_mul(s1[0:1, 2:3], s1[0:1, 1:2], -1.0)
        l.phase({"vector": p_t2})

        def p_t3(e, l):
            e.matmul(u1[:, 0:1], ones_row[0:1, :], s1[0:1, 1:2], start=True, stop=True)
            return e.matmul(u1[:, 1:2], ones_row[0:1, :], s1[0:1, 2:3],
                            start=True, stop=True)
        l.phase({"tensor": p_t3})
        l.phase({"scalar": (lambda e, l: e.copy(out=tau[:], in_=u1[:, 0:2]))})

        for k in range(nsl):
            src_v, ocol = sl_view(k)
            l.phase({"sync": (lambda e, l, src_v=src_v:
                              l.dma("sync", e, wsl[:], src_v))})
            l.phase({"scalar": (lambda e, l: e.sign(out=sgn[:], in_=wsl[:]))})

            def p_g(e, l, ocol=ocol):
                e.tensor_scalar(out=g1[:], in0=wsl[:], scalar1=tau[:, 0:1],
                                scalar2=None, op0=ALU.is_gt)
                e.drain()
                e.scalar_tensor_tensor(out=g1[:], in0=wsl[:], scalar=tau[:, 1:2],
                                       in1=g1[:], op0=ALU.is_lt, op1=ALU.add)
                e.drain()
                e.tensor_tensor(out=w_out_sb[:, ocol:ocol + 1024],
                                in0=g1[:], in1=sgn[:], op=ALU.mult)
            l.phase({"vector": p_g})

    ternarize(wut, D, 4096, wu_sb)
    ternarize(wdt, 4096, D, wd_sb)

    for ci in range(NCH):
        for hf in range(2):
            l.phase({"scalar": (lambda e, l, ci=ci, hf=hf:
                                e.activation(out=sq[:, 0:1024],
                                             in_=xb[:, ci * TC + hf * 1024:
                                                     ci * TC + (hf + 1) * 1024],
                                             func=AF.Square))})

            def p_cs(e, l, ci=ci, hf=hf):
                i = None
                for j in range(0, 1024, 512):
                    i = e.matmul(u1[0:1, hf * 1024 + j:hf * 1024 + j + 512],
                                 ones_bf[:, 0:1], sq[:, j:j + 512],
                                 start=(ci == 0), stop=(ci == NCH - 1))
                return i
            l.phase({"tensor": p_cs})
    l.phase({"scalar": (lambda e, l:
                        e.activation(out=r0[0:1, :], in_=u1[0:1, :], func=AF.Sqrt,
                                     scale=1.0 / D))})

    def p_r1(e, l):
        e.tensor_scalar_add(r0[0:1, :], r0[0:1, :], 1e-8)
        e.drain()
        e.reciprocal(r0[0:1, :], r0[0:1, :])
    l.phase({"vector": p_r1})

    SS = 256
    ht_sem = l.es.enter_context(l.nc.semaphore("h_t"))
    hs_sem = l.es.enter_context(l.nc.semaphore("h_s"))
    hv_sem = l.es.enter_context(l.nc.semaphore("h_v"))
    hd_sem = l.es.enter_context(l.nc.semaphore("h_d"))
    l._h_t = 0
    l._h_s = 0
    l._h_v = 0
    l._h_d = 0
    ACC = [pp[0][:, 0:SS], pp[1][:, 0:SS], u1[:, 0:SS], u1[:, 512:512 + SS]]
    tmp2 = [tmp, l.sbuf("tmp_b", [128, 256], BF16)]
    zt2 = [zt, l.sbuf("zt_b", [128, 256], F32)]

    for s in range(TC // SS):
        l.phase({"tensor": (lambda e, l, s=s:
                            bcast(e, ones_row, r0[0:1, s * SS:(s + 1) * SS],
                                  u1[:, 1024:1024 + SS] if False else u1, SS))})
        l.phase({"scalar": (lambda e, l: e.copy(out=rbc1[:], in_=u1[:, 0:SS]))})

        def loopT1(e, l, s=s):
            v0 = l._h_v
            last = None
            for fc in range(32):
                if fc >= 4:
                    e.wait_ge(hv_sem, v0 + fc - 3)
                for ci in range(NCH):
                    last = e.matmul(ACC[fc % 4],
                                    wu_sb[:, ci * 4096 + fc * 128:
                                          ci * 4096 + (fc + 1) * 128],
                                    xb[:, ci * TC + s * SS:ci * TC + (s + 1) * SS],
                                    start=(ci == 0), stop=(ci == NCH - 1))
                last.then_inc(ht_sem, 1)
                l._h_t += 1
            return None

        def loopV1(e, l):
            t0, s0 = l._h_t - 32, l._h_s
            for fc in range(32):
                e.wait_ge(ht_sem, t0 + fc + 1)
                if fc >= 2:
                    e.wait_ge(hs_sem, s0 + fc - 1)
                i = e.scalar_tensor_tensor(out=tmp2[fc % 2][:], in0=ACC[fc % 4],
                                           scalar=1.0, in1=rbc1[:],
                                           op0=ALU.mult, op1=ALU.mult)
                e.drain()
                i.then_inc(hv_sem, 1)
                l._h_v += 1
            return None

        def loopS1(e, l):
            v0 = l._h_v - 32
            for fc in range(32):
                e.wait_ge(hv_sem, v0 + fc + 1)
                e.activation(out=hs[:, fc * SS:(fc + 1) * SS],
                             in_=tmp2[fc % 2][:],
                             func=AF.Gelu).then_inc(hs_sem, 1)
                l._h_s += 1
            return None

        l.phase({"tensor": loopT1, "vector": loopV1, "scalar": loopS1})

        bases_st = {}

        def loopS2(e, l, bz=bases_st):
            bz.setdefault("t", l._h_t)
            bz.setdefault("s", l._h_s)
            t0 = bz["t"]
            for fc in range(32):
                if fc >= 4:
                    e.wait_ge(ht_sem, t0 + fc - 3)
                e.activation(out=sq[:, (fc % 4) * SS:(fc % 4 + 1) * SS],
                             in_=hs[:, fc * SS:(fc + 1) * SS],
                             func=AF.Square).then_inc(hs_sem, 1)
                l._h_s += 1
            return None

        def loopT2s(e, l, bz=bases_st):
            bz.setdefault("t", l._h_t)
            bz.setdefault("s", l._h_s)
            s0 = bz["s"]
            last = None
            for fc in range(32):
                e.wait_ge(hs_sem, s0 + fc + 1)
                last = e.matmul(u1[0:1, 1024:1024 + SS], ones_bf[:, 0:1],
                                sq[:, (fc % 4) * SS:(fc % 4 + 1) * SS],
                                start=(fc == 0), stop=(fc == 31))
                if fc < 31:
                    last.then_inc(ht_sem, 1)
                    l._h_t += 1
            return last

        l.phase({"scalar": loopS2, "tensor": loopT2s})

        l.phase({"scalar": (lambda e, l:
                            e.activation(out=r1[0:1, 0:SS], in_=u1[0:1, 1024:1024 + SS],
                                         func=AF.Sqrt, scale=1.0 / 4096))})

        def p_r2(e, l):
            e.tensor_scalar_add(r1[0:1, 0:SS], r1[0:1, 0:SS], 1e-8)
            e.drain()
            e.reciprocal(r1[0:1, 0:SS], r1[0:1, 0:SS])
        l.phase({"vector": p_r2})
        l.phase({"tensor": (lambda e, l:
                            bcast(e, ones_row, r1[0:1, 0:SS], u1, SS))})
        l.phase({"scalar": (lambda e, l: e.copy(out=rbc2[:], in_=u1[:, 0:SS]))})

        bases_g2 = {}
        dma_marks = []

        def loopT3(e, l, s=s, bz=bases_g2):
            bz.setdefault("t", l._h_t)
            bz.setdefault("v", l._h_v)
            v0 = bz["v"]
            last = None
            for do in range(NCH):
                if do >= 4:
                    e.wait_ge(hv_sem, v0 + do - 3)
                for fc in range(32):
                    last = e.matmul(ACC[do % 4],
                                    wd_sb[:, fc * D + do * 128:fc * D + (do + 1) * 128],
                                    hs[:, fc * SS:(fc + 1) * SS],
                                    start=(fc == 0), stop=(fc == 31))
                last.then_inc(ht_sem, 1)
                l._h_t += 1
            return None

        def loopV3(e, l, s=s, bz=bases_g2, dm=dma_marks):
            bz.setdefault("t", l._h_t)
            bz.setdefault("v", l._h_v)
            bz.setdefault("d", l._h_d)
            t0, d0 = bz["t"], bz["d"]
            for do in range(NCH):
                e.wait_ge(ht_sem, t0 + do + 1)
                if do >= 2:
                    e.wait_ge(hd_sem, d0 + do - 1)
                e.scalar_tensor_tensor(out=zt2[do % 2][:], in0=ACC[do % 4],
                                       scalar=1.0, in1=rbc2[:],
                                       op0=ALU.mult, op1=ALU.mult)
                e.drain()
                i = e.tensor_tensor(out=zt2[do % 2][:], in0=zt2[do % 2][:],
                                    in1=xb[:, do * TC + s * SS:do * TC + (s + 1) * SS],
                                    op=ALU.add)
                e.drain()
                i.then_inc(hv_sem, 1)
                l._h_v += 1
            return None

        def loopG3(e, l, s=s, bz=bases_g2, dm=dma_marks):
            bz.setdefault("t", l._h_t)
            bz.setdefault("v", l._h_v)
            bz.setdefault("d", l._h_d)
            for do in range(NCH):
                e.wait_ge(hv_sem, bz["v"] + do + 1)
                t = l.dma("gpsimd", e,
                          z4d[do * 128:(do + 1) * 128, s * SS:(s + 1) * SS],
                          zt2[do % 2][:])
                e.wait_ge(l.dsem["gpsimd"], t)
                e.sem_inc(hd_sem, 1)
                l._h_d += 1
            return None

        l.phase({"tensor": loopT3, "vector": loopV3, "gpsimd": loopG3})

    def after_chunk(ci):
        return {"gpsimd": (lambda e, l, ci=ci:
                           l.dma("gpsimd", e, out_o[ci * 128:(ci + 1) * 128, :],
                                 xb[:, ci * TC:(ci + 1) * TC]))}
    emit_ln_start(l, z4d, xb, None, sq, u1, ones_bf, ones_row, (r0, r1),
                  cst[0:1, 0:1], after_chunk=after_chunk)

    def p_fin(e, l):
        e.wait_ge(l.dsem["gpsimd"], l.dcnt["gpsimd"])
    l.phase({"gpsimd": p_fin})

    if max_phases is not None:
        l._phases = l._phases[:max_phases]
    nc = l.build()
    l.close()
    return nc



def _fm(a):
    return np.ascontiguousarray(np.asarray(a, np.float32).T)


def _chunked(v):
    return np.ascontiguousarray(np.asarray(v, np.float32).reshape(NCH, 128).T)


def _rope_perm():
    perm = np.zeros(D, np.int64)
    for j in range(NCH):
        for r in range(128):
            qd = r // 32
            h = 2 * j + (qd // 2)
            fr = 16 * (qd % 2) + (r % 16)
            x2 = (r % 32) >= 16
            perm[j * 128 + r] = h * 64 + 2 * fr + (1 if x2 else 0)
    return perm


def _cos_sin(t0):
    hd = 64
    inv = 1.0 / 10000.0 ** (np.arange(0, hd, 2, dtype=np.float32) / hd)
    t_ = np.arange(t0, t0 + TC, dtype=np.float32)
    f = np.outer(inv, t_)
    cos = np.zeros((128, TC), np.float32)
    ssin = np.zeros((128, TC), np.float32)
    for r in range(128):
        qd = r // 32
        fr = 16 * (qd % 2) + (r % 16)
        x2 = (r % 32) >= 16
        cos[r] = np.cos(f[fr])
        ssin[r] = np.sin(f[fr]) * (1.0 if x2 else -1.0)
    return cos, ssin


_NC_CACHE = {}


def _nc(name, builder):
    if name not in _NC_CACHE:
        _NC_CACHE[name] = builder()
    return _NC_CACHE[name]


def kernel(**inp):
    import ml_dtypes
    BF = ml_dtypes.bfloat16
    f32 = lambda k: np.asarray(inp[k], np.float32)
    x = f32("L0_attn_x")
    cores = list(range(NCORE))

    n1w = _chunked(inp["L0_attn_norm1_w"])
    bw = _chunked(f32("L0_attn_beta_w")[:, 0])
    bb = np.broadcast_to(f32("L0_attn_beta_b").reshape(1, 1), (128, 1)).copy()
    imA = []
    for c in cores:
        b, qd = c // 4, c % 4
        imA.append({"x": _fm(x[b, qd * TC:(qd + 1) * TC]), "n1w": n1w,
                    "bw": bw, "bb": bb,
                    "wq": f32("L0_attn_Wq"), "wk": f32("L0_attn_Wk"),
                    "wv": f32("L0_attn_Wv")})
    rA = run_bass_kernel_spmd(_nc("A", build_launchA), imA, core_ids=cores).results

    qf = [np.concatenate([rA[b * 4 + qd]["q"] for qd in range(4)], 1) for b in range(B)]
    kf = [np.concatenate([rA[b * 4 + qd]["k"] for qd in range(4)], 1) for b in range(B)]
    vf = [np.concatenate([rA[b * 4 + qd]["v"] for qd in range(4)], 1) for b in range(B)]
    beta = [np.concatenate(
        [np.asarray(rA[b * 4 + qd]["beta_tm"], np.float32).T.reshape(-1)
         for qd in range(4)]) for b in range(B)]

    ident = np.eye(128, dtype=BF)
    nlmask = (-np.tril(np.ones((128, 128), np.float32), -1)).astype(BF)
    umaski = np.triu(np.ones((128, 128), np.float32), 0).astype(BF)
    al = f32("L0_attn_alpha_logit")
    imC = []
    for c in cores:
        b, s = c // 4, c % 4
        imC.append({
            "qt": np.ascontiguousarray(qf[b].reshape(NCH, 128, T // 128, 128)
                                       .transpose(2, 0, 1, 3)),
            "kt": np.ascontiguousarray(kf[b].reshape(NCH, 128, T // 128, 128)
                                       .transpose(2, 0, 1, 3)),
            "ktm": np.ascontiguousarray(kf[b].T.reshape(T // 128, 128, D)),
            "bcol": np.ascontiguousarray(beta[b].reshape(T // 128, 128).T),
            "va": np.ascontiguousarray(al[256 * s:256 * (s + 1)].reshape(2, 128).T),
            "vsl": np.ascontiguousarray(vf[b][256 * s:256 * (s + 1), :]),
            "ident": ident, "nlmask": nlmask, "umaski": umaski})
    rC = run_bass_kernel_spmd(_nc("C", build_launchC), imC, core_ids=cores).results
    attn = [np.concatenate([rC[b * 4 + s]["ao"] for s in range(4)], 0) for b in range(B)]

    n2w = _chunked(inp["L0_attn_norm2_w"])
    imD = []
    for c in cores:
        b, qd = c // 4, c % 4
        imD.append({"x": _fm(x[b, qd * TC:(qd + 1) * TC]),
                    "at": np.ascontiguousarray(attn[b][:, qd * TC:(qd + 1) * TC]),
                    "wo": f32("L0_attn_Wo"), "n2w": n2w,
                    "wg": f32("L0_attn_w_gate"), "wu": f32("L0_attn_w_up"),
                    "wd": f32("L0_attn_w_down")})
    rD = run_bass_kernel_spmd(_nc("D", build_launchD), imD, core_ids=cores).results

    b1 = f32("L0_ffn_b1")
    b2 = f32("L0_ffn_b2")
    b1c = np.ascontiguousarray(b1.reshape(8, 32, 128).transpose(2, 0, 1).reshape(128, 256))
    b2c = np.ascontiguousarray(b2.reshape(8, 8, 128).transpose(2, 0, 1).reshape(128, 64))
    imE = []
    for c in cores:
        imE.append({"z": np.asarray(rD[c]["z"], np.float32),
                    "wrt": np.ascontiguousarray(f32("L0_ffn_Wr").T),
                    "brr": np.ascontiguousarray(np.tile(f32("L0_ffn_br"), (128, 1))),
                    "w1t": np.ascontiguousarray(f32("L0_ffn_W1").transpose(0, 2, 1)),
                    "w2t": np.ascontiguousarray(f32("L0_ffn_W2").transpose(0, 2, 1)),
                    "b1c": b1c, "b2c": b2c, "ident": ident})
    rE = run_bass_kernel_spmd(_nc("E", build_launchE), imE, core_ids=cores).results

    perm = _rope_perm()
    wqt = np.ascontiguousarray(f32("L1_attn_W_Q").T[:, perm])
    wkt = np.ascontiguousarray(f32("L1_attn_W_K").T[:, perm])
    wvt = np.ascontiguousarray(f32("L1_attn_W_V").T)
    tabs = [_cos_sin(qd * TC) for qd in range(4)]
    imF = []
    for c in cores:
        b, qd = c // 4, c % 4
        imF.append({"z2": np.asarray(rE[c]["z2"], np.float32),
                    "wqt": wqt, "wkt": wkt, "wvt": wvt,
                    "cosb": tabs[qd][0], "sinb": tabs[qd][1]})
    rF = run_bass_kernel_spmd(_nc("F", build_launchF), imF, core_ids=cores).results

    khb, vhb = [], []
    for b in range(B):
        k_full = np.concatenate([rF[b * 4 + qd]["k"] for qd in range(4)], 1)
        v_full = np.concatenate([rF[b * 4 + qd]["v"] for qd in range(4)], 1)
        khb.append(np.ascontiguousarray(k_full.reshape(16, 64, T)))
        vhb.append(np.ascontiguousarray(v_full.reshape(16, 64, T).transpose(0, 2, 1)))

    wot = np.ascontiguousarray(f32("L1_attn_W_O").T)
    imG = []
    for c in cores:
        b, qd = c // 4, c % 4
        imG.append({"qr": np.asarray(rF[c]["q"]),
                    "kh": khb[b], "vh": vhb[b],
                    "x3": np.asarray(rF[c]["x3"], np.float32),
                    "wot": wot})
    rG = run_bass_kernel_spmd(_nc("G", build_launchG), imG, core_ids=cores).results

    wut = np.ascontiguousarray(f32("L1_ffn_w_up").T)
    wdt = np.ascontiguousarray(f32("L1_ffn_w_down").T)
    imH = [{"z3": np.asarray(rG[c]["z3"], np.float32), "wut": wut, "wdt": wdt}
           for c in cores]
    rH = run_bass_kernel_spmd(_nc("H", build_launchH), imH, core_ids=cores).results

    out = np.empty((B, T, D), np.float32)
    for c in cores:
        b, qd = c // 4, c % 4
        out[b, qd * TC:(qd + 1) * TC, :] = np.asarray(rH[c]["out"], np.float32).T
    return out


if __name__ == "__main__":
    pass


# version 04
# speedup vs baseline: 1.6823x; 1.0481x over previous
import contextlib
import numpy as np
import concourse.bass as bass
import concourse.mybir as mybir
from concourse.bass_utils import run_bass_kernel_spmd

F32 = mybir.dt.float32
BF16 = mybir.dt.bfloat16
U32 = mybir.dt.uint32
AF = mybir.ActivationFunctionType
ALU = mybir.AluOpType
AX = mybir.AxisListType

B, T, D = 2, 8192, 1024
TC = 2048
NCH = D // 128
NCORE = 8

ENGS = ("sync", "gpsimd", "tensor", "vector", "scalar")


class L:

    def __init__(self, name):
        self.nc = bass.Bass("TRN2", target_bir_lowering=False, name=name)
        self.es = contextlib.ExitStack()
        self.sem = {e: self.es.enter_context(self.nc.semaphore(f"sem_{e}"))
                    for e in ENGS}
        self.dsem = {e: self.es.enter_context(self.nc.semaphore(f"dma_{e}"))
                     for e in ("sync", "gpsimd", "scalar")}
        self._pfpool = [self.es.enter_context(self.nc.semaphore(f"pf{i}"))
                        for i in range(8)]
        self._pfcnt = [0] * 8
        self.pfmap = {}
        self.pftarget = {}
        self.cnt = {e: 0 for e in ENGS}
        self.dcnt = {e: 0 for e in self.dsem}
        self.seen = {e: {**{oe: 0 for oe in ENGS},
                         **{"d" + q: 0 for q in self.dsem}} for e in ENGS}
        self._phases = []

    def pfsem(self, key):
        if key not in self.pfmap:
            self.pfmap[key] = len(self.pfmap) % len(self._pfpool)
        return self.pfmap[key]

    def sbuf(self, name, shape, dt):
        return self.es.enter_context(self.nc.sbuf_tensor(name, shape, dt))

    def psum(self, name, shape, dt=F32):
        return self.es.enter_context(self.nc.psum_tensor(name, shape, dt))

    def inp(self, name, shape, dt=F32):
        return self.nc.dram_tensor(name, shape, dt, kind="ExternalInput").ap()

    def out(self, name, shape, dt=F32):
        return self.nc.dram_tensor(name, shape, dt, kind="ExternalOutput").ap()

    def dram(self, name, shape, dt=F32):
        return self.nc.dram_tensor(name, shape, dt, kind="Internal").ap()

    def dma(self, qname, eng, out, in_, pf=None, **kw):
        i = eng.dma_start(out=out, in_=in_, **kw)
        if pf is not None:
            idx = self.pfsem(pf)
            i.then_inc(self._pfpool[idx], 16)
            self._pfcnt[idx] += 16
            self.pftarget[pf] = self._pfcnt[idx]
            return self.pftarget[pf]
        i.then_inc(self.dsem[qname], 16)
        self.dcnt[qname] += 16
        return self.dcnt[qname]

    def wait_pf(self, eng, key):
        idx = self.pfmap[key]
        eng.wait_ge(self._pfpool[idx], self.pftarget[key])

    def phase(self, work):
        self._phases.append(work)

    def build(self):
        nc = self.nc
        with nc.Block() as blk:
            for work in self._phases:
                want = dict(self.cnt)
                dwant = dict(self.dcnt)
                for e in ENGS:
                    if e not in work:
                        continue
                    fn = work[e]

                    def section(eng, fn=fn, e=e, want=want, dwant=dwant):
                        sn = self.seen[e]
                        for oe in ENGS:
                            if oe != e and want[oe] > sn[oe]:
                                eng.wait_ge(self.sem[oe], want[oe])
                                sn[oe] = want[oe]
                        for oq, vv in dwant.items():
                            if vv > sn["d" + oq]:
                                eng.wait_ge(self.dsem[oq], vv)
                                sn["d" + oq] = vv
                        last = fn(eng, self)
                        if last is not None and hasattr(last, "then_inc"):
                            last.then_inc(self.sem[e], 1)
                        elif e in ("tensor", "sync"):
                            eng.nop().then_inc(self.sem[e], 1)
                        else:
                            eng.drain().then_inc(self.sem[e], 1)
                        self.cnt[e] += 1

                    getattr(blk, e)(section)
        return nc

    def close(self):
        self.es.close()



def colsum(t, ones_bf, src, ps_row, n, start, stop=False):
    i = None
    for j in range(0, n, 512):
        i = t.matmul(ps_row[0:1, j:j + 512], ones_bf[:, 0:1], src[:, j:j + 512],
                     start=start, stop=stop)
    return i


def bcast(t, ones_row_f32, row, ps_bc, n):
    i = None
    for j in range(0, n, 512):
        w = min(512, n - j)
        i = t.matmul(ps_bc[:, j:j + w], ones_row_f32[0:1, :], row[0:1, j:j + w],
                     start=True, stop=True)
    return i



def build_launchA(max_phases=None):
    l = L("launchA")
    x = l.inp("x", [D, TC], F32)
    n1w = l.inp("n1w", [128, NCH], F32)
    wq = l.inp("wq", [D, D], F32)
    wk = l.inp("wk", [D, D], F32)
    wv = l.inp("wv", [D, D], F32)
    bw = l.inp("bw", [128, NCH], F32)
    bb = l.inp("bb", [128, 1], F32)
    q_o = l.out("q", [D, TC], BF16)
    k_o = l.out("k", [D, TC], BF16)
    v_o = l.out("v", [D, TC], BF16)
    beta_o = l.out("beta_tm", [128, TC // 128], F32)

    h_sb = l.sbuf("h", [128, NCH * TC], BF16)
    xc = l.sbuf("xc", [128, TC], F32)
    sq = l.sbuf("sq", [128, TC], BF16)
    w_sb = [l.sbuf(f"w{i}", [128, NCH * D], BF16) for i in range(2)]
    out_sb = l.sbuf("outb", [128, NCH * TC], BF16)
    n1w_sb = l.sbuf("n1w_sb", [128, NCH], F32)
    bw_sb = l.sbuf("bw_sb", [128, NCH], BF16)
    bb_sb = l.sbuf("bb_sb", [128, 1], F32)
    ones_bf = l.sbuf("ones_bf", [128, 1], BF16)
    ones_row = l.sbuf("ones_row", [1, 128], F32)
    row_a = l.sbuf("row_a", [1, TC], F32)
    row_b = l.sbuf("row_b", [1, TC], F32)
    beta_sb = l.sbuf("beta_sb", [128, TC // 128], F32)
    cst = l.sbuf("cst", [128, 4], F32)

    u1 = l.psum("u1", [128, TC], F32)
    bps = l.psum("bps", [128, TC // 128], F32)
    pp = [l.psum(f"pp{i}", [128, 512], F32) for i in range(2)]

    wv_views = {
        "q": wq.rearrange("(c p) o -> p c o", p=128),
        "k": wk.rearrange("(c p) o -> p c o", p=128),
        "v": wv.rearrange("(c p) o -> p c o", p=128),
    }
    w_sb3 = [w_sb[i].ap().rearrange("p (c o) -> p c o", c=NCH) for i in range(2)]

    def p0_sync(e, l):
        l.dma("sync", e, n1w_sb[:], n1w)
        l.dma("sync", e, bb_sb[:], bb)

    def p0_gp(e, l):
        l.dma("gpsimd", e, w_sb3[0], wv_views["q"], pf="q")
        l.dma("gpsimd", e, bw_sb[:], bw)

    def p0_vec(e, l):
        e.memset(ones_bf[:], 1.0)
        e.memset(ones_row[:], 1.0)
        e.memset(cst[:, 0:1], 1e-6)
        e.memset(cst[:, 1:2], 1e-5)
        e.memset(cst[:, 2:3], 1e-8)

    l.phase({"sync": p0_sync, "gpsimd": p0_gp, "vector": p0_vec})

    for ci in range(NCH):
        l.phase({"sync": (lambda e, l, ci=ci:
                          l.dma("sync", e, xc[:], x[ci * 128:(ci + 1) * 128, :]))})
        l.phase({"scalar": (lambda e, l:
                            e.activation(out=sq[:], in_=xc[:], func=AF.Square))})
        l.phase({"tensor": (lambda e, l, ci=ci:
                            colsum(e, ones_bf, sq, u1, TC, start=(ci == 0),
                                   stop=(ci == NCH - 1)))})

    l.phase({"scalar": (lambda e, l:
                        e.activation(out=row_a[0:1, :], in_=u1[0:1, :],
                                     func=AF.Sqrt, scale=1.0 / D, bias=cst[0:1, 0:1]))})
    l.phase({"vector": (lambda e, l: e.reciprocal(row_b[0:1, :], row_a[0:1, :]))})
    l.phase({"tensor": (lambda e, l: bcast(e, ones_row, row_b, u1, TC))})

    for ci in range(NCH):
        l.phase({"sync": (lambda e, l, ci=ci:
                          l.dma("sync", e, xc[:], x[ci * 128:(ci + 1) * 128, :]))})
        l.phase({"vector": (lambda e, l, ci=ci:
                            e.scalar_tensor_tensor(
                                out=h_sb[:, ci * TC:(ci + 1) * TC], in0=xc[:],
                                scalar=n1w_sb[:, ci:ci + 1], in1=u1[:],
                                op0=ALU.mult, op1=ALU.mult))})

    def p_beta(e, l):
        i = None
        for tt in range(TC // 128):
            for ci in range(NCH):
                i = e.matmul(bps[:, tt:tt + 1],
                             h_sb[:, ci * TC + tt * 128: ci * TC + (tt + 1) * 128],
                             bw_sb[:, ci:ci + 1],
                             start=(tt == 0 and ci == 0),
                             stop=(tt == TC // 128 - 1 and ci == NCH - 1))
        return i
    l.phase({"tensor": p_beta})
    l.phase({"scalar": (lambda e, l:
                        e.activation(out=beta_sb[:], in_=bps[:], func=AF.Sigmoid,
                                     bias=bb_sb[:, 0:1]))})
    l.phase({"gpsimd": (lambda e, l: l.dma("gpsimd", e, beta_o, beta_sb[:]))})

    def proj(wbuf, pf_key, next_key):
        for do in range(NCH):
            for nt in range(TC // 512):
                ps = pp[(do * (TC // 512) + nt) % 2]

                def p_mm(e, l, do=do, nt=nt, ps=ps, first=(do == 0 and nt == 0)):
                    if first:
                        l.wait_pf(e, pf_key)
                    i = None
                    for ci in range(NCH):
                        i = e.matmul(ps[:],
                                     w_sb[wbuf][:, ci * D + do * 128:ci * D + (do + 1) * 128],
                                     h_sb[:, ci * TC + nt * 512:ci * TC + (nt + 1) * 512],
                                     start=(ci == 0), stop=(ci == NCH - 1))
                    return i
                work = {"tensor": p_mm}
                if do == 0 and nt == 0 and next_key is not None:
                    def p_pf(e, l, nk=next_key, wb=1 - wbuf):
                        l.dma("gpsimd", e, w_sb3[wb], wv_views[nk], pf=nk)
                    work["gpsimd"] = p_pf
                l.phase(work)
                l.phase({"scalar": (lambda e, l, do=do, nt=nt, ps=ps:
                                    e.copy(out=out_sb[:, do * TC + nt * 512:
                                                      do * TC + (nt + 1) * 512],
                                           in_=ps[:]))})

    def store(out_ap):
        def p_st(e, l):
            for ci in range(NCH):
                l.dma("gpsimd", e, out_ap[ci * 128:(ci + 1) * 128, :],
                      out_sb[:, ci * TC:(ci + 1) * TC])
        l.phase({"gpsimd": p_st})

    proj(0, "q", "k")
    store(q_o)

    proj(1, "k", "v")
    for ci in range(NCH):
        l.phase({"scalar": (lambda e, l, ci=ci:
                            e.activation(out=sq[:], in_=out_sb[:, ci * TC:(ci + 1) * TC],
                                         func=AF.Square))})
        l.phase({"tensor": (lambda e, l, ci=ci:
                            colsum(e, ones_bf, sq, u1, TC, start=(ci == 0),
                                   stop=(ci == NCH - 1)))})
    l.phase({"scalar": (lambda e, l:
                        e.activation(out=row_a[0:1, :], in_=u1[0:1, :], func=AF.Sqrt))})

    def p_knv(e, l):
        e.tensor_scalar_max(row_a[0:1, :], row_a[0:1, :], 1e-12)
        e.drain()
        e.reciprocal(row_b[0:1, :], row_a[0:1, :])
    l.phase({"vector": p_knv})
    l.phase({"tensor": (lambda e, l: bcast(e, ones_row, row_b, u1, TC))})

    def p_kmul(e, l):
        for ci in range(NCH):
            e.tensor_tensor(out=out_sb[:, ci * TC:(ci + 1) * TC],
                            in0=out_sb[:, ci * TC:(ci + 1) * TC],
                            in1=u1[:], op=ALU.mult)
    l.phase({"vector": p_kmul})
    store(k_o)

    proj(0, "v", None)
    store(v_o)

    def p_fin(e, l):
        e.wait_ge(l.dsem["gpsimd"], l.dcnt["gpsimd"])
    l.phase({"gpsimd": p_fin})

    if max_phases is not None:
        l._phases = l._phases[:max_phases]
    nc = l.build()
    l.close()
    return nc



def build_launchC(max_phases=None):
    l = L("launchC")
    NC64 = T // 128
    qt = l.inp("qt", [NC64, NCH, 128, 128], BF16)
    kt = l.inp("kt", [NC64, NCH, 128, 128], BF16)
    ktm = l.inp("ktm", [NC64, 128, D], BF16)
    bcol = l.inp("bcol", [128, NC64], F32)
    va = l.inp("va", [128, 2], F32)
    vsl = l.inp("vsl", [256, T], BF16)
    ident = l.inp("ident", [128, 128], BF16)
    nlmask = l.inp("nlmask", [128, 128], BF16)
    umaski = l.inp("umaski", [128, 128], BF16)
    ao = l.out("ao", [256, T], BF16)

    vts = l.sbuf("vts", [128, 2 * T], BF16)
    vtm = l.sbuf("vtm", [128, 2 * T], BF16)
    wT = l.sbuf("wT", [128, NC64 * 128], BF16)
    hS = l.sbuf("hS", [128, NC64 * 128], BF16)
    st_sb = l.sbuf("st_sb", [128, NCH * 256], BF16)
    kc = l.sbuf("kc", [128, D], BF16)
    qc = l.sbuf("qc", [128, D], BF16)
    kmc = l.sbuf("kmc", [128, D], BF16)
    rhs_sb = l.sbuf("rhs_sb", [128, 256], BF16)
    u_sb = l.sbuf("u_sb", [128, 256], BF16)
    d0 = l.sbuf("d0", [128, T], BF16)
    d1 = l.sbuf("d1", [128, T], BF16)
    sS = l.sbuf("sS", [128, 128], BF16)
    sT = l.sbuf("sT", [128, 128], BF16)
    sX = l.sbuf("sX", [128, 128], BF16)
    sY = l.sbuf("sY", [128, 128], BF16)
    bc_sb = l.sbuf("bc_sb", [128, NC64], F32)
    va_sb = l.sbuf("va_sb", [128, 2], F32)
    alt = l.sbuf("alt", [128, 2], F32)
    oma = l.sbuf("oma", [128, 2], F32)
    id_sb = l.sbuf("id_sb", [128, 128], BF16)
    nl_sb = l.sbuf("nl_sb", [128, 128], BF16)
    ui_sb = l.sbuf("ui_sb", [128, 128], BF16)

    st_ps = [l.psum(f"st{i}", [128, 512], F32) for i in range(4)]
    ps_b = l.psum("ps_b", [128, 512], F32)
    ps_c = l.psum("ps_c", [128, 512], F32)
    ps_t = l.psum("ps_t", [128, 512], BF16)

    ktv = kt.rearrange("c dc p t -> c p dc t")
    qtv = qt.rearrange("c dc p t -> c p dc t")
    kc3 = kc.ap().rearrange("p (dc t) -> p dc t", dc=NCH)
    qc3 = qc.ap().rearrange("p (dc t) -> p dc t", dc=NCH)
    vslv = vsl.rearrange("(j p) t -> j p t", p=128)
    aov = ao.rearrange("(j p) t -> j p t", p=128)

    def p0(e, l):
        l.dma("sync", e, vts[:, 0:T], vslv[0])
        l.dma("sync", e, vts[:, T:2 * T], vslv[1])
        l.dma("sync", e, bc_sb[:], bcol)
        l.dma("sync", e, va_sb[:], va)
        l.dma("sync", e, id_sb[:], ident)
        l.dma("sync", e, nl_sb[:], nlmask)
        l.dma("sync", e, ui_sb[:], umaski)
    l.phase({"sync": p0})

    l.phase({"scalar": (lambda e, l: e.activation(out=alt[:], in_=va_sb[:],
                                                  func=AF.Sigmoid))})
    def p_ema_v(e, l):
        e.tensor_scalar(out=oma[:], in0=alt[:], scalar1=-1.0, scalar2=1.0,
                        op0=ALU.mult, op1=ALU.add)
        e.drain()
        for j in range(2):
            e.tensor_scalar(out=d1[:], in0=vts[:, j * T:(j + 1) * T],
                            scalar1=alt[:, j:j + 1], scalar2=None, op0=ALU.mult)
            e.memset(d0[:], 1.0)
            e.drain()
            e.tensor_scalar(out=d0[:], in0=d0[:], scalar1=oma[:, j:j + 1],
                            scalar2=None, op0=ALU.mult)
            e.drain()
            e.tensor_tensor_scan(out=vts[:, j * T:(j + 1) * T], data0=d0[:],
                                 data1=d1[:], initial=0.0,
                                 op0=ALU.mult, op1=ALU.add)
            e.drain()
    l.phase({"vector": p_ema_v})

    for g in range(NC64 * 2 // 4):
        def p_tr(e, l, g=g):
            i = None
            for r in range(4):
                idx = g * 4 + r
                j, c = idx % 2, idx // 2
                i = e.transpose(ps_t[:, r * 128:(r + 1) * 128],
                                vts[:, j * T + c * 128: j * T + (c + 1) * 128],
                                id_sb[:])
            return i
        l.phase({"tensor": p_tr})

        def p_tre(e, l, g=g):
            for r in range(4):
                idx = g * 4 + r
                j, c = idx % 2, idx // 2
                e.copy(out=vtm[:, c * 256 + j * 128: c * 256 + (j + 1) * 128],
                       in_=ps_t[:, r * 128:(r + 1) * 128])
        l.phase({"scalar": p_tre})

    kc4 = d0.ap()
    qc4 = d1.ap()
    kcv = lambda cc: d0.ap()[:, cc * 1024:(cc + 1) * 1024]
    qcv = lambda cc: d0.ap()[:, 4096 + cc * 1024:4096 + (cc + 1) * 1024]
    sS4 = d1.ap()[:, 0:512]
    sT4 = d1.ap()[:, 512:1024]
    sX4 = d1.ap()[:, 1024:1536]
    sY4 = d1.ap()[:, 1536:2048]

    kc3g = [d0.ap()[:, cc * 1024:(cc + 1) * 1024]
            .rearrange("p (dc t) -> p dc t", dc=NCH) for cc in range(4)]
    qc3g = [d0.ap()[:, 4096 + cc * 1024:4096 + (cc + 1) * 1024]
            .rearrange("p (dc t) -> p dc t", dc=NCH) for cc in range(4)]

    for grp in range(NC64 // 4):
        def p_ld(e, l, grp=grp):
            for cc in range(4):
                l.dma("sync", e, kc3g[cc], ktv[grp * 4 + cc])
                l.dma("sync", e, qc3g[cc], qtv[grp * 4 + cc])
        l.phase({"sync": p_ld})

        def p_kk(e, l, grp=grp):
            i = None
            for cc in range(4):
                for dc in range(NCH):
                    e.matmul(ps_b[:, cc * 128:(cc + 1) * 128],
                             kcv(cc)[:, dc * 128:(dc + 1) * 128],
                             kcv(cc)[:, dc * 128:(dc + 1) * 128],
                             start=(dc == 0), stop=(dc == NCH - 1))
            for cc in range(4):
                for dc in range(NCH):
                    i = e.matmul(ps_c[:, cc * 128:(cc + 1) * 128],
                                 kcv(cc)[:, dc * 128:(dc + 1) * 128],
                                 qcv(cc)[:, dc * 128:(dc + 1) * 128],
                                 start=(dc == 0), stop=(dc == NCH - 1))
            return i
        l.phase({"tensor": p_kk})

        def p_xh(e, l, grp=grp):
            for cc in range(4):
                c = grp * 4 + cc
                e.scalar_tensor_tensor(out=sX4[:, cc * 128:(cc + 1) * 128],
                                       in0=ps_b[:, cc * 128:(cc + 1) * 128],
                                       scalar=bc_sb[:, c:c + 1], in1=nl_sb[:],
                                       op0=ALU.mult, op1=ALU.mult)
                e.drain()
                e.scalar_tensor_tensor(out=hS[:, c * 128:(c + 1) * 128],
                                       in0=ps_c[:, cc * 128:(cc + 1) * 128],
                                       scalar=1.0, in1=ui_sb[:],
                                       op0=ALU.mult, op1=ALU.mult)
                e.drain()
        l.phase({"vector": p_xh})

        def p_yt(e, l):
            i = None
            for cc in range(4):
                i = e.transpose(ps_t[:, cc * 128:(cc + 1) * 128],
                                sX4[:, cc * 128:(cc + 1) * 128], id_sb[:])
            return i
        l.phase({"tensor": p_yt})

        def p_ye(e, l):
            for cc in range(4):
                e.copy(out=sY4[:, cc * 128:(cc + 1) * 128],
                       in_=ps_t[:, cc * 128:(cc + 1) * 128])
        l.phase({"scalar": p_ye})

        def p_i1(e, l):
            i = None
            for cc in range(4):
                e.matmul(ps_c[:, cc * 128:(cc + 1) * 128],
                         sY4[:, cc * 128:(cc + 1) * 128],
                         sX4[:, cc * 128:(cc + 1) * 128], start=True, stop=True)
                i = e.matmul(ps_b[:, cc * 128:(cc + 1) * 128],
                             sX4[:, cc * 128:(cc + 1) * 128],
                             sY4[:, cc * 128:(cc + 1) * 128], start=True, stop=True)
            return i
        l.phase({"tensor": p_i1})

        def p_i2(e, l):
            for cc in range(4):
                e.tensor_tensor(out=sS4[:, cc * 128:(cc + 1) * 128],
                                in0=sX4[:, cc * 128:(cc + 1) * 128],
                                in1=id_sb[:], op=ALU.add)
                e.drain()
                e.tensor_tensor(out=sT4[:, cc * 128:(cc + 1) * 128],
                                in0=sY4[:, cc * 128:(cc + 1) * 128],
                                in1=id_sb[:], op=ALU.add)
                e.drain()
        l.phase({"vector": p_i2})

        def p_i3(e, l):
            for cc in range(4):
                e.copy(out=sX4[:, cc * 128:(cc + 1) * 128],
                       in_=ps_c[:, cc * 128:(cc + 1) * 128])
                e.drain()
                e.copy(out=sY4[:, cc * 128:(cc + 1) * 128],
                       in_=ps_b[:, cc * 128:(cc + 1) * 128])
                e.drain()
        l.phase({"scalar": p_i3})

        for it in range(5):
            def p_mm4(e, l):
                i = None
                for cc in range(4):
                    sl = slice(cc * 128, (cc + 1) * 128)
                    e.matmul(ps_b[:, sl], sY4[:, sl], sS4[:, sl], start=True, stop=True)
                    e.matmul(ps_c[:, sl], sS4[:, sl], sY4[:, sl], start=True, stop=True)
                    e.matmul(st_ps[0][:, sl], sY4[:, sl], sX4[:, sl], start=True, stop=True)
                    i = e.matmul(st_ps[1][:, sl], sX4[:, sl], sY4[:, sl],
                                 start=True, stop=True)
                return i
            l.phase({"tensor": p_mm4})

            def p_ev4(e, l):
                for cc in range(4):
                    sl = slice(cc * 128, (cc + 1) * 128)
                    e.tensor_tensor(out=sS4[:, sl], in0=sS4[:, sl], in1=ps_b[:, sl],
                                    op=ALU.add)
                    e.drain()
                    e.tensor_tensor(out=sT4[:, sl], in0=sT4[:, sl], in1=ps_c[:, sl],
                                    op=ALU.add)
                    e.drain()
            l.phase({"vector": p_ev4})

            def p_ev4b(e, l):
                for cc in range(4):
                    sl = slice(cc * 128, (cc + 1) * 128)
                    e.copy(out=sX4[:, sl], in_=st_ps[0][:, sl])
                    e.drain()
                    e.copy(out=sY4[:, sl], in_=st_ps[1][:, sl])
                    e.drain()
            l.phase({"scalar": p_ev4b})

        def p_f1(e, l):
            i = None
            for cc in range(4):
                sl = slice(cc * 128, (cc + 1) * 128)
                i = e.matmul(ps_b[:, sl], sS4[:, sl], sY4[:, sl], start=True, stop=True)
            return i
        l.phase({"tensor": p_f1})

        def p_f2(e, l, grp=grp):
            for cc in range(4):
                c = grp * 4 + cc
                sl = slice(cc * 128, (cc + 1) * 128)
                e.tensor_tensor(out=wT[:, c * 128:(c + 1) * 128],
                                in0=sT4[:, sl], in1=ps_b[:, sl], op=ALU.add)
                e.drain()
        l.phase({"vector": p_f2})

    def p_z(e, l):
        e.memset(st_sb[:], 0.0)
    l.phase({"vector": p_z})

    for c in range(NC64):
        def p_ld(e, l, c=c):
            l.dma("sync", e, kc3, ktv[c])
            l.dma("sync", e, qc3, qtv[c])
            l.dma("sync", e, kmc[:], ktm[c])
        l.phase({"sync": p_ld})

        def p_ks(e, l, c=c):
            i = None
            for dc in range(NCH):
                i = e.matmul(ps_b[:, 0:256], kc[:, dc * 128:(dc + 1) * 128],
                             st_sb[:, dc * 256:(dc + 1) * 256],
                             start=(dc == 0), stop=(dc == NCH - 1))
            return i
        l.phase({"tensor": p_ks})

        def p_rhs(e, l, c=c):
            e.scalar_tensor_tensor(out=rhs_sb[:], in0=ps_b[:, 0:256],
                                   scalar=-1.0, in1=vtm[:, c * 256:(c + 1) * 256],
                                   op0=ALU.mult, op1=ALU.add)
            e.drain()
            e.tensor_scalar(out=rhs_sb[:], in0=rhs_sb[:],
                            scalar1=bc_sb[:, c:c + 1], scalar2=None, op0=ALU.mult)
        l.phase({"vector": p_rhs})

        l.phase({"tensor": (lambda e, l, c=c:
                            e.matmul(ps_b[:, 256:512], wT[:, c * 128:(c + 1) * 128],
                                     rhs_sb[:], start=True, stop=True))})
        l.phase({"scalar": (lambda e, l: e.copy(out=u_sb[:], in_=ps_b[:, 256:512]))})

        def p_ost(e, l, c=c):
            for ic in range(2):
                for dc in range(NCH):
                    e.matmul(ps_c[:, ic * 128:(ic + 1) * 128],
                             st_sb[:, dc * 256 + ic * 128: dc * 256 + (ic + 1) * 128],
                             qc[:, dc * 128:(dc + 1) * 128],
                             start=(dc == 0), stop=False)
                e.matmul(ps_c[:, ic * 128:(ic + 1) * 128],
                         u_sb[:, ic * 128:(ic + 1) * 128],
                         hS[:, c * 128:(c + 1) * 128], start=False, stop=True)
            i = None
            for j in range(NCH):
                i = e.matmul(st_ps[j // 2][:, (j % 2) * 256:(j % 2) * 256 + 256],
                             kmc[:, j * 128:(j + 1) * 128], u_sb[:],
                             start=(c == 0 and j % 2 == 0), stop=(j % 2 == 1),
                             skip_group_check=True)
            return i
        l.phase({"tensor": p_ost})

        def p_ev(e, l, c=c):
            for ic in range(2):
                e.copy(out=vts[:, ic * T + c * 128: ic * T + (c + 1) * 128],
                       in_=ps_c[:, ic * 128:(ic + 1) * 128])
            e.drain()
            for j in range(NCH):
                e.copy(out=st_sb[:, j * 256:(j + 1) * 256],
                       in_=st_ps[j // 2][:, (j % 2) * 256:(j % 2) * 256 + 256])
        l.phase({"scalar": p_ev})

    def p_st(e, l):
        for j in range(2):
            l.dma("gpsimd", e, aov[j], vts[:, j * T:(j + 1) * T])
        e.wait_ge(l.dsem["gpsimd"], l.dcnt["gpsimd"])
    l.phase({"gpsimd": p_st})

    if max_phases is not None:
        l._phases = l._phases[:max_phases]
    nc = l.build()
    l.close()
    return nc



def emit_ln_start(l, z_in, xb_sb, zc, sq, u1, ones_bf, ones_row, rows, cst_eps,
                  after_chunk=None):
    r0, r1 = rows
    def p_zl(e, l):
        for ci in range(NCH):
            l.dma("gpsimd", e, xb_sb[:, ci * TC:(ci + 1) * TC],
                  z_in[ci * 128:(ci + 1) * 128, :])
    l.phase({"gpsimd": p_zl})
    for ci in range(NCH):
        l.phase({"tensor": (lambda e, l, ci=ci:
                            colsum(e, ones_bf, xb_sb[:, ci * TC:(ci + 1) * TC],
                                   u1, TC, start=(ci == 0), stop=(ci == NCH - 1)))})
    l.phase({"scalar": (lambda e, l:
                        e.activation(out=r0[0:1, :], in_=u1[0:1, :],
                                     func=AF.Identity, scale=1.0 / D))})
    for ci in range(NCH):
        for hf in range(4):
            l.phase({"scalar": (lambda e, l, ci=ci, hf=hf:
                                e.activation(out=sq[:, 0:512],
                                             in_=xb_sb[:, ci * TC + hf * 512:
                                                       ci * TC + (hf + 1) * 512],
                                             func=AF.Square))})

            def p_cs(e, l, ci=ci, hf=hf):
                return e.matmul(u1[0:1, hf * 512:(hf + 1) * 512],
                                ones_bf[:, 0:1], sq[:, 0:512],
                                start=(ci == 0), stop=(ci == NCH - 1))
            l.phase({"tensor": p_cs})
    def p_rows_s(e, l):
        e.activation(out=sq[0:1, 0:TC], in_=r0[0:1, :], func=AF.Square)
    def p_rows_v(e, l):
        e.tensor_tensor(out=r1[0:1, :], in0=r0[0:1, :], in1=r0[0:1, :], op=ALU.mult)
    l.phase({"vector": p_rows_v})

    def p_rows_s2(e, l):
        pass
    def p_var_v(e, l):
        e.scalar_tensor_tensor(out=r1[0:1, :], in0=u1[0:1, :], scalar=1.0 / D,
                               in1=r1[0:1, :], op0=ALU.mult, op1=ALU.subtract)
    l.phase({"vector": p_var_v})
    l.phase({"scalar": (lambda e, l:
                        e.activation(out=r1[0:1, :], in_=r1[0:1, :], func=AF.Sqrt,
                                     bias=cst_eps))})
    l.phase({"vector": (lambda e, l: e.reciprocal(r1[0:1, :], r1[0:1, :]))})
    l.phase({"tensor": (lambda e, l: bcast(e, ones_row, r0, u1, TC))})
    for ci in range(NCH):
        l.phase({"vector": (lambda e, l, ci=ci:
                            e.scalar_tensor_tensor(
                                out=xb_sb[:, ci * TC:(ci + 1) * TC],
                                in0=xb_sb[:, ci * TC:(ci + 1) * TC], scalar=1.0,
                                in1=u1[:], op0=ALU.mult, op1=ALU.subtract))})
    l.phase({"tensor": (lambda e, l: bcast(e, ones_row, r1, u1, TC))})
    for ci in range(NCH):
        l.phase({"vector": (lambda e, l, ci=ci:
                            e.tensor_tensor(out=xb_sb[:, ci * TC:(ci + 1) * TC],
                                            in0=xb_sb[:, ci * TC:(ci + 1) * TC],
                                            in1=u1[:], op=ALU.mult))})
        if after_chunk is not None:
            w = after_chunk(ci)
            if w:
                l.phase(w)


def emit_proj(l, pf_key, next_pf, w_sb_cur, w_sb3_next, wview_next, h_sb, out_sb, pp):
    for do in range(NCH):
        for nt in range(TC // 512):
            ps = pp[(do * (TC // 512) + nt) % 2]

            def p_mm(e, l, do=do, nt=nt, ps=ps, first=(do == 0 and nt == 0)):
                if first and pf_key is not None:
                    l.wait_pf(e, pf_key)
                i = None
                for ci in range(NCH):
                    i = e.matmul(ps[:],
                                 w_sb_cur[:, ci * D + do * 128:ci * D + (do + 1) * 128],
                                 h_sb[:, ci * TC + nt * 512:ci * TC + (nt + 1) * 512],
                                 start=(ci == 0), stop=(ci == NCH - 1))
                return i
            work = {"tensor": p_mm}
            if do == 0 and nt == 0 and next_pf is not None:
                def p_pf(e, l, nk=next_pf):
                    l.dma("gpsimd", e, w_sb3_next, wview_next, pf=nk)
                work["gpsimd"] = p_pf
            l.phase(work)
            l.phase({"scalar": (lambda e, l, do=do, nt=nt, ps=ps:
                                e.copy(out=out_sb[:, do * TC + nt * 512:
                                                  do * TC + (nt + 1) * 512],
                                       in_=ps[:]))})



def build_launchD(max_phases=None):
    l = L("launchD")
    x = l.inp("x", [D, TC], F32)
    at = l.inp("at", [D, TC], BF16)
    wo = l.inp("wo", [D, D], F32)
    n2w = l.inp("n2w", [128, NCH], F32)
    wg = l.inp("wg", [D, 2048], F32)
    wu = l.inp("wu", [D, 2048], F32)
    wd = l.inp("wd", [2048, D], F32)
    z_o = l.out("z", [D, TC], F32)

    h2_sb = l.sbuf("h2", [128, NCH * TC], BF16)
    x1_sb = l.sbuf("x1", [128, NCH * TC], BF16)
    gu_sb = l.sbuf("gu", [128, 8192], BF16)
    wo_sb = l.sbuf("wo_sb", [128, NCH * D], BF16)
    wg_sb = l.sbuf("wg_sb", [128, NCH * 2048], BF16)
    wu_sb = l.sbuf("wu_sb", [128, NCH * 2048], BF16)
    wd_sb = l.sbuf("wd_sb", [128, 16 * D], BF16)
    n2w_sb = l.sbuf("n2w_sb", [128, NCH], F32)
    xc = l.sbuf("xc", [128, 512], F32)
    zt = l.sbuf("zt", [128, 256], F32)
    sq = l.sbuf("sq", [128, TC], BF16)
    ones_bf = l.sbuf("ones_bf", [128, 1], BF16)
    ones_row = l.sbuf("ones_row", [1, 128], F32)
    r0 = l.sbuf("r0", [1, TC], F32)
    cst = l.sbuf("cst", [128, 1], F32)

    u1 = l.psum("u1", [128, TC], F32)
    pp = [l.psum(f"pp{i}", [128, 512], F32) for i in range(2)]

    at_v = at.rearrange("(c p) t -> p c t", p=128)
    h2_3 = h2_sb.ap().rearrange("p (c t) -> p c t", c=NCH)
    wo_v = wo.rearrange("(c p) o -> p c o", p=128)
    wo_3 = wo_sb.ap().rearrange("p (c o) -> p c o", c=NCH)
    wg_v = wg.rearrange("(c p) o -> p c o", p=128)
    wg_3 = wg_sb.ap().rearrange("p (c o) -> p c o", c=NCH)
    wu_v = wu.rearrange("(c p) o -> p c o", p=128)
    wu_3 = wu_sb.ap().rearrange("p (c o) -> p c o", c=NCH)
    wd_v = wd.rearrange("(c p) o -> p c o", p=128)
    wd_3 = wd_sb.ap().rearrange("p (c o) -> p c o", c=16)

    def p0(e, l):
        l.dma("sync", e, h2_3, at_v)
        l.dma("sync", e, n2w_sb[:], n2w)
    def p0g(e, l):
        l.dma("gpsimd", e, wo_3, wo_v, pf="wo")
        l.dma("gpsimd", e, wg_3, wg_v, pf="wg")
        l.dma("gpsimd", e, wu_3, wu_v, pf="wu")
        l.dma("gpsimd", e, wd_3, wd_v, pf="wd")
    def p0v(e, l):
        e.memset(ones_bf[:], 1.0)
        e.memset(ones_row[:], 1.0)
        e.memset(cst[:], 1e-6)
    l.phase({"sync": p0, "gpsimd": p0g, "vector": p0v})

    for do in range(NCH):
        for nt in range(TC // 512):
            ps = pp[(do * 4 + nt) % 2]

            def p_mm(e, l, do=do, nt=nt, ps=ps, first=(do == 0 and nt == 0)):
                if first:
                    l.wait_pf(e, "wo")
                i = None
                for ci in range(NCH):
                    i = e.matmul(ps[:],
                                 wo_sb[:, ci * D + do * 128:ci * D + (do + 1) * 128],
                                 h2_sb[:, ci * TC + nt * 512:ci * TC + (nt + 1) * 512],
                                 start=(ci == 0), stop=(ci == NCH - 1))
                return i

            def p_xc(e, l, do=do, nt=nt):
                l.dma("sync", e, xc[:], x[do * 128:(do + 1) * 128, nt * 512:(nt + 1) * 512])
            l.phase({"tensor": p_mm, "sync": p_xc})
            l.phase({"vector": (lambda e, l, do=do, nt=nt, ps=ps:
                                e.scalar_tensor_tensor(
                                    out=x1_sb[:, do * TC + nt * 512:do * TC + (nt + 1) * 512],
                                    in0=ps[:], scalar=1.0, in1=xc[:],
                                    op0=ALU.mult, op1=ALU.add))})

    for ci in range(NCH):
        l.phase({"scalar": (lambda e, l, ci=ci:
                            e.activation(out=sq[:], in_=x1_sb[:, ci * TC:(ci + 1) * TC],
                                         func=AF.Square))})
        l.phase({"tensor": (lambda e, l, ci=ci:
                            colsum(e, ones_bf, sq, u1, TC, start=(ci == 0),
                                   stop=(ci == NCH - 1)))})
    l.phase({"scalar": (lambda e, l:
                        e.activation(out=r0[0:1, :], in_=u1[0:1, :], func=AF.Sqrt,
                                     scale=1.0 / D, bias=cst[0:1, 0:1]))})
    l.phase({"vector": (lambda e, l: e.reciprocal(r0[0:1, :], r0[0:1, :]))})
    l.phase({"tensor": (lambda e, l: bcast(e, ones_row, r0, u1, TC))})
    for ci in range(NCH):
        l.phase({"vector": (lambda e, l, ci=ci:
                            e.scalar_tensor_tensor(
                                out=h2_sb[:, ci * TC:(ci + 1) * TC],
                                in0=x1_sb[:, ci * TC:(ci + 1) * TC],
                                scalar=n2w_sb[:, ci:ci + 1], in1=u1[:],
                                op0=ALU.mult, op1=ALU.mult))})

    NS = 256
    for s in range(TC // NS):
        for fc in range(16):
            def p_mm(e, l, s=s, fc=fc, first=(s == 0 and fc == 0)):
                if first:
                    l.wait_pf(e, "wg")
                    l.wait_pf(e, "wu")
                for ci in range(NCH):
                    e.matmul(pp[0][:, 0:NS],
                             wg_sb[:, ci * 2048 + fc * 128:ci * 2048 + (fc + 1) * 128],
                             h2_sb[:, ci * TC + s * NS:ci * TC + (s + 1) * NS],
                             start=(ci == 0), stop=(ci == NCH - 1))
                i = None
                for ci in range(NCH):
                    i = e.matmul(pp[1][:, 0:NS],
                                 wu_sb[:, ci * 2048 + fc * 128:ci * 2048 + (fc + 1) * 128],
                                 h2_sb[:, ci * TC + s * NS:ci * TC + (s + 1) * NS],
                                 start=(ci == 0), stop=(ci == NCH - 1))
                return i
            l.phase({"tensor": p_mm})

            def p_ev(e, l, fc=fc):
                e.activation(out=gu_sb[:, fc * NS:(fc + 1) * NS], in_=pp[0][:, 0:NS],
                             func=AF.Silu)
            def p_ev2(e, l, fc=fc):
                e.tensor_copy(gu_sb[:, 4096 + fc * NS:4096 + (fc + 1) * NS],
                              pp[1][:, 0:NS])
            l.phase({"scalar": p_ev, "vector": p_ev2})

        l.phase({"vector": (lambda e, l:
                            e.tensor_tensor(out=gu_sb[:, 0:4096], in0=gu_sb[:, 0:4096],
                                            in1=gu_sb[:, 4096:8192], op=ALU.mult))})
        for do in range(NCH):
            def p_mm(e, l, s=s, do=do, first=(s == 0 and do == 0)):
                if first:
                    l.wait_pf(e, "wd")
                i = None
                for fc in range(16):
                    i = e.matmul(pp[0][:, 0:NS],
                                 wd_sb[:, fc * D + do * 128:fc * D + (do + 1) * 128],
                                 gu_sb[:, fc * NS:(fc + 1) * NS],
                                 start=(fc == 0), stop=(fc == 15))
                return i

            def p_xc(e, l, s=s, do=do):
                l.dma("sync", e, xc[:, 0:NS],
                      x[do * 128:(do + 1) * 128, s * NS:(s + 1) * NS])
            l.phase({"tensor": p_mm, "sync": p_xc})

            def p_z(e, l, s=s, do=do):
                e.scalar_tensor_tensor(
                    out=zt[:], in0=pp[0][:, 0:NS], scalar=1.0,
                    in1=x1_sb[:, do * TC + s * NS:do * TC + (s + 1) * NS],
                    op0=ALU.mult, op1=ALU.add)
                e.drain()
                e.tensor_tensor(out=zt[:], in0=zt[:], in1=xc[:, 0:NS], op=ALU.add)
            l.phase({"vector": p_z})
            l.phase({"gpsimd": (lambda e, l, s=s, do=do:
                                l.dma("gpsimd", e,
                                      z_o[do * 128:(do + 1) * 128, s * NS:(s + 1) * NS],
                                      zt[:]))})

    def p_fin(e, l):
        e.wait_ge(l.dsem["gpsimd"], l.dcnt["gpsimd"])
    l.phase({"gpsimd": p_fin})

    if max_phases is not None:
        l._phases = l._phases[:max_phases]
    nc = l.build()
    l.close()
    return nc



def build_launchE(max_phases=None):
    l = L("launchE")
    z_i = l.inp("z", [D, TC], F32)
    wrt = l.inp("wrt", [D, 8], F32)
    brr = l.inp("brr", [128, 8], F32)
    w1t = l.inp("w1t", [8, D, 4096], F32)
    w2t = l.inp("w2t", [8, 4096, D], F32)
    b1c = l.inp("b1c", [128, 8 * 32], F32)
    b2c = l.inp("b2c", [128, 8 * 8], F32)
    ident = l.inp("ident", [128, 128], BF16)
    z2_o = l.out("z2", [D, TC], F32)

    xb = l.sbuf("xb", [128, NCH * TC], BF16)
    x2s = l.sbuf("x2s", [128, NCH * 512], BF16)
    msk = l.sbuf("msk", [128, 512], BF16)
    mrow = l.sbuf("mrow", [1, 512], BF16)
    lg = l.sbuf("lg", [128, 128], F32)
    mx = l.sbuf("mx", [128, 8], F32)
    oh = msk.ap()[:, 0:128]
    wrt_sb = l.sbuf("wrt_sb", [128, NCH * 8], BF16)
    brr_sb = l.sbuf("brr_sb", [128, 8], F32)
    b1_sb = l.sbuf("b1_sb", [128, 8 * 32], F32)
    b2_sb = l.sbuf("b2_sb", [128, 8 * 8], F32)
    id_sb = l.sbuf("id_sb", [128, 128], BF16)
    w1_sb = w2_sb = h_sb = None
    w1_3 = w2_3 = None
    zt2 = l.sbuf("zt2", [128, 128], F32)
    sq = l.sbuf("sq", [128, 512], BF16)
    ones_bf = l.sbuf("ones_bf", [128, 1], BF16)
    ones_row = l.sbuf("ones_row", [1, 128], F32)
    ones_rb = l.sbuf("ones_rb", [1, 128], BF16)
    tmp_bf = l.sbuf("tmp_bf", [128, 512], BF16)
    cst = l.sbuf("cst", [128, 1], F32)
    r0cm = l.nc.sbuf_tensor("r0e", [1, TC], F32)
    r0 = r0cm.__enter__()
    r1cm = l.nc.sbuf_tensor("r1e", [1, TC], F32)
    r1 = r1cm.__enter__()

    u1 = l.psum("u1", [128, TC], F32)
    pp = [l.psum(f"pp{i}", [128, 512], F32) for i in range(2)]
    bps = l.psum("bps", [128, 128], F32)
    msps = l.psum("msps", [128, 512], BF16)

    x2d = l.dram("x2d", [D, TC], BF16)
    x2dv = x2d.rearrange("(c p) t -> p c t", p=128)
    masks_d = l.dram("masks_d", [8, TC], BF16)

    wrt_v = wrt.rearrange("(c p) o -> p c o", p=128)
    wrt_3 = wrt_sb.ap().rearrange("p (c o) -> p c o", c=NCH)
    w1v = w1t.rearrange("e (c p) f -> e p c f", p=128)
    w2v = w2t.rearrange("e (c p) o -> e p c o", p=128)

    def p0(e, l):
        l.dma("sync", e, brr_sb[:], brr)
        l.dma("sync", e, b1_sb[:], b1c)
        l.dma("sync", e, b2_sb[:], b2c)
        l.dma("sync", e, id_sb[:], ident)
    def p0g(e, l):
        l.dma("gpsimd", e, wrt_3, wrt_v)
    def p0v(e, l):
        e.memset(ones_bf[:], 1.0)
        e.memset(ones_row[:], 1.0)
        e.memset(ones_rb[:], 1.0)
        e.memset(cst[:], 1e-5)
    l.phase({"sync": p0, "gpsimd": p0g, "vector": p0v})

    def after_chunk(ci):
        def p_lg(e, l, ci=ci):
            i = None
            for tt in range(TC // 128):
                i = e.matmul(bps[:, tt * 8:(tt + 1) * 8],
                             xb[:, ci * TC + tt * 128:ci * TC + (tt + 1) * 128],
                             wrt_sb[:, ci * 8:(ci + 1) * 8],
                             start=(ci == 0 and tt == 0),
                             stop=(ci == NCH - 1 and tt == TC // 128 - 1),
                             skip_group_check=True)
            return i
        return {"tensor": p_lg,
                "gpsimd": (lambda e, l, ci=ci:
                           l.dma("gpsimd", e, x2d[ci * 128:(ci + 1) * 128, :],
                                 xb[:, ci * TC:(ci + 1) * TC]))}
    emit_ln_start(l, z_i, xb, None, sq, u1, ones_bf, ones_row, (r0, r1),
                  cst[0:1, 0:1], after_chunk=after_chunk)

    def p_lgf(e, l):
        e.tensor_copy(lg[:], bps[:])
        e.drain()
        for tt in range(TC // 128):
            e.tensor_tensor(out=lg[:, tt * 8:(tt + 1) * 8],
                            in0=lg[:, tt * 8:(tt + 1) * 8], in1=brr_sb[:],
                            op=ALU.add)
    l.phase({"vector": p_lgf})
    for tt in range(TC // 128):
        def p_oh(e, l, tt=tt):
            e.max(mx[:], lg[:, tt * 8:(tt + 1) * 8])
            e.drain()
            e.tensor_scalar(out=oh[:, tt * 8:(tt + 1) * 8],
                            in0=lg[:, tt * 8:(tt + 1) * 8],
                            scalar1=mx[:, 0:1], scalar2=None, op0=ALU.is_equal)
        l.phase({"vector": p_oh})
    for g in range(TC // 128 * 8 // 4):
        def p_tr(e, l, g=g):
            i = None
            for r in range(4):
                idx = g * 4 + r
                tt, ee = idx // 8, idx % 8
                i = e.transpose(msps[0:1, r * 128:(r + 1) * 128],
                                oh[:, tt * 8 + ee:tt * 8 + ee + 1], id_sb[:])
            return i
        l.phase({"tensor": p_tr})

        def p_cp(e, l, g=g):
            e.copy(out=mrow[0:1, :], in_=msps[0:1, 0:512])
        l.phase({"scalar": p_cp})

        def p_md(e, l, g=g):
            idx = g * 4
            tt, ee = idx // 8, idx % 8
            l.dma("gpsimd", e,
                  masks_d.rearrange("(o e) t -> o e t", o=1)[0:1, ee:ee + 4,
                                                            tt * 128:(tt + 1) * 128],
                  mrow.ap().rearrange("p (a t) -> p a t", a=4))
        l.phase({"gpsimd": p_md})

    r1cm.__exit__(None, None, None)
    r0cm.__exit__(None, None, None)
    w1_sb = l.sbuf("w1_sb", [128, NCH * 4096], BF16)
    w2_sb = l.sbuf("w2_sb", [128, 32 * D], BF16)
    h_sb = l.sbuf("h_sb", [128, 32 * 512], BF16)
    w1_3 = w1_sb.ap().rearrange("p (c f) -> p c f", c=NCH)
    w2_3 = w2_sb.ap().rearrange("p (c o) -> p c o", c=32)

    l.phase({"vector": (lambda e, l: e.memset(xb[:], 0.0))})

    SS = 512
    et_sem = l.es.enter_context(l.nc.semaphore("e_t"))
    es_sem = l.es.enter_context(l.nc.semaphore("e_s"))
    ev_sem = l.es.enter_context(l.nc.semaphore("e_v"))
    l._e_t = 0
    l._e_s = 0
    l._e_v = 0
    ACC = [pp[0][:, 0:SS], pp[1][:, 0:SS], u1[:, 0:SS], u1[:, 512:512 + SS]]

    def p_pf0(e, l):
        l.dma("gpsimd", e, w1_3, w1v[0], pf="w1_0")
    l.phase({"gpsimd": p_pf0})
    for ee in range(8):
        for s in range(TC // SS):
            def p_xs(e, l, s=s):
                l.dma("sync", e, x2s.ap().rearrange("p (c t) -> p c t", c=NCH),
                      x2dv[:, :, s * SS:(s + 1) * SS])
            def p_mr(e, l, ee=ee, s=s):
                l.dma("sync", e, mrow[0:1, 0:SS],
                      masks_d[ee:ee + 1, s * SS:(s + 1) * SS])
            l.phase({"sync": (lambda e, l, a=p_xs, b=p_mr: (a(e, l), b(e, l)) and None)})

            def p_mbc(e, l):
                return e.matmul(u1[:, 1024:1024 + SS], ones_rb[0:1, :],
                                mrow[0:1, 0:SS], start=True, stop=True)
            l.phase({"tensor": p_mbc})
            l.phase({"scalar": (lambda e, l:
                                e.copy(out=msk[:, 0:SS], in_=u1[:, 1024:1024 + SS]))})

            def loopT1(e, l, ee=ee, s=s):
                s0, v0 = l._e_s, l._e_v
                if s == 0:
                    l.wait_pf(e, f"w1_{ee}")
                last = None
                for fc in range(32):
                    if fc >= 4:
                        pv = fc - 4
                        if pv % 2 == 0:
                            e.wait_ge(es_sem, s0 + pv // 2 + 1)
                        else:
                            e.wait_ge(ev_sem, v0 + pv // 2 + 1)
                    for ci in range(NCH):
                        last = e.matmul(ACC[fc % 4],
                                        w1_sb[:, ci * 4096 + fc * 128:
                                              ci * 4096 + (fc + 1) * 128],
                                        x2s[:, ci * SS:(ci + 1) * SS],
                                        start=(ci == 0), stop=(ci == NCH - 1))
                    last.then_inc(et_sem, 1)
                    l._e_t += 1
                return None

            def loopS1(e, l, ee=ee):
                t0 = l._e_t - 32
                for fc in range(0, 32, 2):
                    e.wait_ge(et_sem, t0 + fc + 1)
                    e.activation(out=h_sb[:, fc * SS:(fc + 1) * SS], in_=ACC[fc % 4],
                                 func=AF.Relu,
                                 bias=b1_sb[:, ee * 32 + fc:ee * 32 + fc + 1]
                                 ).then_inc(es_sem, 1)
                    l._e_s += 1
                return None

            def loopV1(e, l, ee=ee):
                t0 = l._e_t - 32
                for fc in range(1, 32, 2):
                    e.wait_ge(et_sem, t0 + fc + 1)
                    i = e.tensor_scalar(out=h_sb[:, fc * SS:(fc + 1) * SS],
                                        in0=ACC[fc % 4],
                                        scalar1=b1_sb[:, ee * 32 + fc:ee * 32 + fc + 1],
                                        scalar2=0.0, op0=ALU.add, op1=ALU.max)
                    e.drain()
                    i.then_inc(ev_sem, 1)
                    l._e_v += 1
                return None

            if s == 0:
                def p_pf2(e, l, ee=ee):
                    l.dma("gpsimd", e, w2_3, w2v[ee], pf=f"w2_{ee}")
                l.phase({"tensor": loopT1, "scalar": loopS1, "vector": loopV1,
                         "gpsimd": p_pf2})
            else:
                l.phase({"tensor": loopT1, "scalar": loopS1, "vector": loopV1})

            def loopT2(e, l, ee=ee, s=s):
                v0 = l._e_v
                if s == 0:
                    l.wait_pf(e, f"w2_{ee}")
                last = None
                for do in range(NCH):
                    if do >= 4:
                        e.wait_ge(ev_sem, v0 + do - 3)
                    for fc in range(32):
                        last = e.matmul(ACC[do % 4],
                                        w2_sb[:, fc * D + do * 128:
                                              fc * D + (do + 1) * 128],
                                        h_sb[:, fc * SS:(fc + 1) * SS],
                                        start=(fc == 0), stop=(fc == 31))
                    last.then_inc(et_sem, 1)
                    l._e_t += 1
                return None

            def loopV2(e, l, ee=ee, s=s):
                t0 = l._e_t - NCH
                for do in range(NCH):
                    e.wait_ge(et_sem, t0 + do + 1)
                    e.scalar_tensor_tensor(out=tmp_bf[:, 0:SS], in0=ACC[do % 4],
                                           scalar=b2_sb[:, ee * 8 + do:ee * 8 + do + 1],
                                           in1=msk[:, 0:SS],
                                           op0=ALU.add, op1=ALU.mult)
                    e.drain()
                    i = e.tensor_tensor(
                        out=xb[:, do * TC + s * SS:do * TC + (s + 1) * SS],
                        in0=xb[:, do * TC + s * SS:do * TC + (s + 1) * SS],
                        in1=tmp_bf[:, 0:SS], op=ALU.add)
                    e.drain()
                    i.then_inc(ev_sem, 1)
                    l._e_v += 1
                return None

            if s == TC // SS - 1 and ee < 7:
                def p_pf1(e, l, ee=ee):
                    l.dma("gpsimd", e, w1_3, w1v[ee + 1], pf=f"w1_{ee + 1}")
                l.phase({"tensor": loopT2, "vector": loopV2, "gpsimd": p_pf1})
            else:
                l.phase({"tensor": loopT2, "vector": loopV2})

    for ci in range(NCH):
        for hf in range(4):
            def p_x2(e, l, ci=ci, hf=hf):
                l.dma("sync", e, sq[:, 0:512],
                      x2d[ci * 128:(ci + 1) * 128, hf * 512:(hf + 1) * 512])
            l.phase({"sync": p_x2})
            for s4 in range(4):
                def p_add(e, l, ci=ci, hf=hf, s4=s4):
                    e.tensor_tensor(out=zt2[:], in0=sq[:, s4 * 128:(s4 + 1) * 128],
                                    in1=xb[:, ci * TC + hf * 512 + s4 * 128:
                                            ci * TC + hf * 512 + (s4 + 1) * 128],
                                    op=ALU.add)
                l.phase({"vector": p_add})
                l.phase({"gpsimd": (lambda e, l, ci=ci, hf=hf, s4=s4:
                                    l.dma("gpsimd", e,
                                          z2_o[ci * 128:(ci + 1) * 128,
                                               hf * 512 + s4 * 128:hf * 512 + (s4 + 1) * 128],
                                          zt2[:]))})

    def p_fin(e, l):
        e.wait_ge(l.dsem["gpsimd"], l.dcnt["gpsimd"])
    l.phase({"gpsimd": p_fin})

    if max_phases is not None:
        l._phases = l._phases[:max_phases]
    nc = l.build()
    l.close()
    return nc



def build_launchF(max_phases=None):
    l = L("launchF")
    z_i = l.inp("z2", [D, TC], F32)
    wqt = l.inp("wqt", [D, D], F32)
    wkt = l.inp("wkt", [D, D], F32)
    wvt = l.inp("wvt", [D, D], F32)
    cosb = l.inp("cosb", [128, TC], F32)
    sinb = l.inp("sinb", [128, TC], F32)
    x3_o = l.out("x3", [D, TC], F32)
    q_o = l.out("q", [D, TC], BF16)
    k_o = l.out("k", [D, TC], BF16)
    v_o = l.out("v", [D, TC], BF16)

    xb = l.sbuf("xb", [128, NCH * TC], BF16)
    w_sb = [l.sbuf(f"w{i}", [128, NCH * D], BF16) for i in range(2)]
    out_sb = l.sbuf("outb", [128, NCH * TC], BF16)
    cos_sb = l.sbuf("cos_sb", [128, TC], F32)
    sin_sb = l.sbuf("sin_sb", [128, TC], F32)
    t1 = l.sbuf("t1", [128, TC], BF16)
    t2 = l.sbuf("t2", [128, TC], BF16)
    sq = l.sbuf("sq", [128, TC], BF16)
    ones_bf = l.sbuf("ones_bf", [128, 1], BF16)
    ones_row = l.sbuf("ones_row", [1, 128], F32)
    r0 = l.sbuf("r0", [1, TC], F32)
    r1 = l.sbuf("r1", [1, TC], F32)
    cst = l.sbuf("cst", [128, 1], F32)

    u1 = l.psum("u1", [128, TC], F32)
    pp = [l.psum(f"pp{i}", [128, 512], F32) for i in range(2)]

    wv_views = {
        "q": wqt.rearrange("(c p) o -> p c o", p=128),
        "k": wkt.rearrange("(c p) o -> p c o", p=128),
        "v": wvt.rearrange("(c p) o -> p c o", p=128),
    }
    w_sb3 = [w_sb[i].ap().rearrange("p (c o) -> p c o", c=NCH) for i in range(2)]

    def p0(e, l):
        l.dma("sync", e, cos_sb[:], cosb)
        l.dma("sync", e, sin_sb[:], sinb)
    def p0g(e, l):
        l.dma("gpsimd", e, w_sb3[0], wv_views["q"], pf="q")
    def p0v(e, l):
        e.memset(ones_bf[:], 1.0)
        e.memset(ones_row[:], 1.0)
        e.memset(cst[:], 1e-5)
    l.phase({"sync": p0, "gpsimd": p0g, "vector": p0v})

    def after_chunk(ci):
        return {"gpsimd": (lambda e, l, ci=ci:
                           l.dma("gpsimd", e, x3_o[ci * 128:(ci + 1) * 128, :],
                                 xb[:, ci * TC:(ci + 1) * TC]))}
    emit_ln_start(l, z_i, xb, None, sq, u1, ones_bf, ones_row, (r0, r1),
                  cst[0:1, 0:1], after_chunk=after_chunk)

    def rope(src):
        shf = [(i + 16) % 32 for i in range(32)]
        for ci in range(NCH):
            sl = slice(ci * TC, (ci + 1) * TC)

            def p_r1(e, l, sl=sl):
                e.stream_shuffle(t2[:], src[:, sl], shf)
                e.tensor_tensor(out=t1[:], in0=src[:, sl], in1=cos_sb[:], op=ALU.mult)
                e.drain()
                e.tensor_tensor(out=t2[:], in0=t2[:], in1=sin_sb[:], op=ALU.mult)
                e.drain()
                e.tensor_tensor(out=src[:, sl], in0=t1[:], in1=t2[:], op=ALU.add)
            l.phase({"vector": p_r1})

    def store(out_ap):
        def p_st(e, l):
            for ci in range(NCH):
                l.dma("gpsimd", e, out_ap[ci * 128:(ci + 1) * 128, :],
                      out_sb[:, ci * TC:(ci + 1) * TC])
        l.phase({"gpsimd": p_st})

    emit_proj(l, "q", "k", w_sb[0], w_sb3[1], wv_views["k"], xb, out_sb, pp)
    rope(out_sb)
    store(q_o)
    emit_proj(l, "k", "v", w_sb[1], w_sb3[0], wv_views["v"], xb, out_sb, pp)
    rope(out_sb)
    store(k_o)
    emit_proj(l, "v", None, w_sb[0], None, None, xb, out_sb, pp)
    store(v_o)

    def p_fin(e, l):
        e.wait_ge(l.dsem["gpsimd"], l.dcnt["gpsimd"])
    l.phase({"gpsimd": p_fin})

    if max_phases is not None:
        l._phases = l._phases[:max_phases]
    nc = l.build()
    l.close()
    return nc



def build_launchG(max_phases=None):
    l = L("launchG")
    NKT = T // 128
    qr = l.inp("qr", [D, TC], BF16)
    kh = l.inp("kh", [16, 64, T], BF16)
    vh = l.inp("vh", [16, T, 64], BF16)
    x3 = l.inp("x3", [D, TC], F32)
    wot = l.inp("wot", [D, D], F32)
    z3_o = l.out("z3", [D, TC], F32)

    q_sb = l.sbuf("q_sb", [128, NCH * TC], BF16)
    kd_sb = l.sbuf("kd_sb", [128, T], BF16)
    va_sb = l.sbuf("va_sb", [128, NKT * 64], BF16)
    vb_sb = l.sbuf("vb_sb", [128, NKT * 64], BF16)
    ctx_sb = l.sbuf("ctx_sb", [128, NCH * TC], BF16)
    sqk = l.sbuf("sqk", [128, T], BF16)
    et = [l.sbuf(f"et{i}", [128, 1024], BF16) for i in range(2)]
    wot_sb = l.sbuf("wot_sb", [128, NCH * D], BF16)
    bcs = l.sbuf("bcs", [128, 512], F32)
    rws = l.sbuf("rws", [1, 1024], F32)
    mq2 = l.sbuf("mq2", [1, 16], F32)
    mk2 = l.sbuf("mk2", [1, 16], F32)
    sc1 = l.sbuf("sc1", [1, 8], F32)
    nbias = l.sbuf("nbias", [128, 1], F32)
    xc = l.sbuf("xc", [128, 512], F32)
    zt = l.sbuf("zt", [128, 512], F32)
    ones_bf = l.sbuf("ones_bf", [128, 1], BF16)
    ones_row = l.sbuf("ones_row", [1, 128], F32)

    sc = [l.psum(f"scps{i}", [128, 1024], F32) for i in range(2)]
    ctxp = l.psum("ctxp", [128, 1024], F32)
    denp = l.psum("denp", [128, 1024], F32)

    tsem = l.es.enter_context(l.nc.semaphore("g_t"))
    ssem = l.es.enter_context(l.nc.semaphore("g_s"))
    l._g_t = 0
    l._g_s = 0

    q_v = qr.rearrange("(c p) t -> p c t", p=128)
    q_sb3 = q_sb.ap().rearrange("p (c t) -> p c t", c=NCH)
    wot_v = wot.rearrange("(c p) o -> p c o", p=128)
    wot_3 = wot_sb.ap().rearrange("p (c o) -> p c o", c=NCH)

    def p0(e, l):
        l.dma("sync", e, q_sb3, q_v)
    def p0g(e, l):
        l.dma("gpsimd", e, wot_3, wot_v, pf="wo")
    def p0v(e, l):
        e.memset(ones_bf[:], 1.0)
        e.memset(ones_row[:], 1.0)
    l.phase({"sync": p0, "gpsimd": p0g, "vector": p0v})

    for ci in range(NCH):
        l.phase({"scalar": (lambda e, l, ci=ci:
                            e.activation(out=sqk[:, 0:TC],
                                         in_=q_sb[:, ci * TC:(ci + 1) * TC],
                                         func=AF.Square))})
        for hh in range(2):
            def p_cs(e, l, hh=hh):
                i = None
                for j in range(0, TC, 512):
                    i = e.matmul(ctxp[0:1, 0:512],
                                 ones_bf[hh * 64:(hh + 1) * 64, 0:1],
                                 sqk[hh * 64:(hh + 1) * 64, j:j + 512],
                                 start=(j == 0), stop=(j == TC - 512))
                return i
            l.phase({"tensor": p_cs})

            def p_mx(e, l, ci=ci, hh=hh):
                pass
            def p_mx2(e, l, ci=ci, hh=hh):
                e.reduce_max(mq2[0:1, ci * 2 + hh:ci * 2 + hh + 1],
                             ctxp[0:1, 0:512], axis=AX.X)
            l.phase({"vector": p_mx2})

    for dj in range(NCH):
        def p_ld(e, l, dj=dj):
            l.dma("sync", e, kd_sb[:],
                  kh.rearrange("h p t -> (h p) t")[dj * 128:(dj + 1) * 128, :])
            l.dma("sync", e, va_sb.ap().rearrange("p (n d) -> p n d", d=64),
                  vh.rearrange("h (n p) d -> h p n d", p=128)[2 * dj])
            l.dma("sync", e, vb_sb.ap().rearrange("p (n d) -> p n d", d=64),
                  vh.rearrange("h (n p) d -> h p n d", p=128)[2 * dj + 1])
        l.phase({"sync": p_ld})

        l.phase({"scalar": (lambda e, l: e.activation(out=sqk[:], in_=kd_sb[:],
                                                      func=AF.Square))})
        for hh in range(2):
            for pc in range(T // 512):
                def p_cs(e, l, hh=hh, pc=pc):
                    return e.matmul(ctxp[0:1, 0:512],
                                    ones_bf[hh * 64:(hh + 1) * 64, 0:1],
                                    sqk[hh * 64:(hh + 1) * 64,
                                        pc * 512:(pc + 1) * 512],
                                    start=True, stop=True)
                l.phase({"tensor": p_cs})

                def p_mx(e, l, dj=dj, hh=hh, pc=pc):
                    e.reduce_max(sc1[0:1, 0:1], ctxp[0:1, 0:512], axis=AX.X)
                    e.drain()
                    if pc == 0:
                        e.tensor_copy(mk2[0:1, dj * 2 + hh:dj * 2 + hh + 1],
                                      sc1[0:1, 0:1])
                    else:
                        e.tensor_tensor(out=mk2[0:1, dj * 2 + hh:dj * 2 + hh + 1],
                                        in0=mk2[0:1, dj * 2 + hh:dj * 2 + hh + 1],
                                        in1=sc1[0:1, 0:1], op=ALU.max)
                l.phase({"vector": p_mx})

        def p_nb1(e, l, dj=dj):
            e.tensor_tensor(out=sc1[0:1, 0:2], in0=mq2[0:1, dj * 2:dj * 2 + 2],
                            in1=mk2[0:1, dj * 2:dj * 2 + 2], op=ALU.mult)
            e.drain()
            e.reduce_max(sc1[0:1, 2:3], sc1[0:1, 0:2], axis=AX.X)
        l.phase({"vector": p_nb1})
        l.phase({"scalar": (lambda e, l:
                            e.activation(out=sc1[0:1, 3:4], in_=sc1[0:1, 2:3],
                                         func=AF.Sqrt))})
        l.phase({"vector": (lambda e, l:
                            e.tensor_scalar_mul(sc1[0:1, 4:5], sc1[0:1, 3:4], -0.125))})
        l.phase({"tensor": (lambda e, l:
                            e.matmul(ctxp[:, 0:1], ones_row[0:1, :], sc1[0:1, 4:5],
                                     start=True, stop=True))})
        l.phase({"scalar": (lambda e, l: e.copy(out=nbias[:], in_=ctxp[:, 0:1]))})

        for qs in range(TC // 512):
            qsl = slice(dj * TC + qs * 512, dj * TC + (qs + 1) * 512)

            def mk_scores(e, kt, qsl=qsl):
                i = None
                for hh in range(2):
                    i = e.matmul(
                        sc[kt % 2][:, hh * 512:(hh + 1) * 512],
                        kd_sb[hh * 64:(hh + 1) * 64, kt * 128:(kt + 1) * 128],
                        q_sb.ap()[hh * 64:(hh + 1) * 64, qsl],
                        start=True, stop=True,
                        tile_position=(hh * 64, 0))
                return i

            def mk_pv(e, kt):
                first = (kt == 0)
                e.matmul(ctxp[0:64, 0:512], va_sb[:, kt * 64:(kt + 1) * 64],
                         et[kt % 2][:, 0:512],
                         start=first, stop=(kt == NKT - 1),
                         tile_position=(0, 0), skip_group_check=True)
                e.matmul(ctxp[64:128, 512:1024], vb_sb[:, kt * 64:(kt + 1) * 64],
                         et[kt % 2][:, 512:1024],
                         start=first, stop=(kt == NKT - 1),
                         tile_position=(0, 64), skip_group_check=True)
                e.matmul(denp[0:1, 0:512], ones_bf[:, 0:1], et[kt % 2][:, 0:512],
                         start=first, stop=(kt == NKT - 1),
                         tile_position=(0, 0), skip_group_check=True)
                return e.matmul(denp[0:1, 512:1024], ones_bf[:, 0:1],
                                et[kt % 2][:, 512:1024],
                                start=first, stop=(kt == NKT - 1),
                                tile_position=(0, 0), skip_group_check=True)

            def loopT(e, l, mk_scores=mk_scores, mk_pv=mk_pv):
                t0, s0 = l._g_t, l._g_s
                for kt in range(NKT):
                    if kt >= 2:
                        e.wait_ge(ssem, s0 + kt - 1)
                    mk_scores(e, kt).then_inc(tsem, 1)
                    l._g_t += 1
                    if kt >= 1:
                        e.wait_ge(ssem, s0 + kt)
                        mk_pv(e, kt - 1)
                e.wait_ge(ssem, s0 + NKT)
                return mk_pv(e, NKT - 1)

            def loopS(e, l):
                t0 = l._g_t - NKT
                for kt in range(NKT):
                    e.wait_ge(tsem, t0 + kt + 1)
                    e.activation(out=et[kt % 2][:], in_=sc[kt % 2][:],
                                 func=AF.Exp, scale=0.125,
                                 bias=nbias[:, 0:1]).then_inc(ssem, 1)
                    l._g_s += 1
                return None

            l.phase({"tensor": loopT, "scalar": loopS})

            def p_rc(e, l):
                e.reciprocal(rws[0:1, :], denp[0:1, :])
            l.phase({"vector": p_rc})

            def p_bc(e, l):
                e.matmul(sc[0][0:64, 0:512], ones_row[0:1, 0:64], rws[0:1, 0:512],
                         start=True, stop=True, tile_position=(0, 0))
                return e.matmul(sc[0][64:128, 0:512], ones_row[0:1, 0:64],
                                rws[0:1, 512:1024],
                                start=True, stop=True, tile_position=(0, 64))
            l.phase({"tensor": p_bc})
            l.phase({"scalar": (lambda e, l: e.copy(out=bcs[:], in_=sc[0][:, 0:512]))})

            def p_nm(e, l, qsl=qsl):
                e.tensor_tensor(out=ctx_sb.ap()[0:64, qsl], in0=ctxp[0:64, 0:512],
                                in1=bcs[0:64, :], op=ALU.mult)
                e.drain()
                e.tensor_tensor(out=ctx_sb.ap()[64:128, qsl],
                                in0=ctxp[64:128, 512:1024],
                                in1=bcs[64:128, :], op=ALU.mult)
            l.phase({"vector": p_nm})

    for do in range(NCH):
        for nt in range(TC // 512):
            def p_mm(e, l, do=do, nt=nt, first=(do == 0 and nt == 0)):
                if first:
                    l.wait_pf(e, "wo")
                i = None
                for ci in range(NCH):
                    i = e.matmul(ctxp[:, 0:512],
                                 wot_sb[:, ci * D + do * 128:ci * D + (do + 1) * 128],
                                 ctx_sb[:, ci * TC + nt * 512:ci * TC + (nt + 1) * 512],
                                 start=(ci == 0), stop=(ci == NCH - 1))
                return i

            def p_xc(e, l, do=do, nt=nt):
                l.dma("sync", e, xc[:], x3[do * 128:(do + 1) * 128, nt * 512:(nt + 1) * 512])
            l.phase({"tensor": p_mm, "sync": p_xc})

            def p_z(e, l):
                e.scalar_tensor_tensor(out=zt[:], in0=ctxp[:, 0:512], scalar=1.0,
                                       in1=xc[:], op0=ALU.mult, op1=ALU.add)
            l.phase({"vector": p_z})
            l.phase({"gpsimd": (lambda e, l, do=do, nt=nt:
                                l.dma("gpsimd", e,
                                      z3_o[do * 128:(do + 1) * 128, nt * 512:(nt + 1) * 512],
                                      zt[:]))})

    def p_fin(e, l):
        e.wait_ge(l.dsem["gpsimd"], l.dcnt["gpsimd"])
    l.phase({"gpsimd": p_fin})

    if max_phases is not None:
        l._phases = l._phases[:max_phases]
    nc = l.build()
    l.close()
    return nc



def build_launchH(max_phases=None):
    l = L("launchH")
    z_i = l.inp("z3", [D, TC], F32)
    wut = l.inp("wut", [D, 4096], F32)
    wdt = l.inp("wdt", [4096, D], F32)
    out_o = l.out("out", [D, TC], F32)

    xb = l.sbuf("xb", [128, NCH * TC], BF16)
    wu_sb = l.sbuf("wu_sb", [128, NCH * 4096], BF16)
    wd_sb = l.sbuf("wd_sb", [128, 32 * D], BF16)
    hs = l.sbuf("hs", [128, 32 * 256], BF16)
    sq = l.sbuf("sq", [128, 512], BF16)
    tmp = l.sbuf("tmp", [128, 256], BF16)
    zt = l.sbuf("zt", [128, 256], F32)
    rbc1 = l.sbuf("rbc1", [128, 256], F32)
    rbc2 = l.sbuf("rbc2", [128, 256], F32)
    acc = l.sbuf("acc", [128, 32], F32)
    tau = l.sbuf("tau", [128, 2], F32)
    onesf = l.sbuf("onesf", [128, 1], F32)
    ones_bf = l.sbuf("ones_bf", [128, 1], BF16)
    ones_row = l.sbuf("ones_row", [1, 128], F32)
    r0 = l.sbuf("r0", [1, TC], F32)
    r1 = l.sbuf("r1", [1, TC], F32)
    s1 = l.sbuf("s1", [1, 8], F32)
    cst = l.sbuf("cst", [128, 2], F32)

    u1 = l.psum("u1", [128, TC], F32)
    pp = [l.psum(f"pp{i}", [128, 512], F32) for i in range(2)]

    z4d = l.dram("z4d", [D, TC], F32)

    wsl = [l.sbuf(f"wsl{i}", [128, 1024], F32) for i in range(2)]
    sgn = hs.ap()[:, 0:1024]
    g1 = hs.ap()[:, 1024:2048]
    g1b = hs.ap()[:, 2048:3072]

    def p0v(e, l):
        e.memset(ones_bf[:], 1.0)
        e.memset(ones_row[:], 1.0)
        e.memset(onesf[:], 1.0)
        e.memset(cst[:, 0:1], 1e-5)
        e.memset(cst[:, 1:2], 1e-8)
    l.phase({"vector": p0v})

    emit_ln_start(l, z_i, xb, None, sq, u1, ones_bf, ones_row, (r0, r1),
                  cst[0:1, 0:1])

    def ternarize(w_in, R, C, w_out_sb):
        wv = w_in.rearrange("(c p) f -> p c f", p=128)
        nsl = (R // 128) * (C // 1024)

        def sl_view(k):
            c, s = k // (C // 1024), k % (C // 1024)
            return wv[:, c, s * 1024:(s + 1) * 1024], c * C + s * 1024

        l.phase({"sync": (lambda e, l: l.dma("sync", e, wsl[0][:], sl_view(0)[0]))})
        for k in range(nsl):
            work = {"scalar": (lambda e, l, k=k:
                               e.activation(out=g1 if k % 2 == 0 else g1b,
                                            in_=wsl[k % 2][:], func=AF.Abs,
                                            accum_out=acc[:, k:k + 1]))}
            if k + 1 < nsl:
                work["sync"] = (lambda e, l, k=k:
                                l.dma("sync", e, wsl[(k + 1) % 2][:], sl_view(k + 1)[0]))
            l.phase(work)
        def p_t1(e, l, nsl=nsl):
            return e.matmul(u1[0:1, 0:nsl], onesf[:, 0:1], acc[:, 0:nsl],
                            start=True, stop=True)
        l.phase({"tensor": p_t1})

        def p_t2(e, l, nsl=nsl):
            e.reduce_sum(s1[0:1, 0:1], u1[0:1, 0:nsl], axis=AX.X)
            e.drain()
            e.tensor_scalar_mul(s1[0:1, 1:2], s1[0:1, 0:1], 0.5 / (R * C))
            e.drain()
            e.tensor_scalar_mul(s1[0:1, 2:3], s1[0:1, 1:2], -1.0)
        l.phase({"vector": p_t2})

        def p_t3(e, l):
            e.matmul(u1[:, 0:1], ones_row[0:1, :], s1[0:1, 1:2], start=True, stop=True)
            return e.matmul(u1[:, 1:2], ones_row[0:1, :], s1[0:1, 2:3],
                            start=True, stop=True)
        l.phase({"tensor": p_t3})
        l.phase({"scalar": (lambda e, l: e.copy(out=tau[:], in_=u1[:, 0:2]))})

        l.phase({"sync": (lambda e, l: l.dma("sync", e, wsl[0][:], sl_view(0)[0]))})
        for k in range(nsl):
            ocol = sl_view(k)[1]
            work = {"scalar": (lambda e, l, k=k: e.sign(out=sgn[:], in_=wsl[k % 2][:]))}
            if k + 1 < nsl:
                work["sync"] = (lambda e, l, k=k:
                                l.dma("sync", e, wsl[(k + 1) % 2][:], sl_view(k + 1)[0]))
            l.phase(work)

            def p_g(e, l, k=k, ocol=ocol):
                e.tensor_scalar(out=g1[:], in0=wsl[k % 2][:], scalar1=tau[:, 0:1],
                                scalar2=None, op0=ALU.is_gt)
                e.drain()
                e.scalar_tensor_tensor(out=g1[:], in0=wsl[k % 2][:], scalar=tau[:, 1:2],
                                       in1=g1[:], op0=ALU.is_lt, op1=ALU.add)
                e.drain()
                e.tensor_tensor(out=w_out_sb[:, ocol:ocol + 1024],
                                in0=g1[:], in1=sgn[:], op=ALU.mult)
            l.phase({"vector": p_g})

    ternarize(wut, D, 4096, wu_sb)
    ternarize(wdt, 4096, D, wd_sb)

    for ci in range(NCH):
        for hf in range(4):
            l.phase({"scalar": (lambda e, l, ci=ci, hf=hf:
                                e.activation(out=sq[:, 0:512],
                                             in_=xb[:, ci * TC + hf * 512:
                                                     ci * TC + (hf + 1) * 512],
                                             func=AF.Square))})

            def p_cs(e, l, ci=ci, hf=hf):
                return e.matmul(u1[0:1, hf * 512:(hf + 1) * 512],
                                ones_bf[:, 0:1], sq[:, 0:512],
                                start=(ci == 0), stop=(ci == NCH - 1))
            l.phase({"tensor": p_cs})
    l.phase({"scalar": (lambda e, l:
                        e.activation(out=r0[0:1, :], in_=u1[0:1, :], func=AF.Sqrt,
                                     scale=1.0 / D))})

    def p_r1(e, l):
        e.tensor_scalar_add(r0[0:1, :], r0[0:1, :], 1e-8)
        e.drain()
        e.reciprocal(r0[0:1, :], r0[0:1, :])
    l.phase({"vector": p_r1})

    SS = 256
    ht_sem = l.es.enter_context(l.nc.semaphore("h_t"))
    hs_sem = l.es.enter_context(l.nc.semaphore("h_s"))
    hv_sem = l.es.enter_context(l.nc.semaphore("h_v"))
    hd_sem = l.es.enter_context(l.nc.semaphore("h_d"))
    l._h_t = 0
    l._h_s = 0
    l._h_v = 0
    l._h_d = 0
    ACC = [pp[0][:, 0:SS], pp[1][:, 0:SS], u1[:, 0:SS], u1[:, 512:512 + SS]]
    tmp2 = [tmp, l.sbuf("tmp_b", [128, 256], BF16)]
    zt2 = [zt, l.sbuf("zt_b", [128, 256], F32)]

    for s in range(TC // SS):
        l.phase({"tensor": (lambda e, l, s=s:
                            bcast(e, ones_row, r0[0:1, s * SS:(s + 1) * SS],
                                  u1[:, 1024:1024 + SS] if False else u1, SS))})
        l.phase({"scalar": (lambda e, l: e.copy(out=rbc1[:], in_=u1[:, 0:SS]))})

        def loopT1(e, l, s=s):
            v0 = l._h_v
            last = None
            for fc in range(32):
                if fc >= 4:
                    e.wait_ge(hv_sem, v0 + fc - 3)
                for ci in range(NCH):
                    last = e.matmul(ACC[fc % 4],
                                    wu_sb[:, ci * 4096 + fc * 128:
                                          ci * 4096 + (fc + 1) * 128],
                                    xb[:, ci * TC + s * SS:ci * TC + (s + 1) * SS],
                                    start=(ci == 0), stop=(ci == NCH - 1))
                last.then_inc(ht_sem, 1)
                l._h_t += 1
            return None

        def loopV1(e, l):
            t0, s0 = l._h_t - 32, l._h_s
            for fc in range(32):
                e.wait_ge(ht_sem, t0 + fc + 1)
                if fc >= 2:
                    e.wait_ge(hs_sem, s0 + fc - 1)
                i = e.scalar_tensor_tensor(out=tmp2[fc % 2][:], in0=ACC[fc % 4],
                                           scalar=1.0, in1=rbc1[:],
                                           op0=ALU.mult, op1=ALU.mult)
                e.drain()
                i.then_inc(hv_sem, 1)
                l._h_v += 1
            return None

        def loopS1(e, l):
            v0 = l._h_v - 32
            for fc in range(32):
                e.wait_ge(hv_sem, v0 + fc + 1)
                e.activation(out=hs[:, fc * SS:(fc + 1) * SS],
                             in_=tmp2[fc % 2][:],
                             func=AF.Gelu).then_inc(hs_sem, 1)
                l._h_s += 1
            return None

        l.phase({"tensor": loopT1, "vector": loopV1, "scalar": loopS1})

        bases_st = {}

        def loopS2(e, l, bz=bases_st):
            bz.setdefault("t", l._h_t)
            bz.setdefault("s", l._h_s)
            t0 = bz["t"]
            for fc in range(32):
                if fc >= 2:
                    e.wait_ge(ht_sem, t0 + fc - 1)
                e.activation(out=sq[:, (fc % 2) * SS:(fc % 2 + 1) * SS],
                             in_=hs[:, fc * SS:(fc + 1) * SS],
                             func=AF.Square).then_inc(hs_sem, 1)
                l._h_s += 1
            return None

        def loopT2s(e, l, bz=bases_st):
            bz.setdefault("t", l._h_t)
            bz.setdefault("s", l._h_s)
            s0 = bz["s"]
            last = None
            for fc in range(32):
                e.wait_ge(hs_sem, s0 + fc + 1)
                last = e.matmul(u1[0:1, 1024:1024 + SS], ones_bf[:, 0:1],
                                sq[:, (fc % 2) * SS:(fc % 2 + 1) * SS],
                                start=(fc == 0), stop=(fc == 31))
                if fc < 31:
                    last.then_inc(ht_sem, 1)
                    l._h_t += 1
            return last

        l.phase({"scalar": loopS2, "tensor": loopT2s})

        l.phase({"scalar": (lambda e, l:
                            e.activation(out=r1[0:1, 0:SS], in_=u1[0:1, 1024:1024 + SS],
                                         func=AF.Sqrt, scale=1.0 / 4096))})

        def p_r2(e, l):
            e.tensor_scalar_add(r1[0:1, 0:SS], r1[0:1, 0:SS], 1e-8)
            e.drain()
            e.reciprocal(r1[0:1, 0:SS], r1[0:1, 0:SS])
        l.phase({"vector": p_r2})
        l.phase({"tensor": (lambda e, l:
                            bcast(e, ones_row, r1[0:1, 0:SS], u1, SS))})
        l.phase({"scalar": (lambda e, l: e.copy(out=rbc2[:], in_=u1[:, 0:SS]))})

        bases_g2 = {}
        dma_marks = []

        def loopT3(e, l, s=s, bz=bases_g2):
            bz.setdefault("t", l._h_t)
            bz.setdefault("v", l._h_v)
            v0 = bz["v"]
            last = None
            for do in range(NCH):
                if do >= 4:
                    e.wait_ge(hv_sem, v0 + do - 3)
                for fc in range(32):
                    last = e.matmul(ACC[do % 4],
                                    wd_sb[:, fc * D + do * 128:fc * D + (do + 1) * 128],
                                    hs[:, fc * SS:(fc + 1) * SS],
                                    start=(fc == 0), stop=(fc == 31))
                last.then_inc(ht_sem, 1)
                l._h_t += 1
            return None

        def loopV3(e, l, s=s, bz=bases_g2, dm=dma_marks):
            bz.setdefault("t", l._h_t)
            bz.setdefault("v", l._h_v)
            bz.setdefault("d", l._h_d)
            t0, d0 = bz["t"], bz["d"]
            for do in range(NCH):
                e.wait_ge(ht_sem, t0 + do + 1)
                if do >= 2:
                    e.wait_ge(hd_sem, d0 + do - 1)
                e.scalar_tensor_tensor(out=zt2[do % 2][:], in0=ACC[do % 4],
                                       scalar=1.0, in1=rbc2[:],
                                       op0=ALU.mult, op1=ALU.mult)
                e.drain()
                i = e.tensor_tensor(out=zt2[do % 2][:], in0=zt2[do % 2][:],
                                    in1=xb[:, do * TC + s * SS:do * TC + (s + 1) * SS],
                                    op=ALU.add)
                e.drain()
                i.then_inc(hv_sem, 1)
                l._h_v += 1
            return None

        def loopG3(e, l, s=s, bz=bases_g2, dm=dma_marks):
            bz.setdefault("t", l._h_t)
            bz.setdefault("v", l._h_v)
            bz.setdefault("d", l._h_d)
            for do in range(NCH):
                e.wait_ge(hv_sem, bz["v"] + do + 1)
                t = l.dma("gpsimd", e,
                          z4d[do * 128:(do + 1) * 128, s * SS:(s + 1) * SS],
                          zt2[do % 2][:])
                e.wait_ge(l.dsem["gpsimd"], t)
                e.sem_inc(hd_sem, 1)
                l._h_d += 1
            return None

        l.phase({"tensor": loopT3, "vector": loopV3, "gpsimd": loopG3})

    def after_chunk(ci):
        return {"gpsimd": (lambda e, l, ci=ci:
                           l.dma("gpsimd", e, out_o[ci * 128:(ci + 1) * 128, :],
                                 xb[:, ci * TC:(ci + 1) * TC]))}
    emit_ln_start(l, z4d, xb, None, sq, u1, ones_bf, ones_row, (r0, r1),
                  cst[0:1, 0:1], after_chunk=after_chunk)

    def p_fin(e, l):
        e.wait_ge(l.dsem["gpsimd"], l.dcnt["gpsimd"])
    l.phase({"gpsimd": p_fin})

    if max_phases is not None:
        l._phases = l._phases[:max_phases]
    nc = l.build()
    l.close()
    return nc



def _fm(a):
    return np.ascontiguousarray(np.asarray(a, np.float32).T)


def _chunked(v):
    return np.ascontiguousarray(np.asarray(v, np.float32).reshape(NCH, 128).T)


def _rope_perm():
    perm = np.zeros(D, np.int64)
    for j in range(NCH):
        for r in range(128):
            qd = r // 32
            h = 2 * j + (qd // 2)
            fr = 16 * (qd % 2) + (r % 16)
            x2 = (r % 32) >= 16
            perm[j * 128 + r] = h * 64 + 2 * fr + (1 if x2 else 0)
    return perm


def _cos_sin(t0):
    hd = 64
    inv = 1.0 / 10000.0 ** (np.arange(0, hd, 2, dtype=np.float32) / hd)
    t_ = np.arange(t0, t0 + TC, dtype=np.float32)
    f = np.outer(inv, t_)
    cos = np.zeros((128, TC), np.float32)
    ssin = np.zeros((128, TC), np.float32)
    for r in range(128):
        qd = r // 32
        fr = 16 * (qd % 2) + (r % 16)
        x2 = (r % 32) >= 16
        cos[r] = np.cos(f[fr])
        ssin[r] = np.sin(f[fr]) * (1.0 if x2 else -1.0)
    return cos, ssin


_NC_CACHE = {}


def _nc(name, builder):
    if name not in _NC_CACHE:
        _NC_CACHE[name] = builder()
    return _NC_CACHE[name]


def kernel(**inp):
    import ml_dtypes
    BF = ml_dtypes.bfloat16
    f32 = lambda k: np.asarray(inp[k], np.float32)
    x = f32("L0_attn_x")
    cores = list(range(NCORE))

    n1w = _chunked(inp["L0_attn_norm1_w"])
    bw = _chunked(f32("L0_attn_beta_w")[:, 0])
    bb = np.broadcast_to(f32("L0_attn_beta_b").reshape(1, 1), (128, 1)).copy()
    imA = []
    for c in cores:
        b, qd = c // 4, c % 4
        imA.append({"x": _fm(x[b, qd * TC:(qd + 1) * TC]), "n1w": n1w,
                    "bw": bw, "bb": bb,
                    "wq": f32("L0_attn_Wq"), "wk": f32("L0_attn_Wk"),
                    "wv": f32("L0_attn_Wv")})
    rA = run_bass_kernel_spmd(_nc("A", build_launchA), imA, core_ids=cores).results

    qf = [np.concatenate([rA[b * 4 + qd]["q"] for qd in range(4)], 1) for b in range(B)]
    kf = [np.concatenate([rA[b * 4 + qd]["k"] for qd in range(4)], 1) for b in range(B)]
    vf = [np.concatenate([rA[b * 4 + qd]["v"] for qd in range(4)], 1) for b in range(B)]
    beta = [np.concatenate(
        [np.asarray(rA[b * 4 + qd]["beta_tm"], np.float32).T.reshape(-1)
         for qd in range(4)]) for b in range(B)]

    ident = np.eye(128, dtype=BF)
    nlmask = (-np.tril(np.ones((128, 128), np.float32), -1)).astype(BF)
    umaski = np.triu(np.ones((128, 128), np.float32), 0).astype(BF)
    al = f32("L0_attn_alpha_logit")
    imC = []
    for c in cores:
        b, s = c // 4, c % 4
        imC.append({
            "qt": np.ascontiguousarray(qf[b].reshape(NCH, 128, T // 128, 128)
                                       .transpose(2, 0, 1, 3)),
            "kt": np.ascontiguousarray(kf[b].reshape(NCH, 128, T // 128, 128)
                                       .transpose(2, 0, 1, 3)),
            "ktm": np.ascontiguousarray(kf[b].T.reshape(T // 128, 128, D)),
            "bcol": np.ascontiguousarray(beta[b].reshape(T // 128, 128).T),
            "va": np.ascontiguousarray(al[256 * s:256 * (s + 1)].reshape(2, 128).T),
            "vsl": np.ascontiguousarray(vf[b][256 * s:256 * (s + 1), :]),
            "ident": ident, "nlmask": nlmask, "umaski": umaski})
    rC = run_bass_kernel_spmd(_nc("C", build_launchC), imC, core_ids=cores).results
    attn = [np.concatenate([rC[b * 4 + s]["ao"] for s in range(4)], 0) for b in range(B)]

    n2w = _chunked(inp["L0_attn_norm2_w"])
    imD = []
    for c in cores:
        b, qd = c // 4, c % 4
        imD.append({"x": _fm(x[b, qd * TC:(qd + 1) * TC]),
                    "at": np.ascontiguousarray(attn[b][:, qd * TC:(qd + 1) * TC]),
                    "wo": f32("L0_attn_Wo"), "n2w": n2w,
                    "wg": f32("L0_attn_w_gate"), "wu": f32("L0_attn_w_up"),
                    "wd": f32("L0_attn_w_down")})
    rD = run_bass_kernel_spmd(_nc("D", build_launchD), imD, core_ids=cores).results

    b1 = f32("L0_ffn_b1")
    b2 = f32("L0_ffn_b2")
    b1c = np.ascontiguousarray(b1.reshape(8, 32, 128).transpose(2, 0, 1).reshape(128, 256))
    b2c = np.ascontiguousarray(b2.reshape(8, 8, 128).transpose(2, 0, 1).reshape(128, 64))
    imE = []
    for c in cores:
        imE.append({"z": np.asarray(rD[c]["z"], np.float32),
                    "wrt": np.ascontiguousarray(f32("L0_ffn_Wr").T),
                    "brr": np.ascontiguousarray(np.tile(f32("L0_ffn_br"), (128, 1))),
                    "w1t": np.ascontiguousarray(f32("L0_ffn_W1").transpose(0, 2, 1)),
                    "w2t": np.ascontiguousarray(f32("L0_ffn_W2").transpose(0, 2, 1)),
                    "b1c": b1c, "b2c": b2c, "ident": ident})
    rE = run_bass_kernel_spmd(_nc("E", build_launchE), imE, core_ids=cores).results

    perm = _rope_perm()
    wqt = np.ascontiguousarray(f32("L1_attn_W_Q").T[:, perm])
    wkt = np.ascontiguousarray(f32("L1_attn_W_K").T[:, perm])
    wvt = np.ascontiguousarray(f32("L1_attn_W_V").T)
    tabs = [_cos_sin(qd * TC) for qd in range(4)]
    imF = []
    for c in cores:
        b, qd = c // 4, c % 4
        imF.append({"z2": np.asarray(rE[c]["z2"], np.float32),
                    "wqt": wqt, "wkt": wkt, "wvt": wvt,
                    "cosb": tabs[qd][0], "sinb": tabs[qd][1]})
    rF = run_bass_kernel_spmd(_nc("F", build_launchF), imF, core_ids=cores).results

    khb, vhb = [], []
    for b in range(B):
        k_full = np.concatenate([rF[b * 4 + qd]["k"] for qd in range(4)], 1)
        v_full = np.concatenate([rF[b * 4 + qd]["v"] for qd in range(4)], 1)
        khb.append(np.ascontiguousarray(k_full.reshape(16, 64, T)))
        vhb.append(np.ascontiguousarray(v_full.reshape(16, 64, T).transpose(0, 2, 1)))

    wot = np.ascontiguousarray(f32("L1_attn_W_O").T)
    imG = []
    for c in cores:
        b, qd = c // 4, c % 4
        imG.append({"qr": np.asarray(rF[c]["q"]),
                    "kh": khb[b], "vh": vhb[b],
                    "x3": np.asarray(rF[c]["x3"], np.float32),
                    "wot": wot})
    rG = run_bass_kernel_spmd(_nc("G", build_launchG), imG, core_ids=cores).results

    wut = np.ascontiguousarray(f32("L1_ffn_w_up").T)
    wdt = np.ascontiguousarray(f32("L1_ffn_w_down").T)
    imH = [{"z3": np.asarray(rG[c]["z3"], np.float32), "wut": wut, "wdt": wdt}
           for c in cores]
    rH = run_bass_kernel_spmd(_nc("H", build_launchH), imH, core_ids=cores).results

    out = np.empty((B, T, D), np.float32)
    for c in cores:
        b, qd = c // 4, c % 4
        out[b, qd * TC:(qd + 1) * TC, :] = np.asarray(rH[c]["out"], np.float32).T
    return out


if __name__ == "__main__":
    pass


# version 05
# speedup vs baseline: 1.6932x; 1.0065x over previous
import contextlib
import numpy as np
import concourse.bass as bass
import concourse.mybir as mybir
from concourse.bass_utils import run_bass_kernel_spmd

F32 = mybir.dt.float32
BF16 = mybir.dt.bfloat16
U32 = mybir.dt.uint32
AF = mybir.ActivationFunctionType
ALU = mybir.AluOpType
AX = mybir.AxisListType

B, T, D = 2, 8192, 1024
TC = 2048
NCH = D // 128
NCORE = 8

ENGS = ("sync", "gpsimd", "tensor", "vector", "scalar")


class L:

    def __init__(self, name):
        self.nc = bass.Bass("TRN2", target_bir_lowering=False, name=name)
        self.es = contextlib.ExitStack()
        self.sem = {e: self.es.enter_context(self.nc.semaphore(f"sem_{e}"))
                    for e in ENGS}
        self.dsem = {e: self.es.enter_context(self.nc.semaphore(f"dma_{e}"))
                     for e in ("sync", "gpsimd", "scalar")}
        self._pfpool = [self.es.enter_context(self.nc.semaphore(f"pf{i}"))
                        for i in range(8)]
        self._pfcnt = [0] * 8
        self.pfmap = {}
        self.pftarget = {}
        self.cnt = {e: 0 for e in ENGS}
        self.dcnt = {e: 0 for e in self.dsem}
        self.seen = {e: {**{oe: 0 for oe in ENGS},
                         **{"d" + q: 0 for q in self.dsem}} for e in ENGS}
        self._phases = []

    def pfsem(self, key):
        if key not in self.pfmap:
            self.pfmap[key] = len(self.pfmap) % len(self._pfpool)
        return self.pfmap[key]

    def sbuf(self, name, shape, dt):
        return self.es.enter_context(self.nc.sbuf_tensor(name, shape, dt))

    def psum(self, name, shape, dt=F32):
        return self.es.enter_context(self.nc.psum_tensor(name, shape, dt))

    def inp(self, name, shape, dt=F32):
        return self.nc.dram_tensor(name, shape, dt, kind="ExternalInput").ap()

    def out(self, name, shape, dt=F32):
        return self.nc.dram_tensor(name, shape, dt, kind="ExternalOutput").ap()

    def dram(self, name, shape, dt=F32):
        return self.nc.dram_tensor(name, shape, dt, kind="Internal").ap()

    def dma(self, qname, eng, out, in_, pf=None, **kw):
        i = eng.dma_start(out=out, in_=in_, **kw)
        if pf is not None:
            idx = self.pfsem(pf)
            i.then_inc(self._pfpool[idx], 16)
            self._pfcnt[idx] += 16
            self.pftarget[pf] = self._pfcnt[idx]
            return self.pftarget[pf]
        i.then_inc(self.dsem[qname], 16)
        self.dcnt[qname] += 16
        return self.dcnt[qname]

    def wait_pf(self, eng, key):
        idx = self.pfmap[key]
        eng.wait_ge(self._pfpool[idx], self.pftarget[key])

    def phase(self, work):
        self._phases.append(work)

    def build(self):
        nc = self.nc
        with nc.Block() as blk:
            for work in self._phases:
                want = dict(self.cnt)
                dwant = dict(self.dcnt)
                for e in ENGS:
                    if e not in work:
                        continue
                    fn = work[e]

                    def section(eng, fn=fn, e=e, want=want, dwant=dwant):
                        sn = self.seen[e]
                        for oe in ENGS:
                            if oe != e and want[oe] > sn[oe]:
                                eng.wait_ge(self.sem[oe], want[oe])
                                sn[oe] = want[oe]
                        for oq, vv in dwant.items():
                            if vv > sn["d" + oq]:
                                eng.wait_ge(self.dsem[oq], vv)
                                sn["d" + oq] = vv
                        last = fn(eng, self)
                        if last is not None and hasattr(last, "then_inc"):
                            last.then_inc(self.sem[e], 1)
                        elif e in ("tensor", "sync"):
                            eng.nop().then_inc(self.sem[e], 1)
                        else:
                            eng.drain().then_inc(self.sem[e], 1)
                        self.cnt[e] += 1

                    getattr(blk, e)(section)
        return nc

    def close(self):
        self.es.close()



def colsum(t, ones_bf, src, ps_row, n, start, stop=False):
    i = None
    for j in range(0, n, 512):
        i = t.matmul(ps_row[0:1, j:j + 512], ones_bf[:, 0:1], src[:, j:j + 512],
                     start=start, stop=stop)
    return i


def bcast(t, ones_row_f32, row, ps_bc, n):
    i = None
    for j in range(0, n, 512):
        w = min(512, n - j)
        i = t.matmul(ps_bc[:, j:j + w], ones_row_f32[0:1, :], row[0:1, j:j + w],
                     start=True, stop=True)
    return i



def build_launchA(max_phases=None):
    l = L("launchA")
    x = l.inp("x", [D, TC], F32)
    n1w = l.inp("n1w", [128, NCH], F32)
    wq = l.inp("wq", [D, D], F32)
    wk = l.inp("wk", [D, D], F32)
    wv = l.inp("wv", [D, D], F32)
    bw = l.inp("bw", [128, NCH], F32)
    bb = l.inp("bb", [128, 1], F32)
    q_o = l.out("q", [D, TC], BF16)
    k_o = l.out("k", [D, TC], BF16)
    v_o = l.out("v", [D, TC], BF16)
    beta_o = l.out("beta_tm", [128, TC // 128], F32)

    h_sb = l.sbuf("h", [128, NCH * TC], BF16)
    xc = l.sbuf("xc", [128, TC], F32)
    sq = l.sbuf("sq", [128, TC], BF16)
    w_sb = [l.sbuf(f"w{i}", [128, NCH * D], BF16) for i in range(2)]
    out_sb = l.sbuf("outb", [128, NCH * TC], BF16)
    n1w_sb = l.sbuf("n1w_sb", [128, NCH], F32)
    bw_sb = l.sbuf("bw_sb", [128, NCH], BF16)
    bb_sb = l.sbuf("bb_sb", [128, 1], F32)
    ones_bf = l.sbuf("ones_bf", [128, 1], BF16)
    ones_row = l.sbuf("ones_row", [1, 128], F32)
    row_a = l.sbuf("row_a", [1, TC], F32)
    row_b = l.sbuf("row_b", [1, TC], F32)
    beta_sb = l.sbuf("beta_sb", [128, TC // 128], F32)
    cst = l.sbuf("cst", [128, 4], F32)

    u1 = l.psum("u1", [128, TC], F32)
    bps = l.psum("bps", [128, TC // 128], F32)
    pp = [l.psum(f"pp{i}", [128, 512], F32) for i in range(2)]

    wv_views = {
        "q": wq.rearrange("(c p) o -> p c o", p=128),
        "k": wk.rearrange("(c p) o -> p c o", p=128),
        "v": wv.rearrange("(c p) o -> p c o", p=128),
    }
    w_sb3 = [w_sb[i].ap().rearrange("p (c o) -> p c o", c=NCH) for i in range(2)]

    def p0_sync(e, l):
        l.dma("sync", e, n1w_sb[:], n1w)
        l.dma("sync", e, bb_sb[:], bb)

    def p0_gp(e, l):
        l.dma("gpsimd", e, w_sb3[0], wv_views["q"], pf="q")
        l.dma("gpsimd", e, bw_sb[:], bw)

    def p0_vec(e, l):
        e.memset(ones_bf[:], 1.0)
        e.memset(ones_row[:], 1.0)
        e.memset(cst[:, 0:1], 1e-6)
        e.memset(cst[:, 1:2], 1e-5)
        e.memset(cst[:, 2:3], 1e-8)

    l.phase({"sync": p0_sync, "gpsimd": p0_gp, "vector": p0_vec})

    for ci in range(NCH):
        l.phase({"sync": (lambda e, l, ci=ci:
                          l.dma("sync", e, xc[:], x[ci * 128:(ci + 1) * 128, :]))})
        l.phase({"scalar": (lambda e, l:
                            e.activation(out=sq[:], in_=xc[:], func=AF.Square))})
        l.phase({"tensor": (lambda e, l, ci=ci:
                            colsum(e, ones_bf, sq, u1, TC, start=(ci == 0),
                                   stop=(ci == NCH - 1)))})

    l.phase({"scalar": (lambda e, l:
                        e.activation(out=row_a[0:1, :], in_=u1[0:1, :],
                                     func=AF.Sqrt, scale=1.0 / D, bias=cst[0:1, 0:1]))})
    l.phase({"vector": (lambda e, l: e.reciprocal(row_b[0:1, :], row_a[0:1, :]))})
    l.phase({"tensor": (lambda e, l: bcast(e, ones_row, row_b, u1, TC))})

    for ci in range(NCH):
        l.phase({"sync": (lambda e, l, ci=ci:
                          l.dma("sync", e, xc[:], x[ci * 128:(ci + 1) * 128, :]))})
        l.phase({"vector": (lambda e, l, ci=ci:
                            e.scalar_tensor_tensor(
                                out=h_sb[:, ci * TC:(ci + 1) * TC], in0=xc[:],
                                scalar=n1w_sb[:, ci:ci + 1], in1=u1[:],
                                op0=ALU.mult, op1=ALU.mult))})

    def p_beta(e, l):
        i = None
        for tt in range(TC // 128):
            for ci in range(NCH):
                i = e.matmul(bps[:, tt:tt + 1],
                             h_sb[:, ci * TC + tt * 128: ci * TC + (tt + 1) * 128],
                             bw_sb[:, ci:ci + 1],
                             start=(tt == 0 and ci == 0),
                             stop=(tt == TC // 128 - 1 and ci == NCH - 1))
        return i
    l.phase({"tensor": p_beta})
    l.phase({"scalar": (lambda e, l:
                        e.activation(out=beta_sb[:], in_=bps[:], func=AF.Sigmoid,
                                     bias=bb_sb[:, 0:1]))})
    l.phase({"gpsimd": (lambda e, l: l.dma("gpsimd", e, beta_o, beta_sb[:]))})

    def proj(wbuf, pf_key, next_key):
        for do in range(NCH):
            for nt in range(TC // 512):
                ps = pp[(do * (TC // 512) + nt) % 2]

                def p_mm(e, l, do=do, nt=nt, ps=ps, first=(do == 0 and nt == 0)):
                    if first:
                        l.wait_pf(e, pf_key)
                    i = None
                    for ci in range(NCH):
                        i = e.matmul(ps[:],
                                     w_sb[wbuf][:, ci * D + do * 128:ci * D + (do + 1) * 128],
                                     h_sb[:, ci * TC + nt * 512:ci * TC + (nt + 1) * 512],
                                     start=(ci == 0), stop=(ci == NCH - 1))
                    return i
                work = {"tensor": p_mm}
                if do == 0 and nt == 0 and next_key is not None:
                    def p_pf(e, l, nk=next_key, wb=1 - wbuf):
                        l.dma("gpsimd", e, w_sb3[wb], wv_views[nk], pf=nk)
                    work["gpsimd"] = p_pf
                l.phase(work)
                l.phase({"scalar": (lambda e, l, do=do, nt=nt, ps=ps:
                                    e.copy(out=out_sb[:, do * TC + nt * 512:
                                                      do * TC + (nt + 1) * 512],
                                           in_=ps[:]))})

    def store(out_ap):
        def p_st(e, l):
            for ci in range(NCH):
                l.dma("gpsimd", e, out_ap[ci * 128:(ci + 1) * 128, :],
                      out_sb[:, ci * TC:(ci + 1) * TC])
        l.phase({"gpsimd": p_st})

    proj(0, "q", "k")
    store(q_o)

    proj(1, "k", "v")
    for ci in range(NCH):
        l.phase({"scalar": (lambda e, l, ci=ci:
                            e.activation(out=sq[:], in_=out_sb[:, ci * TC:(ci + 1) * TC],
                                         func=AF.Square))})
        l.phase({"tensor": (lambda e, l, ci=ci:
                            colsum(e, ones_bf, sq, u1, TC, start=(ci == 0),
                                   stop=(ci == NCH - 1)))})
    l.phase({"scalar": (lambda e, l:
                        e.activation(out=row_a[0:1, :], in_=u1[0:1, :], func=AF.Sqrt))})

    def p_knv(e, l):
        e.tensor_scalar_max(row_a[0:1, :], row_a[0:1, :], 1e-12)
        e.drain()
        e.reciprocal(row_b[0:1, :], row_a[0:1, :])
    l.phase({"vector": p_knv})
    l.phase({"tensor": (lambda e, l: bcast(e, ones_row, row_b, u1, TC))})

    def p_kmul(e, l):
        for ci in range(NCH):
            e.tensor_tensor(out=out_sb[:, ci * TC:(ci + 1) * TC],
                            in0=out_sb[:, ci * TC:(ci + 1) * TC],
                            in1=u1[:], op=ALU.mult)
    l.phase({"vector": p_kmul})
    store(k_o)

    proj(0, "v", None)
    store(v_o)

    def p_fin(e, l):
        e.wait_ge(l.dsem["gpsimd"], l.dcnt["gpsimd"])
    l.phase({"gpsimd": p_fin})

    if max_phases is not None:
        l._phases = l._phases[:max_phases]
    nc = l.build()
    l.close()
    return nc



def build_launchC(max_phases=None):
    l = L("launchC")
    NC64 = T // 128
    qt = l.inp("qt", [NC64, NCH, 128, 128], BF16)
    kt = l.inp("kt", [NC64, NCH, 128, 128], BF16)
    ktm = l.inp("ktm", [NC64, 128, D], BF16)
    bcol = l.inp("bcol", [128, NC64], F32)
    va = l.inp("va", [128, 2], F32)
    vsl = l.inp("vsl", [256, T], BF16)
    ident = l.inp("ident", [128, 128], BF16)
    nlmask = l.inp("nlmask", [128, 128], BF16)
    umaski = l.inp("umaski", [128, 128], BF16)
    ao = l.out("ao", [256, T], BF16)

    vts = l.sbuf("vts", [128, 2 * T], BF16)
    vtm = l.sbuf("vtm", [128, 2 * T], BF16)
    wT = l.sbuf("wT", [128, NC64 * 128], BF16)
    hS = l.sbuf("hS", [128, NC64 * 128], BF16)
    st_sb = l.sbuf("st_sb", [128, NCH * 256], BF16)
    kc = l.sbuf("kc", [128, D], BF16)
    qc = l.sbuf("qc", [128, D], BF16)
    kmc = l.sbuf("kmc", [128, D], BF16)
    rhs_sb = l.sbuf("rhs_sb", [128, 256], BF16)
    u_sb = l.sbuf("u_sb", [128, 256], BF16)
    d0 = l.sbuf("d0", [128, T], BF16)
    d1 = l.sbuf("d1", [128, T], BF16)
    sS = l.sbuf("sS", [128, 128], BF16)
    sT = l.sbuf("sT", [128, 128], BF16)
    sX = l.sbuf("sX", [128, 128], BF16)
    sY = l.sbuf("sY", [128, 128], BF16)
    bc_sb = l.sbuf("bc_sb", [128, NC64], F32)
    va_sb = l.sbuf("va_sb", [128, 2], F32)
    alt = l.sbuf("alt", [128, 2], F32)
    oma = l.sbuf("oma", [128, 2], F32)
    id_sb = l.sbuf("id_sb", [128, 128], BF16)
    nl_sb = l.sbuf("nl_sb", [128, 128], BF16)
    ui_sb = l.sbuf("ui_sb", [128, 128], BF16)

    st_ps = [l.psum(f"st{i}", [128, 512], F32) for i in range(4)]
    ps_b = l.psum("ps_b", [128, 512], F32)
    ps_c = l.psum("ps_c", [128, 512], F32)
    ps_t = l.psum("ps_t", [128, 512], BF16)

    ktv = kt.rearrange("c dc p t -> c p dc t")
    qtv = qt.rearrange("c dc p t -> c p dc t")
    kc3 = kc.ap().rearrange("p (dc t) -> p dc t", dc=NCH)
    qc3 = qc.ap().rearrange("p (dc t) -> p dc t", dc=NCH)
    vslv = vsl.rearrange("(j p) t -> j p t", p=128)
    aov = ao.rearrange("(j p) t -> j p t", p=128)

    def p0(e, l):
        l.dma("sync", e, vts[:, 0:T], vslv[0])
        l.dma("sync", e, vts[:, T:2 * T], vslv[1])
        l.dma("sync", e, bc_sb[:], bcol)
        l.dma("sync", e, va_sb[:], va)
        l.dma("sync", e, id_sb[:], ident)
        l.dma("sync", e, nl_sb[:], nlmask)
        l.dma("sync", e, ui_sb[:], umaski)
    l.phase({"sync": p0})

    l.phase({"scalar": (lambda e, l: e.activation(out=alt[:], in_=va_sb[:],
                                                  func=AF.Sigmoid))})
    def p_ema_v(e, l):
        e.tensor_scalar(out=oma[:], in0=alt[:], scalar1=-1.0, scalar2=1.0,
                        op0=ALU.mult, op1=ALU.add)
        e.drain()
        for j in range(2):
            e.tensor_scalar(out=d1[:], in0=vts[:, j * T:(j + 1) * T],
                            scalar1=alt[:, j:j + 1], scalar2=None, op0=ALU.mult)
            e.memset(d0[:], 1.0)
            e.drain()
            e.tensor_scalar(out=d0[:], in0=d0[:], scalar1=oma[:, j:j + 1],
                            scalar2=None, op0=ALU.mult)
            e.drain()
            e.tensor_tensor_scan(out=vts[:, j * T:(j + 1) * T], data0=d0[:],
                                 data1=d1[:], initial=0.0,
                                 op0=ALU.mult, op1=ALU.add)
            e.drain()
    l.phase({"vector": p_ema_v})

    for g in range(NC64 * 2 // 4):
        def p_tr(e, l, g=g):
            i = None
            for r in range(4):
                idx = g * 4 + r
                j, c = idx % 2, idx // 2
                i = e.transpose(ps_t[:, r * 128:(r + 1) * 128],
                                vts[:, j * T + c * 128: j * T + (c + 1) * 128],
                                id_sb[:])
            return i
        l.phase({"tensor": p_tr})

        def p_tre(e, l, g=g):
            for r in range(4):
                idx = g * 4 + r
                j, c = idx % 2, idx // 2
                e.copy(out=vtm[:, c * 256 + j * 128: c * 256 + (j + 1) * 128],
                       in_=ps_t[:, r * 128:(r + 1) * 128])
        l.phase({"scalar": p_tre})

    kc4 = d0.ap()
    qc4 = d1.ap()
    kcv = lambda cc: d0.ap()[:, cc * 1024:(cc + 1) * 1024]
    qcv = lambda cc: d0.ap()[:, 4096 + cc * 1024:4096 + (cc + 1) * 1024]
    sS4 = d1.ap()[:, 0:512]
    sT4 = d1.ap()[:, 512:1024]
    sX4 = d1.ap()[:, 1024:1536]
    sY4 = d1.ap()[:, 1536:2048]

    kc3g = [d0.ap()[:, cc * 1024:(cc + 1) * 1024]
            .rearrange("p (dc t) -> p dc t", dc=NCH) for cc in range(4)]
    qc3g = [d0.ap()[:, 4096 + cc * 1024:4096 + (cc + 1) * 1024]
            .rearrange("p (dc t) -> p dc t", dc=NCH) for cc in range(4)]

    for grp in range(NC64 // 4):
        def p_ld(e, l, grp=grp):
            for cc in range(4):
                l.dma("sync", e, kc3g[cc], ktv[grp * 4 + cc])
                l.dma("sync", e, qc3g[cc], qtv[grp * 4 + cc])
        l.phase({"sync": p_ld})

        def p_kk(e, l, grp=grp):
            i = None
            for cc in range(4):
                for dc in range(NCH):
                    e.matmul(ps_b[:, cc * 128:(cc + 1) * 128],
                             kcv(cc)[:, dc * 128:(dc + 1) * 128],
                             kcv(cc)[:, dc * 128:(dc + 1) * 128],
                             start=(dc == 0), stop=(dc == NCH - 1))
            for cc in range(4):
                for dc in range(NCH):
                    i = e.matmul(ps_c[:, cc * 128:(cc + 1) * 128],
                                 kcv(cc)[:, dc * 128:(dc + 1) * 128],
                                 qcv(cc)[:, dc * 128:(dc + 1) * 128],
                                 start=(dc == 0), stop=(dc == NCH - 1))
            return i
        l.phase({"tensor": p_kk})

        def p_xh(e, l, grp=grp):
            for cc in range(4):
                c = grp * 4 + cc
                e.scalar_tensor_tensor(out=sX4[:, cc * 128:(cc + 1) * 128],
                                       in0=ps_b[:, cc * 128:(cc + 1) * 128],
                                       scalar=bc_sb[:, c:c + 1], in1=nl_sb[:],
                                       op0=ALU.mult, op1=ALU.mult)
                e.drain()
                e.scalar_tensor_tensor(out=hS[:, c * 128:(c + 1) * 128],
                                       in0=ps_c[:, cc * 128:(cc + 1) * 128],
                                       scalar=1.0, in1=ui_sb[:],
                                       op0=ALU.mult, op1=ALU.mult)
                e.drain()
        l.phase({"vector": p_xh})

        def p_yt(e, l):
            i = None
            for cc in range(4):
                i = e.transpose(ps_t[:, cc * 128:(cc + 1) * 128],
                                sX4[:, cc * 128:(cc + 1) * 128], id_sb[:])
            return i
        l.phase({"tensor": p_yt})

        def p_ye(e, l):
            for cc in range(4):
                e.copy(out=sY4[:, cc * 128:(cc + 1) * 128],
                       in_=ps_t[:, cc * 128:(cc + 1) * 128])
        l.phase({"scalar": p_ye})

        def p_i1(e, l):
            i = None
            for cc in range(4):
                e.matmul(ps_c[:, cc * 128:(cc + 1) * 128],
                         sY4[:, cc * 128:(cc + 1) * 128],
                         sX4[:, cc * 128:(cc + 1) * 128], start=True, stop=True)
                i = e.matmul(ps_b[:, cc * 128:(cc + 1) * 128],
                             sX4[:, cc * 128:(cc + 1) * 128],
                             sY4[:, cc * 128:(cc + 1) * 128], start=True, stop=True)
            return i
        l.phase({"tensor": p_i1})

        def p_i2(e, l):
            for cc in range(4):
                e.tensor_tensor(out=sS4[:, cc * 128:(cc + 1) * 128],
                                in0=sX4[:, cc * 128:(cc + 1) * 128],
                                in1=id_sb[:], op=ALU.add)
                e.drain()
                e.tensor_tensor(out=sT4[:, cc * 128:(cc + 1) * 128],
                                in0=sY4[:, cc * 128:(cc + 1) * 128],
                                in1=id_sb[:], op=ALU.add)
                e.drain()
        l.phase({"vector": p_i2})

        def p_i3(e, l):
            for cc in range(4):
                e.copy(out=sX4[:, cc * 128:(cc + 1) * 128],
                       in_=ps_c[:, cc * 128:(cc + 1) * 128])
                e.drain()
                e.copy(out=sY4[:, cc * 128:(cc + 1) * 128],
                       in_=ps_b[:, cc * 128:(cc + 1) * 128])
                e.drain()
        l.phase({"scalar": p_i3})

        for it in range(5):
            def p_mm4(e, l):
                i = None
                for cc in range(4):
                    sl = slice(cc * 128, (cc + 1) * 128)
                    e.matmul(ps_b[:, sl], sY4[:, sl], sS4[:, sl], start=True, stop=True)
                    e.matmul(ps_c[:, sl], sS4[:, sl], sY4[:, sl], start=True, stop=True)
                    e.matmul(st_ps[0][:, sl], sY4[:, sl], sX4[:, sl], start=True, stop=True)
                    i = e.matmul(st_ps[1][:, sl], sX4[:, sl], sY4[:, sl],
                                 start=True, stop=True)
                return i
            l.phase({"tensor": p_mm4})

            def p_ev4(e, l):
                for cc in range(4):
                    sl = slice(cc * 128, (cc + 1) * 128)
                    e.tensor_tensor(out=sS4[:, sl], in0=sS4[:, sl], in1=ps_b[:, sl],
                                    op=ALU.add)
                    e.drain()
                    e.tensor_tensor(out=sT4[:, sl], in0=sT4[:, sl], in1=ps_c[:, sl],
                                    op=ALU.add)
                    e.drain()
            l.phase({"vector": p_ev4})

            def p_ev4b(e, l):
                for cc in range(4):
                    sl = slice(cc * 128, (cc + 1) * 128)
                    e.copy(out=sX4[:, sl], in_=st_ps[0][:, sl])
                    e.drain()
                    e.copy(out=sY4[:, sl], in_=st_ps[1][:, sl])
                    e.drain()
            l.phase({"scalar": p_ev4b})

        def p_f1(e, l):
            i = None
            for cc in range(4):
                sl = slice(cc * 128, (cc + 1) * 128)
                i = e.matmul(ps_b[:, sl], sS4[:, sl], sY4[:, sl], start=True, stop=True)
            return i
        l.phase({"tensor": p_f1})

        def p_f2(e, l, grp=grp):
            for cc in range(4):
                c = grp * 4 + cc
                sl = slice(cc * 128, (cc + 1) * 128)
                e.tensor_tensor(out=wT[:, c * 128:(c + 1) * 128],
                                in0=sT4[:, sl], in1=ps_b[:, sl], op=ALU.add)
                e.drain()
        l.phase({"vector": p_f2})

    def p_z(e, l):
        e.memset(st_sb[:], 0.0)
    l.phase({"vector": p_z})

    for c in range(NC64):
        def p_ld(e, l, c=c):
            l.dma("sync", e, kc3, ktv[c])
            l.dma("sync", e, qc3, qtv[c])
            l.dma("sync", e, kmc[:], ktm[c])
        l.phase({"sync": p_ld})

        def p_ks(e, l, c=c):
            i = None
            for dc in range(NCH):
                i = e.matmul(ps_b[:, 0:256], kc[:, dc * 128:(dc + 1) * 128],
                             st_sb[:, dc * 256:(dc + 1) * 256],
                             start=(dc == 0), stop=(dc == NCH - 1))
            return i
        l.phase({"tensor": p_ks})

        def p_rhs(e, l, c=c):
            e.scalar_tensor_tensor(out=rhs_sb[:], in0=ps_b[:, 0:256],
                                   scalar=-1.0, in1=vtm[:, c * 256:(c + 1) * 256],
                                   op0=ALU.mult, op1=ALU.add)
            e.drain()
            e.tensor_scalar(out=rhs_sb[:], in0=rhs_sb[:],
                            scalar1=bc_sb[:, c:c + 1], scalar2=None, op0=ALU.mult)
        l.phase({"vector": p_rhs})

        l.phase({"tensor": (lambda e, l, c=c:
                            e.matmul(ps_b[:, 256:512], wT[:, c * 128:(c + 1) * 128],
                                     rhs_sb[:], start=True, stop=True))})
        l.phase({"scalar": (lambda e, l: e.copy(out=u_sb[:], in_=ps_b[:, 256:512]))})

        def p_ost(e, l, c=c):
            for ic in range(2):
                for dc in range(NCH):
                    e.matmul(ps_c[:, ic * 128:(ic + 1) * 128],
                             st_sb[:, dc * 256 + ic * 128: dc * 256 + (ic + 1) * 128],
                             qc[:, dc * 128:(dc + 1) * 128],
                             start=(dc == 0), stop=False)
                e.matmul(ps_c[:, ic * 128:(ic + 1) * 128],
                         u_sb[:, ic * 128:(ic + 1) * 128],
                         hS[:, c * 128:(c + 1) * 128], start=False, stop=True)
            i = None
            for j in range(NCH):
                i = e.matmul(st_ps[j // 2][:, (j % 2) * 256:(j % 2) * 256 + 256],
                             kmc[:, j * 128:(j + 1) * 128], u_sb[:],
                             start=(c == 0 and j % 2 == 0), stop=(j % 2 == 1),
                             skip_group_check=True)
            return i
        l.phase({"tensor": p_ost})

        def p_ev(e, l, c=c):
            for ic in range(2):
                e.copy(out=vts[:, ic * T + c * 128: ic * T + (c + 1) * 128],
                       in_=ps_c[:, ic * 128:(ic + 1) * 128])
            e.drain()
            for j in range(NCH):
                e.copy(out=st_sb[:, j * 256:(j + 1) * 256],
                       in_=st_ps[j // 2][:, (j % 2) * 256:(j % 2) * 256 + 256])
        l.phase({"scalar": p_ev})

    def p_st(e, l):
        for j in range(2):
            l.dma("gpsimd", e, aov[j], vts[:, j * T:(j + 1) * T])
        e.wait_ge(l.dsem["gpsimd"], l.dcnt["gpsimd"])
    l.phase({"gpsimd": p_st})

    if max_phases is not None:
        l._phases = l._phases[:max_phases]
    nc = l.build()
    l.close()
    return nc



def emit_ln_start(l, z_in, xb_sb, zc, sq, u1, ones_bf, ones_row, rows, cst_eps,
                  after_chunk=None):
    r0, r1 = rows
    def p_zl(e, l):
        for ci in range(NCH):
            l.dma("gpsimd", e, xb_sb[:, ci * TC:(ci + 1) * TC],
                  z_in[ci * 128:(ci + 1) * 128, :])
    l.phase({"gpsimd": p_zl})
    for ci in range(NCH):
        l.phase({"tensor": (lambda e, l, ci=ci:
                            colsum(e, ones_bf, xb_sb[:, ci * TC:(ci + 1) * TC],
                                   u1, TC, start=(ci == 0), stop=(ci == NCH - 1)))})
    l.phase({"scalar": (lambda e, l:
                        e.activation(out=r0[0:1, :], in_=u1[0:1, :],
                                     func=AF.Identity, scale=1.0 / D))})
    for ci in range(NCH):
        for hf in range(4):
            l.phase({"scalar": (lambda e, l, ci=ci, hf=hf:
                                e.activation(out=sq[:, 0:512],
                                             in_=xb_sb[:, ci * TC + hf * 512:
                                                       ci * TC + (hf + 1) * 512],
                                             func=AF.Square))})

            def p_cs(e, l, ci=ci, hf=hf):
                return e.matmul(u1[0:1, hf * 512:(hf + 1) * 512],
                                ones_bf[:, 0:1], sq[:, 0:512],
                                start=(ci == 0), stop=(ci == NCH - 1))
            l.phase({"tensor": p_cs})
    def p_rows_s(e, l):
        e.activation(out=sq[0:1, 0:TC], in_=r0[0:1, :], func=AF.Square)
    def p_rows_v(e, l):
        e.tensor_tensor(out=r1[0:1, :], in0=r0[0:1, :], in1=r0[0:1, :], op=ALU.mult)
    l.phase({"vector": p_rows_v})

    def p_rows_s2(e, l):
        pass
    def p_var_v(e, l):
        e.scalar_tensor_tensor(out=r1[0:1, :], in0=u1[0:1, :], scalar=1.0 / D,
                               in1=r1[0:1, :], op0=ALU.mult, op1=ALU.subtract)
    l.phase({"vector": p_var_v})
    l.phase({"scalar": (lambda e, l:
                        e.activation(out=r1[0:1, :], in_=r1[0:1, :], func=AF.Sqrt,
                                     bias=cst_eps))})
    l.phase({"vector": (lambda e, l: e.reciprocal(r1[0:1, :], r1[0:1, :]))})
    l.phase({"tensor": (lambda e, l: bcast(e, ones_row, r0, u1, TC))})
    for ci in range(NCH):
        l.phase({"vector": (lambda e, l, ci=ci:
                            e.scalar_tensor_tensor(
                                out=xb_sb[:, ci * TC:(ci + 1) * TC],
                                in0=xb_sb[:, ci * TC:(ci + 1) * TC], scalar=1.0,
                                in1=u1[:], op0=ALU.mult, op1=ALU.subtract))})
    l.phase({"tensor": (lambda e, l: bcast(e, ones_row, r1, u1, TC))})
    for ci in range(NCH):
        l.phase({"vector": (lambda e, l, ci=ci:
                            e.tensor_tensor(out=xb_sb[:, ci * TC:(ci + 1) * TC],
                                            in0=xb_sb[:, ci * TC:(ci + 1) * TC],
                                            in1=u1[:], op=ALU.mult))})
        if after_chunk is not None:
            w = after_chunk(ci)
            if w:
                l.phase(w)


def emit_proj(l, pf_key, next_pf, w_sb_cur, w_sb3_next, wview_next, h_sb, out_sb, pp):
    for do in range(NCH):
        for nt in range(TC // 512):
            ps = pp[(do * (TC // 512) + nt) % 2]

            def p_mm(e, l, do=do, nt=nt, ps=ps, first=(do == 0 and nt == 0)):
                if first and pf_key is not None:
                    l.wait_pf(e, pf_key)
                i = None
                for ci in range(NCH):
                    i = e.matmul(ps[:],
                                 w_sb_cur[:, ci * D + do * 128:ci * D + (do + 1) * 128],
                                 h_sb[:, ci * TC + nt * 512:ci * TC + (nt + 1) * 512],
                                 start=(ci == 0), stop=(ci == NCH - 1))
                return i
            work = {"tensor": p_mm}
            if do == 0 and nt == 0 and next_pf is not None:
                def p_pf(e, l, nk=next_pf):
                    l.dma("gpsimd", e, w_sb3_next, wview_next, pf=nk)
                work["gpsimd"] = p_pf
            l.phase(work)
            l.phase({"scalar": (lambda e, l, do=do, nt=nt, ps=ps:
                                e.copy(out=out_sb[:, do * TC + nt * 512:
                                                  do * TC + (nt + 1) * 512],
                                       in_=ps[:]))})



def build_launchD(max_phases=None):
    l = L("launchD")
    x = l.inp("x", [D, TC], F32)
    at = l.inp("at", [D, TC], BF16)
    wo = l.inp("wo", [D, D], F32)
    n2w = l.inp("n2w", [128, NCH], F32)
    wg = l.inp("wg", [D, 2048], F32)
    wu = l.inp("wu", [D, 2048], F32)
    wd = l.inp("wd", [2048, D], F32)
    z_o = l.out("z", [D, TC], F32)

    h2_sb = l.sbuf("h2", [128, NCH * TC], BF16)
    x1_sb = l.sbuf("x1", [128, NCH * TC], BF16)
    gu_sb = l.sbuf("gu", [128, 8192], BF16)
    wo_sb = l.sbuf("wo_sb", [128, NCH * D], BF16)
    wg_sb = l.sbuf("wg_sb", [128, NCH * 2048], BF16)
    wu_sb = l.sbuf("wu_sb", [128, NCH * 2048], BF16)
    wd_sb = l.sbuf("wd_sb", [128, 16 * D], BF16)
    n2w_sb = l.sbuf("n2w_sb", [128, NCH], F32)
    xc = l.sbuf("xc", [128, 512], F32)
    zt = l.sbuf("zt", [128, 256], F32)
    sq = l.sbuf("sq", [128, TC], BF16)
    ones_bf = l.sbuf("ones_bf", [128, 1], BF16)
    ones_row = l.sbuf("ones_row", [1, 128], F32)
    r0 = l.sbuf("r0", [1, TC], F32)
    cst = l.sbuf("cst", [128, 1], F32)

    u1 = l.psum("u1", [128, TC], F32)
    pp = [l.psum(f"pp{i}", [128, 512], F32) for i in range(2)]

    at_v = at.rearrange("(c p) t -> p c t", p=128)
    h2_3 = h2_sb.ap().rearrange("p (c t) -> p c t", c=NCH)
    wo_v = wo.rearrange("(c p) o -> p c o", p=128)
    wo_3 = wo_sb.ap().rearrange("p (c o) -> p c o", c=NCH)
    wg_v = wg.rearrange("(c p) o -> p c o", p=128)
    wg_3 = wg_sb.ap().rearrange("p (c o) -> p c o", c=NCH)
    wu_v = wu.rearrange("(c p) o -> p c o", p=128)
    wu_3 = wu_sb.ap().rearrange("p (c o) -> p c o", c=NCH)
    wd_v = wd.rearrange("(c p) o -> p c o", p=128)
    wd_3 = wd_sb.ap().rearrange("p (c o) -> p c o", c=16)

    def p0(e, l):
        l.dma("sync", e, h2_3, at_v)
        l.dma("sync", e, n2w_sb[:], n2w)
    def p0g(e, l):
        l.dma("gpsimd", e, wo_3, wo_v, pf="wo")
        l.dma("gpsimd", e, wg_3, wg_v, pf="wg")
        l.dma("gpsimd", e, wu_3, wu_v, pf="wu")
        l.dma("gpsimd", e, wd_3, wd_v, pf="wd")
    def p0v(e, l):
        e.memset(ones_bf[:], 1.0)
        e.memset(ones_row[:], 1.0)
        e.memset(cst[:], 1e-6)
    l.phase({"sync": p0, "gpsimd": p0g, "vector": p0v})

    for do in range(NCH):
        for nt in range(TC // 512):
            ps = pp[(do * 4 + nt) % 2]

            def p_mm(e, l, do=do, nt=nt, ps=ps, first=(do == 0 and nt == 0)):
                if first:
                    l.wait_pf(e, "wo")
                i = None
                for ci in range(NCH):
                    i = e.matmul(ps[:],
                                 wo_sb[:, ci * D + do * 128:ci * D + (do + 1) * 128],
                                 h2_sb[:, ci * TC + nt * 512:ci * TC + (nt + 1) * 512],
                                 start=(ci == 0), stop=(ci == NCH - 1))
                return i

            def p_xc(e, l, do=do, nt=nt):
                l.dma("sync", e, xc[:], x[do * 128:(do + 1) * 128, nt * 512:(nt + 1) * 512])
            l.phase({"tensor": p_mm, "sync": p_xc})
            l.phase({"vector": (lambda e, l, do=do, nt=nt, ps=ps:
                                e.scalar_tensor_tensor(
                                    out=x1_sb[:, do * TC + nt * 512:do * TC + (nt + 1) * 512],
                                    in0=ps[:], scalar=1.0, in1=xc[:],
                                    op0=ALU.mult, op1=ALU.add))})

    for ci in range(NCH):
        l.phase({"scalar": (lambda e, l, ci=ci:
                            e.activation(out=sq[:], in_=x1_sb[:, ci * TC:(ci + 1) * TC],
                                         func=AF.Square))})
        l.phase({"tensor": (lambda e, l, ci=ci:
                            colsum(e, ones_bf, sq, u1, TC, start=(ci == 0),
                                   stop=(ci == NCH - 1)))})
    l.phase({"scalar": (lambda e, l:
                        e.activation(out=r0[0:1, :], in_=u1[0:1, :], func=AF.Sqrt,
                                     scale=1.0 / D, bias=cst[0:1, 0:1]))})
    l.phase({"vector": (lambda e, l: e.reciprocal(r0[0:1, :], r0[0:1, :]))})
    l.phase({"tensor": (lambda e, l: bcast(e, ones_row, r0, u1, TC))})
    for ci in range(NCH):
        l.phase({"vector": (lambda e, l, ci=ci:
                            e.scalar_tensor_tensor(
                                out=h2_sb[:, ci * TC:(ci + 1) * TC],
                                in0=x1_sb[:, ci * TC:(ci + 1) * TC],
                                scalar=n2w_sb[:, ci:ci + 1], in1=u1[:],
                                op0=ALU.mult, op1=ALU.mult))})

    NS = 256
    for s in range(TC // NS):
        for fc in range(16):
            def p_mm(e, l, s=s, fc=fc, first=(s == 0 and fc == 0)):
                if first:
                    l.wait_pf(e, "wg")
                    l.wait_pf(e, "wu")
                for ci in range(NCH):
                    e.matmul(pp[0][:, 0:NS],
                             wg_sb[:, ci * 2048 + fc * 128:ci * 2048 + (fc + 1) * 128],
                             h2_sb[:, ci * TC + s * NS:ci * TC + (s + 1) * NS],
                             start=(ci == 0), stop=(ci == NCH - 1))
                i = None
                for ci in range(NCH):
                    i = e.matmul(pp[1][:, 0:NS],
                                 wu_sb[:, ci * 2048 + fc * 128:ci * 2048 + (fc + 1) * 128],
                                 h2_sb[:, ci * TC + s * NS:ci * TC + (s + 1) * NS],
                                 start=(ci == 0), stop=(ci == NCH - 1))
                return i
            l.phase({"tensor": p_mm})

            def p_ev(e, l, fc=fc):
                e.activation(out=gu_sb[:, fc * NS:(fc + 1) * NS], in_=pp[0][:, 0:NS],
                             func=AF.Silu)
            def p_ev2(e, l, fc=fc):
                e.tensor_copy(gu_sb[:, 4096 + fc * NS:4096 + (fc + 1) * NS],
                              pp[1][:, 0:NS])
            l.phase({"scalar": p_ev, "vector": p_ev2})

        l.phase({"vector": (lambda e, l:
                            e.tensor_tensor(out=gu_sb[:, 0:4096], in0=gu_sb[:, 0:4096],
                                            in1=gu_sb[:, 4096:8192], op=ALU.mult))})
        for do in range(NCH):
            def p_mm(e, l, s=s, do=do, first=(s == 0 and do == 0)):
                if first:
                    l.wait_pf(e, "wd")
                i = None
                for fc in range(16):
                    i = e.matmul(pp[0][:, 0:NS],
                                 wd_sb[:, fc * D + do * 128:fc * D + (do + 1) * 128],
                                 gu_sb[:, fc * NS:(fc + 1) * NS],
                                 start=(fc == 0), stop=(fc == 15))
                return i

            def p_xc(e, l, s=s, do=do):
                l.dma("sync", e, xc[:, 0:NS],
                      x[do * 128:(do + 1) * 128, s * NS:(s + 1) * NS])
            l.phase({"tensor": p_mm, "sync": p_xc})

            def p_z(e, l, s=s, do=do):
                e.scalar_tensor_tensor(
                    out=zt[:], in0=pp[0][:, 0:NS], scalar=1.0,
                    in1=x1_sb[:, do * TC + s * NS:do * TC + (s + 1) * NS],
                    op0=ALU.mult, op1=ALU.add)
                e.drain()
                e.tensor_tensor(out=zt[:], in0=zt[:], in1=xc[:, 0:NS], op=ALU.add)
            l.phase({"vector": p_z})
            l.phase({"gpsimd": (lambda e, l, s=s, do=do:
                                l.dma("gpsimd", e,
                                      z_o[do * 128:(do + 1) * 128, s * NS:(s + 1) * NS],
                                      zt[:]))})

    def p_fin(e, l):
        e.wait_ge(l.dsem["gpsimd"], l.dcnt["gpsimd"])
    l.phase({"gpsimd": p_fin})

    if max_phases is not None:
        l._phases = l._phases[:max_phases]
    nc = l.build()
    l.close()
    return nc



def build_launchE(max_phases=None):
    l = L("launchE")
    z_i = l.inp("z", [D, TC], F32)
    wrt = l.inp("wrt", [D, 8], F32)
    brr = l.inp("brr", [128, 8], F32)
    w1t = l.inp("w1t", [8, D, 4096], F32)
    w2t = l.inp("w2t", [8, 4096, D], F32)
    b1c = l.inp("b1c", [128, 8 * 32], F32)
    b2c = l.inp("b2c", [128, 8 * 8], F32)
    ident = l.inp("ident", [128, 128], BF16)
    z2_o = l.out("z2", [D, TC], F32)

    xb = l.sbuf("xb", [128, NCH * TC], BF16)
    x2s = l.sbuf("x2s", [128, NCH * 512], BF16)
    msk = l.sbuf("msk", [128, 512], BF16)
    mrow = l.sbuf("mrow", [1, 512], BF16)
    lg = l.sbuf("lg", [128, 128], F32)
    mx = l.sbuf("mx", [128, 8], F32)
    oh = msk.ap()[:, 0:128]
    wrt_sb = l.sbuf("wrt_sb", [128, NCH * 8], BF16)
    brr_sb = l.sbuf("brr_sb", [128, 8], F32)
    b1_sb = l.sbuf("b1_sb", [128, 8 * 32], F32)
    b2_sb = l.sbuf("b2_sb", [128, 8 * 8], F32)
    id_sb = l.sbuf("id_sb", [128, 128], BF16)
    w1_sb = w2_sb = h_sb = None
    w1_3 = w2_3 = None
    zt2 = l.sbuf("zt2", [128, 128], F32)
    sq = l.sbuf("sq", [128, 512], BF16)
    ones_bf = l.sbuf("ones_bf", [128, 1], BF16)
    ones_row = l.sbuf("ones_row", [1, 128], F32)
    ones_rb = l.sbuf("ones_rb", [1, 128], BF16)
    tmp_bf = l.sbuf("tmp_bf", [128, 512], BF16)
    cst = l.sbuf("cst", [128, 1], F32)
    r0cm = l.nc.sbuf_tensor("r0e", [1, TC], F32)
    r0 = r0cm.__enter__()
    r1cm = l.nc.sbuf_tensor("r1e", [1, TC], F32)
    r1 = r1cm.__enter__()

    u1 = l.psum("u1", [128, TC], F32)
    pp = [l.psum(f"pp{i}", [128, 512], F32) for i in range(2)]
    bps = l.psum("bps", [128, 128], F32)
    msps = l.psum("msps", [128, 512], BF16)

    x2d = l.dram("x2d", [D, TC], BF16)
    x2dv = x2d.rearrange("(c p) t -> p c t", p=128)
    masks_d = l.dram("masks_d", [8, TC], BF16)

    wrt_v = wrt.rearrange("(c p) o -> p c o", p=128)
    wrt_3 = wrt_sb.ap().rearrange("p (c o) -> p c o", c=NCH)
    w1v = w1t.rearrange("e (c p) f -> e p c f", p=128)
    w2v = w2t.rearrange("e (c p) o -> e p c o", p=128)

    def p0(e, l):
        l.dma("sync", e, brr_sb[:], brr)
        l.dma("sync", e, b1_sb[:], b1c)
        l.dma("sync", e, b2_sb[:], b2c)
        l.dma("sync", e, id_sb[:], ident)
    def p0g(e, l):
        l.dma("gpsimd", e, wrt_3, wrt_v)
    def p0v(e, l):
        e.memset(ones_bf[:], 1.0)
        e.memset(ones_row[:], 1.0)
        e.memset(ones_rb[:], 1.0)
        e.memset(cst[:], 1e-5)
    l.phase({"sync": p0, "gpsimd": p0g, "vector": p0v})

    def after_chunk(ci):
        def p_lg(e, l, ci=ci):
            i = None
            for tt in range(TC // 128):
                i = e.matmul(bps[:, tt * 8:(tt + 1) * 8],
                             xb[:, ci * TC + tt * 128:ci * TC + (tt + 1) * 128],
                             wrt_sb[:, ci * 8:(ci + 1) * 8],
                             start=(ci == 0 and tt == 0),
                             stop=(ci == NCH - 1 and tt == TC // 128 - 1),
                             skip_group_check=True)
            return i
        return {"tensor": p_lg,
                "gpsimd": (lambda e, l, ci=ci:
                           l.dma("gpsimd", e, x2d[ci * 128:(ci + 1) * 128, :],
                                 xb[:, ci * TC:(ci + 1) * TC]))}
    emit_ln_start(l, z_i, xb, None, sq, u1, ones_bf, ones_row, (r0, r1),
                  cst[0:1, 0:1], after_chunk=after_chunk)

    def p_lgf(e, l):
        e.tensor_copy(lg[:], bps[:])
        e.drain()
        for tt in range(TC // 128):
            e.tensor_tensor(out=lg[:, tt * 8:(tt + 1) * 8],
                            in0=lg[:, tt * 8:(tt + 1) * 8], in1=brr_sb[:],
                            op=ALU.add)
    l.phase({"vector": p_lgf})
    for tt in range(TC // 128):
        def p_oh(e, l, tt=tt):
            e.max(mx[:], lg[:, tt * 8:(tt + 1) * 8])
            e.drain()
            e.tensor_scalar(out=oh[:, tt * 8:(tt + 1) * 8],
                            in0=lg[:, tt * 8:(tt + 1) * 8],
                            scalar1=mx[:, 0:1], scalar2=None, op0=ALU.is_equal)
        l.phase({"vector": p_oh})
    for g in range(TC // 128 * 8 // 4):
        def p_tr(e, l, g=g):
            i = None
            for r in range(4):
                idx = g * 4 + r
                tt, ee = idx // 8, idx % 8
                i = e.transpose(msps[0:1, r * 128:(r + 1) * 128],
                                oh[:, tt * 8 + ee:tt * 8 + ee + 1], id_sb[:])
            return i
        l.phase({"tensor": p_tr})

        def p_cp(e, l, g=g):
            e.copy(out=mrow[0:1, :], in_=msps[0:1, 0:512])
        l.phase({"scalar": p_cp})

        def p_md(e, l, g=g):
            idx = g * 4
            tt, ee = idx // 8, idx % 8
            l.dma("gpsimd", e,
                  masks_d.rearrange("(o e) t -> o e t", o=1)[0:1, ee:ee + 4,
                                                            tt * 128:(tt + 1) * 128],
                  mrow.ap().rearrange("p (a t) -> p a t", a=4))
        l.phase({"gpsimd": p_md})

    r1cm.__exit__(None, None, None)
    r0cm.__exit__(None, None, None)
    w1_sb = l.sbuf("w1_sb", [128, NCH * 4096], BF16)
    w2_sb = l.sbuf("w2_sb", [128, 32 * D], BF16)
    h_sb = l.sbuf("h_sb", [128, 32 * 512], BF16)
    w1_3 = w1_sb.ap().rearrange("p (c f) -> p c f", c=NCH)
    w2_3 = w2_sb.ap().rearrange("p (c o) -> p c o", c=32)

    l.phase({"vector": (lambda e, l: e.memset(xb[:], 0.0))})

    SS = 512
    et_sem = l.es.enter_context(l.nc.semaphore("e_t"))
    es_sem = l.es.enter_context(l.nc.semaphore("e_s"))
    ev_sem = l.es.enter_context(l.nc.semaphore("e_v"))
    l._e_t = 0
    l._e_s = 0
    l._e_v = 0
    ACC = [pp[0][:, 0:SS], pp[1][:, 0:SS], u1[:, 0:SS], u1[:, 512:512 + SS]]

    def p_pf0(e, l):
        l.dma("gpsimd", e, w1_3, w1v[0], pf="w1_0")
    l.phase({"gpsimd": p_pf0})
    for ee in range(8):
        for s in range(TC // SS):
            def p_xs(e, l, s=s):
                l.dma("sync", e, x2s.ap().rearrange("p (c t) -> p c t", c=NCH),
                      x2dv[:, :, s * SS:(s + 1) * SS])
            def p_mr(e, l, ee=ee, s=s):
                l.dma("sync", e, mrow[0:1, 0:SS],
                      masks_d[ee:ee + 1, s * SS:(s + 1) * SS])
            l.phase({"sync": (lambda e, l, a=p_xs, b=p_mr: (a(e, l), b(e, l)) and None)})

            def p_mbc(e, l):
                return e.matmul(u1[:, 1024:1024 + SS], ones_rb[0:1, :],
                                mrow[0:1, 0:SS], start=True, stop=True)
            l.phase({"tensor": p_mbc})
            l.phase({"scalar": (lambda e, l:
                                e.copy(out=msk[:, 0:SS], in_=u1[:, 1024:1024 + SS]))})

            def loopT1(e, l, ee=ee, s=s):
                s0, v0 = l._e_s, l._e_v
                if s == 0:
                    l.wait_pf(e, f"w1_{ee}")
                last = None
                for fc in range(32):
                    if fc >= 4:
                        pv = fc - 4
                        if pv % 2 == 0:
                            e.wait_ge(es_sem, s0 + pv // 2 + 1)
                        else:
                            e.wait_ge(ev_sem, v0 + pv // 2 + 1)
                    for ci in range(NCH):
                        last = e.matmul(ACC[fc % 4],
                                        w1_sb[:, ci * 4096 + fc * 128:
                                              ci * 4096 + (fc + 1) * 128],
                                        x2s[:, ci * SS:(ci + 1) * SS],
                                        start=(ci == 0), stop=(ci == NCH - 1))
                    last.then_inc(et_sem, 1)
                    l._e_t += 1
                return None

            def loopS1(e, l, ee=ee):
                t0 = l._e_t - 32
                for fc in range(0, 32, 2):
                    e.wait_ge(et_sem, t0 + fc + 1)
                    e.activation(out=h_sb[:, fc * SS:(fc + 1) * SS], in_=ACC[fc % 4],
                                 func=AF.Relu,
                                 bias=b1_sb[:, ee * 32 + fc:ee * 32 + fc + 1]
                                 ).then_inc(es_sem, 1)
                    l._e_s += 1
                return None

            def loopV1(e, l, ee=ee):
                t0 = l._e_t - 32
                for fc in range(1, 32, 2):
                    e.wait_ge(et_sem, t0 + fc + 1)
                    i = e.tensor_scalar(out=h_sb[:, fc * SS:(fc + 1) * SS],
                                        in0=ACC[fc % 4],
                                        scalar1=b1_sb[:, ee * 32 + fc:ee * 32 + fc + 1],
                                        scalar2=0.0, op0=ALU.add, op1=ALU.max)
                    e.drain()
                    i.then_inc(ev_sem, 1)
                    l._e_v += 1
                return None

            if s == 0:
                def p_pf2(e, l, ee=ee):
                    l.dma("gpsimd", e, w2_3, w2v[ee], pf=f"w2_{ee}")
                l.phase({"tensor": loopT1, "scalar": loopS1, "vector": loopV1,
                         "gpsimd": p_pf2})
            else:
                l.phase({"tensor": loopT1, "scalar": loopS1, "vector": loopV1})

            def loopT2(e, l, ee=ee, s=s):
                v0 = l._e_v
                if s == 0:
                    l.wait_pf(e, f"w2_{ee}")
                last = None
                for do in range(NCH):
                    if do >= 4:
                        e.wait_ge(ev_sem, v0 + do - 3)
                    for fc in range(32):
                        last = e.matmul(ACC[do % 4],
                                        w2_sb[:, fc * D + do * 128:
                                              fc * D + (do + 1) * 128],
                                        h_sb[:, fc * SS:(fc + 1) * SS],
                                        start=(fc == 0), stop=(fc == 31))
                    last.then_inc(et_sem, 1)
                    l._e_t += 1
                return None

            def loopV2(e, l, ee=ee, s=s):
                t0 = l._e_t - NCH
                for do in range(NCH):
                    e.wait_ge(et_sem, t0 + do + 1)
                    e.scalar_tensor_tensor(out=tmp_bf[:, 0:SS], in0=ACC[do % 4],
                                           scalar=b2_sb[:, ee * 8 + do:ee * 8 + do + 1],
                                           in1=msk[:, 0:SS],
                                           op0=ALU.add, op1=ALU.mult)
                    e.drain()
                    i = e.tensor_tensor(
                        out=xb[:, do * TC + s * SS:do * TC + (s + 1) * SS],
                        in0=xb[:, do * TC + s * SS:do * TC + (s + 1) * SS],
                        in1=tmp_bf[:, 0:SS], op=ALU.add)
                    e.drain()
                    i.then_inc(ev_sem, 1)
                    l._e_v += 1
                return None

            if s == TC // SS - 1 and ee < 7:
                def p_pf1(e, l, ee=ee):
                    l.dma("gpsimd", e, w1_3, w1v[ee + 1], pf=f"w1_{ee + 1}")
                l.phase({"tensor": loopT2, "vector": loopV2, "gpsimd": p_pf1})
            else:
                l.phase({"tensor": loopT2, "vector": loopV2})

    for ci in range(NCH):
        for hf in range(4):
            def p_x2(e, l, ci=ci, hf=hf):
                l.dma("sync", e, sq[:, 0:512],
                      x2d[ci * 128:(ci + 1) * 128, hf * 512:(hf + 1) * 512])
            l.phase({"sync": p_x2})
            for s4 in range(4):
                def p_add(e, l, ci=ci, hf=hf, s4=s4):
                    e.tensor_tensor(out=zt2[:], in0=sq[:, s4 * 128:(s4 + 1) * 128],
                                    in1=xb[:, ci * TC + hf * 512 + s4 * 128:
                                            ci * TC + hf * 512 + (s4 + 1) * 128],
                                    op=ALU.add)
                l.phase({"vector": p_add})
                l.phase({"gpsimd": (lambda e, l, ci=ci, hf=hf, s4=s4:
                                    l.dma("gpsimd", e,
                                          z2_o[ci * 128:(ci + 1) * 128,
                                               hf * 512 + s4 * 128:hf * 512 + (s4 + 1) * 128],
                                          zt2[:]))})

    def p_fin(e, l):
        e.wait_ge(l.dsem["gpsimd"], l.dcnt["gpsimd"])
    l.phase({"gpsimd": p_fin})

    if max_phases is not None:
        l._phases = l._phases[:max_phases]
    nc = l.build()
    l.close()
    return nc



def build_launchF(max_phases=None):
    l = L("launchF")
    z_i = l.inp("z2", [D, TC], F32)
    wqt = l.inp("wqt", [D, D], F32)
    wkt = l.inp("wkt", [D, D], F32)
    wvt = l.inp("wvt", [D, D], F32)
    cosb = l.inp("cosb", [128, TC], F32)
    sinb = l.inp("sinb", [128, TC], F32)
    x3_o = l.out("x3", [D, TC], F32)
    q_o = l.out("q", [D, TC], BF16)
    k_o = l.out("k", [D, TC], BF16)
    v_o = l.out("v", [D, TC], BF16)

    xb = l.sbuf("xb", [128, NCH * TC], BF16)
    w_sb = [l.sbuf(f"w{i}", [128, NCH * D], BF16) for i in range(2)]
    out_sb = l.sbuf("outb", [128, NCH * TC], BF16)
    cos_sb = l.sbuf("cos_sb", [128, TC], F32)
    sin_sb = l.sbuf("sin_sb", [128, TC], F32)
    t1 = l.sbuf("t1", [128, TC], BF16)
    t2 = l.sbuf("t2", [128, TC], BF16)
    sq = l.sbuf("sq", [128, TC], BF16)
    ones_bf = l.sbuf("ones_bf", [128, 1], BF16)
    ones_row = l.sbuf("ones_row", [1, 128], F32)
    r0 = l.sbuf("r0", [1, TC], F32)
    r1 = l.sbuf("r1", [1, TC], F32)
    cst = l.sbuf("cst", [128, 1], F32)

    u1 = l.psum("u1", [128, TC], F32)
    pp = [l.psum(f"pp{i}", [128, 512], F32) for i in range(2)]

    wv_views = {
        "q": wqt.rearrange("(c p) o -> p c o", p=128),
        "k": wkt.rearrange("(c p) o -> p c o", p=128),
        "v": wvt.rearrange("(c p) o -> p c o", p=128),
    }
    w_sb3 = [w_sb[i].ap().rearrange("p (c o) -> p c o", c=NCH) for i in range(2)]

    def p0(e, l):
        l.dma("sync", e, cos_sb[:], cosb)
        l.dma("sync", e, sin_sb[:], sinb)
    def p0g(e, l):
        l.dma("gpsimd", e, w_sb3[0], wv_views["q"], pf="q")
    def p0v(e, l):
        e.memset(ones_bf[:], 1.0)
        e.memset(ones_row[:], 1.0)
        e.memset(cst[:], 1e-5)
    l.phase({"sync": p0, "gpsimd": p0g, "vector": p0v})

    def after_chunk(ci):
        return {"gpsimd": (lambda e, l, ci=ci:
                           l.dma("gpsimd", e, x3_o[ci * 128:(ci + 1) * 128, :],
                                 xb[:, ci * TC:(ci + 1) * TC]))}
    emit_ln_start(l, z_i, xb, None, sq, u1, ones_bf, ones_row, (r0, r1),
                  cst[0:1, 0:1], after_chunk=after_chunk)

    def rope(src):
        shf = [(i + 16) % 32 for i in range(32)]
        for ci in range(NCH):
            sl = slice(ci * TC, (ci + 1) * TC)

            def p_r1(e, l, sl=sl):
                e.stream_shuffle(t2[:], src[:, sl], shf)
                e.tensor_tensor(out=t1[:], in0=src[:, sl], in1=cos_sb[:], op=ALU.mult)
                e.drain()
                e.tensor_tensor(out=t2[:], in0=t2[:], in1=sin_sb[:], op=ALU.mult)
                e.drain()
                e.tensor_tensor(out=src[:, sl], in0=t1[:], in1=t2[:], op=ALU.add)
            l.phase({"vector": p_r1})

    def store(out_ap):
        def p_st(e, l):
            for ci in range(NCH):
                l.dma("gpsimd", e, out_ap[ci * 128:(ci + 1) * 128, :],
                      out_sb[:, ci * TC:(ci + 1) * TC])
        l.phase({"gpsimd": p_st})

    emit_proj(l, "q", "k", w_sb[0], w_sb3[1], wv_views["k"], xb, out_sb, pp)
    rope(out_sb)
    store(q_o)
    emit_proj(l, "k", "v", w_sb[1], w_sb3[0], wv_views["v"], xb, out_sb, pp)
    rope(out_sb)
    store(k_o)
    emit_proj(l, "v", None, w_sb[0], None, None, xb, out_sb, pp)
    store(v_o)

    def p_fin(e, l):
        e.wait_ge(l.dsem["gpsimd"], l.dcnt["gpsimd"])
    l.phase({"gpsimd": p_fin})

    if max_phases is not None:
        l._phases = l._phases[:max_phases]
    nc = l.build()
    l.close()
    return nc



def build_launchG(max_phases=None):
    l = L("launchG")
    NKT = T // 128
    qr = l.inp("qr", [D, TC], BF16)
    kh = l.inp("kh", [16, 64, T], BF16)
    vh = l.inp("vh", [16, T, 64], BF16)
    x3 = l.inp("x3", [D, TC], F32)
    wot = l.inp("wot", [D, D], F32)
    z3_o = l.out("z3", [D, TC], F32)

    q_sb = l.sbuf("q_sb", [128, NCH * TC], BF16)
    kd_sb = l.sbuf("kd_sb", [128, T], BF16)
    va_sb = l.sbuf("va_sb", [128, NKT * 64], BF16)
    vb_sb = l.sbuf("vb_sb", [128, NKT * 64], BF16)
    ctx_sb = l.sbuf("ctx_sb", [128, NCH * TC], BF16)
    sqk = l.sbuf("sqk", [128, T], BF16)
    et = [l.sbuf(f"et{i}", [128, 1024], BF16) for i in range(2)]
    wot_sb = l.sbuf("wot_sb", [128, NCH * D], BF16)
    bcs = l.sbuf("bcs", [128, 512], F32)
    rws = l.sbuf("rws", [1, 1024], F32)
    mq2 = l.sbuf("mq2", [1, 16], F32)
    mk2 = l.sbuf("mk2", [1, 16], F32)
    sc1 = l.sbuf("sc1", [1, 8], F32)
    nbias = l.sbuf("nbias", [128, 1], F32)
    xc = l.sbuf("xc", [128, 512], F32)
    zt = l.sbuf("zt", [128, 512], F32)
    ones_bf = l.sbuf("ones_bf", [128, 1], BF16)
    ones_row = l.sbuf("ones_row", [1, 128], F32)

    sc = [l.psum(f"scps{i}", [128, 1024], F32) for i in range(2)]
    ctxp = l.psum("ctxp", [128, 1024], F32)
    denp = l.psum("denp", [128, 1024], F32)

    tsem = l.es.enter_context(l.nc.semaphore("g_t"))
    ssem = l.es.enter_context(l.nc.semaphore("g_s"))
    l._g_t = 0
    l._g_s = 0

    q_v = qr.rearrange("(c p) t -> p c t", p=128)
    q_sb3 = q_sb.ap().rearrange("p (c t) -> p c t", c=NCH)
    wot_v = wot.rearrange("(c p) o -> p c o", p=128)
    wot_3 = wot_sb.ap().rearrange("p (c o) -> p c o", c=NCH)

    def p0(e, l):
        l.dma("sync", e, q_sb3, q_v)
    def p0g(e, l):
        l.dma("gpsimd", e, wot_3, wot_v, pf="wo")
    def p0v(e, l):
        e.memset(ones_bf[:], 1.0)
        e.memset(ones_row[:], 1.0)
    l.phase({"sync": p0, "gpsimd": p0g, "vector": p0v})

    for ci in range(NCH):
        l.phase({"scalar": (lambda e, l, ci=ci:
                            e.activation(out=sqk[:, 0:TC],
                                         in_=q_sb[:, ci * TC:(ci + 1) * TC],
                                         func=AF.Square))})
        for hh in range(2):
            def p_cs(e, l, hh=hh):
                i = None
                for j in range(0, TC, 512):
                    i = e.matmul(ctxp[0:1, 0:512],
                                 ones_bf[hh * 64:(hh + 1) * 64, 0:1],
                                 sqk[hh * 64:(hh + 1) * 64, j:j + 512],
                                 start=(j == 0), stop=(j == TC - 512))
                return i
            l.phase({"tensor": p_cs})

            def p_mx(e, l, ci=ci, hh=hh):
                pass
            def p_mx2(e, l, ci=ci, hh=hh):
                e.reduce_max(mq2[0:1, ci * 2 + hh:ci * 2 + hh + 1],
                             ctxp[0:1, 0:512], axis=AX.X)
            l.phase({"vector": p_mx2})

    for dj in range(NCH):
        def p_ld(e, l, dj=dj):
            l.dma("sync", e, kd_sb[:],
                  kh.rearrange("h p t -> (h p) t")[dj * 128:(dj + 1) * 128, :])
            l.dma("sync", e, va_sb.ap().rearrange("p (n d) -> p n d", d=64),
                  vh.rearrange("h (n p) d -> h p n d", p=128)[2 * dj])
            l.dma("sync", e, vb_sb.ap().rearrange("p (n d) -> p n d", d=64),
                  vh.rearrange("h (n p) d -> h p n d", p=128)[2 * dj + 1])
        l.phase({"sync": p_ld})

        l.phase({"scalar": (lambda e, l: e.activation(out=sqk[:], in_=kd_sb[:],
                                                      func=AF.Square))})
        bz_mk = {}

        def mk_loopT(e, l, dj=dj, bz=bz_mk):
            bz.setdefault("t", l._g_t)
            bz.setdefault("s", l._g_s)
            rows = [ctxp[0:1, 0:512], denp[0:1, 0:512]]
            for i2 in range(32):
                hh, pc = i2 // 16, i2 % 16
                if i2 >= 2:
                    e.wait_ge(ssem, bz["s"] + i2 - 1)
                e.matmul(rows[i2 % 2], ones_bf[hh * 64:(hh + 1) * 64, 0:1],
                         sqk[hh * 64:(hh + 1) * 64, pc * 512:(pc + 1) * 512],
                         start=True, stop=True).then_inc(tsem, 1)
                l._g_t += 1
            return None

        def mk_loopV(e, l, dj=dj, bz=bz_mk):
            bz.setdefault("t", l._g_t)
            bz.setdefault("s", l._g_s)
            rows = [ctxp[0:1, 0:512], denp[0:1, 0:512]]
            for i2 in range(32):
                hh, pc = i2 // 16, i2 % 16
                e.wait_ge(tsem, bz["t"] + i2 + 1)
                e.reduce_max(sc1[0:1, 0:1], rows[i2 % 2], axis=AX.X)
                e.drain()
                if pc == 0:
                    i = e.tensor_copy(mk2[0:1, dj * 2 + hh:dj * 2 + hh + 1],
                                      sc1[0:1, 0:1])
                else:
                    i = e.tensor_tensor(out=mk2[0:1, dj * 2 + hh:dj * 2 + hh + 1],
                                        in0=mk2[0:1, dj * 2 + hh:dj * 2 + hh + 1],
                                        in1=sc1[0:1, 0:1], op=ALU.max)
                e.drain()
                i.then_inc(ssem, 1)
                l._g_s += 1
            return None

        l.phase({"tensor": mk_loopT, "vector": mk_loopV})

        def p_nb1(e, l, dj=dj):
            e.tensor_tensor(out=sc1[0:1, 0:2], in0=mq2[0:1, dj * 2:dj * 2 + 2],
                            in1=mk2[0:1, dj * 2:dj * 2 + 2], op=ALU.mult)
            e.drain()
            e.reduce_max(sc1[0:1, 2:3], sc1[0:1, 0:2], axis=AX.X)
        l.phase({"vector": p_nb1})
        l.phase({"scalar": (lambda e, l:
                            e.activation(out=sc1[0:1, 3:4], in_=sc1[0:1, 2:3],
                                         func=AF.Sqrt))})
        l.phase({"vector": (lambda e, l:
                            e.tensor_scalar_mul(sc1[0:1, 4:5], sc1[0:1, 3:4], -0.125))})
        l.phase({"tensor": (lambda e, l:
                            e.matmul(ctxp[:, 0:1], ones_row[0:1, :], sc1[0:1, 4:5],
                                     start=True, stop=True))})
        l.phase({"scalar": (lambda e, l: e.copy(out=nbias[:], in_=ctxp[:, 0:1]))})

        for qs in range(TC // 512):
            qsl = slice(dj * TC + qs * 512, dj * TC + (qs + 1) * 512)

            def mk_scores(e, kt, qsl=qsl):
                i = None
                for hh in range(2):
                    i = e.matmul(
                        sc[kt % 2][:, hh * 512:(hh + 1) * 512],
                        kd_sb[hh * 64:(hh + 1) * 64, kt * 128:(kt + 1) * 128],
                        q_sb.ap()[hh * 64:(hh + 1) * 64, qsl],
                        start=True, stop=True,
                        tile_position=(hh * 64, 0))
                return i

            def mk_pv(e, kt):
                first = (kt == 0)
                e.matmul(ctxp[0:64, 0:512], va_sb[:, kt * 64:(kt + 1) * 64],
                         et[kt % 2][:, 0:512],
                         start=first, stop=(kt == NKT - 1),
                         tile_position=(0, 0), skip_group_check=True)
                e.matmul(ctxp[64:128, 512:1024], vb_sb[:, kt * 64:(kt + 1) * 64],
                         et[kt % 2][:, 512:1024],
                         start=first, stop=(kt == NKT - 1),
                         tile_position=(0, 64), skip_group_check=True)
                e.matmul(denp[0:1, 0:512], ones_bf[:, 0:1], et[kt % 2][:, 0:512],
                         start=first, stop=(kt == NKT - 1),
                         tile_position=(0, 0), skip_group_check=True)
                return e.matmul(denp[0:1, 512:1024], ones_bf[:, 0:1],
                                et[kt % 2][:, 512:1024],
                                start=first, stop=(kt == NKT - 1),
                                tile_position=(0, 0), skip_group_check=True)

            def loopT(e, l, mk_scores=mk_scores, mk_pv=mk_pv):
                t0, s0 = l._g_t, l._g_s
                for kt in range(NKT):
                    if kt >= 2:
                        e.wait_ge(ssem, s0 + kt - 1)
                    mk_scores(e, kt).then_inc(tsem, 1)
                    l._g_t += 1
                    if kt >= 1:
                        e.wait_ge(ssem, s0 + kt)
                        mk_pv(e, kt - 1)
                e.wait_ge(ssem, s0 + NKT)
                return mk_pv(e, NKT - 1)

            def loopS(e, l):
                t0 = l._g_t - NKT
                for kt in range(NKT):
                    e.wait_ge(tsem, t0 + kt + 1)
                    e.activation(out=et[kt % 2][:], in_=sc[kt % 2][:],
                                 func=AF.Exp, scale=0.125,
                                 bias=nbias[:, 0:1]).then_inc(ssem, 1)
                    l._g_s += 1
                return None

            l.phase({"tensor": loopT, "scalar": loopS})

            def p_rc(e, l):
                e.reciprocal(rws[0:1, :], denp[0:1, :])
            l.phase({"vector": p_rc})

            def p_bc(e, l):
                e.matmul(sc[0][0:64, 0:512], ones_row[0:1, 0:64], rws[0:1, 0:512],
                         start=True, stop=True, tile_position=(0, 0))
                return e.matmul(sc[0][64:128, 0:512], ones_row[0:1, 0:64],
                                rws[0:1, 512:1024],
                                start=True, stop=True, tile_position=(0, 64))
            l.phase({"tensor": p_bc})
            l.phase({"scalar": (lambda e, l: e.copy(out=bcs[:], in_=sc[0][:, 0:512]))})

            def p_nm(e, l, qsl=qsl):
                e.tensor_tensor(out=ctx_sb.ap()[0:64, qsl], in0=ctxp[0:64, 0:512],
                                in1=bcs[0:64, :], op=ALU.mult)
                e.drain()
                e.tensor_tensor(out=ctx_sb.ap()[64:128, qsl],
                                in0=ctxp[64:128, 512:1024],
                                in1=bcs[64:128, :], op=ALU.mult)
            l.phase({"vector": p_nm})

    for do in range(NCH):
        for nt in range(TC // 512):
            def p_mm(e, l, do=do, nt=nt, first=(do == 0 and nt == 0)):
                if first:
                    l.wait_pf(e, "wo")
                i = None
                for ci in range(NCH):
                    i = e.matmul(ctxp[:, 0:512],
                                 wot_sb[:, ci * D + do * 128:ci * D + (do + 1) * 128],
                                 ctx_sb[:, ci * TC + nt * 512:ci * TC + (nt + 1) * 512],
                                 start=(ci == 0), stop=(ci == NCH - 1))
                return i

            def p_xc(e, l, do=do, nt=nt):
                l.dma("sync", e, xc[:], x3[do * 128:(do + 1) * 128, nt * 512:(nt + 1) * 512])
            l.phase({"tensor": p_mm, "sync": p_xc})

            def p_z(e, l):
                e.scalar_tensor_tensor(out=zt[:], in0=ctxp[:, 0:512], scalar=1.0,
                                       in1=xc[:], op0=ALU.mult, op1=ALU.add)
            l.phase({"vector": p_z})
            l.phase({"gpsimd": (lambda e, l, do=do, nt=nt:
                                l.dma("gpsimd", e,
                                      z3_o[do * 128:(do + 1) * 128, nt * 512:(nt + 1) * 512],
                                      zt[:]))})

    def p_fin(e, l):
        e.wait_ge(l.dsem["gpsimd"], l.dcnt["gpsimd"])
    l.phase({"gpsimd": p_fin})

    if max_phases is not None:
        l._phases = l._phases[:max_phases]
    nc = l.build()
    l.close()
    return nc



def build_launchH(max_phases=None):
    l = L("launchH")
    z_i = l.inp("z3", [D, TC], F32)
    wut = l.inp("wut", [D, 4096], F32)
    wdt = l.inp("wdt", [4096, D], F32)
    out_o = l.out("out", [D, TC], F32)

    xb = l.sbuf("xb", [128, NCH * TC], BF16)
    wu_sb = l.sbuf("wu_sb", [128, NCH * 4096], BF16)
    wd_sb = l.sbuf("wd_sb", [128, 32 * D], BF16)
    hs = l.sbuf("hs", [128, 32 * 256], BF16)
    sq = l.sbuf("sq", [128, 512], BF16)
    tmp = l.sbuf("tmp", [128, 256], BF16)
    zt = l.sbuf("zt", [128, 256], F32)
    rbc1 = l.sbuf("rbc1", [128, 256], F32)
    rbc2 = l.sbuf("rbc2", [128, 256], F32)
    acc = l.sbuf("acc", [128, 32], F32)
    tau = l.sbuf("tau", [128, 2], F32)
    onesf = l.sbuf("onesf", [128, 1], F32)
    ones_bf = l.sbuf("ones_bf", [128, 1], BF16)
    ones_row = l.sbuf("ones_row", [1, 128], F32)
    r0 = l.sbuf("r0", [1, TC], F32)
    r1 = l.sbuf("r1", [1, TC], F32)
    s1 = l.sbuf("s1", [1, 8], F32)
    cst = l.sbuf("cst", [128, 2], F32)

    u1 = l.psum("u1", [128, TC], F32)
    pp = [l.psum(f"pp{i}", [128, 512], F32) for i in range(2)]

    z4d = l.dram("z4d", [D, TC], F32)

    wsl = [l.sbuf(f"wsl{i}", [128, 1024], F32) for i in range(2)]
    sgn = hs.ap()[:, 0:1024]
    g1 = hs.ap()[:, 1024:2048]
    g1b = hs.ap()[:, 2048:3072]

    def p0v(e, l):
        e.memset(ones_bf[:], 1.0)
        e.memset(ones_row[:], 1.0)
        e.memset(onesf[:], 1.0)
        e.memset(cst[:, 0:1], 1e-5)
        e.memset(cst[:, 1:2], 1e-8)
    l.phase({"vector": p0v})

    emit_ln_start(l, z_i, xb, None, sq, u1, ones_bf, ones_row, (r0, r1),
                  cst[0:1, 0:1])

    def ternarize(w_in, R, C, w_out_sb):
        wv = w_in.rearrange("(c p) f -> p c f", p=128)
        nsl = (R // 128) * (C // 1024)

        def sl_view(k):
            c, s = k // (C // 1024), k % (C // 1024)
            return wv[:, c, s * 1024:(s + 1) * 1024], c * C + s * 1024

        l.phase({"sync": (lambda e, l: l.dma("sync", e, wsl[0][:], sl_view(0)[0]))})
        for k in range(nsl):
            work = {"scalar": (lambda e, l, k=k:
                               e.activation(out=g1 if k % 2 == 0 else g1b,
                                            in_=wsl[k % 2][:], func=AF.Abs,
                                            accum_out=acc[:, k:k + 1]))}
            if k + 1 < nsl:
                work["sync"] = (lambda e, l, k=k:
                                l.dma("sync", e, wsl[(k + 1) % 2][:], sl_view(k + 1)[0]))
            l.phase(work)
        def p_t1(e, l, nsl=nsl):
            return e.matmul(u1[0:1, 0:nsl], onesf[:, 0:1], acc[:, 0:nsl],
                            start=True, stop=True)
        l.phase({"tensor": p_t1})

        def p_t2(e, l, nsl=nsl):
            e.reduce_sum(s1[0:1, 0:1], u1[0:1, 0:nsl], axis=AX.X)
            e.drain()
            e.tensor_scalar_mul(s1[0:1, 1:2], s1[0:1, 0:1], 0.5 / (R * C))
            e.drain()
            e.tensor_scalar_mul(s1[0:1, 2:3], s1[0:1, 1:2], -1.0)
        l.phase({"vector": p_t2})

        def p_t3(e, l):
            e.matmul(u1[:, 0:1], ones_row[0:1, :], s1[0:1, 1:2], start=True, stop=True)
            return e.matmul(u1[:, 1:2], ones_row[0:1, :], s1[0:1, 2:3],
                            start=True, stop=True)
        l.phase({"tensor": p_t3})
        l.phase({"scalar": (lambda e, l: e.copy(out=tau[:], in_=u1[:, 0:2]))})

        l.phase({"sync": (lambda e, l: l.dma("sync", e, wsl[0][:], sl_view(0)[0]))})
        for k in range(nsl):
            ocol = sl_view(k)[1]
            work = {"scalar": (lambda e, l, k=k: e.sign(out=sgn[:], in_=wsl[k % 2][:]))}
            if k + 1 < nsl:
                work["sync"] = (lambda e, l, k=k:
                                l.dma("sync", e, wsl[(k + 1) % 2][:], sl_view(k + 1)[0]))
            l.phase(work)

            def p_g(e, l, k=k, ocol=ocol):
                e.tensor_scalar(out=g1[:], in0=wsl[k % 2][:], scalar1=tau[:, 0:1],
                                scalar2=None, op0=ALU.is_gt)
                e.drain()
                e.scalar_tensor_tensor(out=g1[:], in0=wsl[k % 2][:], scalar=tau[:, 1:2],
                                       in1=g1[:], op0=ALU.is_lt, op1=ALU.add)
                e.drain()
                e.tensor_tensor(out=w_out_sb[:, ocol:ocol + 1024],
                                in0=g1[:], in1=sgn[:], op=ALU.mult)
            l.phase({"vector": p_g})

    ternarize(wut, D, 4096, wu_sb)
    ternarize(wdt, 4096, D, wd_sb)

    for ci in range(NCH):
        for hf in range(4):
            l.phase({"scalar": (lambda e, l, ci=ci, hf=hf:
                                e.activation(out=sq[:, 0:512],
                                             in_=xb[:, ci * TC + hf * 512:
                                                     ci * TC + (hf + 1) * 512],
                                             func=AF.Square))})

            def p_cs(e, l, ci=ci, hf=hf):
                return e.matmul(u1[0:1, hf * 512:(hf + 1) * 512],
                                ones_bf[:, 0:1], sq[:, 0:512],
                                start=(ci == 0), stop=(ci == NCH - 1))
            l.phase({"tensor": p_cs})
    l.phase({"scalar": (lambda e, l:
                        e.activation(out=r0[0:1, :], in_=u1[0:1, :], func=AF.Sqrt,
                                     scale=1.0 / D))})

    def p_r1(e, l):
        e.tensor_scalar_add(r0[0:1, :], r0[0:1, :], 1e-8)
        e.drain()
        e.reciprocal(r0[0:1, :], r0[0:1, :])
    l.phase({"vector": p_r1})

    SS = 256
    ht_sem = l.es.enter_context(l.nc.semaphore("h_t"))
    hs_sem = l.es.enter_context(l.nc.semaphore("h_s"))
    hv_sem = l.es.enter_context(l.nc.semaphore("h_v"))
    hd_sem = l.es.enter_context(l.nc.semaphore("h_d"))
    l._h_t = 0
    l._h_s = 0
    l._h_v = 0
    l._h_d = 0
    ACC = [pp[0][:, 0:SS], pp[1][:, 0:SS], u1[:, 0:SS], u1[:, 512:512 + SS]]
    tmp2 = [tmp, l.sbuf("tmp_b", [128, 256], BF16)]
    zt2 = [zt, l.sbuf("zt_b", [128, 256], F32)]

    for s in range(TC // SS):
        l.phase({"tensor": (lambda e, l, s=s:
                            bcast(e, ones_row, r0[0:1, s * SS:(s + 1) * SS],
                                  u1[:, 1024:1024 + SS] if False else u1, SS))})
        l.phase({"scalar": (lambda e, l: e.copy(out=rbc1[:], in_=u1[:, 0:SS]))})

        def loopT1(e, l, s=s):
            v0 = l._h_v
            last = None
            for fc in range(32):
                if fc >= 4:
                    e.wait_ge(hv_sem, v0 + fc - 3)
                for ci in range(NCH):
                    last = e.matmul(ACC[fc % 4],
                                    wu_sb[:, ci * 4096 + fc * 128:
                                          ci * 4096 + (fc + 1) * 128],
                                    xb[:, ci * TC + s * SS:ci * TC + (s + 1) * SS],
                                    start=(ci == 0), stop=(ci == NCH - 1))
                last.then_inc(ht_sem, 1)
                l._h_t += 1
            return None

        def loopV1(e, l):
            t0, s0 = l._h_t - 32, l._h_s
            for fc in range(32):
                e.wait_ge(ht_sem, t0 + fc + 1)
                if fc >= 2:
                    e.wait_ge(hs_sem, s0 + fc - 1)
                i = e.scalar_tensor_tensor(out=tmp2[fc % 2][:], in0=ACC[fc % 4],
                                           scalar=1.0, in1=rbc1[:],
                                           op0=ALU.mult, op1=ALU.mult)
                e.drain()
                i.then_inc(hv_sem, 1)
                l._h_v += 1
            return None

        def loopS1(e, l):
            v0 = l._h_v - 32
            for fc in range(32):
                e.wait_ge(hv_sem, v0 + fc + 1)
                e.activation(out=hs[:, fc * SS:(fc + 1) * SS],
                             in_=tmp2[fc % 2][:],
                             func=AF.Gelu).then_inc(hs_sem, 1)
                l._h_s += 1
            return None

        l.phase({"tensor": loopT1, "vector": loopV1, "scalar": loopS1})

        bases_st = {}

        def loopS2(e, l, bz=bases_st):
            bz.setdefault("t", l._h_t)
            bz.setdefault("s", l._h_s)
            t0 = bz["t"]
            for fc in range(32):
                if fc >= 2:
                    e.wait_ge(ht_sem, t0 + fc - 1)
                e.activation(out=sq[:, (fc % 2) * SS:(fc % 2 + 1) * SS],
                             in_=hs[:, fc * SS:(fc + 1) * SS],
                             func=AF.Square).then_inc(hs_sem, 1)
                l._h_s += 1
            return None

        def loopT2s(e, l, bz=bases_st):
            bz.setdefault("t", l._h_t)
            bz.setdefault("s", l._h_s)
            s0 = bz["s"]
            last = None
            for fc in range(32):
                e.wait_ge(hs_sem, s0 + fc + 1)
                last = e.matmul(u1[0:1, 1024:1024 + SS], ones_bf[:, 0:1],
                                sq[:, (fc % 2) * SS:(fc % 2 + 1) * SS],
                                start=(fc == 0), stop=(fc == 31))
                if fc < 31:
                    last.then_inc(ht_sem, 1)
                    l._h_t += 1
            return last

        l.phase({"scalar": loopS2, "tensor": loopT2s})

        l.phase({"scalar": (lambda e, l:
                            e.activation(out=r1[0:1, 0:SS], in_=u1[0:1, 1024:1024 + SS],
                                         func=AF.Sqrt, scale=1.0 / 4096))})

        def p_r2(e, l):
            e.tensor_scalar_add(r1[0:1, 0:SS], r1[0:1, 0:SS], 1e-8)
            e.drain()
            e.reciprocal(r1[0:1, 0:SS], r1[0:1, 0:SS])
        l.phase({"vector": p_r2})
        l.phase({"tensor": (lambda e, l:
                            bcast(e, ones_row, r1[0:1, 0:SS], u1, SS))})
        l.phase({"scalar": (lambda e, l: e.copy(out=rbc2[:], in_=u1[:, 0:SS]))})

        bases_g2 = {}
        dma_marks = []

        def loopT3(e, l, s=s, bz=bases_g2):
            bz.setdefault("t", l._h_t)
            bz.setdefault("v", l._h_v)
            v0 = bz["v"]
            last = None
            for do in range(NCH):
                if do >= 4:
                    e.wait_ge(hv_sem, v0 + do - 3)
                for fc in range(32):
                    last = e.matmul(ACC[do % 4],
                                    wd_sb[:, fc * D + do * 128:fc * D + (do + 1) * 128],
                                    hs[:, fc * SS:(fc + 1) * SS],
                                    start=(fc == 0), stop=(fc == 31))
                last.then_inc(ht_sem, 1)
                l._h_t += 1
            return None

        def loopV3(e, l, s=s, bz=bases_g2, dm=dma_marks):
            bz.setdefault("t", l._h_t)
            bz.setdefault("v", l._h_v)
            bz.setdefault("d", l._h_d)
            t0, d0 = bz["t"], bz["d"]
            for do in range(NCH):
                e.wait_ge(ht_sem, t0 + do + 1)
                if do >= 2:
                    e.wait_ge(hd_sem, d0 + do - 1)
                e.scalar_tensor_tensor(out=zt2[do % 2][:], in0=ACC[do % 4],
                                       scalar=1.0, in1=rbc2[:],
                                       op0=ALU.mult, op1=ALU.mult)
                e.drain()
                i = e.tensor_tensor(out=zt2[do % 2][:], in0=zt2[do % 2][:],
                                    in1=xb[:, do * TC + s * SS:do * TC + (s + 1) * SS],
                                    op=ALU.add)
                e.drain()
                i.then_inc(hv_sem, 1)
                l._h_v += 1
            return None

        def loopG3(e, l, s=s, bz=bases_g2, dm=dma_marks):
            bz.setdefault("t", l._h_t)
            bz.setdefault("v", l._h_v)
            bz.setdefault("d", l._h_d)
            for do in range(NCH):
                e.wait_ge(hv_sem, bz["v"] + do + 1)
                t = l.dma("gpsimd", e,
                          z4d[do * 128:(do + 1) * 128, s * SS:(s + 1) * SS],
                          zt2[do % 2][:])
                e.wait_ge(l.dsem["gpsimd"], t)
                e.sem_inc(hd_sem, 1)
                l._h_d += 1
            return None

        l.phase({"tensor": loopT3, "vector": loopV3, "gpsimd": loopG3})

    def after_chunk(ci):
        return {"gpsimd": (lambda e, l, ci=ci:
                           l.dma("gpsimd", e, out_o[ci * 128:(ci + 1) * 128, :],
                                 xb[:, ci * TC:(ci + 1) * TC]))}
    emit_ln_start(l, z4d, xb, None, sq, u1, ones_bf, ones_row, (r0, r1),
                  cst[0:1, 0:1], after_chunk=after_chunk)

    def p_fin(e, l):
        e.wait_ge(l.dsem["gpsimd"], l.dcnt["gpsimd"])
    l.phase({"gpsimd": p_fin})

    if max_phases is not None:
        l._phases = l._phases[:max_phases]
    nc = l.build()
    l.close()
    return nc



def _fm(a):
    return np.ascontiguousarray(np.asarray(a, np.float32).T)


def _chunked(v):
    return np.ascontiguousarray(np.asarray(v, np.float32).reshape(NCH, 128).T)


def _rope_perm():
    perm = np.zeros(D, np.int64)
    for j in range(NCH):
        for r in range(128):
            qd = r // 32
            h = 2 * j + (qd // 2)
            fr = 16 * (qd % 2) + (r % 16)
            x2 = (r % 32) >= 16
            perm[j * 128 + r] = h * 64 + 2 * fr + (1 if x2 else 0)
    return perm


def _cos_sin(t0):
    hd = 64
    inv = 1.0 / 10000.0 ** (np.arange(0, hd, 2, dtype=np.float32) / hd)
    t_ = np.arange(t0, t0 + TC, dtype=np.float32)
    f = np.outer(inv, t_)
    cos = np.zeros((128, TC), np.float32)
    ssin = np.zeros((128, TC), np.float32)
    for r in range(128):
        qd = r // 32
        fr = 16 * (qd % 2) + (r % 16)
        x2 = (r % 32) >= 16
        cos[r] = np.cos(f[fr])
        ssin[r] = np.sin(f[fr]) * (1.0 if x2 else -1.0)
    return cos, ssin


_NC_CACHE = {}


def _nc(name, builder):
    if name not in _NC_CACHE:
        _NC_CACHE[name] = builder()
    return _NC_CACHE[name]


def kernel(**inp):
    import ml_dtypes
    BF = ml_dtypes.bfloat16
    f32 = lambda k: np.asarray(inp[k], np.float32)
    x = f32("L0_attn_x")
    cores = list(range(NCORE))

    n1w = _chunked(inp["L0_attn_norm1_w"])
    bw = _chunked(f32("L0_attn_beta_w")[:, 0])
    bb = np.broadcast_to(f32("L0_attn_beta_b").reshape(1, 1), (128, 1)).copy()
    imA = []
    for c in cores:
        b, qd = c // 4, c % 4
        imA.append({"x": _fm(x[b, qd * TC:(qd + 1) * TC]), "n1w": n1w,
                    "bw": bw, "bb": bb,
                    "wq": f32("L0_attn_Wq"), "wk": f32("L0_attn_Wk"),
                    "wv": f32("L0_attn_Wv")})
    rA = run_bass_kernel_spmd(_nc("A", build_launchA), imA, core_ids=cores).results

    qf = [np.concatenate([rA[b * 4 + qd]["q"] for qd in range(4)], 1) for b in range(B)]
    kf = [np.concatenate([rA[b * 4 + qd]["k"] for qd in range(4)], 1) for b in range(B)]
    vf = [np.concatenate([rA[b * 4 + qd]["v"] for qd in range(4)], 1) for b in range(B)]
    beta = [np.concatenate(
        [np.asarray(rA[b * 4 + qd]["beta_tm"], np.float32).T.reshape(-1)
         for qd in range(4)]) for b in range(B)]

    ident = np.eye(128, dtype=BF)
    nlmask = (-np.tril(np.ones((128, 128), np.float32), -1)).astype(BF)
    umaski = np.triu(np.ones((128, 128), np.float32), 0).astype(BF)
    al = f32("L0_attn_alpha_logit")
    imC = []
    for c in cores:
        b, s = c // 4, c % 4
        imC.append({
            "qt": np.ascontiguousarray(qf[b].reshape(NCH, 128, T // 128, 128)
                                       .transpose(2, 0, 1, 3)),
            "kt": np.ascontiguousarray(kf[b].reshape(NCH, 128, T // 128, 128)
                                       .transpose(2, 0, 1, 3)),
            "ktm": np.ascontiguousarray(kf[b].T.reshape(T // 128, 128, D)),
            "bcol": np.ascontiguousarray(beta[b].reshape(T // 128, 128).T),
            "va": np.ascontiguousarray(al[256 * s:256 * (s + 1)].reshape(2, 128).T),
            "vsl": np.ascontiguousarray(vf[b][256 * s:256 * (s + 1), :]),
            "ident": ident, "nlmask": nlmask, "umaski": umaski})
    rC = run_bass_kernel_spmd(_nc("C", build_launchC), imC, core_ids=cores).results
    attn = [np.concatenate([rC[b * 4 + s]["ao"] for s in range(4)], 0) for b in range(B)]

    n2w = _chunked(inp["L0_attn_norm2_w"])
    imD = []
    for c in cores:
        b, qd = c // 4, c % 4
        imD.append({"x": _fm(x[b, qd * TC:(qd + 1) * TC]),
                    "at": np.ascontiguousarray(attn[b][:, qd * TC:(qd + 1) * TC]),
                    "wo": f32("L0_attn_Wo"), "n2w": n2w,
                    "wg": f32("L0_attn_w_gate"), "wu": f32("L0_attn_w_up"),
                    "wd": f32("L0_attn_w_down")})
    rD = run_bass_kernel_spmd(_nc("D", build_launchD), imD, core_ids=cores).results

    b1 = f32("L0_ffn_b1")
    b2 = f32("L0_ffn_b2")
    b1c = np.ascontiguousarray(b1.reshape(8, 32, 128).transpose(2, 0, 1).reshape(128, 256))
    b2c = np.ascontiguousarray(b2.reshape(8, 8, 128).transpose(2, 0, 1).reshape(128, 64))
    imE = []
    for c in cores:
        imE.append({"z": np.asarray(rD[c]["z"], np.float32),
                    "wrt": np.ascontiguousarray(f32("L0_ffn_Wr").T),
                    "brr": np.ascontiguousarray(np.tile(f32("L0_ffn_br"), (128, 1))),
                    "w1t": np.ascontiguousarray(f32("L0_ffn_W1").transpose(0, 2, 1)),
                    "w2t": np.ascontiguousarray(f32("L0_ffn_W2").transpose(0, 2, 1)),
                    "b1c": b1c, "b2c": b2c, "ident": ident})
    rE = run_bass_kernel_spmd(_nc("E", build_launchE), imE, core_ids=cores).results

    perm = _rope_perm()
    wqt = np.ascontiguousarray(f32("L1_attn_W_Q").T[:, perm])
    wkt = np.ascontiguousarray(f32("L1_attn_W_K").T[:, perm])
    wvt = np.ascontiguousarray(f32("L1_attn_W_V").T)
    tabs = [_cos_sin(qd * TC) for qd in range(4)]
    imF = []
    for c in cores:
        b, qd = c // 4, c % 4
        imF.append({"z2": np.asarray(rE[c]["z2"], np.float32),
                    "wqt": wqt, "wkt": wkt, "wvt": wvt,
                    "cosb": tabs[qd][0], "sinb": tabs[qd][1]})
    rF = run_bass_kernel_spmd(_nc("F", build_launchF), imF, core_ids=cores).results

    khb, vhb = [], []
    for b in range(B):
        k_full = np.concatenate([rF[b * 4 + qd]["k"] for qd in range(4)], 1)
        v_full = np.concatenate([rF[b * 4 + qd]["v"] for qd in range(4)], 1)
        khb.append(np.ascontiguousarray(k_full.reshape(16, 64, T)))
        vhb.append(np.ascontiguousarray(v_full.reshape(16, 64, T).transpose(0, 2, 1)))

    wot = np.ascontiguousarray(f32("L1_attn_W_O").T)
    imG = []
    for c in cores:
        b, qd = c // 4, c % 4
        imG.append({"qr": np.asarray(rF[c]["q"]),
                    "kh": khb[b], "vh": vhb[b],
                    "x3": np.asarray(rF[c]["x3"], np.float32),
                    "wot": wot})
    rG = run_bass_kernel_spmd(_nc("G", build_launchG), imG, core_ids=cores).results

    wut = np.ascontiguousarray(f32("L1_ffn_w_up").T)
    wdt = np.ascontiguousarray(f32("L1_ffn_w_down").T)
    imH = [{"z3": np.asarray(rG[c]["z3"], np.float32), "wut": wut, "wdt": wdt}
           for c in cores]
    rH = run_bass_kernel_spmd(_nc("H", build_launchH), imH, core_ids=cores).results

    out = np.empty((B, T, D), np.float32)
    for c in cores:
        b, qd = c // 4, c % 4
        out[b, qd * TC:(qd + 1) * TC, :] = np.asarray(rH[c]["out"], np.float32).T
    return out


if __name__ == "__main__":
    pass
